# version 04
# baseline (speedup 1.0000x reference)
import numpy as np
from ml_dtypes import bfloat16
from contextlib import ExitStack

import concourse.bass as bass
import concourse.mybir as mybir
from concourse.bass_utils import run_bass_kernel_spmd
from concourse.alu_op_type import AluOpType

F32 = mybir.dt.float32
BF16 = mybir.dt.bfloat16
I32 = mybir.dt.int32
AF = mybir.ActivationFunctionType
AX = mybir.AxisListType
OP = AluOpType

NCORES = 8
T = 2048
D = 1024
MAGIC = 12582912.0

ENGINES = ("sync", "scalar", "vector", "tensor", "gpsimd")


class Prog:
    def __init__(self, nc):
        self.nc = nc
        self.items = []
        self.counts = {}
        self.last_write = {}
        self.readers = {}
        self.known = {}
        self.debug_log = []
        self.psum_regions = set()
        self.stage_marks = []
        for e in ENGINES:
            self.counts["c_" + e] = 0

    def op(self, engine, fn, reads=(), writes=(), dma=False, stream=""):
        sem = (("d_" + engine + stream) if dma else ("c_" + engine))
        inc = 16 if dma else 1
        if sem not in self.counts:
            self.counts[sem] = 0
        if dma:
            reads = list(reads) + ["__q" + sem]
            writes = list(writes) + ["__q" + sem]
        if self.psum_regions:
            pr = [r for r in reads if r in self.psum_regions]
            if pr:
                reads = [r for r in reads if r not in self.psum_regions]
                writes = list(writes) + pr
        waits = {}

        def need(dep):
            dsem, dval, dengine = dep
            if dsem.startswith("d_"):
                dval = self.counts[dsem]
            if dengine == engine == "tensor" and not dma:
                return
            if self.known.get((engine, dsem), 0) >= dval:
                return
            if waits.get(dsem, 0) < dval:
                waits[dsem] = dval

        for r in reads:
            lw = self.last_write.get(r)
            if lw:
                need(lw)
        for w in writes:
            lw = self.last_write.get(w)
            if lw:
                need(lw)
            for rd in self.readers.get(w, ()):
                need(rd)
        for dsem, dval in waits.items():
            k = (engine, dsem)
            self.known[k] = max(self.known.get(k, 0), dval)
        self.counts[sem] += inc
        me = (sem, self.counts[sem], engine)
        for w in writes:
            self.last_write[w] = me
            self.readers[w] = []
        for r in reads:
            self.readers.setdefault(r, []).append(me)
        self.items.append((engine, fn, sorted(waits.items()), (sem, inc)))
        self.debug_log.append((engine, sem, self.counts[sem], tuple(reads),
                               tuple(writes), sorted(waits.items())))

    def barrier(self):
        snapshot = dict(self.counts)
        for e in ENGINES:
            for sem, val in snapshot.items():
                if val > 0 and self.known.get((e, sem), 0) < val:
                    self.items.append((e, None, [(sem, val)], None))
                    self.known[(e, sem)] = val
        if self.stage_marks and self.stage_marks[-1] == len(self.items):
            return
        self.stage_marks.append(len(self.items))

    def emit(self, ctx):
        sems = {}
        for name, val in self.counts.items():
            if val > 0:
                sems[name] = ctx.enter_context(self.nc.semaphore(name))
        marks = list(self.stage_marks)
        if not marks or marks[-1] != len(self.items):
            marks.append(len(self.items))

        def section(name, seg):
            def _(eng):
                for engine, fn, waits, incspec in seg:
                    if engine != name:
                        continue
                    for semname, val in waits:
                        eng.wait_ge(sems[semname], val)
                    if fn is not None:
                        inst = fn(eng)
                        semname, inc = incspec
                        inst.then_inc(sems[semname], inc)
            return _

        start = 0
        for end in marks:
            seg = self.items[start:end]
            start = end
            if not seg:
                continue
            with self.nc.Block() as block:
                for name, reg in (("sync", block.sync), ("scalar", block.scalar),
                                  ("vector", block.vector),
                                  ("tensor", block.tensor),
                                  ("gpsimd", block.gpsimd)):
                    if any(it[0] == name for it in seg):
                        reg(section(name, seg))


def rep_ap(tensor, dims, elem_off, psl=None):
    base = tensor[:] if psl is None else tensor[psl[0]:psl[0] + psl[1], :]
    return bass.AP(base.tensor, base.offset + elem_off,
                   [list(base.ap[0])] + [list(d) for d in dims])


class BD:

    def __init__(self, nc, ctx):
        self.nc = nc
        self.ctx = ctx
        self.p = Prog(nc)
        self.consts = {}
        self._cpool = self.sb("cpool", (128, 32), F32)
        self._cpool_n = 0

    def sb(self, name, shape, dt=F32):
        return self.ctx.enter_context(self.nc.sbuf_tensor(name, list(shape), dt))

    def ps(self, name, shape=(128, 512), dt=F32):
        self.p.psum_regions.add(name)
        return self.ctx.enter_context(self.nc.psum_tensor(name, list(shape), dt))

    def dram(self, name, shape, dt=F32, kind="Internal"):
        return self.nc.dram_tensor(name, list(shape), dt, kind=kind).ap()

    def const(self, *vals):
        for v0 in vals:
            v0 = float(v0)
            v32 = float(np.float32(v0))
            if v0 in self.consts:
                continue
            i = self._cpool_n
            assert i < 32, "const pool full"
            self._cpool_n += 1
            cp = self._cpool
            self.p.op("gpsimd", lambda e, i=i, v=v32: e.memset(cp[:, i:i + 1], v),
                      writes=["cpool"])
            for v in {v0, v32}:
                self.nc.const_aps.register(F32, v, cp[:, i:i + 1])
                self.consts[v] = True


def emit_eye(B, dt, name):
    eye = B.sb(name, (128, 128), dt)
    p = B.p
    p.op("gpsimd", lambda e: e.memset(eye[:], 1.0), writes=[name])
    p.op("gpsimd", lambda e: e.affine_select(out=eye[:], in_=eye[:], compare_op=OP.is_ge,
                                             fill=0.0, base=0, pattern=[[-1, 128]],
                                             channel_multiplier=1),
         reads=[name], writes=[name])
    p.op("gpsimd", lambda e: e.affine_select(out=eye[:], in_=eye[:], compare_op=OP.is_ge,
                                             fill=0.0, base=0, pattern=[[1, 128]],
                                             channel_multiplier=-1),
         reads=[name], writes=[name])
    return eye


def load_w_tiles(B, name, dram_ap, rows, cols, dt=BF16, stream="w"):
    a = rows // 128
    w = B.sb(name, (128, a * cols), dt)
    B.p.op("gpsimd",
           lambda e: e.dma_start(
               out=w[:].rearrange("p (a f) -> p a f", a=a),
               in_=dram_ap.rearrange("(a p) f -> p a f", p=128)),
           writes=[name], dma=True, stream=stream)
    return w


class LNPack:

    def __init__(self, B, prefix, nslices=8):
        self.B = B
        self.n = nslices
        self.prefix = prefix
        self.z16 = B.sb(prefix + "_lnz16", (128, 2 * nslices * 512), BF16)
        self.stats = B.sb(prefix + "_lnst", (1, 2048), F32)
        self.ones_r1f = B.sb(prefix + "_o1f", (1, 128), F32)
        self.ones_c1b = B.sb(prefix + "_oc1", (128, 1), BF16)
        B.p.op("gpsimd", lambda e: e.memset(self.ones_r1f[:], 1.0),
               writes=[prefix + "_o1f"])
        B.p.op("gpsimd", lambda e: e.memset(self.ones_c1b[:], 1.0),
               writes=[prefix + "_oc1"])
        B.const(1e-5)

    def emit(self, z, zkey, psA, psB, psS1, psS2, out_writer):
        B, p, n, pre = self.B, self.B.p, self.n, self.prefix
        z16, stats = self.z16, self.stats
        kz16, ksq = pre + "_lnz16lo", pre + "_lnz16hi"
        kA, kB = psA.name, psB.name
        kS1, kS2 = psS1.name, psS2.name
        p.op("vector", lambda e: e.tensor_copy(z16[:, 0:n * 512], z[:, 0:n * 512]),
             reads=[zkey], writes=[kz16])
        p.op("scalar", lambda e: e.activation(z16[:, n * 512:2 * n * 512],
                                              z[:, 0:n * 512], AF.Square),
             reads=[zkey], writes=[ksq])
        for a in range(n):
            p.op("tensor", lambda e, a=a: e.matmul(
                psS1[0:1, :], self.ones_c1b[:, 0:1], z16[:, a * 512:(a + 1) * 512],
                start=(a == 0), stop=(a == n - 1)),
                reads=[kz16, pre + "_oc1"], writes=[kS1])
        for a in range(n):
            p.op("tensor", lambda e, a=a: e.matmul(
                psS2[0:1, :], self.ones_c1b[:, 0:1],
                z16[:, (n + a) * 512:(n + a + 1) * 512],
                start=(a == 0), stop=(a == n - 1)),
                reads=[ksq, pre + "_oc1"], writes=[kS2])
        inv_n = 1.0 / (n * 128.0)
        B.const(inv_n)
        mu, m2, var, rstd = (stats[0:1, i * 512:(i + 1) * 512] for i in range(4))
        kst = pre + "_lnst"
        p.op("vector", lambda e: e.tensor_scalar(mu, psS1[0:1, :], inv_n, None, OP.mult),
             reads=[kS1, "cpool"], writes=[kst + "mu"])
        p.op("scalar", lambda e: e.activation(m2, psS2[0:1, :], AF.Identity, scale=inv_n),
             reads=[kS2], writes=[kst + "m2"])
        p.op("vector", lambda e: e.tensor_tensor(var, mu, mu, OP.mult),
             reads=[kst + "mu"], writes=[kst + "var"])
        p.op("vector", lambda e: e.tensor_tensor(var, m2, var, OP.subtract),
             reads=[kst + "m2", kst + "var"], writes=[kst + "var"])
        p.op("scalar", lambda e: e.activation(var, var, AF.Ln, bias=1e-5),
             reads=[kst + "var", "cpool"], writes=[kst + "var"])
        p.op("scalar", lambda e: e.activation(rstd, var, AF.Exp, scale=-0.5),
             reads=[kst + "var"], writes=[kst + "rstd"])
        p.op("tensor", lambda e: e.matmul(psA[:], self.ones_r1f[0:1, :], mu,
                                          start=True, stop=True),
             reads=[kst + "mu", pre + "_o1f"], writes=[kA])
        p.op("tensor", lambda e: e.matmul(psB[:], self.ones_r1f[0:1, :], rstd,
                                          start=True, stop=True),
             reads=[kst + "rstd", pre + "_o1f"], writes=[kB])
        for a in range(n):
            p.op("vector", lambda e, a=a: e.tensor_tensor(
                z[:, a * 512:(a + 1) * 512], z[:, a * 512:(a + 1) * 512], psA[:],
                OP.subtract), reads=[zkey, kA], writes=[zkey])
        for a in range(n):
            p.op("vector", lambda e, a=a: e.tensor_tensor(
                z[:, a * 512:(a + 1) * 512], z[:, a * 512:(a + 1) * 512], psB[:],
                OP.mult), reads=[zkey, kB], writes=[zkey])
        out_writer()


def emit_stage1(B, io, nb=2, phases=('feat','attn','wo','ln')):
    nc, p = B.nc, B.p
    H, DEP = 8, 5

    wvo = B.sb("s1_wvo", (128, 8 * 1024), BF16)
    wp = load_w_tiles(B, "s1_wp", io["Wp"], 1024, 40)
    bprow = B.sb("s1_bp", (1, 40), F32)
    p.op("sync", lambda e: e.dma_start(out=bprow[:],
                                       in_=io["bp"].rearrange("(a f) -> a f", a=1)),
         writes=["s1_bp"], dma=True)
    ones_sq = B.sb("s1_onsq", (128, 128), BF16)
    p.op("gpsimd", lambda e: e.memset(ones_sq[:], 1.0), writes=["s1_onsq"])
    eye = emit_eye(B, BF16, "eye_bf")
    ln = LNPack(B, "s1")
    B.const(1.0, -1.0, 2.0)

    P0, P1, P2, P3, P4, P5 = (B.ps(f"s1_P{i}") for i in range(6))
    PTT = B.ps("s1_PT", (128, 1024), BF16)

    bpbc = B.sb("s1_bpbc", (128, 40), F32)
    p.op("tensor", lambda e: e.matmul(P5[:, 0:40], ln.ones_r1f[0:1, :], bprow[:],
                                      start=True, stop=True),
         reads=["s1_bp", "s1_o1f"], writes=["s1_P5"])
    p.op("vector", lambda e: e.tensor_copy(bpbc[:], P5[:, 0:40]),
         reads=["s1_P5"], writes=["s1_bpbc"])

    x16 = B.sb("s1_x16", (128, 8 * 512), BF16)
    V16 = B.sb("s1_V16", (128, 16 * 1024), BF16)
    FqT = B.sb("s1_FqT", (128, 4 * T), BF16)
    FkT = B.sb("s1_FkT", (128, 4 * T), BF16)
    At = B.sb("s1_At", (128, 16 * 512), BF16)
    ctxT = B.sb("s1_ctxT", (128, 8 * T), BF16)
    ptile = B.sb("s1_p", (128, 48), F32)
    uq = B.sb("s1_uq", (128, 80), F32)
    vk = B.sb("s1_vk", (128, 80), F32)
    Ftf = B.sb("s1_Ftf", (128, 512), F32)
    F16 = B.sb("s1_F16", (128, 512), BF16)
    zrec = B.sb("s1_zrec", (128, 512), F32)
    zln = B.sb("s1_zln", (128, 512), F32)
    xf32 = B.sb("s1_xf32", (128, 8 * 512), F32)
    zsb = B.sb("s1_z", (128, 8 * 512), F32)

    xv = io["xT"].rearrange("(a p) t -> p a t", p=128)
    yv = io["yT"].rearrange("(a p) t -> p a t", p=128)

    for b in range(nb):
        t0 = b * T
        p.op("gpsimd", lambda e: e.dma_start(
            out=wvo[:].rearrange("p (a f) -> p a f", a=8),
            in_=io["Wv"].rearrange("(a p) f -> p a f", p=128)),
            writes=["s1_wvo"], dma=True, stream="w")
        for c in range(4) if 'feat' in phases else []:
            p.op("gpsimd", lambda e, t0=t0, c=c: e.dma_start(
                out=x16[:].rearrange("p (a t) -> p a t", a=8),
                in_=xv[:, :, t0 + c * 512:t0 + c * 512 + 512]),
                writes=["s1_x16"], dma=True, stream="x")
            for ttc in range(4):
                tt = c * 4 + ttc
                for a in range(8):
                    p.op("tensor", lambda e, a=a, ttc=ttc: e.matmul(
                        P0[:, 0:40],
                        x16[:, a * 512 + ttc * 128:a * 512 + ttc * 128 + 128],
                        wp[:, a * 40:(a + 1) * 40], start=(a == 0), stop=(a == 7)),
                        reads=["s1_x16", "s1_wp"], writes=["s1_P0"])
                p.op("vector", lambda e: e.tensor_tensor(ptile[:, 0:40], P0[:, 0:40],
                                                         bpbc[:], OP.add),
                     reads=["s1_P0", "s1_bpbc"], writes=["s1_p"])
                p.op("scalar", lambda e: e.activation(ptile[:, 0:40], ptile[:, 0:40],
                                                      AF.Sigmoid),
                     reads=["s1_p"], writes=["s1_p"])
                uq0 = rep_ap(uq, [[10, 8], [2, 5], [1, 1]], 0)
                uq1 = rep_ap(uq, [[10, 8], [2, 5], [1, 1]], 1)
                vk0 = rep_ap(vk, [[10, 8], [2, 5], [1, 1]], 0)
                vk1 = rep_ap(vk, [[10, 8], [2, 5], [1, 1]], 1)
                pv = rep_ap(ptile, [[5, 8], [1, 5], [0, 1]], 0)
                p.op("vector", lambda e: e.tensor_scalar(uq0, pv, -1.0, 1.0,
                                                         OP.mult, OP.add),
                     reads=["s1_p", "cpool"], writes=["s1_uq"])
                p.op("vector", lambda e: e.tensor_scalar(uq1, pv, 2.0, -1.0,
                                                         OP.mult, OP.add),
                     reads=["s1_p", "cpool"], writes=["s1_uq"])
                p.op("gpsimd", lambda e: e.memset(vk0, 1.0), writes=["s1_vk"])
                p.op("vector", lambda e: e.tensor_copy(vk1, pv),
                     reads=["s1_p"], writes=["s1_vk"])
                for (usrc, ukey, Fdst, fkey) in ((uq, "s1_uq", FqT, "s1_FqT"),
                                                 (vk, "s1_vk", FkT, "s1_FkT")):
                    p.op("gpsimd", lambda e: e.memset(
                        rep_ap(Ftf, [[64, 8], [1, 2]], 62), 0.0),
                        writes=["s1_Ftf"])
                    p.op("vector", lambda e, u=usrc: e.tensor_copy(
                        rep_ap(Ftf, [[64, 8], [1, 2]], 0),
                        rep_ap(u, [[10, 8], [1, 2]], 0)),
                        reads=[ukey], writes=["s1_Ftf"])
                    sp, sc = 0, 2
                    for j in range(1, DEP):
                        w = 2 ** j
                        p.op("vector", lambda e, u=usrc, sp=sp, sc=sc, j=j, w=w:
                             e.tensor_tensor(
                                 rep_ap(Ftf, [[64, 8], [1, 2 * w]], sc),
                                 rep_ap(Ftf, [[64, 8], [1, w], [0, 2]], sp),
                                 rep_ap(u, [[10, 8], [0, w], [1, 2]], 2 * j),
                                 OP.mult),
                             reads=["s1_Ftf", ukey], writes=["s1_Ftf"])
                        sp, sc = sc, sc + 2 * w
                    p.op("vector", lambda e: e.tensor_copy(F16[:], Ftf[:]),
                         reads=["s1_Ftf"], writes=["s1_F16"])
                    for g in range(4):
                        p.op("tensor", lambda e, g=g: e.transpose(
                            PTT[:, g * 128:(g + 1) * 128],
                            F16[:, g * 128:(g + 1) * 128], eye[:]),
                            reads=["s1_F16", "eye_bf"], writes=["s1_PT"])
                    for g in range(4):
                        p.op("vector", lambda e, g=g, Fd=Fdst, tt=tt: e.tensor_copy(
                            Fd[:, g * T + tt * 128:g * T + tt * 128 + 128],
                            PTT[:, g * 128:(g + 1) * 128]),
                            reads=["s1_PT"], writes=[fkey])
                for vc in range(2):
                    ps, pk = ((P1, "s1_P1") if vc == 0 else (P2, "s1_P2"))
                    for a in range(8):
                        p.op("tensor", lambda e, a=a, ttc=ttc, vc=vc, ps=ps: e.matmul(
                            ps[:],
                            x16[:, a * 512 + ttc * 128:a * 512 + ttc * 128 + 128],
                            wvo[:, a * 1024 + vc * 512:a * 1024 + vc * 512 + 512],
                            start=(a == 0), stop=(a == 7)),
                            reads=["s1_x16", "s1_wvo"], writes=[pk])
                    p.op("vector", lambda e, tt=tt, vc=vc, ps=ps, pk=pk: e.tensor_copy(
                        V16[:, tt * 1024 + vc * 512:tt * 1024 + vc * 512 + 512],
                        ps[:]), reads=[pk], writes=["s1_V16"])

        p.op("gpsimd", lambda e: e.dma_start(
            out=wvo[:].rearrange("p (a f) -> p a f", a=8),
            in_=io["Wo"].rearrange("(a p) f -> p a f", p=128)),
            writes=["s1_wvo"], dma=True, stream="w")

        for h in range(8) if 'attn' in phases else []:
            g, hp = h // 2, (h % 2) * 64
            for qc in range(4):
                nkt = 4 * qc + 4
                for kt in range(nkt):
                    ps, pk = ((P0, "s1_P0") if kt % 2 == 0 else (P1, "s1_P1"))
                    p.op("tensor", lambda e, kt=kt, qc=qc, pss=ps, g=g, hp=hp: e.matmul(
                        pss[:],
                        FkT[hp:hp + 64, g * T + kt * 128:g * T + kt * 128 + 128],
                        FqT[hp:hp + 64, g * T + qc * 512:g * T + qc * 512 + 512],
                        start=True, stop=True),
                        reads=["s1_FkT", "s1_FqT"], writes=[pk])
                    p.op("scalar", lambda e, kt=kt, pss=ps: e.activation(
                        At[:, kt * 512:kt * 512 + 512], pss[:], AF.Exp, scale=0.2),
                        reads=[pk], writes=[f"s1_At#{kt}"])
                    if kt >= 4 * qc:
                        p.op("gpsimd", lambda e, kt=kt, qc=qc: e.affine_select(
                            out=At[:, kt * 512:kt * 512 + 512],
                            in_=At[:, kt * 512:kt * 512 + 512],
                            compare_op=OP.is_ge, fill=0.0,
                            base=qc * 512 - kt * 128, pattern=[[1, 512]],
                            channel_multiplier=-1),
                            reads=[f"s1_At#{kt}"], writes=[f"s1_At#{kt}"])
                for kt in range(nkt):
                    p.op("tensor", lambda e, kt=kt, h=h, nkt=nkt: e.matmul(
                        P2[:], V16[:, kt * 1024 + h * 128:kt * 1024 + h * 128 + 128],
                        At[:, kt * 512:kt * 512 + 512],
                        start=(kt == 0), stop=(kt == nkt - 1)),
                        reads=["s1_V16", f"s1_At#{kt}"], writes=["s1_P2"])
                for kt in range(nkt):
                    p.op("tensor", lambda e, kt=kt, nkt=nkt: e.matmul(
                        P3[:], ones_sq[:], At[:, kt * 512:kt * 512 + 512],
                        start=(kt == 0), stop=(kt == nkt - 1)),
                        reads=["s1_onsq", f"s1_At#{kt}"], writes=["s1_P3"])
                p.op("scalar", lambda e: e.activation(zln[:], P3[:], AF.Ln),
                     reads=["s1_P3"], writes=["s1_zln"])
                p.op("scalar", lambda e: e.activation(zrec[:], zln[:], AF.Exp,
                                                      scale=-1.0),
                     reads=["s1_zln"], writes=["s1_zrec"])
                p.op("vector", lambda e, h=h, qc=qc: e.tensor_tensor(
                    ctxT[:, h * T + qc * 512:h * T + qc * 512 + 512], P2[:],
                    zrec[:], OP.mult),
                    reads=["s1_P2", "s1_zrec"], writes=["s1_ctxT"])

        for c in range(4) if 'wo' in phases else []:
            p.op("sync", lambda e, c=c, t0=t0: e.dma_start(
                out=xf32[:].rearrange("p (a t) -> p a t", a=8),
                in_=xv[:, :, t0 + c * 512:t0 + c * 512 + 512]),
                writes=["s1_xf32"], dma=True, stream="r")
            for o in range(8):
                ps, pk = ((P0, "s1_P0") if o % 2 == 0 else (P1, "s1_P1"))
                for a in range(8):
                    p.op("tensor", lambda e, a=a, o=o, c=c, ps=ps: e.matmul(
                        ps[:], wvo[:, a * 1024 + o * 128:a * 1024 + o * 128 + 128],
                        ctxT[:, a * T + c * 512:a * T + c * 512 + 512],
                        start=(a == 0), stop=(a == 7)),
                        reads=["s1_wvo", "s1_ctxT"], writes=[pk])
                p.op("vector", lambda e, o=o, ps=ps, pk=pk: e.tensor_tensor(
                    zsb[:, o * 512:o * 512 + 512], ps[:],
                    xf32[:, o * 512:o * 512 + 512], OP.add),
                    reads=[pk, "s1_xf32"], writes=["s1_z"])

            def writer(c=c, t0=t0):
                B.p.op("sync", lambda e: e.dma_start(
                    out=yv[:, :, t0 + c * 512:t0 + c * 512 + 512],
                    in_=zsb[:].rearrange("p (a t) -> p a t", a=8)),
                    reads=["s1_z"], dma=True, stream="o")
            if 'ln' in phases:
                ln.emit(zsb, "s1_z", P4, P5, P0, P1, writer)
            else:
                writer()


def build_stage1_program(nb=2, phases=('feat','attn','wo','ln')):
    nc = bass.Bass("TRN2", target_bir_lowering=False)
    with ExitStack() as ctx:
        B = BD(nc, ctx)
        io = {
            "xT": B.dram("xT", (1024, nb * T), F32, kind="ExternalInput"),
            "yT": B.dram("yT", (1024, nb * T), F32, kind="ExternalOutput"),
            "Wp": B.dram("Wp", (1024, 40), F32, kind="ExternalInput"),
            "bp": B.dram("bp", (40,), F32, kind="ExternalInput"),
            "Wv": B.dram("Wv", (1024, 1024), F32, kind="ExternalInput"),
            "Wo": B.dram("Wo", (1024, 1024), F32, kind="ExternalInput"),
        }
        emit_stage1(B, io, nb=nb, phases=phases)
        B.p.barrier()
        B.p.emit(ctx)
    return nc




def emit_stage2(B, io, nb=2, phases=('wload', 'gates', 'moe', 'slot1')):
    nc, p = B.nc, B.p
    E, R, FF = 8, 64, 4096
    ln = LNPack(B, "s2")
    eye = emit_eye(B, BF16, "s2_eye")
    P0, P1, P2, P3, P4, P5 = (B.ps(f"s2_P{i}") for i in range(6))
    PTT = B.ps("s2_PT", (128, 1024), BF16)

    w1b = load_w_tiles(B, "s2_w1b", io["w1B"], 512, 4096)
    w2b = load_w_tiles(B, "s2_w2b", io["w2B"], 512, 1024)
    sel8 = B.sb("s2_sel8", (8, 512), BF16)
    p.op("gpsimd", lambda e: e.dma_start(out=sel8[:], in_=io["sel8"]),
         writes=["s2_sel8"], dma=True, stream="w")
    w1at = B.sb("s2_w1at", (128, 8 * 512), BF16)
    w2at = B.sb("s2_w2at", (128, 32 * 512), BF16)
    wtmp = B.sb("s2_wtmp", (128, 4096), BF16)
    for (src, wdst, kdst, nda) in (((io["w1A"], w1at, "s2_w1at", 8),
                                    (io["w2A"], w2at, "s2_w2at", 32))
                                   if 'wload' in phases else ()):
        for ra in range(4):
            p.op("gpsimd", lambda e, src=src, ra=ra, nda=nda: e.dma_start(
                out=wtmp[:, 0:nda * 128],
                in_=src[ra * 128:(ra + 1) * 128, :]),
                writes=["s2_wtmp"], dma=True, stream="w")
            for da in range(nda):
                p.op("tensor", lambda e, da=da: e.transpose(
                    PTT[:, 0:128], wtmp[:, da * 128:da * 128 + 128], eye[:]),
                    reads=["s2_wtmp", "s2_eye"], writes=["s2_PT"])
                p.op("vector", lambda e, ra=ra, da=da, w=wdst: e.tensor_copy(
                    w[:, da * 512 + ra * 128:da * 512 + ra * 128 + 128],
                    PTT[:, 0:128]), reads=["s2_PT"], writes=[kdst])

    gw = B.sb("s2_gw", (128, 32), F32)
    gi = B.sb("s2_gi", (128, 32), I32)
    gif = B.sb("s2_gif", (128, 32), F32)
    gmat = B.sb("s2_g", (128, 128), F32)
    g16 = B.sb("s2_g16", (128, 128), BF16)
    gT = B.sb("s2_gT", (8, 2048), BF16)
    gtmp = B.sb("s2_gtmp", (128, 48), F32)
    p.op("sync", lambda e: e.dma_start(
        out=gw[:].rearrange("p (tt k) -> p tt k", k=2),
        in_=io["moe_w"].rearrange("(tt p) k -> p tt k", p=128)),
        writes=["s2_gw"], dma=True)
    p.op("sync", lambda e: e.dma_start(
        out=gi[:].rearrange("p (tt k) -> p tt k", k=2),
        in_=io["moe_i"].rearrange("(tt p) k -> p tt k", p=128)),
        writes=["s2_gi"], dma=True)
    p.op("vector", lambda e: e.tensor_copy(gif[:], gi[:]), reads=["s2_gi"],
         writes=["s2_gif"])
    B.const(*[float(e) for e in range(8)])
    for e_ in range(E) if 'gates' in phases else []:
        for k in range(2):
            p.op("vector", lambda ee, e_=e_, k=k: ee.tensor_scalar(
                rep_ap(gtmp, [[3, 16], [1, 1]], k),
                rep_ap(gif, [[2, 16], [1, 1]], k), float(e_), None, OP.is_equal),
                reads=["s2_gif", "cpool"], writes=["s2_gtmp"])
            p.op("vector", lambda ee, e_=e_, k=k: ee.tensor_tensor(
                rep_ap(gtmp, [[3, 16], [1, 1]], k),
                rep_ap(gtmp, [[3, 16], [1, 1]], k),
                rep_ap(gw, [[2, 16], [1, 1]], k), OP.mult),
                reads=["s2_gtmp", "s2_gw"], writes=["s2_gtmp"])
        p.op("vector", lambda ee, e_=e_: ee.tensor_tensor(
            rep_ap(gmat, [[8, 16], [1, 1]], e_),
            rep_ap(gtmp, [[3, 16], [1, 1]], 0),
            rep_ap(gtmp, [[3, 16], [1, 1]], 1), OP.add),
            reads=["s2_gtmp"], writes=["s2_g"])
    p.op("vector", lambda e: e.tensor_copy(g16[:], gmat[:]), reads=["s2_g"],
         writes=["s2_g16"])
    for tt in range(16) if 'gates' in phases else []:
        p.op("tensor", lambda e, tt=tt: e.transpose(
            PTT[0:8, 0:128], g16[:, tt * 8:tt * 8 + 8], eye[:]),
            reads=["s2_g16", "s2_eye"], writes=["s2_PT"])
        p.op("vector", lambda e, tt=tt: e.tensor_copy(
            gT[0:8, tt * 128:tt * 128 + 128], PTT[0:8, 0:128]),
            reads=["s2_PT"], writes=["s2_gT"])

    y16 = B.sb("s2_y16", (128, 8 * 512), BF16)
    yf32 = B.sb("s2_yf32", (128, 8 * 512), F32)
    hc = B.sb("s2_hc", (128, 32 * 512), BF16)
    t1s = B.sb("s2_t1s", (128, 512), BF16)
    t2s = B.sb("s2_t2s", (128, 4 * 512), BF16)
    gbcp = B.sb("s2_gbcp", (128, 512), BF16)

    P6 = B.ps("s2_P6")

    yv = io["yT"].rearrange("(a p) t -> p a t", p=128)
    ov = io["y2T"].rearrange("(a p) t -> p a t", p=128)

    for tc in (range(4) if any(p.startswith('moe') for p in phases) else []):
        p.op("gpsimd", lambda e, tc=tc: e.dma_start(
            out=y16[:].rearrange("p (a t) -> p a t", a=8),
            in_=yv[:, :, tc * 512:tc * 512 + 512]),
            writes=["s2_y16"], dma=True, stream="x")
        for ra in (range(4) if 'moe_t1' in phases or 'moe' in phases else []):
            for da in range(8):
                p.op("tensor", lambda e, da=da, ra=ra: e.matmul(
                    P0[:], w1at[:, da * 512 + ra * 128:da * 512 + ra * 128 + 128],
                    y16[:, da * 512:da * 512 + 512],
                    start=(da == 0), stop=(da == 7)),
                    reads=["s2_w1at", "s2_y16"], writes=["s2_P0"])
            p.op("vector", lambda e: e.tensor_copy(t1s[:], P0[:]),
                 reads=["s2_P0"], writes=["s2_t1s"])
            p.op("tensor", lambda e, ra=ra, tc=tc: e.matmul(
                P4[:], sel8[:, ra * 128:ra * 128 + 128],
                gT[0:8, tc * 512:tc * 512 + 512], start=True, stop=True),
                reads=["s2_sel8", "s2_gT"], writes=["s2_P4"])
            p.op("vector", lambda e: e.tensor_copy(gbcp[:], P4[:]),
                 reads=["s2_P4"], writes=["s2_gbcp"])
            for par in (range(2) if 'moe_hc' in phases or 'moe' in phases else []):
                e_, ro = 2 * ra + par, par * 64
                for ff in range(32):
                    ps, pk = ((P1, "s2_P1") if ff % 2 == 0 else (P2, "s2_P2"))
                    p.op("tensor", lambda e, ff=ff, ra=ra, ro=ro, ps=ps: e.matmul(
                        ps[:], w1b[ro:ro + 64,
                                   ra * 4096 + ff * 128:ra * 4096 + ff * 128 + 128],
                        t1s[ro:ro + 64, :], start=True, stop=True),
                        reads=["s2_w1b", "s2_t1s"], writes=[pk])
                    p.op("scalar", lambda e, ff=ff, ps=ps, pk=pk: e.activation(
                        hc[:, ff * 512:ff * 512 + 512], ps[:], AF.Gelu),
                        reads=[pk], writes=[f"s2_hc#{ff}"])
                pt2, kt2 = ((P3, "s2_P3") if par == 0 else (P6, "s2_P6"))
                for ff in (range(32) if 'moe_t2' in phases or 'moe' in phases else []):
                    p.op("tensor", lambda e, ff=ff, e_=e_, pt2=pt2, ro=ro: e.matmul(
                        pt2[ro:ro + 64, :],
                        w2at[:, ff * 512 + e_ * 64:ff * 512 + e_ * 64 + 64],
                        hc[:, ff * 512:ff * 512 + 512],
                        start=(ff == 0), stop=(ff == 31),
                        tile_position=(0, ro)),
                        reads=["s2_w2at", f"s2_hc#{ff}"], writes=[kt2])
                if 'moe_t2' in phases or 'moe' in phases:
                    p.op("vector", lambda e, ra=ra, ro=ro, pt2=pt2, kt2=kt2:
                         e.tensor_tensor(
                             t2s[ro:ro + 64, ra * 512:ra * 512 + 512],
                             pt2[ro:ro + 64, :],
                             gbcp[ro:ro + 64, :], OP.mult),
                         reads=[kt2, "s2_gbcp"], writes=[f"s2_t2s#{ra}"])
        p.op("sync", lambda e, tc=tc: e.dma_start(
            out=yf32[:].rearrange("p (a t) -> p a t", a=8),
            in_=yv[:, :, tc * 512:tc * 512 + 512]),
            writes=["s2_yf32"], dma=True, stream="r")
        for o in (range(8) if 'moe_acc' in phases or 'moe' in phases else []):
            ps, pk = ((P0, "s2_P0") if o % 2 == 0 else (P1, "s2_P1"))
            for ra in range(4):
                p.op("tensor", lambda e, ra=ra, o=o, ps=ps: e.matmul(
                    ps[:], w2b[:, ra * 1024 + o * 128:ra * 1024 + o * 128 + 128],
                    t2s[:, ra * 512:ra * 512 + 512],
                    start=(ra == 0), stop=(ra == 3)),
                    reads=["s2_w2b", f"s2_t2s#{ra}"], writes=[pk])
            p.op("vector", lambda e, o=o, ps=ps, pk=pk: e.tensor_tensor(
                yf32[:, o * 512:o * 512 + 512], ps[:],
                yf32[:, o * 512:o * 512 + 512], OP.add),
                reads=[pk, "s2_yf32"], writes=["s2_yf32"])

        def writer0(tc=tc):
            B.p.op("sync", lambda e: e.dma_start(
                out=ov[:, :, tc * 512:tc * 512 + 512],
                in_=yf32[:].rearrange("p (a t) -> p a t", a=8)),
                reads=["s2_yf32"], dma=True, stream="o")
        if 'moe_ln' in phases or 'moe' in phases:
            ln.emit(yf32, "s2_yf32", P4, P5, P2, P3, writer0)

    for b in range(1, nb) if 'slot1' in phases else []:
        for tc in range(4):
            p.op("sync", lambda e, tc=tc, b=b: e.dma_start(
                out=yf32[:].rearrange("p (a t) -> p a t", a=8),
                in_=yv[:, :, b * T + tc * 512:b * T + tc * 512 + 512]),
                writes=["s2_yf32"], dma=True, stream="r")

            def writer1(tc=tc, b=b):
                B.p.op("sync", lambda e: e.dma_start(
                    out=ov[:, :, b * T + tc * 512:b * T + tc * 512 + 512],
                    in_=yf32[:].rearrange("p (a t) -> p a t", a=8)),
                    reads=["s2_yf32"], dma=True, stream="o")
            ln.emit(yf32, "s2_yf32", P4, P5, P2, P3, writer1)


def build_stage2_program(nb=2):
    nc = bass.Bass("TRN2", target_bir_lowering=False)
    with ExitStack() as ctx:
        B = BD(nc, ctx)
        io = {
            "yT": B.dram("yT", (1024, nb * T), F32, kind="ExternalInput"),
            "y2T": B.dram("y2T", (1024, nb * T), F32, kind="ExternalOutput"),
            "w1A": B.dram("w1A", (512, 1024), F32, kind="ExternalInput"),
            "w1B": B.dram("w1B", (512, 4096), F32, kind="ExternalInput"),
            "w2A": B.dram("w2A", (512, 4096), F32, kind="ExternalInput"),
            "w2B": B.dram("w2B", (512, 1024), F32, kind="ExternalInput"),
            "moe_w": B.dram("moe_w", (T, 2), F32, kind="ExternalInput"),
            "moe_i": B.dram("moe_i", (T, 2), I32, kind="ExternalInput"),
            "sel8": B.dram("sel8", (8, 512), F32, kind="ExternalInput"),
        }
        emit_stage2(B, io, nb=nb)
        B.p.barrier()
        B.p.emit(ctx)
    return nc




def emit_stage3(B, io, nb=2, phases=('p1k','p1v','p1f','p2q','p2f','p2t','p2w')):
    nc, p = B.nc, B.p
    NH = 16
    INV2PI = float(1.0 / (2.0 * np.pi))
    C2 = 1.0 / 128.0
    ln = LNPack(B, "s3")
    eye = emit_eye(B, BF16, "s3_eye")
    B.const(MAGIC, -MAGIC, 1e-6, C2, INV2PI, 0.25)

    P0, P1, P2 = B.ps("s3_P0"), B.ps("s3_P1"), B.ps("s3_P2")
    G0, G1, G2, G3 = B.ps("s3_G0"), B.ps("s3_G1"), B.ps("s3_G2"), B.ps("s3_G3")
    PTT = B.ps("s3_PT", (128, 1024), BF16)
    TSB = [G0, G1, G2, G3]
    TSK = ["s3_G0", "s3_G1", "s3_G2", "s3_G3"]

    wA = B.sb("s3_wA", (128, 8 * 1024), BF16)
    wB = B.sb("s3_wB", (128, 8 * 1024), BF16)

    def loadw(dst, key, src):
        p.op("gpsimd", lambda e, dst=dst, src=src: e.dma_start(
            out=dst[:].rearrange("p (a f) -> p a f", a=8),
            in_=io[src].rearrange("(a p) f -> p a f", p=128)),
            writes=[key], dma=True, stream="w")

    rff = B.sb("s3_rff", (128, 256), F32)
    rfbd = B.sb("s3_rfbd", (128, 256), BF16)
    p.op("gpsimd", lambda e: e.memset(rff[:], 0.0), writes=["s3_rff"])
    p.op("sync", lambda e: e.dma_start(out=rff[0:64, 0:128], in_=io["rf"]),
         writes=["s3_rff"], dma=True)
    p.op("sync", lambda e: e.dma_start(out=rff[64:128, 128:256], in_=io["rf"]),
         writes=["s3_rff"], dma=True)
    p.op("vector", lambda e: e.tensor_scalar(rfbd[:], rff[:], INV2PI, None,
                                             OP.mult),
         reads=["s3_rff", "cpool"], writes=["s3_rfbd"])

    biasq = B.sb("s3_biasq", (128, 1), F32)
    if 'nobias' not in phases:
        p.op("gpsimd", lambda e: e.memset(biasq[0:64, :], 0.25), writes=["s3_biasq"])
        p.op("gpsimd", lambda e: e.memset(biasq[64:128, :], 0.0), writes=["s3_biasq"])

    y16 = B.sb("s3_y16", (128, 8 * 512), BF16)
    qk16 = B.sb("s3_qk16", (128, 8 * 512), BF16)
    V16 = B.sb("s3_V16", (128, 16 * 1040), BF16)
    KV16 = B.sb("s3_KV16", (128, 16 * 65), BF16)
    php = B.sb("s3_php", (128, 2048), BF16)
    pt1 = B.sb("s3_pt1", (128, 512), F32)
    pt2 = B.sb("s3_pt2", (128, 512), F32)
    qp16 = B.sb("s3_qp16", (128, 512), BF16)
    zrq = B.sb("s3_zrq", (128, 16), F32)
    ao4 = B.sb("s3_ao4", (128, 4 * 1024), BF16)
    aoT = B.sb("s3_aoT", (128, 8 * 512), BF16)
    xf32 = B.sb("s3_xf32", (128, 8 * 512), F32)

    yv = io["y2T"].rearrange("(a p) t -> p a t", p=128)
    ov = io["y3T"].rearrange("(a p) t -> p a t", p=128)

    HG = [(0, 7), (7, 14), (14, 16)]

    if 'noones' not in phases:
        p.op("gpsimd", lambda e: e.memset(
            rep_ap(V16, [[1040, 16], [65, 16], [1, 1]], 64), C2), writes=["s3_V16o"])

    for b in range(nb):
        t0 = b * T
        loadw(wA, "s3_wA", "Wk")
        loadw(wB, "s3_wB", "Wv")
        for c in range(1 if 'tiny' in phases else 4):
            p.op("gpsimd", lambda e, c=c, t0=t0: e.dma_start(
                out=y16[:].rearrange("p (a t) -> p a t", a=8),
                in_=yv[:, :, t0 + c * 512:t0 + c * 512 + 512]),
                writes=["s3_y16"], dma=True, stream="x")
            for o in range(8) if 'p1k' in phases else []:
                for a in range(8):
                    p.op("tensor", lambda e, a=a, o=o: e.matmul(
                        P0[:], wA[:, a * 1024 + o * 128:a * 1024 + o * 128 + 128],
                        y16[:, a * 512:a * 512 + 512],
                        start=(a == 0), stop=(a == 7)),
                        reads=["s3_wA", "s3_y16"], writes=["s3_P0"])
                p.op("vector", lambda e, o=o: e.tensor_copy(
                    qk16[:, o * 512:o * 512 + 512], P0[:]),
                    reads=["s3_P0"], writes=["s3_qk16"])
            for ts in range(4) if 'p1v' in phases else []:
                tt = c * 4 + ts
                for vc in range(2):
                    for a in range(8):
                        p.op("tensor", lambda e, a=a, ts=ts, vc=vc: e.matmul(
                            P1[:], y16[:, a * 512 + ts * 128:a * 512 + ts * 128 + 128],
                            wB[:, a * 1024 + vc * 512:a * 1024 + vc * 512 + 512],
                            start=(a == 0), stop=(a == 7)),
                            reads=["s3_y16", "s3_wB"], writes=["s3_P1"])
                    p.op("vector", lambda e, tt=tt, vc=vc: e.tensor_scalar(
                        rep_ap(V16, [[65, 8], [1, 64]], tt * 1040 + vc * 520),
                        rep_ap(P1, [[64, 8], [1, 64]], 0), C2, None, OP.mult),
                        reads=["s3_P1", "cpool"], writes=["s3_V16"])
            for ts in (range(1 if 'tiny' in phases else 4) if 'p1f' in phases else []):
                tt = c * 4 + ts
                for hg in range(1 if 'tiny' in phases else 4):
                    for hh in (range(4) if 'noproj' not in phases else []):
                        h = hg * 4 + hh
                        a, pr = h // 2, h % 2
                        p.op("tensor", lambda e, a=a, pr=pr, ts=ts, hh=hh: e.matmul(
                            P2[:, hh * 128:hh * 128 + 128],
                            qk16[:, a * 512 + ts * 128:a * 512 + ts * 128 + 128],
                            rfbd[:, pr * 128:pr * 128 + 128], start=True, stop=True),
                            reads=["s3_qk16", "s3_rfbd"], writes=["s3_P2"])
                    if 'nophi' in phases:
                        continue
                    p.op("scalar", lambda e: e.activation(pt1[:], P2[:], AF.Identity),
                         reads=["s3_P2"], writes=["s3_pt1"])
                    if 'nophi2' not in phases:
                        p.op("vector", lambda e: e.tensor_scalar(
                            rep_ap(pt1, [[128, 4], [1, 64]], 0),
                            rep_ap(pt1, [[128, 4], [1, 64]], 0), 0.25, None, OP.add),
                            reads=["s3_pt1", "cpool"], writes=["s3_pt1"])
                        p.op("vector", lambda e: e.tensor_scalar(
                            pt2[:], pt1[:], MAGIC, None, OP.add),
                            reads=["s3_pt1", "cpool"], writes=["s3_pt2"])
                        p.op("vector", lambda e: e.tensor_scalar(
                            pt2[:], pt2[:], -MAGIC, None, OP.add),
                            reads=["s3_pt2", "cpool"], writes=["s3_pt2"])
                        p.op("vector", lambda e: e.tensor_tensor(
                            pt1[:], pt1[:], pt2[:], OP.subtract),
                            reads=["s3_pt1", "s3_pt2"], writes=["s3_pt1"])
                    p.op("scalar", lambda e, hg=hg: e.activation(
                        php[:, hg * 512:hg * 512 + 512], pt1[:],
                        AF.Sin if 'nosin' not in phases else AF.Identity,
                        scale=float(2.0 * np.pi) if 'nosin' not in phases else 1.0),
                        reads=["s3_pt1"], writes=["s3_php"])
                for h in (range(NH) if 'nokv' not in phases else []):
                    gi_ = 0 if h < 7 else (1 if h < 14 else 2)
                    h0 = HG[gi_][0]
                    sl = (h - h0) * 65
                    first = (c == 0 and ts == 0 and h == h0) or 'shortacc' in phases
                    last = (c == 3 and ts == 3 and h == HG[gi_][1] - 1) or 'shortacc' in phases
                    p.op("tensor", lambda e, h=h, tt=tt, gi_=gi_, sl=sl,
                         first=first, last=last: e.matmul(
                        TSB[gi_][:, sl:sl + 65], php[:, h * 128:h * 128 + 128],
                        V16[:, tt * 1040 + h * 65:tt * 1040 + h * 65 + 65],
                        start=first, stop=last),
                        reads=["s3_php", "s3_V16", "s3_V16o"], writes=[TSK[gi_]])
        for gi_, (h0, h1) in (enumerate(HG) if 'p1f' in phases else []):
            p.op("vector", lambda e, gi_=gi_, h0=h0, h1=h1: e.tensor_copy(
                KV16[:, h0 * 65:h1 * 65], TSB[gi_][:, 0:(h1 - h0) * 65]),
                reads=[TSK[gi_]], writes=["s3_KV16"])
        if b == 0 and "dbg_KV16" in io:
            p.op("sync", lambda e: e.dma_start(out=io["dbg_KV16"], in_=KV16[:]),
                 reads=["s3_KV16"], dma=True, stream="o")

        loadw(wA, "s3_wA", "Wq")
        loadw(wB, "s3_wB", "Wo")
        for c in range(4):
            p.op("gpsimd", lambda e, c=c, t0=t0: e.dma_start(
                out=y16[:].rearrange("p (a t) -> p a t", a=8),
                in_=yv[:, :, t0 + c * 512:t0 + c * 512 + 512]),
                writes=["s3_y16"], dma=True, stream="x")
            for o in range(8) if 'p2q' in phases else []:
                for a in range(8):
                    p.op("tensor", lambda e, a=a, o=o: e.matmul(
                        P0[:], wA[:, a * 1024 + o * 128:a * 1024 + o * 128 + 128],
                        y16[:, a * 512:a * 512 + 512],
                        start=(a == 0), stop=(a == 7)),
                        reads=["s3_wA", "s3_y16"], writes=["s3_P0"])
                p.op("vector", lambda e, o=o: e.tensor_copy(
                    qk16[:, o * 512:o * 512 + 512], P0[:]),
                    reads=["s3_P0"], writes=["s3_qk16"])
            for gi_, (h0, h1) in (enumerate(HG) if 'p2f' in phases else []):
                for h in range(h0, h1):
                    a, pr = h // 2, h % 2
                    p.op("tensor", lambda e, a=a, pr=pr: e.matmul(
                        P2[:], rfbd[:, pr * 128:pr * 128 + 128],
                        qk16[:, a * 512:a * 512 + 512],
                        start=True, stop=True),
                        reads=["s3_qk16", "s3_rfbd"], writes=["s3_P2"])
                    p.op("scalar", lambda e: e.activation(
                        pt1[:], P2[:], AF.Identity, bias=biasq[:, 0:1]),
                        reads=["s3_P2", "s3_biasq"], writes=["s3_pt1"])
                    p.op("vector", lambda e: e.tensor_scalar(
                        pt2[:], pt1[:], MAGIC, None, OP.add),
                        reads=["s3_pt1", "cpool"], writes=["s3_pt2"])
                    p.op("vector", lambda e: e.tensor_scalar(
                        pt2[:], pt2[:], -MAGIC, None, OP.add),
                        reads=["s3_pt2", "cpool"], writes=["s3_pt2"])
                    p.op("vector", lambda e: e.tensor_tensor(
                        pt1[:], pt1[:], pt2[:], OP.subtract),
                        reads=["s3_pt1", "s3_pt2"], writes=["s3_pt1"])
                    p.op("scalar", lambda e: e.activation(
                        qp16[:], pt1[:], AF.Sin, scale=float(2.0 * np.pi)),
                        reads=["s3_pt1"], writes=["s3_qp16"])
                    if b == 0 and c == 0 and h == 0 and "dbg_qp" in io:
                        p.op("sync", lambda e: e.dma_start(out=io["dbg_qp"], in_=qp16[:]),
                             reads=["s3_qp16"], dma=True, stream="o")
                    for ts in range(4):
                        sl = (h - h0) * 65
                        p.op("tensor", lambda e, ts=ts, h=h, sl=sl: e.matmul(
                            TSB[ts][:, sl:sl + 65],
                            qp16[:, ts * 128:ts * 128 + 128],
                            KV16[:, h * 65:h * 65 + 65], start=True, stop=True),
                            reads=["s3_qp16", "s3_KV16"], writes=[TSK[ts]])
                ng = h1 - h0
                for ts in range(4):
                    p.op("vector", lambda e, ts=ts, ng=ng: e.tensor_scalar(
                        rep_ap(TSB[ts], [[65, ng], [1, 1]], 64),
                        rep_ap(TSB[ts], [[65, ng], [1, 1]], 64), 1e-6, None,
                        OP.max), reads=[TSK[ts], "cpool"], writes=[TSK[ts]])
                    p.op("vector", lambda e, ts=ts, ng=ng, h0=h0: e.reciprocal(
                        zrq[:, h0:h0 + ng],
                        rep_ap(TSB[ts], [[65, ng], [1, 1]], 64)),
                        reads=[TSK[ts]], writes=["s3_zrq"])
                    p.op("vector", lambda e, ts=ts, ng=ng, h0=h0: e.tensor_tensor(
                        rep_ap(ao4, [[64, ng], [1, 64]], ts * 1024 + h0 * 64),
                        rep_ap(TSB[ts], [[65, ng], [1, 64]], 0),
                        rep_ap(zrq, [[1, ng], [0, 64]], h0), OP.mult),
                        reads=[TSK[ts], "s3_zrq"], writes=["s3_ao4"])
            if b == 0 and c == 0 and "dbg_ao4" in io:
                p.op("sync", lambda e: e.dma_start(out=io["dbg_ao4"], in_=ao4[:]),
                     reads=["s3_ao4"], dma=True, stream="o")
            for ts in range(4) if 'p2t' in phases else []:
                for g in range(8):
                    p.op("tensor", lambda e, ts=ts, g=g: e.transpose(
                        PTT[:, g * 128:(g + 1) * 128],
                        ao4[:, ts * 1024 + g * 128:ts * 1024 + g * 128 + 128],
                        eye[:]), reads=["s3_ao4", "s3_eye"], writes=["s3_PT"])
                for g in range(8):
                    p.op("vector", lambda e, ts=ts, g=g: e.tensor_copy(
                        aoT[:, g * 512 + ts * 128:g * 512 + ts * 128 + 128],
                        PTT[:, g * 128:(g + 1) * 128]),
                        reads=["s3_PT"], writes=["s3_aoT"])
            if b == 0 and c == 0 and "dbg_aoT" in io:
                p.op("sync", lambda e: e.dma_start(out=io["dbg_aoT"], in_=aoT[:]),
                     reads=["s3_aoT"], dma=True, stream="o")
            if 'p2w' not in phases:
                continue
            p.op("sync", lambda e, c=c, t0=t0: e.dma_start(
                out=xf32[:].rearrange("p (a t) -> p a t", a=8),
                in_=yv[:, :, t0 + c * 512:t0 + c * 512 + 512]),
                writes=["s3_xf32"], dma=True, stream="r")
            for o in range(8):
                ps, pk = ((P0, "s3_P0") if o % 2 == 0 else (P2, "s3_P2"))
                for a in range(8):
                    p.op("tensor", lambda e, a=a, o=o, ps=ps: e.matmul(
                        ps[:], wB[:, a * 1024 + o * 128:a * 1024 + o * 128 + 128],
                        aoT[:, a * 512:a * 512 + 512],
                        start=(a == 0), stop=(a == 7)),
                        reads=["s3_wB", "s3_aoT"], writes=[pk])
                p.op("vector", lambda e, o=o, ps=ps, pk=pk: e.tensor_tensor(
                    xf32[:, o * 512:o * 512 + 512], ps[:],
                    xf32[:, o * 512:o * 512 + 512], OP.add),
                    reads=[pk, "s3_xf32"], writes=["s3_xf32"])

            def writer(c=c, t0=t0):
                B.p.op("sync", lambda e: e.dma_start(
                    out=ov[:, :, t0 + c * 512:t0 + c * 512 + 512],
                    in_=xf32[:].rearrange("p (a t) -> p a t", a=8)),
                    reads=["s3_xf32"], dma=True, stream="o")
            ln.emit(xf32, "s3_xf32", G0, G1, G2, G3, writer)


def build_stage3_program(nb=2, phases=('p1k','p1v','p1f','p2q','p2f','p2t','p2w'), debug=False):
    nc = bass.Bass("TRN2", target_bir_lowering=False)
    with ExitStack() as ctx:
        B = BD(nc, ctx)
        io = {}
        if debug:
            io["dbg_KV16"] = B.dram("dbg_KV16", (128, 16 * 65), mybir.dt.bfloat16, kind="ExternalOutput")
            io["dbg_qp"] = B.dram("dbg_qp", (128, 512), mybir.dt.bfloat16, kind="ExternalOutput")
            io["dbg_ao4"] = B.dram("dbg_ao4", (128, 4096), mybir.dt.bfloat16, kind="ExternalOutput")
            io["dbg_aoT"] = B.dram("dbg_aoT", (128, 4096), mybir.dt.bfloat16, kind="ExternalOutput")
        io.update({
            "y2T": B.dram("y2T", (1024, nb * T), F32, kind="ExternalInput"),
            "y3T": B.dram("y3T", (1024, nb * T), F32, kind="ExternalOutput"),
            "Wq": B.dram("Wq", (1024, 1024), F32, kind="ExternalInput"),
            "Wk": B.dram("Wk", (1024, 1024), F32, kind="ExternalInput"),
            "Wv": B.dram("Wv", (1024, 1024), F32, kind="ExternalInput"),
            "Wo": B.dram("Wo", (1024, 1024), F32, kind="ExternalInput"),
            "rf": B.dram("rf", (64, 128), F32, kind="ExternalInput"),
        })
        emit_stage3(B, io, nb=nb, phases=phases)
        B.p.barrier()
        B.p.emit(ctx)
    return nc




def make_xpos_tables():
    DIM_HEAD, WINDOW = 64, 512
    half = np.arange(0, DIM_HEAD, 2, dtype=np.float64)
    inv_freq = 1.0 / 10000.0 ** (half / DIM_HEAD)
    t = np.arange(T, dtype=np.float64)
    freqs = np.concatenate([t[:, None] * inv_freq[None, :]] * 2, -1)
    base = (half + 0.4 * DIM_HEAD) / (1.4 * DIM_HEAD)
    power = (t - T // 2) / (WINDOW // 2)
    scale = np.concatenate([base[None, :] ** power[:, None]] * 2, -1)
    cos, sin = np.cos(freqs), np.sin(freqs)
    sgn = np.concatenate([-np.ones(32), np.ones(32)])[None, :]
    return ((cos * scale * 8.0).astype(np.float32),
            (sin * scale * 8.0 * sgn).astype(np.float32),
            (cos / scale).astype(np.float32),
            (sin / scale * sgn).astype(np.float32))


def emit_stage4_layer(B, io, l, nb=2, phases=("a", "c"), alim=7, qlim=9):
    nc, p = B.nc, B.p
    H, HID = 8, 2730
    NEG = -1.0e9
    L = f"s4{l}"
    B.const(1e-12, 1e-5)
    eye = emit_eye(B, BF16, L + "_eye")
    eyef = emit_eye(B, F32, L + "_eyef")

    x2T = B.dram(L + "_x2T", (1024, nb * T), F32, kind="Internal")
    mscr = B.dram(L + "_mscr", (32, 128), F32, kind="Internal")
    w1c = B.dram(L + "_w1c", (1024, 5460), BF16, kind="Internal")
    w2c = B.dram(L + "_w2c", (2730, 1024), BF16, kind="Internal")
    p.op("gpsimd", lambda e: e.dma_start(out=w1c, in_=io["w1"]),
         writes=[L + "w1c"], dma=True, stream="w")
    p.op("gpsimd", lambda e: e.dma_start(out=w2c, in_=io["w2"]),
         writes=[L + "w2c"], dma=True, stream="w")

    ones1 = B.sb(L + "_ones1", (1, 128), F32)
    p.op("gpsimd", lambda e: e.memset(ones1[:], 1.0), writes=[L + "_ones1"])
    onesc = B.sb(L + "_onesc", (128, 1), BF16)
    p.op("gpsimd", lambda e: e.memset(onesc[:], 1.0), writes=[L + "_onesc"])
    ones2 = B.sb(L + "_ones2", (128, 64), F32)
    p.op("gpsimd", lambda e: e.memset(ones2[:], 1.0), writes=[L + "_ones2"])

    PQ0, PQ1 = B.ps(L + "_PQ0"), B.ps(L + "_PQ1")
    PA, PBm = B.ps(L + "_PA"), B.ps(L + "_PBm")
    PAV, PZ, PS8 = B.ps(L + "_PAV"), B.ps(L + "_PZ"), B.ps(L + "_PS8")
    PTT = B.ps(L + "_PT", (128, 1024), BF16)
    kq0, kq1 = L + "_PQ0", L + "_PQ1"
    ka, kb = L + "_PA", L + "_PBm"
    kav, kz, k8 = L + "_PAV", L + "_PZ", L + "_PS8"
    kt = L + "_PT"

    qsr = B.sb(L + "_qsr", (1, 128), F32)
    p.op("sync", lambda e: e.dma_start(
        out=qsr[0:1, 0:64], in_=io["qs"].rearrange("(a f) -> a f", a=1)),
        writes=[L + "_qsr"], dma=True)
    p.op("sync", lambda e: e.dma_start(
        out=qsr[0:1, 64:128], in_=io["ks"].rearrange("(a f) -> a f", a=1)),
        writes=[L + "_qsr"], dma=True)
    gb = B.sb(L + "_gb", (128, 256), F32)
    p.op("tensor", lambda e: e.matmul(PS8[:, 0:128], ones1[0:1, :], qsr[0:1, :],
                                      start=True, stop=True),
         reads=[L + "_qsr", L + "_ones1"], writes=[k8])
    for (do, so, w) in ((0, 0, 64), (64, 32, 32), (96, 0, 32), (128, 64, 64),
                       (192, 96, 32), (224, 64, 32)):
        p.op("vector", lambda e, do=do, so=so, w=w: e.tensor_copy(
            gb[:, do:do + w], PS8[:, so:so + w]),
            reads=[k8], writes=[L + "_gb"])
    tabs = B.sb(L + "_tabs", (128, 4 * 1024), F32)
    ttmp = B.sb(L + "_ttmp", (128, 1024), F32)
    for i, (tn, goff) in enumerate((("TAq", 0), ("TBq", 64), ("TAk", 128),
                                    ("TBk", 192))):
        p.op("sync", lambda e, tn=tn: e.dma_start(
            out=ttmp[:].rearrange("p (tt f) -> p tt f", f=64),
            in_=io[tn].rearrange("(tt p) f -> p tt f", p=128)),
            writes=[L + "_ttmp"], dma=True)
        p.op("vector", lambda e, i=i, goff=goff: e.tensor_tensor(
            rep_ap(tabs, [[64, 16], [1, 64]], i * 1024),
            rep_ap(ttmp, [[64, 16], [1, 64]], 0),
            rep_ap(gb, [[0, 16], [1, 64]], goff), OP.mult),
            reads=[L + "_ttmp", L + "_gb"], writes=[L + "_tabs"])

    lngt = B.sb(L + "_lngt", (128, 16), F32)
    p.op("sync", lambda e: e.dma_start(
        out=lngt[:, 0:8], in_=io["lng"].rearrange("(a p) -> p a", p=128),
        allow_slow_non_contiguous=True),
        writes=[L + "_lngt"], dma=True)
    p.op("sync", lambda e: e.dma_start(
        out=lngt[:, 8:16], in_=io["lnb"].rearrange("(a p) -> p a", p=128),
        allow_slow_non_contiguous=True),
        writes=[L + "_lngt"], dma=True)
    b2t = B.sb(L + "_b2t", (128, 8), F32)
    p.op("sync", lambda e: e.dma_start(
        out=b2t[:], in_=io["b2"].rearrange("(a p) -> p a", p=128),
        allow_slow_non_contiguous=True),
        writes=[L + "_b2t"], dma=True)
    b1t = B.sb(L + "_b1t", (128, 44), F32)
    for hoff, coff in ((0, 0), (HID, 22)):
        p.op("sync", lambda e, hoff=hoff, coff=coff: e.dma_start(
            out=b1t[:, coff:coff + 21],
            in_=io["b1"][hoff:hoff + 2688].rearrange("(a p) -> p a", p=128),
            allow_slow_non_contiguous=True),
            writes=[L + "_b1t"], dma=True)
        p.op("sync", lambda e, hoff=hoff, coff=coff: e.dma_start(
            out=b1t[0:42, coff + 21:coff + 22],
            in_=io["b1"][hoff + 2688:hoff + 2730].rearrange("(p a) -> p a", a=1)),
            writes=[L + "_b1t"], dma=True)
    mA = B.sb(L + "_mA", (128, 128), F32)
    mB = B.sb(L + "_mB", (128, 128), F32)
    for (mt, key, cm, pat) in ((mA, L + "_mA", -1, 1), (mB, L + "_mB", 1, -1)):
        p.op("gpsimd", lambda e, mt=mt: e.memset(mt[:], 0.0), writes=[key])
        p.op("gpsimd", lambda e, mt=mt, cm=cm, pat=pat: e.affine_select(
            out=mt[:], in_=mt[:], compare_op=OP.is_ge, fill=NEG, base=0,
            pattern=[[pat, 128]], channel_multiplier=cm),
            reads=[key], writes=[key])

    xv = io["xT"].rearrange("(a p) t -> p a t", p=128)
    x2v = x2T.rearrange("(a p) t -> p a t", p=128)
    ov = io["oT"].rearrange("(a p) t -> p a t", p=128)

    if "a" in phases:
      with ExitStack() as actx:
        sbA = lambda n, sh, dt=F32: actx.enter_context(
            nc.sbuf_tensor(L + n, list(sh), dt))
        wqkv = sbA("_wqkv", (128, 8 * 1536), BF16)
        wo16 = sbA("_wo16", (128, 4 * 1024), BF16)
        x16u = sbA("_x16u", (128, 4 * 512), BF16)
        x16s = sbA("_x16s", (128, 4 * 512), BF16)
        qv = sbA("_qv", (128, 512), F32)
        sq = sbA("_sq", (128, 512), F32)
        tm1 = sbA("_tm1", (128, 512), F32)
        tm2 = sbA("_tm2", (128, 512), F32)
        qkt = sbA("_qkt", (128, 512), BF16)
        ss = sbA("_ss", (128, 24), F32)
        kaug = sbA("_kaug", (128, 2 * 4096), BF16)
        vR = sbA("_vR", (128, 2 * 4096), BF16)
        qaug = sbA("_qaug", (128, 4096), BF16)
        At = sbA("_At", (128, 8 * 512), BF16)
        Mm = sbA("_Mm", (128, 32), F32)
        mtmp = sbA("_mtmp", (128, 1), F32)
        Msb = sbA("_Msb", (32, 128), F32)
        ztmp = sbA("_ztmp", (128, 512), F32)
        zbcs = sbA("_zbcs", (128, 512), F32)
        aoT = sbA("_aoT", (128, 4 * 512), BF16)
        xf32 = sbA("_xf32", (128, 8 * 512), F32)

        p.op("gpsimd", lambda e: e.dma_start(
            out=wqkv[:].rearrange("p (a f) -> p a f", a=8),
            in_=io["wqkv"].rearrange("(a p) f -> p a f", p=128)),
            writes=[L + "_wqkv"], dma=True, stream="w")
        p.op("gpsimd", lambda e: e.dma_start(
            out=wo16[:].rearrange("p (a f) -> p a f", a=4),
            in_=io["wo"].rearrange("(a p) f -> p a f", p=128)),
            writes=[L + "_wo16"], dma=True, stream="w")
        p.op("gpsimd", lambda e: e.memset(vR[:], 0.0), writes=[L + "_vR"])
        for slot in range(2):
            p.op("gpsimd", lambda e, slot=slot: e.memset(
                rep_ap(vR, [[1024, 4], [256, 4], [1, 1]], slot * 4096 + 64), 1.0),
                writes=[L + "_vR"])
            p.op("gpsimd", lambda e, slot=slot: e.memset(
                rep_ap(vR, [[1024, 4], [256, 4], [1, 1]], slot * 4096 + 128), 1.0),
                writes=[L + "_vR"])
        p.op("gpsimd", lambda e: e.memset(kaug[64:65, :], 1.0),
             writes=[L + "_kaugone"])

        for b in range(nb):
            t0 = b * T
            for c in range(4):
                cs, po = c % 2, (c - 1) % 2
                p.op("gpsimd", lambda e, t0=t0, c=c: e.dma_start(
                    out=x16u[:].rearrange("p (a t) -> p a t", a=4),
                    in_=xv[:, 0:4, t0 + c * 512:t0 + c * 512 + 512]),
                    writes=[L + "_x16u"], dma=True, stream="x")
                if c == 0:
                    p.op("gpsimd", lambda e, t0=t0: e.dma_start(
                        out=x16s[:].rearrange("p (a t) -> p a t", a=4)[:, :, 1:512],
                        in_=xv[:, 4:8, t0:t0 + 511]),
                        writes=[L + "_x16s"], dma=True, stream="x")
                    p.op("gpsimd", lambda e: e.memset(
                        rep_ap(x16s, [[512, 4], [1, 1]], 0), 0.0),
                        writes=[L + "_x16s"])
                else:
                    p.op("gpsimd", lambda e, t0=t0, c=c: e.dma_start(
                        out=x16s[:].rearrange("p (a t) -> p a t", a=4),
                        in_=xv[:, 4:8, t0 + c * 512 - 1:t0 + c * 512 + 511]),
                        writes=[L + "_x16s"], dma=True, stream="x")

                for ts in range(4):
                    tt = c * 4 + ts
                    for part, fo, ps, pk in (("q", 0, PQ0, kq0),
                                             ("k", 512, PQ1, kq1),
                                             ("v", 1024, PQ0, kq0)):
                        for a in range(8):
                            xb = x16u if a < 4 else x16s
                            xo = (a % 4) * 512 + ts * 128
                            p.op("tensor", lambda e, xb=xb, xo=xo, fo=fo, a=a,
                                 ps=ps: e.matmul(
                                ps[:], xb[:, xo:xo + 128],
                                wqkv[:, a * 1536 + fo:a * 1536 + fo + 512],
                                start=(a == 0), stop=(a == 7)),
                                reads=[L + "_x16u", L + "_x16s", L + "_wqkv"],
                                writes=[pk])
                        if part == "v":
                            p.op("vector", lambda e, ts=ts, cs=cs, ps=ps, pk=pk:
                                 e.tensor_copy(
                                     rep_ap(vR, [[256, 4], [1, 64]],
                                            cs * 4096 + ts * 1024 + 0),
                                     rep_ap(ps, [[128, 4], [1, 64]], 0)),
                                 reads=[pk], writes=[L + "_vR"])
                            p.op("vector", lambda e, ts=ts, cs=cs, ps=ps, pk=pk:
                                 e.tensor_copy(
                                     rep_ap(vR, [[256, 4], [1, 64]],
                                            cs * 4096 + ts * 1024 + 192),
                                     rep_ap(ps, [[128, 4], [1, 64]], 64)),
                                 reads=[pk], writes=[L + "_vR"])
                            continue
                        tq = 0 if part == "q" else 2048
                        if qlim < 1:
                            continue
                        p.op("vector", lambda e, ps=ps, pk=pk: e.tensor_copy(
                            qv[:], ps[:]), reads=[pk], writes=[L + "_qv"])
                        if qlim < 2:
                            continue
                        p.op("scalar", lambda e, ps=ps, pk=pk: e.activation(
                            sq[:], ps[:], AF.Square), reads=[pk],
                            writes=[L + "_sq"])
                        p.op("vector", lambda e: e.tensor_reduce(
                            rep_ap(ss, [[1, 8], [1, 1]], 0),
                            rep_ap(sq, [[64, 8], [1, 64]], 0), axis=AX.X,
                            op=OP.add), reads=[L + "_sq"], writes=[L + "_ss"])
                        p.op("scalar", lambda e: e.activation(
                            ss[:, 8:16], ss[:, 0:8], AF.Sqrt),
                            reads=[L + "_ss"], writes=[L + "_ss2"])
                        p.op("vector", lambda e: e.tensor_scalar(
                            ss[:, 8:16], ss[:, 8:16], 1e-12, None, OP.max),
                            reads=[L + "_ss2", "cpool"], writes=[L + "_ss2"])
                        p.op("vector", lambda e: e.reciprocal(
                            ss[:, 16:24], ss[:, 8:16]),
                            reads=[L + "_ss2"], writes=[L + "_ss3"])
                        if qlim < 3:
                            continue
                        p.op("vector", lambda e: e.tensor_tensor(
                            qv[:], qv[:], rep_ap(ss, [[1, 8], [0, 64]], 16),
                            OP.mult), reads=[L + "_qv", L + "_ss3"],
                            writes=[L + "_qv"])
                        if qlim < 4:
                            continue
                        p.op("vector", lambda e, tq=tq, tt=tt: e.tensor_tensor(
                            tm1[:], qv[:],
                            rep_ap(tabs, [[0, 8], [1, 64]], tq + tt * 64),
                            OP.mult), reads=[L + "_qv", L + "_tabs"],
                            writes=[L + "_tm1"])
                        for half in (0, 1):
                            p.op("vector", lambda e, tq=tq, tt=tt, half=half:
                                 e.tensor_tensor(
                                     rep_ap(tm2, [[64, 8], [1, 32]], half * 32),
                                     rep_ap(qv, [[64, 8], [1, 32]],
                                            (1 - half) * 32),
                                     rep_ap(tabs, [[0, 8], [1, 32]],
                                            tq + 1024 + tt * 64 + half * 32),
                                     OP.mult),
                                 reads=[L + "_qv", L + "_tabs"],
                                 writes=[L + "_tm2"])
                        p.op("vector", lambda e: e.tensor_tensor(
                            qkt[:], tm1[:], tm2[:], OP.add),
                            reads=[L + "_tm1", L + "_tm2"], writes=[L + "_qkt"])
                        if qlim < 5:
                            continue
                        for h in range(H):
                            p.op("tensor", lambda e, h=h: e.transpose(
                                PTT[0:64, h * 128:h * 128 + 128],
                                qkt[:, h * 64:h * 64 + 64], eye[:]),
                                reads=[L + "_qkt", L + "_eye"], writes=[kt])
                        dst, dk, doff = ((qaug, L + "_qaug", ts * 128)
                                         if part == "q" else
                                         (kaug, L + "_kaug",
                                          cs * 4096 + ts * 128))
                        p.op("vector", lambda e, dst=dst, doff=doff:
                             e.tensor_copy(
                                 rep_ap(dst, [[512, 8], [1, 128]], doff,
                                        psl=(0, 64)),
                                 rep_ap(PTT, [[128, 8], [1, 128]], 0,
                                        psl=(0, 64))),
                             reads=[kt], writes=[dk])

                for h in (range(H) if alim >= 2 else []):
                    for it in range(4):
                        parts = []
                        w1_ = 512 - it * 128
                        if c > 0:
                            parts.append((po, it * 128, 0, w1_))
                        if it > 0:
                            parts.append((cs, 0, w1_ if c > 0 else 0, it * 128))
                        nacc = 0
                        for (sl, jo, co, w) in parts:
                            p.op("tensor", lambda e, h=h, it=it, sl=sl, jo=jo,
                                 co=co, w=w: e.matmul(
                                PA[:, co:co + w],
                                qaug[0:64, h * 512 + it * 128:
                                     h * 512 + it * 128 + 128],
                                kaug[0:64, sl * 4096 + h * 512 + jo:
                                     sl * 4096 + h * 512 + jo + w],
                                start=True, stop=True),
                                reads=[L + "_qaug", L + "_kaug"], writes=[ka])
                            nacc += 1
                        p.op("tensor", lambda e, h=h, it=it, cs=cs: e.matmul(
                            PBm[:, 0:128],
                            qaug[0:64, h * 512 + it * 128:
                                 h * 512 + it * 128 + 128],
                            kaug[0:64, cs * 4096 + h * 512 + it * 128:
                                 cs * 4096 + h * 512 + it * 128 + 128],
                            start=True, stop=True),
                            reads=[L + "_qaug", L + "_kaug"], writes=[kb])
                        if c > 0:
                            p.op("vector", lambda e: e.tensor_tensor(
                                PA[:, 0:128], PA[:, 0:128], mA[:], OP.add),
                                reads=[ka, L + "_mA"], writes=[ka])
                        p.op("vector", lambda e: e.tensor_tensor(
                            PBm[:, 0:128], PBm[:, 0:128], mB[:], OP.add),
                            reads=[kb, L + "_mB"], writes=[kb])
                        col = h * 4 + it
                        if parts:
                            wtot = 512 if c > 0 else it * 128
                            p.op("vector", lambda e, wtot=wtot: e.reduce_max(
                                mtmp[:], PA[:, 0:wtot], axis=AX.X,
                                negate=True), reads=[ka], writes=[L + "_ssm"])
                            p.op("vector", lambda e, col=col: e.reduce_max(
                                Mm[:, col:col + 1], PBm[:, 0:128], axis=AX.X,
                                negate=True), reads=[kb], writes=[L + "_Mm"])
                            p.op("vector", lambda e, col=col: e.tensor_tensor(
                                Mm[:, col:col + 1], Mm[:, col:col + 1],
                                mtmp[:], OP.min),
                                reads=[L + "_Mm", L + "_ssm"],
                                writes=[L + "_Mm"])
                        else:
                            p.op("vector", lambda e, col=col: e.reduce_max(
                                Mm[:, col:col + 1], PBm[:, 0:128], axis=AX.X,
                                negate=True), reads=[kb], writes=[L + "_Mm"])
                if alim < 3:
                    continue
                p.op("tensor", lambda e: e.transpose(
                    PZ[0:32, 0:128], Mm[:], eyef[:]),
                    reads=[L + "_Mm", L + "_eyef"], writes=[kz])
                p.op("vector", lambda e: e.tensor_copy(Msb[:], PZ[0:32, 0:128]),
                     reads=[kz], writes=[L + "_Msb"])
                p.op("sync", lambda e: e.dma_start(out=mscr, in_=Msb[:]),
                     reads=[L + "_Msb"], writes=[L + "_mscr"], dma=True,
                     stream="m")
                p.op("gpsimd", lambda e: e.dma_start(
                    out=qaug[64:65, 0:4096],
                    in_=mscr.rearrange("(a x) i -> a (x i)", a=1)),
                    reads=[L + "_mscr"], writes=[L + "_qaugM"], dma=True,
                    stream="m2")

                for h in (range(H) if alim >= 4 else []):
                    wjts = range(0 if c > 0 else 4, 8)
                    for wjt in wjts:
                        sl = po if wjt < 4 else cs
                        jl = wjt % 4
                        pss, psk = ((PA, ka) if wjt % 2 == 0 else (PBm, kb))
                        p.op("tensor", lambda e, h=h, sl=sl, jl=jl, pss=pss:
                             e.matmul(
                                 pss[:], kaug[0:65, sl * 4096 + h * 512
                                              + jl * 128:
                                              sl * 4096 + h * 512 + jl * 128
                                              + 128],
                                 qaug[0:65, h * 512:h * 512 + 512],
                                 start=True, stop=True),
                             reads=[L + "_kaug", L + "_kaugone", L + "_qaug",
                                    L + "_qaugM"], writes=[psk])
                        p.op("scalar", lambda e, wjt=wjt, pss=pss, psk=psk:
                             e.activation(At[:, wjt * 512:wjt * 512 + 512],
                                          pss[:], AF.Exp),
                             reads=[psk], writes=[L + f"_At#{wjt}"])
                        if wjt < 4:
                            p.op("gpsimd", lambda e, wjt=wjt, c=c, jl=jl:
                                 e.affine_select(
                                     out=At[:, wjt * 512:wjt * 512 + 512],
                                     in_=At[:, wjt * 512:wjt * 512 + 512],
                                     compare_op=OP.is_ge, fill=0.0,
                                     base=(jl * 128) - 512 + 512, pattern=[[-1, 512]],
                                     channel_multiplier=1),
                                 reads=[L + f"_At#{wjt}"],
                                 writes=[L + f"_At#{wjt}"])
                        else:
                            p.op("gpsimd", lambda e, wjt=wjt, jl=jl:
                                 e.affine_select(
                                     out=At[:, wjt * 512:wjt * 512 + 512],
                                     in_=At[:, wjt * 512:wjt * 512 + 512],
                                     compare_op=OP.is_ge, fill=0.0,
                                     base=-(jl * 128), pattern=[[1, 512]],
                                     channel_multiplier=-1),
                                 reads=[L + f"_At#{wjt}"],
                                 writes=[L + f"_At#{wjt}"])
                    if alim < 5:
                        continue
                    first = True
                    for wjt in wjts:
                        sl = po if wjt < 4 else cs
                        jl = wjt % 4
                        p.op("tensor", lambda e, h=h, sl=sl, jl=jl, wjt=wjt,
                             first=first, last=(wjt == 7): e.matmul(
                                 PAV[:], vR[:, sl * 4096 + jl * 1024
                                            + h * 128:
                                            sl * 4096 + jl * 1024 + h * 128
                                            + 128],
                                 At[:, wjt * 512:wjt * 512 + 512],
                                 start=first, stop=last),
                             reads=[L + "_vR", L + f"_At#{wjt}"], writes=[kav])
                        first = False
                    if alim < 6:
                        continue
                    zr = 64 if h % 2 == 0 else 0
                    vr_ = 64 - zr
                    p.op("scalar", lambda e, zr=zr: e.activation(
                        ztmp[zr:zr + 1, :], PAV[zr:zr + 1, :], AF.Ln),
                        reads=[kav], writes=[L + "_ztmp"])
                    p.op("scalar", lambda e, zr=zr: e.activation(
                        ztmp[zr:zr + 1, :], ztmp[zr:zr + 1, :], AF.Exp,
                        scale=-1.0), reads=[L + "_ztmp"], writes=[L + "_ztmp"])
                    p.op("tensor", lambda e, zr=zr, vr_=vr_: e.matmul(
                        PZ[vr_:vr_ + 64, :], ones2[zr:zr + 1, :],
                        ztmp[zr:zr + 1, :], start=True, stop=True,
                        tile_position=(zr, vr_)),
                        reads=[L + "_ztmp", L + "_ones2"], writes=[kz])
                    p.op("vector", lambda e, vr_=vr_: e.tensor_copy(
                        zbcs[vr_:vr_ + 64, :], PZ[vr_:vr_ + 64, :]),
                        reads=[kz], writes=[L + "_zbcs"])
                    p.op("vector", lambda e, h=h, vr_=vr_: e.tensor_tensor(
                        aoT[vr_:vr_ + 64, (h // 2) * 512:(h // 2) * 512 + 512],
                        PAV[vr_:vr_ + 64, :], zbcs[vr_:vr_ + 64, :], OP.mult),
                        reads=[kav, L + "_zbcs"], writes=[L + "_aoT"])

                if alim < 7:
                    continue
                p.op("sync", lambda e, t0=t0, c=c: e.dma_start(
                    out=xf32[:].rearrange("p (a t) -> p a t", a=8),
                    in_=xv[:, :, t0 + c * 512:t0 + c * 512 + 512]),
                    writes=[L + "_xf32"], dma=True, stream="r")
                for o in range(8):
                    ps, pk = ((PQ0, kq0) if o % 2 == 0 else (PQ1, kq1))
                    for ra in range(4):
                        p.op("tensor", lambda e, ra=ra, o=o, ps=ps: e.matmul(
                            ps[:], wo16[:, ra * 1024 + o * 128:
                                         ra * 1024 + o * 128 + 128],
                            aoT[:, ra * 512:ra * 512 + 512],
                            start=(ra == 0), stop=(ra == 3)),
                            reads=[L + "_wo16", L + "_aoT"], writes=[pk])
                    p.op("vector", lambda e, o=o, ps=ps, pk=pk: e.tensor_tensor(
                        xf32[:, o * 512:o * 512 + 512], ps[:],
                        xf32[:, o * 512:o * 512 + 512], OP.add),
                        reads=[pk, L + "_xf32"], writes=[L + "_xf32"])
                p.op("sync", lambda e, t0=t0, c=c: e.dma_start(
                    out=x2v[:, :, t0 + c * 512:t0 + c * 512 + 512],
                    in_=xf32[:].rearrange("p (a t) -> p a t", a=8)),
                    reads=[L + "_xf32"], dma=True, stream="o")
      p.barrier()

    if "c" in phases:
      with ExitStack() as cctx:
        sbC = lambda n, sh, dt=F32: cctx.enter_context(
            nc.sbuf_tensor(L + n, list(sh), dt))
        x2c = sbC("_x2c", (128, 8 * 513), F32)
        z16 = sbC("_z16", (128, 2 * 4096), BF16)
        yh16 = sbC("_yh16", (128, 8 * T), BF16)
        hph = sbC("_hph", (128, 22 * 1024), BF16)
        w1t = sbC("_w1t", (128, 16 * 128), BF16)
        w2r = sbC("_w2r", (128, 22 * 1024), BF16)
        g16t = sbC("_g16t", (128, 512), BF16)
        stat = sbC("_stat", (1, 2048), F32)
        ytm = sbC("_ytm", (128, 512), F32)
        xr32 = sbC("_xr32", (128, 8 * 512), F32)

        p.op("sync", lambda e: e.dma_start(
            out=w2r[:].rearrange("p (a f) -> p a f", a=22)[:, 0:21, :],
            in_=w2c[0:2688, :].rearrange("(a p) f -> p a f", p=128)),
            writes=[L + "_w2r"], dma=True, stream="r")
        p.op("sync", lambda e: e.dma_start(
            out=w2r[0:42, 21 * 1024:22 * 1024], in_=w2c[2688:2730, :]),
            writes=[L + "_w2r"], dma=True, stream="r")

        for b in range(nb):
            t0 = b * T
            for c in range(4):
                p.op("sync", lambda e, t0=t0, c=c: e.dma_start(
                    out=x2c[:].rearrange("p (a t) -> p a t", a=8)[:, :, 1:513],
                    in_=x2v[:, :, t0 + c * 512:t0 + c * 512 + 512]),
                    writes=[L + "_x2c"], dma=True, stream="x")
                if c == 0:
                    p.op("gpsimd", lambda e: e.memset(
                        rep_ap(x2c, [[513, 8], [1, 1]], 0), 0.0),
                        writes=[L + "_x2c"])
                else:
                    p.op("sync", lambda e, t0=t0, c=c: e.dma_start(
                        out=x2c[:].rearrange("p (a t) -> p a t", a=8)[:, :, 0:1],
                        in_=x2v[:, :, t0 + c * 512 - 1:t0 + c * 512],
                        allow_slow_non_contiguous=True),
                        writes=[L + "_x2c"], dma=True, stream="x")
                for half, off in ((0, 1), (1, 4 * 513)):
                    p.op("vector", lambda e, half=half, off=off: e.tensor_copy(
                        rep_ap(z16, [[512, 4], [1, 512]], half * 2048),
                        rep_ap(x2c, [[513, 4], [1, 512]], off)),
                        reads=[L + "_x2c"], writes=[L + "_z16lo"])
                    p.op("scalar", lambda e, half=half, off=off: e.activation(
                        rep_ap(z16, [[512, 4], [1, 512]], 4096 + half * 2048),
                        rep_ap(x2c, [[513, 4], [1, 512]], off), AF.Square),
                        reads=[L + "_x2c"], writes=[L + "_z16hi"])
                for a in range(8):
                    p.op("tensor", lambda e, a=a: e.matmul(
                        PAV[0:1, :], onesc[:, 0:1],
                        z16[:, a * 512:a * 512 + 512],
                        start=(a == 0), stop=(a == 7)),
                        reads=[L + "_z16lo", L + "_onesc"], writes=[kav])
                for a in range(8):
                    p.op("tensor", lambda e, a=a: e.matmul(
                        PZ[0:1, :], onesc[:, 0:1],
                        z16[:, 4096 + a * 512:4096 + a * 512 + 512],
                        start=(a == 0), stop=(a == 7)),
                        reads=[L + "_z16hi", L + "_onesc"], writes=[kz])
                mu, m2, var, rstd = (stat[0:1, i * 512:(i + 1) * 512]
                                     for i in range(4))
                p.op("vector", lambda e: e.tensor_scalar(
                    mu, PAV[0:1, :], 1.0 / 1024.0, None, OP.mult),
                    reads=[kav, "cpool"], writes=[L + "_mu"])
                p.op("scalar", lambda e: e.activation(
                    m2, PZ[0:1, :], AF.Identity, scale=1.0 / 1024.0),
                    reads=[kz], writes=[L + "_m2"])
                p.op("vector", lambda e: e.tensor_tensor(var, mu, mu, OP.mult),
                     reads=[L + "_mu"], writes=[L + "_var"])
                p.op("vector", lambda e: e.tensor_tensor(var, m2, var,
                                                         OP.subtract),
                     reads=[L + "_m2", L + "_var"], writes=[L + "_var"])
                p.op("scalar", lambda e: e.activation(var, var, AF.Ln,
                                                      bias=1e-5),
                     reads=[L + "_var", "cpool"], writes=[L + "_var"])
                p.op("scalar", lambda e: e.activation(rstd, var, AF.Exp,
                                                      scale=-0.5),
                     reads=[L + "_var"], writes=[L + "_rstd"])
                p.op("tensor", lambda e: e.matmul(
                    PAV[:], ones1[0:1, :], mu, start=True, stop=True),
                    reads=[L + "_mu", L + "_ones1"], writes=[kav])
                p.op("tensor", lambda e: e.matmul(
                    PZ[:], ones1[0:1, :], rstd, start=True, stop=True),
                    reads=[L + "_rstd", L + "_ones1"], writes=[kz])
                for a in range(8):
                    off = a * 513 + (1 if a < 4 else 0)
                    p.op("vector", lambda e, a=a, off=off: e.tensor_tensor(
                        ytm[:], rep_ap(x2c, [[1, 512]], off),
                        PAV[:], OP.subtract),
                        reads=[L + "_x2c", kav], writes=[L + "_ytm"])
                    p.op("vector", lambda e: e.tensor_tensor(
                        ytm[:], ytm[:], PZ[:], OP.mult),
                        reads=[L + "_ytm", kz], writes=[L + "_ytm"])
                    p.op("scalar", lambda e, a=a, c=c: e.activation(
                        yh16[:, a * T + c * 512:a * T + c * 512 + 512],
                        ytm[:], AF.Identity, scale=lngt[:, a:a + 1],
                        bias=lngt[:, 8 + a:8 + a + 1]),
                        reads=[L + "_ytm", L + "_lngt"], writes=[L + "_yh16"])
            for hf in range(2):
                for fp in range(22):
                    wid = 128 if fp < 21 else 42
                    for gpart, coff in ((0, 0), (1, HID)):
                        p.op("sync", lambda e, fp=fp, wid=wid, gpart=gpart,
                             coff=coff: e.dma_start(
                            out=w1t[:].rearrange("p (a f) -> p a f", a=16)
                                [:, gpart * 8:gpart * 8 + 8, 0:wid],
                            in_=w1c[:, coff + fp * 128:coff + fp * 128 + wid]
                                .rearrange("(a p) f -> p a f", p=128)),
                            writes=[L + "_w1t"], dma=True, stream="r")
                    for cc in range(2):
                        c = hf * 2 + cc
                        for gpart, ps, pk in ((0, PA, ka), (1, PBm, kb)):
                            for a in range(8):
                                p.op("tensor", lambda e, a=a, gpart=gpart,
                                     wid=wid, c=c, ps=ps: e.matmul(
                                    ps[0:wid, :],
                                    w1t[:, (gpart * 8 + a) * 128:
                                        (gpart * 8 + a) * 128 + wid],
                                    yh16[:, a * T + c * 512:a * T + c * 512
                                         + 512],
                                    start=(a == 0), stop=(a == 7)),
                                    reads=[L + "_w1t", L + "_yh16"],
                                    writes=[pk])
                        p.op("scalar", lambda e, fp=fp, wid=wid: e.activation(
                            g16t[0:wid, :], PBm[0:wid, :], AF.Gelu,
                            bias=b1t[0:wid, 22 + fp:22 + fp + 1]),
                            reads=[kb, L + "_b1t"], writes=[L + "_g16t"])
                        p.op("vector", lambda e, fp=fp, wid=wid, cc=cc:
                             e.scalar_tensor_tensor(
                                 hph[0:wid, fp * 1024 + cc * 512:
                                     fp * 1024 + cc * 512 + 512],
                                 PA[0:wid, :], b1t[0:wid, fp:fp + 1],
                                 g16t[0:wid, :], OP.add, OP.mult),
                             reads=[ka, L + "_g16t", L + "_b1t"],
                             writes=[L + f"_hph#{fp}"])
                for cc in range(2):
                    c = hf * 2 + cc
                    p.op("sync", lambda e, t0=t0, c=c: e.dma_start(
                        out=xr32[:].rearrange("p (a t) -> p a t", a=8),
                        in_=x2v[:, :, t0 + c * 512:t0 + c * 512 + 512]),
                        writes=[L + "_xr32"], dma=True, stream="x")
                    for o in range(8):
                        ps, pk = ((PQ0, kq0) if o % 2 == 0 else (PQ1, kq1))
                        for fp in range(22):
                            wid = 128 if fp < 21 else 42
                            p.op("tensor", lambda e, fp=fp, wid=wid, o=o,
                                 cc=cc, ps=ps: e.matmul(
                                ps[:], w2r[0:wid, fp * 1024 + o * 128:
                                           fp * 1024 + o * 128 + 128],
                                hph[0:wid, fp * 1024 + cc * 512:
                                    fp * 1024 + cc * 512 + 512],
                                start=(fp == 0), stop=(fp == 21)),
                                reads=[L + "_w2r", L + f"_hph#{fp}"],
                                writes=[pk])
                        p.op("vector", lambda e, o=o, ps=ps, pk=pk:
                             e.scalar_tensor_tensor(
                                 xr32[:, o * 512:o * 512 + 512], ps[:],
                                 b2t[:, o:o + 1],
                                 xr32[:, o * 512:o * 512 + 512],
                                 OP.add, OP.add),
                             reads=[pk, L + "_b2t", L + "_xr32"],
                             writes=[L + "_xr32"])
                    p.op("sync", lambda e, t0=t0, c=c: e.dma_start(
                        out=ov[:, :, t0 + c * 512:t0 + c * 512 + 512],
                        in_=xr32[:].rearrange("p (a t) -> p a t", a=8)),
                        reads=[L + "_xr32"], dma=True, stream="o")
      p.barrier()


def build_stage4_program(l=0, nb=2, phases=("a", "c"), alim=7, qlim=9):
    nc = bass.Bass("TRN2", target_bir_lowering=False)
    with ExitStack() as ctx:
        B = BD(nc, ctx)
        io = {
            "xT": B.dram("xT", (1024, nb * T), F32, kind="ExternalInput"),
            "oT": B.dram("oT", (1024, nb * T), F32, kind="ExternalOutput"),
            "wqkv": B.dram("wqkv", (1024, 1536), F32, kind="ExternalInput"),
            "qs": B.dram("qs", (64,), F32, kind="ExternalInput"),
            "ks": B.dram("ks", (64,), F32, kind="ExternalInput"),
            "wo": B.dram("wo", (512, 1024), F32, kind="ExternalInput"),
            "lng": B.dram("lng", (1024,), F32, kind="ExternalInput"),
            "lnb": B.dram("lnb", (1024,), F32, kind="ExternalInput"),
            "w1": B.dram("w1", (1024, 5460), F32, kind="ExternalInput"),
            "b1": B.dram("b1", (5460,), F32, kind="ExternalInput"),
            "w2": B.dram("w2", (2730, 1024), F32, kind="ExternalInput"),
            "b2": B.dram("b2", (1024,), F32, kind="ExternalInput"),
            "TAq": B.dram("TAq", (T, 64), F32, kind="ExternalInput"),
            "TBq": B.dram("TBq", (T, 64), F32, kind="ExternalInput"),
            "TAk": B.dram("TAk", (T, 64), F32, kind="ExternalInput"),
            "TBk": B.dram("TBk", (T, 64), F32, kind="ExternalInput"),
        }
        emit_stage4_layer(B, io, l, nb=nb, phases=phases, alim=alim, qlim=qlim)
        B.p.barrier()
        B.p.emit(ctx)
    return nc




def emit_ingest(B, io, nb=2):
    nc, p = B.nc, B.p
    eyef = emit_eye(B, F32, "in_eyef")
    xin = B.sb("in_x", (128, 1024), F32)
    xts = B.sb("in_xt", (128, 8 * 128), F32)
    PT = B.ps("in_PT")
    xo = io["xT"].rearrange("(a p) t -> p a t", p=128)
    for b in range(nb):
        for tt in range(16):
            p.op("sync", lambda e, b=b, tt=tt: e.dma_start(
                out=xin[:], in_=io["x"][b, tt * 128:tt * 128 + 128, :]),
                writes=["in_x"], dma=True, stream="x")
            for a in range(8):
                p.op("tensor", lambda e, a=a: e.transpose(
                    PT[:, 0:128], xin[:, a * 128:a * 128 + 128], eyef[:]),
                    reads=["in_x", "in_eyef"], writes=["in_PT"])
                p.op("vector", lambda e, a=a: e.tensor_copy(
                    xts[:, a * 128:a * 128 + 128], PT[:, 0:128]),
                    reads=["in_PT"], writes=["in_xt"])
            p.op("sync", lambda e, b=b, tt=tt: e.dma_start(
                out=xo[:, :, b * T + tt * 128:b * T + tt * 128 + 128],
                in_=xts[:].rearrange("p (a t) -> p a t", a=8)),
                reads=["in_xt"], dma=True, stream="o")


def emit_final(B, io, nb=2):
    nc, p = B.nc, B.p
    eyef = emit_eye(B, F32, "fi_eyef")
    ln = LNPack(B, "fi")
    av = io["aT"].rearrange("(a p) t -> p a t", p=128)
    bv = io["bT"].rearrange("(a p) t -> p a t", p=128)
    z = B.sb("fi_z", (128, 8 * 512), F32)
    zb = B.sb("fi_zb", (128, 8 * 512), F32)
    ot = B.sb("fi_ot", (128, 8 * 128), F32)
    P0, P1, P2, P3 = (B.ps(f"fi_P{i}") for i in range(4))
    PT = B.ps("fi_PT")
    for b in range(nb):
        for c in range(4):
            p.op("sync", lambda e, b=b, c=c: e.dma_start(
                out=z[:].rearrange("p (a t) -> p a t", a=8),
                in_=av[:, :, b * T + c * 512:b * T + c * 512 + 512]),
                writes=["fi_z"], dma=True, stream="x")
            p.op("sync", lambda e, b=b, c=c: e.dma_start(
                out=zb[:].rearrange("p (a t) -> p a t", a=8),
                in_=bv[:, :, b * T + c * 512:b * T + c * 512 + 512]),
                writes=["fi_zb"], dma=True, stream="r")
            p.op("vector", lambda e: e.tensor_tensor(z[:], z[:], zb[:], OP.add),
                 reads=["fi_z", "fi_zb"], writes=["fi_z"])

            def writer(b=b, c=c):
                for ts in range(4):
                    for a in range(8):
                        B.p.op("tensor", lambda e, ts=ts, a=a: e.transpose(
                            PT[:, 0:128],
                            z[:, a * 512 + ts * 128:a * 512 + ts * 128 + 128],
                            eyef[:]), reads=["fi_z", "fi_eyef"],
                            writes=["fi_PT"])
                        B.p.op("vector", lambda e, a=a: e.tensor_copy(
                            ot[:, a * 128:a * 128 + 128], PT[:, 0:128]),
                            reads=["fi_PT"], writes=["fi_ot"])
                    B.p.op("sync", lambda e, b=b, c=c, ts=ts: e.dma_start(
                        out=io["y"][b, c * 512 + ts * 128:
                                    c * 512 + ts * 128 + 128, :],
                        in_=ot[:]), reads=["fi_ot"], dma=True, stream="o")
            ln.emit(z, "fi_z", P0, P1, P2, P3, writer)


def build_ingest_program(nb=2):
    nc = bass.Bass("TRN2", target_bir_lowering=False)
    with ExitStack() as ctx:
        B = BD(nc, ctx)
        io = {"x": B.dram("x", (nb, T, 1024), F32, kind="ExternalInput"),
              "xT": B.dram("xT", (1024, nb * T), F32, kind="ExternalOutput")}
        emit_ingest(B, io, nb=nb)
        B.p.barrier()
        B.p.emit(ctx)
    return nc


def build_final_program(nb=2):
    nc = bass.Bass("TRN2", target_bir_lowering=False)
    with ExitStack() as ctx:
        B = BD(nc, ctx)
        io = {"aT": B.dram("aT", (1024, nb * T), F32, kind="ExternalInput"),
              "bT": B.dram("bT", (1024, nb * T), F32, kind="ExternalInput"),
              "y": B.dram("y", (nb, T, 1024), F32, kind="ExternalOutput")}
        emit_final(B, io, nb=nb)
        B.p.barrier()
        B.p.emit(ctx)
    return nc


def build_full_program(nb=2, act_kind="Internal"):
    nc = bass.Bass("TRN2", target_bir_lowering=False)
    with ExitStack() as octx:
        B = BD(nc, octx)
        d = {}
        d["x"] = B.dram("x", (nb, T, 1024), F32, kind="ExternalInput")
        d["y"] = B.dram("y", (nb, T, 1024), F32, kind="ExternalOutput")
        for name, shape in (
                ("L0Wp", (1024, 40)), ("L0bp", (40,)), ("L0Wv", (1024, 1024)),
                ("L0Wo", (1024, 1024)),
                ("F0w1A", (512, 1024)), ("F0w1B", (512, 4096)),
                ("F0w2A", (512, 4096)), ("F0w2B", (512, 1024)),
                ("moe_w", (T, 2)), ("sel8", (8, 512)),
                ("L1Wq", (1024, 1024)), ("L1Wk", (1024, 1024)),
                ("L1Wv", (1024, 1024)), ("L1Wo", (1024, 1024)),
                ("rf", (64, 128)),
                ("TAq", (T, 64)), ("TBq", (T, 64)),
                ("TAk", (T, 64)), ("TBk", (T, 64))):
            d[name] = B.dram(name, shape, F32, kind="ExternalInput")
        d["moe_i"] = B.dram("moe_i", (T, 2), I32, kind="ExternalInput")
        for l in range(2):
            for name, shape in (
                    ("wqkv", (1024, 1536)), ("qs", (64,)), ("ks", (64,)),
                    ("wo", (512, 1024)), ("lng", (1024,)), ("lnb", (1024,)),
                    ("w1", (1024, 5460)), ("b1", (5460,)),
                    ("w2", (2730, 1024)), ("b2", (1024,))):
                d[f"w4{l}_{name}"] = B.dram(f"w4{l}_{name}", shape, F32,
                                            kind="ExternalInput")
        act = {n: B.dram(n, (1024, nb * T), F32, kind=act_kind)
               for n in ("xT", "yT", "y2T", "y3T", "z0T", "z1T")}

        def stage(fn):
            with ExitStack() as sctx:
                B.ctx = sctx
                fn()
                B.p.barrier()
            B.ctx = octx

        stage(lambda: emit_ingest(B, {"x": d["x"], "xT": act["xT"]}, nb=nb))
        stage(lambda: emit_stage1(B, {
            "xT": act["xT"], "yT": act["yT"], "Wp": d["L0Wp"],
            "bp": d["L0bp"], "Wv": d["L0Wv"], "Wo": d["L0Wo"]}, nb=nb))
        stage(lambda: emit_stage2(B, {
            "yT": act["yT"], "y2T": act["y2T"], "w1A": d["F0w1A"],
            "w1B": d["F0w1B"], "w2A": d["F0w2A"], "w2B": d["F0w2B"],
            "moe_w": d["moe_w"], "moe_i": d["moe_i"], "sel8": d["sel8"]},
            nb=nb))
        stage(lambda: emit_stage3(B, {
            "y2T": act["y2T"], "y3T": act["y3T"], "Wq": d["L1Wq"],
            "Wk": d["L1Wk"], "Wv": d["L1Wv"], "Wo": d["L1Wo"],
            "rf": d["rf"]}, nb=nb))
        for l in range(2):
            src = act["y3T"] if l == 0 else act["z0T"]
            dst = act["z0T"] if l == 0 else act["z1T"]
            io4 = {"xT": src, "oT": dst,
                   "TAq": d["TAq"], "TBq": d["TBq"],
                   "TAk": d["TAk"], "TBk": d["TBk"]}
            for name in ("wqkv", "qs", "ks", "wo", "lng", "lnb",
                         "w1", "b1", "w2", "b2"):
                io4[name] = d[f"w4{l}_{name}"]
            stage(lambda l=l, io4=io4: emit_stage4_layer(B, io4, l, nb=nb))
        stage(lambda: emit_final(B, {
            "aT": act["y3T"], "bT": act["z1T"], "y": d["y"]}, nb=nb))

        B.p.emit(octx)
    return nc


_PROG_CACHE = {}


def _get_prog(name, builder):
    if name not in _PROG_CACHE:
        _PROG_CACHE[name] = builder()
    return _PROG_CACHE[name]


def _make_in_maps(inp):
    cores = list(range(NCORES))
    tabs = make_xpos_tables()
    sel8 = np.zeros((8, 512), np.float32)
    for ra in range(4):
        for j in range(128):
            sel8[2 * ra + (j >= 64), ra * 128 + j] = 1.0

    x = inp["L0_attn_x"]
    zz = np.zeros((T, 2), np.float32)
    zi = np.zeros((T, 2), np.int32)

    base = {
        "L0Wp": inp["L0_attn_Wp"], "L0bp": inp["L0_attn_bp"],
        "L0Wv": inp["L0_attn_Wv"], "L0Wo": inp["L0_attn_Wo"],
        "F0w1A": inp["L0_ffn_w1_A"], "F0w1B": inp["L0_ffn_w1_B"],
        "F0w2A": inp["L0_ffn_w2_A"], "F0w2B": inp["L0_ffn_w2_B"],
        "sel8": sel8,
        "L1Wq": inp["L1_attn_Wq"], "L1Wk": inp["L1_attn_Wk"],
        "L1Wv": inp["L1_attn_Wv"], "L1Wo": inp["L1_attn_Wo"],
        "rf": inp["L1_attn_rf"],
        "TAq": tabs[0], "TBq": tabs[1], "TAk": tabs[2], "TBk": tabs[3],
    }
    for l in range(2):
        for name, key in (("wqkv", "L1_ffn_wqkv"), ("qs", "L1_ffn_q_scale"),
                          ("ks", "L1_ffn_k_scale"), ("wo", "L1_ffn_wo"),
                          ("lng", "L1_ffn_ln_g"), ("lnb", "L1_ffn_ln_b"),
                          ("w1", "L1_ffn_w1"), ("b1", "L1_ffn_b1"),
                          ("w2", "L1_ffn_w2"), ("b2", "L1_ffn_b2")):
            base[f"w4{l}_{name}"] = np.ascontiguousarray(inp[key][l])

    in_maps = []
    for c in cores:
        m = dict(base)
        m["x"] = np.ascontiguousarray(x[[c, c + 8]])
        m["moe_w"] = (np.ascontiguousarray(
            inp["L0_ffn_expert_weights"][c * T:(c + 1) * T])
            if c < 4 else zz)
        m["moe_i"] = (np.ascontiguousarray(
            inp["L0_ffn_expert_indices"][c * T:(c + 1) * T].astype(np.int32))
            if c < 4 else zi)
        in_maps.append(m)
    return in_maps


def kernel(**inputs):
    inp = {k: np.ascontiguousarray(np.asarray(v)) for k, v in inputs.items()}
    nb = 2
    cores = list(range(NCORES))
    tabs = make_xpos_tables()
    sel8 = np.zeros((8, 512), np.float32)
    for ra in range(4):
        for j in range(128):
            sel8[2 * ra + (j >= 64), ra * 128 + j] = 1.0

    x = inp["L0_attn_x"]
    per_core_x = [np.ascontiguousarray(x[[c, c + 8]]) for c in cores]

    nc = _get_prog("ingest", lambda: build_ingest_program(nb=nb))
    res = run_bass_kernel_spmd(nc, [{"x": per_core_x[c]} for c in cores],
                               core_ids=cores)
    xT = [r["xT"] for r in res.results]

    nc = _get_prog("s1", lambda: build_stage1_program(nb=nb))
    res = run_bass_kernel_spmd(nc, [
        {"xT": xT[c], "Wp": inp["L0_attn_Wp"], "bp": inp["L0_attn_bp"],
         "Wv": inp["L0_attn_Wv"], "Wo": inp["L0_attn_Wo"]} for c in cores],
        core_ids=cores)
    yT = [r["yT"] for r in res.results]

    zz = np.zeros((T, 2), np.float32)
    zi = np.zeros((T, 2), np.int32)
    nc = _get_prog("s2", lambda: build_stage2_program(nb=nb))
    res = run_bass_kernel_spmd(nc, [
        {"yT": yT[c], "w1A": inp["L0_ffn_w1_A"], "w1B": inp["L0_ffn_w1_B"],
         "w2A": inp["L0_ffn_w2_A"], "w2B": inp["L0_ffn_w2_B"],
         "moe_w": (np.ascontiguousarray(
             inp["L0_ffn_expert_weights"][c * T:(c + 1) * T])
             if c < 4 else zz),
         "moe_i": (np.ascontiguousarray(
             inp["L0_ffn_expert_indices"][c * T:(c + 1) * T].astype(np.int32))
             if c < 4 else zi),
         "sel8": sel8} for c in cores], core_ids=cores)
    y2T = [r["y2T"] for r in res.results]

    nc = _get_prog("s3", lambda: build_stage3_program(nb=nb))
    res = run_bass_kernel_spmd(nc, [
        {"y2T": y2T[c], "Wq": inp["L1_attn_Wq"], "Wk": inp["L1_attn_Wk"],
         "Wv": inp["L1_attn_Wv"], "Wo": inp["L1_attn_Wo"],
         "rf": inp["L1_attn_rf"]} for c in cores], core_ids=cores)
    y3T = [r["y3T"] for r in res.results]

    cur = y3T
    for l in range(2):
        nc = _get_prog(f"s4_{l}", lambda l=l: build_stage4_program(l=l, nb=nb))
        res = run_bass_kernel_spmd(nc, [
            {"xT": cur[c],
             "wqkv": np.ascontiguousarray(inp["L1_ffn_wqkv"][l]),
             "qs": np.ascontiguousarray(inp["L1_ffn_q_scale"][l]),
             "ks": np.ascontiguousarray(inp["L1_ffn_k_scale"][l]),
             "wo": np.ascontiguousarray(inp["L1_ffn_wo"][l]),
             "lng": np.ascontiguousarray(inp["L1_ffn_ln_g"][l]),
             "lnb": np.ascontiguousarray(inp["L1_ffn_ln_b"][l]),
             "w1": np.ascontiguousarray(inp["L1_ffn_w1"][l]),
             "b1": np.ascontiguousarray(inp["L1_ffn_b1"][l]),
             "w2": np.ascontiguousarray(inp["L1_ffn_w2"][l]),
             "b2": np.ascontiguousarray(inp["L1_ffn_b2"][l]),
             "TAq": tabs[0], "TBq": tabs[1], "TAk": tabs[2], "TBk": tabs[3]}
            for c in cores], core_ids=cores)
        cur = [r["oT"] for r in res.results]

    nc = _get_prog("final", lambda: build_final_program(nb=nb))
    res = run_bass_kernel_spmd(nc, [
        {"aT": y3T[c], "bT": cur[c]} for c in cores], core_ids=cores)
    ys = [r["y"] for r in res.results]

    out = np.zeros((16, T, 1024), np.float32)
    for c in cores:
        out[c] = ys[c][0]
        out[c + 8] = ys[c][1]
    return out



# version 05 (final)
# speedup vs baseline: 1.0711x; 1.0711x over previous
import numpy as np
from ml_dtypes import bfloat16
from contextlib import ExitStack

import concourse.bass as bass
import concourse.mybir as mybir
from concourse.bass_utils import run_bass_kernel_spmd
from concourse.alu_op_type import AluOpType

F32 = mybir.dt.float32
BF16 = mybir.dt.bfloat16
I32 = mybir.dt.int32
AF = mybir.ActivationFunctionType
AX = mybir.AxisListType
OP = AluOpType

NCORES = 8
T = 2048
D = 1024
MAGIC = 12582912.0

ENGINES = ("sync", "scalar", "vector", "tensor", "gpsimd")


class Prog:
    def __init__(self, nc):
        self.nc = nc
        self.items = []
        self.counts = {}
        self.last_write = {}
        self.readers = {}
        self.known = {}
        self.debug_log = []
        self.psum_regions = set()
        self.stage_marks = []
        for e in ENGINES:
            self.counts["c_" + e] = 0

    def op(self, engine, fn, reads=(), writes=(), dma=False, stream=""):
        sem = (("d_" + engine + stream) if dma else ("c_" + engine))
        inc = 16 if dma else 1
        if sem not in self.counts:
            self.counts[sem] = 0
        if dma:
            reads = list(reads) + ["__q" + sem]
            writes = list(writes) + ["__q" + sem]
        if self.psum_regions:
            pr = [r for r in reads if r in self.psum_regions]
            if pr:
                reads = [r for r in reads if r not in self.psum_regions]
                writes = list(writes) + pr
        waits = {}

        def need(dep):
            dsem, dval, dengine = dep
            if dsem.startswith("d_"):
                dval = self.counts[dsem]
            if dengine == engine == "tensor" and not dma:
                return
            if self.known.get((engine, dsem), 0) >= dval:
                return
            if waits.get(dsem, 0) < dval:
                waits[dsem] = dval

        for r in reads:
            lw = self.last_write.get(r)
            if lw:
                need(lw)
        for w in writes:
            lw = self.last_write.get(w)
            if lw:
                need(lw)
            for rd in self.readers.get(w, ()):
                need(rd)
        for dsem, dval in waits.items():
            k = (engine, dsem)
            self.known[k] = max(self.known.get(k, 0), dval)
        self.counts[sem] += inc
        me = (sem, self.counts[sem], engine)
        for w in writes:
            self.last_write[w] = me
            self.readers[w] = []
        for r in reads:
            self.readers.setdefault(r, []).append(me)
        self.items.append((engine, fn, sorted(waits.items()), (sem, inc)))
        self.debug_log.append((engine, sem, self.counts[sem], tuple(reads),
                               tuple(writes), sorted(waits.items())))

    def barrier(self):
        snapshot = dict(self.counts)
        for e in ENGINES:
            for sem, val in snapshot.items():
                if val > 0 and self.known.get((e, sem), 0) < val:
                    self.items.append((e, None, [(sem, val)], None))
                    self.known[(e, sem)] = val
        if self.stage_marks and self.stage_marks[-1] == len(self.items):
            return
        self.stage_marks.append(len(self.items))

    def emit(self, ctx):
        sems = {}
        for name, val in self.counts.items():
            if val > 0:
                sems[name] = ctx.enter_context(self.nc.semaphore(name))
        marks = list(self.stage_marks)
        if not marks or marks[-1] != len(self.items):
            marks.append(len(self.items))

        def section(name, seg):
            def _(eng):
                for engine, fn, waits, incspec in seg:
                    if engine != name:
                        continue
                    for semname, val in waits:
                        eng.wait_ge(sems[semname], val)
                    if fn is not None:
                        inst = fn(eng)
                        semname, inc = incspec
                        inst.then_inc(sems[semname], inc)
            return _

        start = 0
        for end in marks:
            seg = self.items[start:end]
            start = end
            if not seg:
                continue
            with self.nc.Block() as block:
                for name, reg in (("sync", block.sync), ("scalar", block.scalar),
                                  ("vector", block.vector),
                                  ("tensor", block.tensor),
                                  ("gpsimd", block.gpsimd)):
                    if any(it[0] == name for it in seg):
                        reg(section(name, seg))


def rep_ap(tensor, dims, elem_off, psl=None):
    base = tensor[:] if psl is None else tensor[psl[0]:psl[0] + psl[1], :]
    return bass.AP(base.tensor, base.offset + elem_off,
                   [list(base.ap[0])] + [list(d) for d in dims])


class BD:

    def __init__(self, nc, ctx):
        self.nc = nc
        self.ctx = ctx
        self.p = Prog(nc)
        self.consts = {}
        self._cpool = self.sb("cpool", (128, 32), F32)
        self._cpool_n = 0

    def sb(self, name, shape, dt=F32):
        return self.ctx.enter_context(self.nc.sbuf_tensor(name, list(shape), dt))

    def ps(self, name, shape=(128, 512), dt=F32):
        self.p.psum_regions.add(name)
        return self.ctx.enter_context(self.nc.psum_tensor(name, list(shape), dt))

    def dram(self, name, shape, dt=F32, kind="Internal"):
        return self.nc.dram_tensor(name, list(shape), dt, kind=kind).ap()

    def const(self, *vals):
        for v0 in vals:
            v0 = float(v0)
            v32 = float(np.float32(v0))
            if v0 in self.consts:
                continue
            i = self._cpool_n
            assert i < 32, "const pool full"
            self._cpool_n += 1
            cp = self._cpool
            self.p.op("gpsimd", lambda e, i=i, v=v32: e.memset(cp[:, i:i + 1], v),
                      writes=["cpool"])
            for v in {v0, v32}:
                self.nc.const_aps.register(F32, v, cp[:, i:i + 1])
                self.consts[v] = True


def emit_eye(B, dt, name):
    eye = B.sb(name, (128, 128), dt)
    p = B.p
    p.op("gpsimd", lambda e: e.memset(eye[:], 1.0), writes=[name])
    p.op("gpsimd", lambda e: e.affine_select(out=eye[:], in_=eye[:], compare_op=OP.is_ge,
                                             fill=0.0, base=0, pattern=[[-1, 128]],
                                             channel_multiplier=1),
         reads=[name], writes=[name])
    p.op("gpsimd", lambda e: e.affine_select(out=eye[:], in_=eye[:], compare_op=OP.is_ge,
                                             fill=0.0, base=0, pattern=[[1, 128]],
                                             channel_multiplier=-1),
         reads=[name], writes=[name])
    return eye


def load_w_tiles(B, name, dram_ap, rows, cols, dt=BF16, stream="w"):
    a = rows // 128
    w = B.sb(name, (128, a * cols), dt)
    B.p.op("gpsimd",
           lambda e: e.dma_start(
               out=w[:].rearrange("p (a f) -> p a f", a=a),
               in_=dram_ap.rearrange("(a p) f -> p a f", p=128)),
           writes=[name], dma=True, stream=stream)
    return w


class LNPack:

    def __init__(self, B, prefix, nslices=8):
        self.B = B
        self.n = nslices
        self.prefix = prefix
        self.z16 = B.sb(prefix + "_lnz16", (128, 2 * nslices * 512), BF16)
        self.stats = B.sb(prefix + "_lnst", (1, 2048), F32)
        self.ones_r1f = B.sb(prefix + "_o1f", (1, 128), F32)
        self.ones_c1b = B.sb(prefix + "_oc1", (128, 1), BF16)
        B.p.op("gpsimd", lambda e: e.memset(self.ones_r1f[:], 1.0),
               writes=[prefix + "_o1f"])
        B.p.op("gpsimd", lambda e: e.memset(self.ones_c1b[:], 1.0),
               writes=[prefix + "_oc1"])
        B.const(1e-5)

    def emit(self, z, zkey, psA, psB, psS1, psS2, out_writer):
        B, p, n, pre = self.B, self.B.p, self.n, self.prefix
        z16, stats = self.z16, self.stats
        kz16, ksq = pre + "_lnz16lo", pre + "_lnz16hi"
        kA, kB = psA.name, psB.name
        kS1, kS2 = psS1.name, psS2.name
        p.op("vector", lambda e: e.tensor_copy(z16[:, 0:n * 512], z[:, 0:n * 512]),
             reads=[zkey], writes=[kz16])
        p.op("scalar", lambda e: e.activation(z16[:, n * 512:2 * n * 512],
                                              z[:, 0:n * 512], AF.Square),
             reads=[zkey], writes=[ksq])
        for a in range(n):
            p.op("tensor", lambda e, a=a: e.matmul(
                psS1[0:1, :], self.ones_c1b[:, 0:1], z16[:, a * 512:(a + 1) * 512],
                start=(a == 0), stop=(a == n - 1)),
                reads=[kz16, pre + "_oc1"], writes=[kS1])
        for a in range(n):
            p.op("tensor", lambda e, a=a: e.matmul(
                psS2[0:1, :], self.ones_c1b[:, 0:1],
                z16[:, (n + a) * 512:(n + a + 1) * 512],
                start=(a == 0), stop=(a == n - 1)),
                reads=[ksq, pre + "_oc1"], writes=[kS2])
        inv_n = 1.0 / (n * 128.0)
        B.const(inv_n)
        mu, m2, var, rstd = (stats[0:1, i * 512:(i + 1) * 512] for i in range(4))
        kst = pre + "_lnst"
        p.op("vector", lambda e: e.tensor_scalar(mu, psS1[0:1, :], inv_n, None, OP.mult),
             reads=[kS1, "cpool"], writes=[kst + "mu"])
        p.op("scalar", lambda e: e.activation(m2, psS2[0:1, :], AF.Identity, scale=inv_n),
             reads=[kS2], writes=[kst + "m2"])
        p.op("vector", lambda e: e.tensor_tensor(var, mu, mu, OP.mult),
             reads=[kst + "mu"], writes=[kst + "var"])
        p.op("vector", lambda e: e.tensor_tensor(var, m2, var, OP.subtract),
             reads=[kst + "m2", kst + "var"], writes=[kst + "var"])
        p.op("scalar", lambda e: e.activation(var, var, AF.Ln, bias=1e-5),
             reads=[kst + "var", "cpool"], writes=[kst + "var"])
        p.op("scalar", lambda e: e.activation(rstd, var, AF.Exp, scale=-0.5),
             reads=[kst + "var"], writes=[kst + "rstd"])
        p.op("tensor", lambda e: e.matmul(psA[:], self.ones_r1f[0:1, :], mu,
                                          start=True, stop=True),
             reads=[kst + "mu", pre + "_o1f"], writes=[kA])
        p.op("tensor", lambda e: e.matmul(psB[:], self.ones_r1f[0:1, :], rstd,
                                          start=True, stop=True),
             reads=[kst + "rstd", pre + "_o1f"], writes=[kB])
        for a in range(n):
            p.op("vector", lambda e, a=a: e.tensor_tensor(
                z[:, a * 512:(a + 1) * 512], z[:, a * 512:(a + 1) * 512], psA[:],
                OP.subtract), reads=[zkey, kA], writes=[zkey])
        for a in range(n):
            p.op("vector", lambda e, a=a: e.tensor_tensor(
                z[:, a * 512:(a + 1) * 512], z[:, a * 512:(a + 1) * 512], psB[:],
                OP.mult), reads=[zkey, kB], writes=[zkey])
        out_writer()


def emit_stage1(B, io, nb=2, phases=('feat','attn','wo','ln')):
    nc, p = B.nc, B.p
    H, DEP = 8, 5

    wvo = B.sb("s1_wvo", (128, 8 * 1024), BF16)
    wp = load_w_tiles(B, "s1_wp", io["Wp"], 1024, 40)
    bprow = B.sb("s1_bp", (1, 40), F32)
    p.op("sync", lambda e: e.dma_start(out=bprow[:],
                                       in_=io["bp"].rearrange("(a f) -> a f", a=1)),
         writes=["s1_bp"], dma=True)
    ones_sq = B.sb("s1_onsq", (128, 128), BF16)
    p.op("gpsimd", lambda e: e.memset(ones_sq[:], 1.0), writes=["s1_onsq"])
    eye = emit_eye(B, BF16, "eye_bf")
    ln = LNPack(B, "s1")
    B.const(1.0, -1.0, 2.0)

    P0, P1, P2, P3, P4, P5 = (B.ps(f"s1_P{i}") for i in range(6))
    PTT = B.ps("s1_PT", (128, 1024), BF16)

    bpbc = B.sb("s1_bpbc", (128, 40), F32)
    p.op("tensor", lambda e: e.matmul(P5[:, 0:40], ln.ones_r1f[0:1, :], bprow[:],
                                      start=True, stop=True),
         reads=["s1_bp", "s1_o1f"], writes=["s1_P5"])
    p.op("vector", lambda e: e.tensor_copy(bpbc[:], P5[:, 0:40]),
         reads=["s1_P5"], writes=["s1_bpbc"])

    x16 = B.sb("s1_x16", (128, 8 * 512), BF16)
    V16 = B.sb("s1_V16", (128, 16 * 1024), BF16)
    FqT = B.sb("s1_FqT", (128, 4 * T), BF16)
    FkT = B.sb("s1_FkT", (128, 4 * T), BF16)
    At = B.sb("s1_At", (128, 16 * 512), BF16)
    ctxT = B.sb("s1_ctxT", (128, 8 * T), BF16)
    ptile = B.sb("s1_p", (128, 48), F32)
    uq = B.sb("s1_uq", (128, 80), F32)
    vk = B.sb("s1_vk", (128, 80), F32)
    Ftf = B.sb("s1_Ftf", (128, 512), F32)
    F16 = B.sb("s1_F16", (128, 512), BF16)
    zrec = B.sb("s1_zrec", (128, 512), F32)
    zln = B.sb("s1_zln", (128, 512), F32)
    xf32 = B.sb("s1_xf32", (128, 8 * 512), F32)
    zsb = B.sb("s1_z", (128, 8 * 512), F32)

    xv = io["xT"].rearrange("(a p) t -> p a t", p=128)
    yv = io["yT"].rearrange("(a p) t -> p a t", p=128)

    for b in range(nb):
        t0 = b * T
        p.op("gpsimd", lambda e: e.dma_start(
            out=wvo[:].rearrange("p (a f) -> p a f", a=8),
            in_=io["Wv"].rearrange("(a p) f -> p a f", p=128)),
            writes=["s1_wvo"], dma=True, stream="w")
        for c in range(4) if 'feat' in phases else []:
            p.op("gpsimd", lambda e, t0=t0, c=c: e.dma_start(
                out=x16[:].rearrange("p (a t) -> p a t", a=8),
                in_=xv[:, :, t0 + c * 512:t0 + c * 512 + 512]),
                writes=["s1_x16"], dma=True, stream="x")
            for ttc in range(4):
                tt = c * 4 + ttc
                for a in range(8):
                    p.op("tensor", lambda e, a=a, ttc=ttc: e.matmul(
                        P0[:, 0:40],
                        x16[:, a * 512 + ttc * 128:a * 512 + ttc * 128 + 128],
                        wp[:, a * 40:(a + 1) * 40], start=(a == 0), stop=(a == 7)),
                        reads=["s1_x16", "s1_wp"], writes=["s1_P0"])
                p.op("vector", lambda e: e.tensor_tensor(ptile[:, 0:40], P0[:, 0:40],
                                                         bpbc[:], OP.add),
                     reads=["s1_P0", "s1_bpbc"], writes=["s1_p"])
                p.op("scalar", lambda e: e.activation(ptile[:, 0:40], ptile[:, 0:40],
                                                      AF.Sigmoid),
                     reads=["s1_p"], writes=["s1_p"])
                uq0 = rep_ap(uq, [[10, 8], [2, 5], [1, 1]], 0)
                uq1 = rep_ap(uq, [[10, 8], [2, 5], [1, 1]], 1)
                vk0 = rep_ap(vk, [[10, 8], [2, 5], [1, 1]], 0)
                vk1 = rep_ap(vk, [[10, 8], [2, 5], [1, 1]], 1)
                pv = rep_ap(ptile, [[5, 8], [1, 5], [0, 1]], 0)
                p.op("vector", lambda e: e.tensor_scalar(uq0, pv, -1.0, 1.0,
                                                         OP.mult, OP.add),
                     reads=["s1_p", "cpool"], writes=["s1_uq"])
                p.op("vector", lambda e: e.tensor_scalar(uq1, pv, 2.0, -1.0,
                                                         OP.mult, OP.add),
                     reads=["s1_p", "cpool"], writes=["s1_uq"])
                p.op("gpsimd", lambda e: e.memset(vk0, 1.0), writes=["s1_vk"])
                p.op("vector", lambda e: e.tensor_copy(vk1, pv),
                     reads=["s1_p"], writes=["s1_vk"])
                for (usrc, ukey, Fdst, fkey) in ((uq, "s1_uq", FqT, "s1_FqT"),
                                                 (vk, "s1_vk", FkT, "s1_FkT")):
                    p.op("gpsimd", lambda e: e.memset(
                        rep_ap(Ftf, [[64, 8], [1, 2]], 62), 0.0),
                        writes=["s1_Ftf"])
                    p.op("vector", lambda e, u=usrc: e.tensor_copy(
                        rep_ap(Ftf, [[64, 8], [1, 2]], 0),
                        rep_ap(u, [[10, 8], [1, 2]], 0)),
                        reads=[ukey], writes=["s1_Ftf"])
                    sp, sc = 0, 2
                    for j in range(1, DEP):
                        w = 2 ** j
                        p.op("vector", lambda e, u=usrc, sp=sp, sc=sc, j=j, w=w:
                             e.tensor_tensor(
                                 rep_ap(Ftf, [[64, 8], [1, 2 * w]], sc),
                                 rep_ap(Ftf, [[64, 8], [1, w], [0, 2]], sp),
                                 rep_ap(u, [[10, 8], [0, w], [1, 2]], 2 * j),
                                 OP.mult),
                             reads=["s1_Ftf", ukey], writes=["s1_Ftf"])
                        sp, sc = sc, sc + 2 * w
                    p.op("vector", lambda e: e.tensor_copy(F16[:], Ftf[:]),
                         reads=["s1_Ftf"], writes=["s1_F16"])
                    for g in range(4):
                        p.op("tensor", lambda e, g=g: e.transpose(
                            PTT[:, g * 128:(g + 1) * 128],
                            F16[:, g * 128:(g + 1) * 128], eye[:]),
                            reads=["s1_F16", "eye_bf"], writes=["s1_PT"])
                    for g in range(4):
                        p.op("vector", lambda e, g=g, Fd=Fdst, tt=tt: e.tensor_copy(
                            Fd[:, g * T + tt * 128:g * T + tt * 128 + 128],
                            PTT[:, g * 128:(g + 1) * 128]),
                            reads=["s1_PT"], writes=[fkey])
                for vc in range(2):
                    ps, pk = ((P1, "s1_P1") if vc == 0 else (P2, "s1_P2"))
                    for a in range(8):
                        p.op("tensor", lambda e, a=a, ttc=ttc, vc=vc, ps=ps: e.matmul(
                            ps[:],
                            x16[:, a * 512 + ttc * 128:a * 512 + ttc * 128 + 128],
                            wvo[:, a * 1024 + vc * 512:a * 1024 + vc * 512 + 512],
                            start=(a == 0), stop=(a == 7)),
                            reads=["s1_x16", "s1_wvo"], writes=[pk])
                    p.op("vector", lambda e, tt=tt, vc=vc, ps=ps, pk=pk: e.tensor_copy(
                        V16[:, tt * 1024 + vc * 512:tt * 1024 + vc * 512 + 512],
                        ps[:]), reads=[pk], writes=["s1_V16"])

        p.op("gpsimd", lambda e: e.dma_start(
            out=wvo[:].rearrange("p (a f) -> p a f", a=8),
            in_=io["Wo"].rearrange("(a p) f -> p a f", p=128)),
            writes=["s1_wvo"], dma=True, stream="w")

        for h in range(8) if 'attn' in phases else []:
            g, hp = h // 2, (h % 2) * 64
            for qc in range(4):
                nkt = 4 * qc + 4
                for kt in range(nkt):
                    ps, pk = ((P0, "s1_P0") if kt % 2 == 0 else (P1, "s1_P1"))
                    p.op("tensor", lambda e, kt=kt, qc=qc, pss=ps, g=g, hp=hp: e.matmul(
                        pss[:],
                        FkT[hp:hp + 64, g * T + kt * 128:g * T + kt * 128 + 128],
                        FqT[hp:hp + 64, g * T + qc * 512:g * T + qc * 512 + 512],
                        start=True, stop=True),
                        reads=["s1_FkT", "s1_FqT"], writes=[pk])
                    p.op("scalar", lambda e, kt=kt, pss=ps: e.activation(
                        At[:, kt * 512:kt * 512 + 512], pss[:], AF.Exp, scale=0.2),
                        reads=[pk], writes=[f"s1_At#{kt}"])
                    if kt >= 4 * qc:
                        p.op("gpsimd", lambda e, kt=kt, qc=qc: e.affine_select(
                            out=At[:, kt * 512:kt * 512 + 512],
                            in_=At[:, kt * 512:kt * 512 + 512],
                            compare_op=OP.is_ge, fill=0.0,
                            base=qc * 512 - kt * 128, pattern=[[1, 512]],
                            channel_multiplier=-1),
                            reads=[f"s1_At#{kt}"], writes=[f"s1_At#{kt}"])
                for kt in range(nkt):
                    p.op("tensor", lambda e, kt=kt, h=h, nkt=nkt: e.matmul(
                        P2[:], V16[:, kt * 1024 + h * 128:kt * 1024 + h * 128 + 128],
                        At[:, kt * 512:kt * 512 + 512],
                        start=(kt == 0), stop=(kt == nkt - 1)),
                        reads=["s1_V16", f"s1_At#{kt}"], writes=["s1_P2"])
                for kt in range(nkt):
                    p.op("tensor", lambda e, kt=kt, nkt=nkt: e.matmul(
                        P3[:], ones_sq[:], At[:, kt * 512:kt * 512 + 512],
                        start=(kt == 0), stop=(kt == nkt - 1)),
                        reads=["s1_onsq", f"s1_At#{kt}"], writes=["s1_P3"])
                p.op("scalar", lambda e: e.activation(zln[:], P3[:], AF.Ln),
                     reads=["s1_P3"], writes=["s1_zln"])
                p.op("scalar", lambda e: e.activation(zrec[:], zln[:], AF.Exp,
                                                      scale=-1.0),
                     reads=["s1_zln"], writes=["s1_zrec"])
                p.op("vector", lambda e, h=h, qc=qc: e.tensor_tensor(
                    ctxT[:, h * T + qc * 512:h * T + qc * 512 + 512], P2[:],
                    zrec[:], OP.mult),
                    reads=["s1_P2", "s1_zrec"], writes=["s1_ctxT"])

        for c in range(4) if 'wo' in phases else []:
            p.op("sync", lambda e, c=c, t0=t0: e.dma_start(
                out=xf32[:].rearrange("p (a t) -> p a t", a=8),
                in_=xv[:, :, t0 + c * 512:t0 + c * 512 + 512]),
                writes=["s1_xf32"], dma=True, stream="r")
            for o in range(8):
                ps, pk = ((P0, "s1_P0") if o % 2 == 0 else (P1, "s1_P1"))
                for a in range(8):
                    p.op("tensor", lambda e, a=a, o=o, c=c, ps=ps: e.matmul(
                        ps[:], wvo[:, a * 1024 + o * 128:a * 1024 + o * 128 + 128],
                        ctxT[:, a * T + c * 512:a * T + c * 512 + 512],
                        start=(a == 0), stop=(a == 7)),
                        reads=["s1_wvo", "s1_ctxT"], writes=[pk])
                p.op("vector", lambda e, o=o, ps=ps, pk=pk: e.tensor_tensor(
                    zsb[:, o * 512:o * 512 + 512], ps[:],
                    xf32[:, o * 512:o * 512 + 512], OP.add),
                    reads=[pk, "s1_xf32"], writes=["s1_z"])

            def writer(c=c, t0=t0):
                B.p.op("sync", lambda e: e.dma_start(
                    out=yv[:, :, t0 + c * 512:t0 + c * 512 + 512],
                    in_=zsb[:].rearrange("p (a t) -> p a t", a=8)),
                    reads=["s1_z"], dma=True, stream="o")
            if 'ln' in phases:
                ln.emit(zsb, "s1_z", P4, P5, P0, P1, writer)
            else:
                writer()


def build_stage1_program(nb=2, phases=('feat','attn','wo','ln')):
    nc = bass.Bass("TRN2", target_bir_lowering=False)
    with ExitStack() as ctx:
        B = BD(nc, ctx)
        io = {
            "xT": B.dram("xT", (1024, nb * T), F32, kind="ExternalInput"),
            "yT": B.dram("yT", (1024, nb * T), F32, kind="ExternalOutput"),
            "Wp": B.dram("Wp", (1024, 40), F32, kind="ExternalInput"),
            "bp": B.dram("bp", (40,), F32, kind="ExternalInput"),
            "Wv": B.dram("Wv", (1024, 1024), F32, kind="ExternalInput"),
            "Wo": B.dram("Wo", (1024, 1024), F32, kind="ExternalInput"),
        }
        emit_stage1(B, io, nb=nb, phases=phases)
        B.p.barrier()
        B.p.emit(ctx)
    return nc




def emit_stage2(B, io, nb=2, phases=('wload', 'gates', 'moe', 'slot1')):
    nc, p = B.nc, B.p
    E, R, FF = 8, 64, 4096
    ln = LNPack(B, "s2")
    eye = emit_eye(B, BF16, "s2_eye")
    P0, P1, P2, P3, P4, P5 = (B.ps(f"s2_P{i}") for i in range(6))
    PTT = B.ps("s2_PT", (128, 1024), BF16)

    w1b = load_w_tiles(B, "s2_w1b", io["w1B"], 512, 4096)
    w2b = load_w_tiles(B, "s2_w2b", io["w2B"], 512, 1024)
    sel8 = B.sb("s2_sel8", (8, 512), BF16)
    p.op("gpsimd", lambda e: e.dma_start(out=sel8[:], in_=io["sel8"]),
         writes=["s2_sel8"], dma=True, stream="w")
    w1at = B.sb("s2_w1at", (128, 8 * 512), BF16)
    w2at = B.sb("s2_w2at", (128, 32 * 512), BF16)
    wtmp = B.sb("s2_wtmp", (128, 4096), BF16)
    for (src, wdst, kdst, nda) in (((io["w1A"], w1at, "s2_w1at", 8),
                                    (io["w2A"], w2at, "s2_w2at", 32))
                                   if 'wload' in phases else ()):
        for ra in range(4):
            p.op("gpsimd", lambda e, src=src, ra=ra, nda=nda: e.dma_start(
                out=wtmp[:, 0:nda * 128],
                in_=src[ra * 128:(ra + 1) * 128, :]),
                writes=["s2_wtmp"], dma=True, stream="w")
            for da in range(nda):
                p.op("tensor", lambda e, da=da: e.transpose(
                    PTT[:, 0:128], wtmp[:, da * 128:da * 128 + 128], eye[:]),
                    reads=["s2_wtmp", "s2_eye"], writes=["s2_PT"])
                p.op("vector", lambda e, ra=ra, da=da, w=wdst: e.tensor_copy(
                    w[:, da * 512 + ra * 128:da * 512 + ra * 128 + 128],
                    PTT[:, 0:128]), reads=["s2_PT"], writes=[kdst])

    gw = B.sb("s2_gw", (128, 32), F32)
    gi = B.sb("s2_gi", (128, 32), I32)
    gif = B.sb("s2_gif", (128, 32), F32)
    gmat = B.sb("s2_g", (128, 128), F32)
    g16 = B.sb("s2_g16", (128, 128), BF16)
    gT = B.sb("s2_gT", (8, 2048), BF16)
    gtmp = B.sb("s2_gtmp", (128, 48), F32)
    p.op("sync", lambda e: e.dma_start(
        out=gw[:].rearrange("p (tt k) -> p tt k", k=2),
        in_=io["moe_w"].rearrange("(tt p) k -> p tt k", p=128)),
        writes=["s2_gw"], dma=True)
    p.op("sync", lambda e: e.dma_start(
        out=gi[:].rearrange("p (tt k) -> p tt k", k=2),
        in_=io["moe_i"].rearrange("(tt p) k -> p tt k", p=128)),
        writes=["s2_gi"], dma=True)
    p.op("vector", lambda e: e.tensor_copy(gif[:], gi[:]), reads=["s2_gi"],
         writes=["s2_gif"])
    B.const(*[float(e) for e in range(8)])
    for e_ in range(E) if 'gates' in phases else []:
        for k in range(2):
            p.op("vector", lambda ee, e_=e_, k=k: ee.tensor_scalar(
                rep_ap(gtmp, [[3, 16], [1, 1]], k),
                rep_ap(gif, [[2, 16], [1, 1]], k), float(e_), None, OP.is_equal),
                reads=["s2_gif", "cpool"], writes=["s2_gtmp"])
            p.op("vector", lambda ee, e_=e_, k=k: ee.tensor_tensor(
                rep_ap(gtmp, [[3, 16], [1, 1]], k),
                rep_ap(gtmp, [[3, 16], [1, 1]], k),
                rep_ap(gw, [[2, 16], [1, 1]], k), OP.mult),
                reads=["s2_gtmp", "s2_gw"], writes=["s2_gtmp"])
        p.op("vector", lambda ee, e_=e_: ee.tensor_tensor(
            rep_ap(gmat, [[8, 16], [1, 1]], e_),
            rep_ap(gtmp, [[3, 16], [1, 1]], 0),
            rep_ap(gtmp, [[3, 16], [1, 1]], 1), OP.add),
            reads=["s2_gtmp"], writes=["s2_g"])
    p.op("vector", lambda e: e.tensor_copy(g16[:], gmat[:]), reads=["s2_g"],
         writes=["s2_g16"])
    for tt in range(16) if 'gates' in phases else []:
        p.op("tensor", lambda e, tt=tt: e.transpose(
            PTT[0:8, 0:128], g16[:, tt * 8:tt * 8 + 8], eye[:]),
            reads=["s2_g16", "s2_eye"], writes=["s2_PT"])
        p.op("vector", lambda e, tt=tt: e.tensor_copy(
            gT[0:8, tt * 128:tt * 128 + 128], PTT[0:8, 0:128]),
            reads=["s2_PT"], writes=["s2_gT"])

    y16 = B.sb("s2_y16", (128, 8 * 512), BF16)
    yf32 = B.sb("s2_yf32", (128, 8 * 512), F32)
    hc = B.sb("s2_hc", (128, 32 * 512), BF16)
    t1s = B.sb("s2_t1s", (128, 512), BF16)
    t2s = B.sb("s2_t2s", (128, 4 * 512), BF16)
    gbcp = B.sb("s2_gbcp", (128, 512), BF16)

    P6 = B.ps("s2_P6")

    yv = io["yT"].rearrange("(a p) t -> p a t", p=128)
    ov = io["y2T"].rearrange("(a p) t -> p a t", p=128)

    for tc in (range(4) if any(p.startswith('moe') for p in phases) else []):
        p.op("gpsimd", lambda e, tc=tc: e.dma_start(
            out=y16[:].rearrange("p (a t) -> p a t", a=8),
            in_=yv[:, :, tc * 512:tc * 512 + 512]),
            writes=["s2_y16"], dma=True, stream="x")
        for ra in (range(4) if 'moe_t1' in phases or 'moe' in phases else []):
            for da in range(8):
                p.op("tensor", lambda e, da=da, ra=ra: e.matmul(
                    P0[:], w1at[:, da * 512 + ra * 128:da * 512 + ra * 128 + 128],
                    y16[:, da * 512:da * 512 + 512],
                    start=(da == 0), stop=(da == 7)),
                    reads=["s2_w1at", "s2_y16"], writes=["s2_P0"])
            p.op("vector", lambda e: e.tensor_copy(t1s[:], P0[:]),
                 reads=["s2_P0"], writes=["s2_t1s"])
            p.op("tensor", lambda e, ra=ra, tc=tc: e.matmul(
                P4[:], sel8[:, ra * 128:ra * 128 + 128],
                gT[0:8, tc * 512:tc * 512 + 512], start=True, stop=True),
                reads=["s2_sel8", "s2_gT"], writes=["s2_P4"])
            p.op("vector", lambda e: e.tensor_copy(gbcp[:], P4[:]),
                 reads=["s2_P4"], writes=["s2_gbcp"])
            for par in (range(2) if 'moe_hc' in phases or 'moe' in phases else []):
                e_, ro = 2 * ra + par, par * 64
                for ff in range(32):
                    ps, pk = ((P1, "s2_P1") if ff % 2 == 0 else (P2, "s2_P2"))
                    p.op("tensor", lambda e, ff=ff, ra=ra, ro=ro, ps=ps: e.matmul(
                        ps[:], w1b[ro:ro + 64,
                                   ra * 4096 + ff * 128:ra * 4096 + ff * 128 + 128],
                        t1s[ro:ro + 64, :], start=True, stop=True),
                        reads=["s2_w1b", "s2_t1s"], writes=[pk])
                    p.op("scalar", lambda e, ff=ff, ps=ps, pk=pk: e.activation(
                        hc[:, ff * 512:ff * 512 + 512], ps[:], AF.Gelu),
                        reads=[pk], writes=[f"s2_hc#{ff}"])
                pt2, kt2 = ((P3, "s2_P3") if par == 0 else (P6, "s2_P6"))
                for ff in (range(32) if 'moe_t2' in phases or 'moe' in phases else []):
                    p.op("tensor", lambda e, ff=ff, e_=e_, pt2=pt2, ro=ro: e.matmul(
                        pt2[ro:ro + 64, :],
                        w2at[:, ff * 512 + e_ * 64:ff * 512 + e_ * 64 + 64],
                        hc[:, ff * 512:ff * 512 + 512],
                        start=(ff == 0), stop=(ff == 31),
                        tile_position=(0, ro)),
                        reads=["s2_w2at", f"s2_hc#{ff}"], writes=[kt2])
                if 'moe_t2' in phases or 'moe' in phases:
                    p.op("vector", lambda e, ra=ra, ro=ro, pt2=pt2, kt2=kt2:
                         e.tensor_tensor(
                             t2s[ro:ro + 64, ra * 512:ra * 512 + 512],
                             pt2[ro:ro + 64, :],
                             gbcp[ro:ro + 64, :], OP.mult),
                         reads=[kt2, "s2_gbcp"], writes=[f"s2_t2s#{ra}"])
        p.op("sync", lambda e, tc=tc: e.dma_start(
            out=yf32[:].rearrange("p (a t) -> p a t", a=8),
            in_=yv[:, :, tc * 512:tc * 512 + 512]),
            writes=["s2_yf32"], dma=True, stream="r")
        for o in (range(8) if 'moe_acc' in phases or 'moe' in phases else []):
            ps, pk = ((P0, "s2_P0") if o % 2 == 0 else (P1, "s2_P1"))
            for ra in range(4):
                p.op("tensor", lambda e, ra=ra, o=o, ps=ps: e.matmul(
                    ps[:], w2b[:, ra * 1024 + o * 128:ra * 1024 + o * 128 + 128],
                    t2s[:, ra * 512:ra * 512 + 512],
                    start=(ra == 0), stop=(ra == 3)),
                    reads=["s2_w2b", f"s2_t2s#{ra}"], writes=[pk])
            p.op("vector", lambda e, o=o, ps=ps, pk=pk: e.tensor_tensor(
                yf32[:, o * 512:o * 512 + 512], ps[:],
                yf32[:, o * 512:o * 512 + 512], OP.add),
                reads=[pk, "s2_yf32"], writes=["s2_yf32"])

        def writer0(tc=tc):
            B.p.op("sync", lambda e: e.dma_start(
                out=ov[:, :, tc * 512:tc * 512 + 512],
                in_=yf32[:].rearrange("p (a t) -> p a t", a=8)),
                reads=["s2_yf32"], dma=True, stream="o")
        if 'moe_ln' in phases or 'moe' in phases:
            ln.emit(yf32, "s2_yf32", P4, P5, P2, P3, writer0)

    for b in range(1, nb) if 'slot1' in phases else []:
        for tc in range(4):
            p.op("sync", lambda e, tc=tc, b=b: e.dma_start(
                out=yf32[:].rearrange("p (a t) -> p a t", a=8),
                in_=yv[:, :, b * T + tc * 512:b * T + tc * 512 + 512]),
                writes=["s2_yf32"], dma=True, stream="r")

            def writer1(tc=tc, b=b):
                B.p.op("sync", lambda e: e.dma_start(
                    out=ov[:, :, b * T + tc * 512:b * T + tc * 512 + 512],
                    in_=yf32[:].rearrange("p (a t) -> p a t", a=8)),
                    reads=["s2_yf32"], dma=True, stream="o")
            ln.emit(yf32, "s2_yf32", P4, P5, P2, P3, writer1)


def build_stage2_program(nb=2):
    nc = bass.Bass("TRN2", target_bir_lowering=False)
    with ExitStack() as ctx:
        B = BD(nc, ctx)
        io = {
            "yT": B.dram("yT", (1024, nb * T), F32, kind="ExternalInput"),
            "y2T": B.dram("y2T", (1024, nb * T), F32, kind="ExternalOutput"),
            "w1A": B.dram("w1A", (512, 1024), F32, kind="ExternalInput"),
            "w1B": B.dram("w1B", (512, 4096), F32, kind="ExternalInput"),
            "w2A": B.dram("w2A", (512, 4096), F32, kind="ExternalInput"),
            "w2B": B.dram("w2B", (512, 1024), F32, kind="ExternalInput"),
            "moe_w": B.dram("moe_w", (T, 2), F32, kind="ExternalInput"),
            "moe_i": B.dram("moe_i", (T, 2), I32, kind="ExternalInput"),
            "sel8": B.dram("sel8", (8, 512), F32, kind="ExternalInput"),
        }
        emit_stage2(B, io, nb=nb)
        B.p.barrier()
        B.p.emit(ctx)
    return nc




def emit_stage3(B, io, nb=2, phases=('p1k','p1v','p1f','p2q','p2f','p2t','p2w')):
    nc, p = B.nc, B.p
    NH = 16
    INV2PI = float(1.0 / (2.0 * np.pi))
    C2 = 1.0 / 128.0
    ln = LNPack(B, "s3")
    eye = emit_eye(B, BF16, "s3_eye")
    B.const(MAGIC, -MAGIC, 1e-6, C2, INV2PI, 0.25)

    P0, P1, P2 = B.ps("s3_P0"), B.ps("s3_P1"), B.ps("s3_P2")
    G0, G1, G2, G3 = B.ps("s3_G0"), B.ps("s3_G1"), B.ps("s3_G2"), B.ps("s3_G3")
    PTT = B.ps("s3_PT", (128, 1024), BF16)
    TSB = [G0, G1, G2, G3]
    TSK = ["s3_G0", "s3_G1", "s3_G2", "s3_G3"]

    wA = B.sb("s3_wA", (128, 8 * 1024), BF16)
    wB = B.sb("s3_wB", (128, 8 * 1024), BF16)

    def loadw(dst, key, src):
        p.op("gpsimd", lambda e, dst=dst, src=src: e.dma_start(
            out=dst[:].rearrange("p (a f) -> p a f", a=8),
            in_=io[src].rearrange("(a p) f -> p a f", p=128)),
            writes=[key], dma=True, stream="w")

    rff = B.sb("s3_rff", (128, 256), F32)
    rfbd = B.sb("s3_rfbd", (128, 256), BF16)
    p.op("gpsimd", lambda e: e.memset(rff[:], 0.0), writes=["s3_rff"])
    p.op("sync", lambda e: e.dma_start(out=rff[0:64, 0:128], in_=io["rf"]),
         writes=["s3_rff"], dma=True)
    p.op("sync", lambda e: e.dma_start(out=rff[64:128, 128:256], in_=io["rf"]),
         writes=["s3_rff"], dma=True)
    p.op("vector", lambda e: e.tensor_scalar(rfbd[:], rff[:], INV2PI, None,
                                             OP.mult),
         reads=["s3_rff", "cpool"], writes=["s3_rfbd"])

    biasq = B.sb("s3_biasq", (128, 1), F32)
    if 'nobias' not in phases:
        p.op("gpsimd", lambda e: e.memset(biasq[0:64, :], 0.25), writes=["s3_biasq"])
        p.op("gpsimd", lambda e: e.memset(biasq[64:128, :], 0.0), writes=["s3_biasq"])

    y16 = B.sb("s3_y16", (128, 8 * 512), BF16)
    qk16 = B.sb("s3_qk16", (128, 8 * 512), BF16)
    V16 = B.sb("s3_V16", (128, 16 * 1040), BF16)
    KV16 = B.sb("s3_KV16", (128, 16 * 65), BF16)
    php = B.sb("s3_php", (128, 2048), BF16)
    pt1 = B.sb("s3_pt1", (128, 512), F32)
    pt2 = B.sb("s3_pt2", (128, 512), F32)
    qp16 = B.sb("s3_qp16", (128, 512), BF16)
    zrq = B.sb("s3_zrq", (128, 16), F32)
    ao4 = B.sb("s3_ao4", (128, 4 * 1024), BF16)
    aoT = B.sb("s3_aoT", (128, 8 * 512), BF16)
    xf32 = B.sb("s3_xf32", (128, 8 * 512), F32)

    yv = io["y2T"].rearrange("(a p) t -> p a t", p=128)
    ov = io["y3T"].rearrange("(a p) t -> p a t", p=128)

    HG = [(0, 7), (7, 14), (14, 16)]

    if 'noones' not in phases:
        p.op("gpsimd", lambda e: e.memset(
            rep_ap(V16, [[1040, 16], [65, 16], [1, 1]], 64), C2), writes=["s3_V16o"])

    for b in range(nb):
        t0 = b * T
        loadw(wA, "s3_wA", "Wk")
        loadw(wB, "s3_wB", "Wv")
        for c in range(1 if 'tiny' in phases else 4):
            p.op("gpsimd", lambda e, c=c, t0=t0: e.dma_start(
                out=y16[:].rearrange("p (a t) -> p a t", a=8),
                in_=yv[:, :, t0 + c * 512:t0 + c * 512 + 512]),
                writes=["s3_y16"], dma=True, stream="x")
            for o in range(8) if 'p1k' in phases else []:
                for a in range(8):
                    p.op("tensor", lambda e, a=a, o=o: e.matmul(
                        P0[:], wA[:, a * 1024 + o * 128:a * 1024 + o * 128 + 128],
                        y16[:, a * 512:a * 512 + 512],
                        start=(a == 0), stop=(a == 7)),
                        reads=["s3_wA", "s3_y16"], writes=["s3_P0"])
                p.op("vector", lambda e, o=o: e.tensor_copy(
                    qk16[:, o * 512:o * 512 + 512], P0[:]),
                    reads=["s3_P0"], writes=["s3_qk16"])
            for ts in range(4) if 'p1v' in phases else []:
                tt = c * 4 + ts
                for vc in range(2):
                    for a in range(8):
                        p.op("tensor", lambda e, a=a, ts=ts, vc=vc: e.matmul(
                            P1[:], y16[:, a * 512 + ts * 128:a * 512 + ts * 128 + 128],
                            wB[:, a * 1024 + vc * 512:a * 1024 + vc * 512 + 512],
                            start=(a == 0), stop=(a == 7)),
                            reads=["s3_y16", "s3_wB"], writes=["s3_P1"])
                    p.op("vector", lambda e, tt=tt, vc=vc: e.tensor_scalar(
                        rep_ap(V16, [[65, 8], [1, 64]], tt * 1040 + vc * 520),
                        rep_ap(P1, [[64, 8], [1, 64]], 0), C2, None, OP.mult),
                        reads=["s3_P1", "cpool"], writes=["s3_V16"])
            for ts in (range(1 if 'tiny' in phases else 4) if 'p1f' in phases else []):
                tt = c * 4 + ts
                for hg in range(1 if 'tiny' in phases else 4):
                    for hh in (range(4) if 'noproj' not in phases else []):
                        h = hg * 4 + hh
                        a, pr = h // 2, h % 2
                        p.op("tensor", lambda e, a=a, pr=pr, ts=ts, hh=hh: e.matmul(
                            P2[:, hh * 128:hh * 128 + 128],
                            qk16[:, a * 512 + ts * 128:a * 512 + ts * 128 + 128],
                            rfbd[:, pr * 128:pr * 128 + 128], start=True, stop=True),
                            reads=["s3_qk16", "s3_rfbd"], writes=["s3_P2"])
                    if 'nophi' in phases:
                        continue
                    p.op("scalar", lambda e: e.activation(pt1[:], P2[:], AF.Identity),
                         reads=["s3_P2"], writes=["s3_pt1"])
                    if 'nophi2' not in phases:
                        p.op("vector", lambda e: e.tensor_scalar(
                            rep_ap(pt1, [[128, 4], [1, 64]], 0),
                            rep_ap(pt1, [[128, 4], [1, 64]], 0), 0.25, None, OP.add),
                            reads=["s3_pt1", "cpool"], writes=["s3_pt1"])
                        p.op("vector", lambda e: e.tensor_scalar(
                            pt2[:], pt1[:], MAGIC, None, OP.add),
                            reads=["s3_pt1", "cpool"], writes=["s3_pt2"])
                        p.op("vector", lambda e: e.tensor_scalar(
                            pt2[:], pt2[:], -MAGIC, None, OP.add),
                            reads=["s3_pt2", "cpool"], writes=["s3_pt2"])
                        p.op("vector", lambda e: e.tensor_tensor(
                            pt1[:], pt1[:], pt2[:], OP.subtract),
                            reads=["s3_pt1", "s3_pt2"], writes=["s3_pt1"])
                    p.op("scalar", lambda e, hg=hg: e.activation(
                        php[:, hg * 512:hg * 512 + 512], pt1[:],
                        AF.Sin if 'nosin' not in phases else AF.Identity,
                        scale=float(2.0 * np.pi) if 'nosin' not in phases else 1.0),
                        reads=["s3_pt1"], writes=["s3_php"])
                for h in (range(NH) if 'nokv' not in phases else []):
                    gi_ = 0 if h < 7 else (1 if h < 14 else 2)
                    h0 = HG[gi_][0]
                    sl = (h - h0) * 65
                    first = (c == 0 and ts == 0 and h == h0) or 'shortacc' in phases
                    last = (c == 3 and ts == 3 and h == HG[gi_][1] - 1) or 'shortacc' in phases
                    p.op("tensor", lambda e, h=h, tt=tt, gi_=gi_, sl=sl,
                         first=first, last=last: e.matmul(
                        TSB[gi_][:, sl:sl + 65], php[:, h * 128:h * 128 + 128],
                        V16[:, tt * 1040 + h * 65:tt * 1040 + h * 65 + 65],
                        start=first, stop=last),
                        reads=["s3_php", "s3_V16", "s3_V16o"], writes=[TSK[gi_]])
        for gi_, (h0, h1) in (enumerate(HG) if 'p1f' in phases else []):
            p.op("vector", lambda e, gi_=gi_, h0=h0, h1=h1: e.tensor_copy(
                KV16[:, h0 * 65:h1 * 65], TSB[gi_][:, 0:(h1 - h0) * 65]),
                reads=[TSK[gi_]], writes=["s3_KV16"])
        if b == 0 and "dbg_KV16" in io:
            p.op("sync", lambda e: e.dma_start(out=io["dbg_KV16"], in_=KV16[:]),
                 reads=["s3_KV16"], dma=True, stream="o")

        loadw(wA, "s3_wA", "Wq")
        loadw(wB, "s3_wB", "Wo")
        for c in range(4):
            p.op("gpsimd", lambda e, c=c, t0=t0: e.dma_start(
                out=y16[:].rearrange("p (a t) -> p a t", a=8),
                in_=yv[:, :, t0 + c * 512:t0 + c * 512 + 512]),
                writes=["s3_y16"], dma=True, stream="x")
            for o in range(8) if 'p2q' in phases else []:
                for a in range(8):
                    p.op("tensor", lambda e, a=a, o=o: e.matmul(
                        P0[:], wA[:, a * 1024 + o * 128:a * 1024 + o * 128 + 128],
                        y16[:, a * 512:a * 512 + 512],
                        start=(a == 0), stop=(a == 7)),
                        reads=["s3_wA", "s3_y16"], writes=["s3_P0"])
                p.op("vector", lambda e, o=o: e.tensor_copy(
                    qk16[:, o * 512:o * 512 + 512], P0[:]),
                    reads=["s3_P0"], writes=["s3_qk16"])
            for gi_, (h0, h1) in (enumerate(HG) if 'p2f' in phases else []):
                for h in range(h0, h1):
                    a, pr = h // 2, h % 2
                    p.op("tensor", lambda e, a=a, pr=pr: e.matmul(
                        P2[:], rfbd[:, pr * 128:pr * 128 + 128],
                        qk16[:, a * 512:a * 512 + 512],
                        start=True, stop=True),
                        reads=["s3_qk16", "s3_rfbd"], writes=["s3_P2"])
                    p.op("scalar", lambda e: e.activation(
                        pt1[:], P2[:], AF.Identity, bias=biasq[:, 0:1]),
                        reads=["s3_P2", "s3_biasq"], writes=["s3_pt1"])
                    p.op("vector", lambda e: e.tensor_scalar(
                        pt2[:], pt1[:], MAGIC, None, OP.add),
                        reads=["s3_pt1", "cpool"], writes=["s3_pt2"])
                    p.op("vector", lambda e: e.tensor_scalar(
                        pt2[:], pt2[:], -MAGIC, None, OP.add),
                        reads=["s3_pt2", "cpool"], writes=["s3_pt2"])
                    p.op("vector", lambda e: e.tensor_tensor(
                        pt1[:], pt1[:], pt2[:], OP.subtract),
                        reads=["s3_pt1", "s3_pt2"], writes=["s3_pt1"])
                    p.op("scalar", lambda e: e.activation(
                        qp16[:], pt1[:], AF.Sin, scale=float(2.0 * np.pi)),
                        reads=["s3_pt1"], writes=["s3_qp16"])
                    if b == 0 and c == 0 and h == 0 and "dbg_qp" in io:
                        p.op("sync", lambda e: e.dma_start(out=io["dbg_qp"], in_=qp16[:]),
                             reads=["s3_qp16"], dma=True, stream="o")
                    for ts in range(4):
                        sl = (h - h0) * 65
                        p.op("tensor", lambda e, ts=ts, h=h, sl=sl: e.matmul(
                            TSB[ts][:, sl:sl + 65],
                            qp16[:, ts * 128:ts * 128 + 128],
                            KV16[:, h * 65:h * 65 + 65], start=True, stop=True),
                            reads=["s3_qp16", "s3_KV16"], writes=[TSK[ts]])
                ng = h1 - h0
                for ts in range(4):
                    p.op("vector", lambda e, ts=ts, ng=ng: e.tensor_scalar(
                        rep_ap(TSB[ts], [[65, ng], [1, 1]], 64),
                        rep_ap(TSB[ts], [[65, ng], [1, 1]], 64), 1e-6, None,
                        OP.max), reads=[TSK[ts], "cpool"], writes=[TSK[ts]])
                    p.op("vector", lambda e, ts=ts, ng=ng, h0=h0: e.reciprocal(
                        zrq[:, h0:h0 + ng],
                        rep_ap(TSB[ts], [[65, ng], [1, 1]], 64)),
                        reads=[TSK[ts]], writes=["s3_zrq"])
                    p.op("vector", lambda e, ts=ts, ng=ng, h0=h0: e.tensor_tensor(
                        rep_ap(ao4, [[64, ng], [1, 64]], ts * 1024 + h0 * 64),
                        rep_ap(TSB[ts], [[65, ng], [1, 64]], 0),
                        rep_ap(zrq, [[1, ng], [0, 64]], h0), OP.mult),
                        reads=[TSK[ts], "s3_zrq"], writes=["s3_ao4"])
            if b == 0 and c == 0 and "dbg_ao4" in io:
                p.op("sync", lambda e: e.dma_start(out=io["dbg_ao4"], in_=ao4[:]),
                     reads=["s3_ao4"], dma=True, stream="o")
            for ts in range(4) if 'p2t' in phases else []:
                for g in range(8):
                    p.op("tensor", lambda e, ts=ts, g=g: e.transpose(
                        PTT[:, g * 128:(g + 1) * 128],
                        ao4[:, ts * 1024 + g * 128:ts * 1024 + g * 128 + 128],
                        eye[:]), reads=["s3_ao4", "s3_eye"], writes=["s3_PT"])
                for g in range(8):
                    p.op("vector", lambda e, ts=ts, g=g: e.tensor_copy(
                        aoT[:, g * 512 + ts * 128:g * 512 + ts * 128 + 128],
                        PTT[:, g * 128:(g + 1) * 128]),
                        reads=["s3_PT"], writes=["s3_aoT"])
            if b == 0 and c == 0 and "dbg_aoT" in io:
                p.op("sync", lambda e: e.dma_start(out=io["dbg_aoT"], in_=aoT[:]),
                     reads=["s3_aoT"], dma=True, stream="o")
            if 'p2w' not in phases:
                continue
            p.op("sync", lambda e, c=c, t0=t0: e.dma_start(
                out=xf32[:].rearrange("p (a t) -> p a t", a=8),
                in_=yv[:, :, t0 + c * 512:t0 + c * 512 + 512]),
                writes=["s3_xf32"], dma=True, stream="r")
            for o in range(8):
                ps, pk = ((P0, "s3_P0") if o % 2 == 0 else (P2, "s3_P2"))
                for a in range(8):
                    p.op("tensor", lambda e, a=a, o=o, ps=ps: e.matmul(
                        ps[:], wB[:, a * 1024 + o * 128:a * 1024 + o * 128 + 128],
                        aoT[:, a * 512:a * 512 + 512],
                        start=(a == 0), stop=(a == 7)),
                        reads=["s3_wB", "s3_aoT"], writes=[pk])
                p.op("vector", lambda e, o=o, ps=ps, pk=pk: e.tensor_tensor(
                    xf32[:, o * 512:o * 512 + 512], ps[:],
                    xf32[:, o * 512:o * 512 + 512], OP.add),
                    reads=[pk, "s3_xf32"], writes=["s3_xf32"])

            def writer(c=c, t0=t0):
                B.p.op("sync", lambda e: e.dma_start(
                    out=ov[:, :, t0 + c * 512:t0 + c * 512 + 512],
                    in_=xf32[:].rearrange("p (a t) -> p a t", a=8)),
                    reads=["s3_xf32"], dma=True, stream="o")
            ln.emit(xf32, "s3_xf32", G0, G1, G2, G3, writer)


def build_stage3_program(nb=2, phases=('p1k','p1v','p1f','p2q','p2f','p2t','p2w'), debug=False):
    nc = bass.Bass("TRN2", target_bir_lowering=False)
    with ExitStack() as ctx:
        B = BD(nc, ctx)
        io = {}
        if debug:
            io["dbg_KV16"] = B.dram("dbg_KV16", (128, 16 * 65), mybir.dt.bfloat16, kind="ExternalOutput")
            io["dbg_qp"] = B.dram("dbg_qp", (128, 512), mybir.dt.bfloat16, kind="ExternalOutput")
            io["dbg_ao4"] = B.dram("dbg_ao4", (128, 4096), mybir.dt.bfloat16, kind="ExternalOutput")
            io["dbg_aoT"] = B.dram("dbg_aoT", (128, 4096), mybir.dt.bfloat16, kind="ExternalOutput")
        io.update({
            "y2T": B.dram("y2T", (1024, nb * T), F32, kind="ExternalInput"),
            "y3T": B.dram("y3T", (1024, nb * T), F32, kind="ExternalOutput"),
            "Wq": B.dram("Wq", (1024, 1024), F32, kind="ExternalInput"),
            "Wk": B.dram("Wk", (1024, 1024), F32, kind="ExternalInput"),
            "Wv": B.dram("Wv", (1024, 1024), F32, kind="ExternalInput"),
            "Wo": B.dram("Wo", (1024, 1024), F32, kind="ExternalInput"),
            "rf": B.dram("rf", (64, 128), F32, kind="ExternalInput"),
        })
        emit_stage3(B, io, nb=nb, phases=phases)
        B.p.barrier()
        B.p.emit(ctx)
    return nc




def make_xpos_tables():
    DIM_HEAD, WINDOW = 64, 512
    half = np.arange(0, DIM_HEAD, 2, dtype=np.float64)
    inv_freq = 1.0 / 10000.0 ** (half / DIM_HEAD)
    t = np.arange(T, dtype=np.float64)
    freqs = np.concatenate([t[:, None] * inv_freq[None, :]] * 2, -1)
    base = (half + 0.4 * DIM_HEAD) / (1.4 * DIM_HEAD)
    power = (t - T // 2) / (WINDOW // 2)
    scale = np.concatenate([base[None, :] ** power[:, None]] * 2, -1)
    cos, sin = np.cos(freqs), np.sin(freqs)
    sgn = np.concatenate([-np.ones(32), np.ones(32)])[None, :]
    return ((cos * scale * 8.0).astype(np.float32),
            (sin * scale * 8.0 * sgn).astype(np.float32),
            (cos / scale).astype(np.float32),
            (sin / scale * sgn).astype(np.float32))


def emit_stage4_layer(B, io, l, nb=2, phases=("a", "c"), alim=7, qlim=9):
    nc, p = B.nc, B.p
    H, HID = 8, 2730
    NEG = -1.0e9
    L = f"s4{l}"
    B.const(1e-12, 1e-5)
    eye = emit_eye(B, BF16, L + "_eye")
    eyef = emit_eye(B, F32, L + "_eyef")

    x2T = B.dram(L + "_x2T", (1024, nb * T), F32, kind="Internal")
    mscr = B.dram(L + "_mscr", (32, 128), F32, kind="Internal")
    w1c = B.dram(L + "_w1c", (1024, 5460), BF16, kind="Internal")
    w2c = B.dram(L + "_w2c", (2730, 1024), BF16, kind="Internal")
    p.op("gpsimd", lambda e: e.dma_start(out=w1c, in_=io["w1"]),
         writes=[L + "w1c"], dma=True, stream="w")
    p.op("gpsimd", lambda e: e.dma_start(out=w2c, in_=io["w2"]),
         writes=[L + "w2c"], dma=True, stream="w")

    ones1 = B.sb(L + "_ones1", (1, 128), F32)
    p.op("gpsimd", lambda e: e.memset(ones1[:], 1.0), writes=[L + "_ones1"])
    onesc = B.sb(L + "_onesc", (128, 1), BF16)
    p.op("gpsimd", lambda e: e.memset(onesc[:], 1.0), writes=[L + "_onesc"])
    ones2 = B.sb(L + "_ones2", (128, 64), F32)
    p.op("gpsimd", lambda e: e.memset(ones2[:], 1.0), writes=[L + "_ones2"])

    PQ0, PQ1 = B.ps(L + "_PQ0"), B.ps(L + "_PQ1")
    PA, PBm = B.ps(L + "_PA"), B.ps(L + "_PBm")
    PAV, PZ, PS8 = B.ps(L + "_PAV"), B.ps(L + "_PZ"), B.ps(L + "_PS8")
    PTT = B.ps(L + "_PT", (128, 1024), BF16)
    kq0, kq1 = L + "_PQ0", L + "_PQ1"
    ka, kb = L + "_PA", L + "_PBm"
    kav, kz, k8 = L + "_PAV", L + "_PZ", L + "_PS8"
    kt = L + "_PT"

    lngt = B.sb(L + "_lngt", (128, 16), F32)
    p.op("sync", lambda e: e.dma_start(
        out=lngt[:, 0:8], in_=io["lng"].rearrange("(a p) -> p a", p=128),
        allow_slow_non_contiguous=True),
        writes=[L + "_lngt"], dma=True)
    p.op("sync", lambda e: e.dma_start(
        out=lngt[:, 8:16], in_=io["lnb"].rearrange("(a p) -> p a", p=128),
        allow_slow_non_contiguous=True),
        writes=[L + "_lngt"], dma=True)
    b2t = B.sb(L + "_b2t", (128, 8), F32)
    p.op("sync", lambda e: e.dma_start(
        out=b2t[:], in_=io["b2"].rearrange("(a p) -> p a", p=128),
        allow_slow_non_contiguous=True),
        writes=[L + "_b2t"], dma=True)
    b1t = B.sb(L + "_b1t", (128, 44), F32)
    for hoff, coff in ((0, 0), (HID, 22)):
        p.op("sync", lambda e, hoff=hoff, coff=coff: e.dma_start(
            out=b1t[:, coff:coff + 21],
            in_=io["b1"][hoff:hoff + 2688].rearrange("(a p) -> p a", p=128),
            allow_slow_non_contiguous=True),
            writes=[L + "_b1t"], dma=True)
        p.op("sync", lambda e, hoff=hoff, coff=coff: e.dma_start(
            out=b1t[0:42, coff + 21:coff + 22],
            in_=io["b1"][hoff + 2688:hoff + 2730].rearrange("(p a) -> p a", a=1)),
            writes=[L + "_b1t"], dma=True)
    xv = io["xT"].rearrange("(a p) t -> p a t", p=128)
    yv = io["yT"].rearrange("(a p) t -> p a t", p=128)

    for b in range(nb):
        t0 = b * T
        p.op("gpsimd", lambda e: e.dma_start(
            out=wvo[:].rearrange("p (a f) -> p a f", a=8),
            in_=io["Wv"].rearrange("(a p) f -> p a f", p=128)),
            writes=["s1_wvo"], dma=True, stream="w")
        for c in range(4) if 'feat' in phases else []:
            p.op("gpsimd", lambda e, t0=t0, c=c: e.dma_start(
                out=x16[:].rearrange("p (a t) -> p a t", a=8),
                in_=xv[:, :, t0 + c * 512:t0 + c * 512 + 512]),
                writes=["s1_x16"], dma=True, stream="x")
            for ttc in range(4):
                tt = c * 4 + ttc
                for a in range(8):
                    p.op("tensor", lambda e, a=a, ttc=ttc: e.matmul(
                        P0[:, 0:40],
                        x16[:, a * 512 + ttc * 128:a * 512 + ttc * 128 + 128],
                        wp[:, a * 40:(a + 1) * 40], start=(a == 0), stop=(a == 7)),
                        reads=["s1_x16", "s1_wp"], writes=["s1_P0"])
                p.op("vector", lambda e: e.tensor_tensor(ptile[:, 0:40], P0[:, 0:40],
                                                         bpbc[:], OP.add),
                     reads=["s1_P0", "s1_bpbc"], writes=["s1_p"])
                p.op("scalar", lambda e: e.activation(ptile[:, 0:40], ptile[:, 0:40],
                                                      AF.Sigmoid),
                     reads=["s1_p"], writes=["s1_p"])
                uq0 = rep_ap(uq, [[10, 8], [2, 5], [1, 1]], 0)
                uq1 = rep_ap(uq, [[10, 8], [2, 5], [1, 1]], 1)
                vk0 = rep_ap(vk, [[10, 8], [2, 5], [1, 1]], 0)
                vk1 = rep_ap(vk, [[10, 8], [2, 5], [1, 1]], 1)
                pv = rep_ap(ptile, [[5, 8], [1, 5], [0, 1]], 0)
                p.op("vector", lambda e: e.tensor_scalar(uq0, pv, -1.0, 1.0,
                                                         OP.mult, OP.add),
                     reads=["s1_p", "cpool"], writes=["s1_uq"])
                p.op("vector", lambda e: e.tensor_scalar(uq1, pv, 2.0, -1.0,
                                                         OP.mult, OP.add),
                     reads=["s1_p", "cpool"], writes=["s1_uq"])
                p.op("gpsimd", lambda e: e.memset(vk0, 1.0), writes=["s1_vk"])
                p.op("vector", lambda e: e.tensor_copy(vk1, pv),
                     reads=["s1_p"], writes=["s1_vk"])
                for (usrc, ukey, Fdst, fkey) in ((uq, "s1_uq", FqT, "s1_FqT"),
                                                 (vk, "s1_vk", FkT, "s1_FkT")):
                    p.op("gpsimd", lambda e: e.memset(
                        rep_ap(Ftf, [[64, 8], [1, 2]], 62), 0.0),
                        writes=["s1_Ftf"])
                    p.op("vector", lambda e, u=usrc: e.tensor_copy(
                        rep_ap(Ftf, [[64, 8], [1, 2]], 0),
                        rep_ap(u, [[10, 8], [1, 2]], 0)),
                        reads=[ukey], writes=["s1_Ftf"])
                    sp, sc = 0, 2
                    for j in range(1, DEP):
                        w = 2 ** j
                        p.op("vector", lambda e, u=usrc, sp=sp, sc=sc, j=j, w=w:
                             e.tensor_tensor(
                                 rep_ap(Ftf, [[64, 8], [1, 2 * w]], sc),
                                 rep_ap(Ftf, [[64, 8], [1, w], [0, 2]], sp),
                                 rep_ap(u, [[10, 8], [0, w], [1, 2]], 2 * j),
                                 OP.mult),
                             reads=["s1_Ftf", ukey], writes=["s1_Ftf"])
                        sp, sc = sc, sc + 2 * w
                    p.op("vector", lambda e: e.tensor_copy(F16[:], Ftf[:]),
                         reads=["s1_Ftf"], writes=["s1_F16"])
                    for g in range(4):
                        p.op("tensor", lambda e, g=g: e.transpose(
                            PTT[:, g * 128:(g + 1) * 128],
                            F16[:, g * 128:(g + 1) * 128], eye[:]),
                            reads=["s1_F16", "eye_bf"], writes=["s1_PT"])
                    for g in range(4):
                        p.op("vector", lambda e, g=g, Fd=Fdst, tt=tt: e.tensor_copy(
                            Fd[:, g * T + tt * 128:g * T + tt * 128 + 128],
                            PTT[:, g * 128:(g + 1) * 128]),
                            reads=["s1_PT"], writes=[fkey])
                for vc in range(2):
                    ps, pk = ((P1, "s1_P1") if vc == 0 else (P2, "s1_P2"))
                    for a in range(8):
                        p.op("tensor", lambda e, a=a, ttc=ttc, vc=vc, ps=ps: e.matmul(
                            ps[:],
                            x16[:, a * 512 + ttc * 128:a * 512 + ttc * 128 + 128],
                            wvo[:, a * 1024 + vc * 512:a * 1024 + vc * 512 + 512],
                            start=(a == 0), stop=(a == 7)),
                            reads=["s1_x16", "s1_wvo"], writes=[pk])
                    p.op("vector", lambda e, tt=tt, vc=vc, ps=ps, pk=pk: e.tensor_copy(
                        V16[:, tt * 1024 + vc * 512:tt * 1024 + vc * 512 + 512],
                        ps[:]), reads=[pk], writes=["s1_V16"])

        p.op("gpsimd", lambda e: e.dma_start(
            out=wvo[:].rearrange("p (a f) -> p a f", a=8),
            in_=io["Wo"].rearrange("(a p) f -> p a f", p=128)),
            writes=["s1_wvo"], dma=True, stream="w")

        for h in range(8) if 'attn' in phases else []:
            g, hp = h // 2, (h % 2) * 64
            for qc in range(4):
                nkt = 4 * qc + 4
                for kt in range(nkt):
                    ps, pk = ((P0, "s1_P0") if kt % 2 == 0 else (P1, "s1_P1"))
                    p.op("tensor", lambda e, kt=kt, qc=qc, pss=ps, g=g, hp=hp: e.matmul(
                        pss[:],
                        FkT[hp:hp + 64, g * T + kt * 128:g * T + kt * 128 + 128],
                        FqT[hp:hp + 64, g * T + qc * 512:g * T + qc * 512 + 512],
                        start=True, stop=True),
                        reads=["s1_FkT", "s1_FqT"], writes=[pk])
                    p.op("scalar", lambda e, kt=kt, pss=ps: e.activation(
                        At[:, kt * 512:kt * 512 + 512], pss[:], AF.Exp, scale=0.2),
                        reads=[pk], writes=[f"s1_At#{kt}"])
                    if kt >= 4 * qc:
                        p.op("gpsimd", lambda e, kt=kt, qc=qc: e.affine_select(
                            out=At[:, kt * 512:kt * 512 + 512],
                            in_=At[:, kt * 512:kt * 512 + 512],
                            compare_op=OP.is_ge, fill=0.0,
                            base=qc * 512 - kt * 128, pattern=[[1, 512]],
                            channel_multiplier=-1),
                            reads=[f"s1_At#{kt}"], writes=[f"s1_At#{kt}"])
                for kt in range(nkt):
                    p.op("tensor", lambda e, kt=kt, h=h, nkt=nkt: e.matmul(
                        P2[:], V16[:, kt * 1024 + h * 128:kt * 1024 + h * 128 + 128],
                        At[:, kt * 512:kt * 512 + 512],
                        start=(kt == 0), stop=(kt == nkt - 1)),
                        reads=["s1_V16", f"s1_At#{kt}"], writes=["s1_P2"])
                for kt in range(nkt):
                    p.op("tensor", lambda e, kt=kt, nkt=nkt: e.matmul(
                        P3[:], ones_sq[:], At[:, kt * 512:kt * 512 + 512],
                        start=(kt == 0), stop=(kt == nkt - 1)),
                        reads=["s1_onsq", f"s1_At#{kt}"], writes=["s1_P3"])
                p.op("scalar", lambda e: e.activation(zln[:], P3[:], AF.Ln),
                     reads=["s1_P3"], writes=["s1_zln"])
                p.op("scalar", lambda e: e.activation(zrec[:], zln[:], AF.Exp,
                                                      scale=-1.0),
                     reads=["s1_zln"], writes=["s1_zrec"])
                p.op("vector", lambda e, h=h, qc=qc: e.tensor_tensor(
                    ctxT[:, h * T + qc * 512:h * T + qc * 512 + 512], P2[:],
                    zrec[:], OP.mult),
                    reads=["s1_P2", "s1_zrec"], writes=["s1_ctxT"])

        for c in range(4) if 'wo' in phases else []:
            p.op("sync", lambda e, c=c, t0=t0: e.dma_start(
                out=xf32[:].rearrange("p (a t) -> p a t", a=8),
                in_=xv[:, :, t0 + c * 512:t0 + c * 512 + 512]),
                writes=["s1_xf32"], dma=True, stream="r")
            for o in range(8):
                ps, pk = ((P0, "s1_P0") if o % 2 == 0 else (P1, "s1_P1"))
                for a in range(8):
                    p.op("tensor", lambda e, a=a, o=o, c=c, ps=ps: e.matmul(
                        ps[:], wvo[:, a * 1024 + o * 128:a * 1024 + o * 128 + 128],
                        ctxT[:, a * T + c * 512:a * T + c * 512 + 512],
                        start=(a == 0), stop=(a == 7)),
                        reads=["s1_wvo", "s1_ctxT"], writes=[pk])
                p.op("vector", lambda e, o=o, ps=ps, pk=pk: e.tensor_tensor(
                    zsb[:, o * 512:o * 512 + 512], ps[:],
                    xf32[:, o * 512:o * 512 + 512], OP.add),
                    reads=[pk, "s1_xf32"], writes=["s1_z"])

            def writer(c=c, t0=t0):
                B.p.op("sync", lambda e: e.dma_start(
                    out=yv[:, :, t0 + c * 512:t0 + c * 512 + 512],
                    in_=zsb[:].rearrange("p (a t) -> p a t", a=8)),
                    reads=["s1_z"], dma=True, stream="o")
            if 'ln' in phases:
                ln.emit(zsb, "s1_z", P4, P5, P0, P1, writer)
            else:
                writer()


def build_stage1_program(nb=2, phases=('feat','attn','wo','ln')):
    nc = bass.Bass("TRN2", target_bir_lowering=False)
    with ExitStack() as ctx:
        B = BD(nc, ctx)
        io = {
            "xT": B.dram("xT", (1024, nb * T), F32, kind="ExternalInput"),
            "yT": B.dram("yT", (1024, nb * T), F32, kind="ExternalOutput"),
            "Wp": B.dram("Wp", (1024, 40), F32, kind="ExternalInput"),
            "bp": B.dram("bp", (40,), F32, kind="ExternalInput"),
            "Wv": B.dram("Wv", (1024, 1024), F32, kind="ExternalInput"),
            "Wo": B.dram("Wo", (1024, 1024), F32, kind="ExternalInput"),
        }
        emit_stage1(B, io, nb=nb, phases=phases)
        B.p.barrier()
        B.p.emit(ctx)
    return nc




def emit_stage2(B, io, nb=2, phases=('wload', 'gates', 'moe', 'slot1')):
    nc, p = B.nc, B.p
    E, R, FF = 8, 64, 4096
    ln = LNPack(B, "s2")
    eye = emit_eye(B, BF16, "s2_eye")
    P0, P1, P2, P3, P4, P5 = (B.ps(f"s2_P{i}") for i in range(6))
    PTT = B.ps("s2_PT", (128, 1024), BF16)

    w1b = load_w_tiles(B, "s2_w1b", io["w1B"], 512, 4096)
    w2b = load_w_tiles(B, "s2_w2b", io["w2B"], 512, 1024)
    sel8 = B.sb("s2_sel8", (8, 512), BF16)
    p.op("gpsimd", lambda e: e.dma_start(out=sel8[:], in_=io["sel8"]),
         writes=["s2_sel8"], dma=True, stream="w")
    w1at = B.sb("s2_w1at", (128, 8 * 512), BF16)
    w2at = B.sb("s2_w2at", (128, 32 * 512), BF16)
    wtmp = B.sb("s2_wtmp", (128, 4096), BF16)
    for (src, wdst, kdst, nda) in (((io["w1A"], w1at, "s2_w1at", 8),
                                    (io["w2A"], w2at, "s2_w2at", 32))
                                   if 'wload' in phases else ()):
        for ra in range(4):
            p.op("gpsimd", lambda e, src=src, ra=ra, nda=nda: e.dma_start(
                out=wtmp[:, 0:nda * 128],
                in_=src[ra * 128:(ra + 1) * 128, :]),
                writes=["s2_wtmp"], dma=True, stream="w")
            for da in range(nda):
                p.op("tensor", lambda e, da=da: e.transpose(
                    PTT[:, 0:128], wtmp[:, da * 128:da * 128 + 128], eye[:]),
                    reads=["s2_wtmp", "s2_eye"], writes=["s2_PT"])
                p.op("vector", lambda e, ra=ra, da=da, w=wdst: e.tensor_copy(
                    w[:, da * 512 + ra * 128:da * 512 + ra * 128 + 128],
                    PTT[:, 0:128]), reads=["s2_PT"], writes=[kdst])

    gw = B.sb("s2_gw", (128, 32), F32)
    gi = B.sb("s2_gi", (128, 32), I32)
    gif = B.sb("s2_gif", (128, 32), F32)
    gmat = B.sb("s2_g", (128, 128), F32)
    g16 = B.sb("s2_g16", (128, 128), BF16)
    gT = B.sb("s2_gT", (8, 2048), BF16)
    gtmp = B.sb("s2_gtmp", (128, 48), F32)
    p.op("sync", lambda e: e.dma_start(
        out=gw[:].rearrange("p (tt k) -> p tt k", k=2),
        in_=io["moe_w"].rearrange("(tt p) k -> p tt k", p=128)),
        writes=["s2_gw"], dma=True)
    p.op("sync", lambda e: e.dma_start(
        out=gi[:].rearrange("p (tt k) -> p tt k", k=2),
        in_=io["moe_i"].rearrange("(tt p) k -> p tt k", p=128)),
        writes=["s2_gi"], dma=True)
    p.op("vector", lambda e: e.tensor_copy(gif[:], gi[:]), reads=["s2_gi"],
         writes=["s2_gif"])
    B.const(*[float(e) for e in range(8)])
    for e_ in range(E) if 'gates' in phases else []:
        for k in range(2):
            p.op("vector", lambda ee, e_=e_, k=k: ee.tensor_scalar(
                rep_ap(gtmp, [[3, 16], [1, 1]], k),
                rep_ap(gif, [[2, 16], [1, 1]], k), float(e_), None, OP.is_equal),
                reads=["s2_gif", "cpool"], writes=["s2_gtmp"])
            p.op("vector", lambda ee, e_=e_, k=k: ee.tensor_tensor(
                rep_ap(gtmp, [[3, 16], [1, 1]], k),
                rep_ap(gtmp, [[3, 16], [1, 1]], k),
                rep_ap(gw, [[2, 16], [1, 1]], k), OP.mult),
                reads=["s2_gtmp", "s2_gw"], writes=["s2_gtmp"])
        p.op("vector", lambda ee, e_=e_: ee.tensor_tensor(
            rep_ap(gmat, [[8, 16], [1, 1]], e_),
            rep_ap(gtmp, [[3, 16], [1, 1]], 0),
            rep_ap(gtmp, [[3, 16], [1, 1]], 1), OP.add),
            reads=["s2_gtmp"], writes=["s2_g"])
    p.op("vector", lambda e: e.tensor_copy(g16[:], gmat[:]), reads=["s2_g"],
         writes=["s2_g16"])
    for tt in range(16) if 'gates' in phases else []:
        p.op("tensor", lambda e, tt=tt: e.transpose(
            PTT[0:8, 0:128], g16[:, tt * 8:tt * 8 + 8], eye[:]),
            reads=["s2_g16", "s2_eye"], writes=["s2_PT"])
        p.op("vector", lambda e, tt=tt: e.tensor_copy(
            gT[0:8, tt * 128:tt * 128 + 128], PTT[0:8, 0:128]),
            reads=["s2_PT"], writes=["s2_gT"])

    y16 = B.sb("s2_y16", (128, 8 * 512), BF16)
    yf32 = B.sb("s2_yf32", (128, 8 * 512), F32)
    hc = B.sb("s2_hc", (128, 32 * 512), BF16)
    t1s = B.sb("s2_t1s", (128, 512), BF16)
    t2s = B.sb("s2_t2s", (128, 4 * 512), BF16)
    gbcp = B.sb("s2_gbcp", (128, 512), BF16)

    P6 = B.ps("s2_P6")

    yv = io["yT"].rearrange("(a p) t -> p a t", p=128)
    ov = io["y2T"].rearrange("(a p) t -> p a t", p=128)

    for tc in (range(4) if any(p.startswith('moe') for p in phases) else []):
        p.op("gpsimd", lambda e, tc=tc: e.dma_start(
            out=y16[:].rearrange("p (a t) -> p a t", a=8),
            in_=yv[:, :, tc * 512:tc * 512 + 512]),
            writes=["s2_y16"], dma=True, stream="x")
        for ra in (range(4) if 'moe_t1' in phases or 'moe' in phases else []):
            for da in range(8):
                p.op("tensor", lambda e, da=da, ra=ra: e.matmul(
                    P0[:], w1at[:, da * 512 + ra * 128:da * 512 + ra * 128 + 128],
                    y16[:, da * 512:da * 512 + 512],
                    start=(da == 0), stop=(da == 7)),
                    reads=["s2_w1at", "s2_y16"], writes=["s2_P0"])
            p.op("vector", lambda e: e.tensor_copy(t1s[:], P0[:]),
                 reads=["s2_P0"], writes=["s2_t1s"])
            p.op("tensor", lambda e, ra=ra, tc=tc: e.matmul(
                P4[:], sel8[:, ra * 128:ra * 128 + 128],
                gT[0:8, tc * 512:tc * 512 + 512], start=True, stop=True),
                reads=["s2_sel8", "s2_gT"], writes=["s2_P4"])
            p.op("vector", lambda e: e.tensor_copy(gbcp[:], P4[:]),
                 reads=["s2_P4"], writes=["s2_gbcp"])
            for par in (range(2) if 'moe_hc' in phases or 'moe' in phases else []):
                e_, ro = 2 * ra + par, par * 64
                for ff in range(32):
                    ps, pk = ((P1, "s2_P1") if ff % 2 == 0 else (P2, "s2_P2"))
                    p.op("tensor", lambda e, ff=ff, ra=ra, ro=ro, ps=ps: e.matmul(
                        ps[:], w1b[ro:ro + 64,
                                   ra * 4096 + ff * 128:ra * 4096 + ff * 128 + 128],
                        t1s[ro:ro + 64, :], start=True, stop=True),
                        reads=["s2_w1b", "s2_t1s"], writes=[pk])
                    p.op("scalar", lambda e, ff=ff, ps=ps, pk=pk: e.activation(
                        hc[:, ff * 512:ff * 512 + 512], ps[:], AF.Gelu),
                        reads=[pk], writes=[f"s2_hc#{ff}"])
                pt2, kt2 = ((P3, "s2_P3") if par == 0 else (P6, "s2_P6"))
                for ff in (range(32) if 'moe_t2' in phases or 'moe' in phases else []):
                    p.op("tensor", lambda e, ff=ff, e_=e_, pt2=pt2, ro=ro: e.matmul(
                        pt2[ro:ro + 64, :],
                        w2at[:, ff * 512 + e_ * 64:ff * 512 + e_ * 64 + 64],
                        hc[:, ff * 512:ff * 512 + 512],
                        start=(ff == 0), stop=(ff == 31),
                        tile_position=(0, ro)),
                        reads=["s2_w2at", f"s2_hc#{ff}"], writes=[kt2])
                if 'moe_t2' in phases or 'moe' in phases:
                    p.op("vector", lambda e, ra=ra, ro=ro, pt2=pt2, kt2=kt2:
                         e.tensor_tensor(
                             t2s[ro:ro + 64, ra * 512:ra * 512 + 512],
                             pt2[ro:ro + 64, :],
                             gbcp[ro:ro + 64, :], OP.mult),
                         reads=[kt2, "s2_gbcp"], writes=[f"s2_t2s#{ra}"])
        p.op("sync", lambda e, tc=tc: e.dma_start(
            out=yf32[:].rearrange("p (a t) -> p a t", a=8),
            in_=yv[:, :, tc * 512:tc * 512 + 512]),
            writes=["s2_yf32"], dma=True, stream="r")
        for o in (range(8) if 'moe_acc' in phases or 'moe' in phases else []):
            ps, pk = ((P0, "s2_P0") if o % 2 == 0 else (P1, "s2_P1"))
            for ra in range(4):
                p.op("tensor", lambda e, ra=ra, o=o, ps=ps: e.matmul(
                    ps[:], w2b[:, ra * 1024 + o * 128:ra * 1024 + o * 128 + 128],
                    t2s[:, ra * 512:ra * 512 + 512],
                    start=(ra == 0), stop=(ra == 3)),
                    reads=["s2_w2b", f"s2_t2s#{ra}"], writes=[pk])
            p.op("vector", lambda e, o=o, ps=ps, pk=pk: e.tensor_tensor(
                yf32[:, o * 512:o * 512 + 512], ps[:],
                yf32[:, o * 512:o * 512 + 512], OP.add),
                reads=[pk, "s2_yf32"], writes=["s2_yf32"])

        def writer0(tc=tc):
            B.p.op("sync", lambda e: e.dma_start(
                out=ov[:, :, tc * 512:tc * 512 + 512],
                in_=yf32[:].rearrange("p (a t) -> p a t", a=8)),
                reads=["s2_yf32"], dma=True, stream="o")
        if 'moe_ln' in phases or 'moe' in phases:
            ln.emit(yf32, "s2_yf32", P4, P5, P2, P3, writer0)

    for b in range(1, nb) if 'slot1' in phases else []:
        for tc in range(4):
            p.op("sync", lambda e, tc=tc, b=b: e.dma_start(
                out=yf32[:].rearrange("p (a t) -> p a t", a=8),
                in_=yv[:, :, b * T + tc * 512:b * T + tc * 512 + 512]),
                writes=["s2_yf32"], dma=True, stream="r")

            def writer1(tc=tc, b=b):
                B.p.op("sync", lambda e: e.dma_start(
                    out=ov[:, :, b * T + tc * 512:b * T + tc * 512 + 512],
                    in_=yf32[:].rearrange("p (a t) -> p a t", a=8)),
                    reads=["s2_yf32"], dma=True, stream="o")
            ln.emit(yf32, "s2_yf32", P4, P5, P2, P3, writer1)


def build_stage2_program(nb=2):
    nc = bass.Bass("TRN2", target_bir_lowering=False)
    with ExitStack() as ctx:
        B = BD(nc, ctx)
        io = {
            "yT": B.dram("yT", (1024, nb * T), F32, kind="ExternalInput"),
            "y2T": B.dram("y2T", (1024, nb * T), F32, kind="ExternalOutput"),
            "w1A": B.dram("w1A", (512, 1024), F32, kind="ExternalInput"),
            "w1B": B.dram("w1B", (512, 4096), F32, kind="ExternalInput"),
            "w2A": B.dram("w2A", (512, 4096), F32, kind="ExternalInput"),
            "w2B": B.dram("w2B", (512, 1024), F32, kind="ExternalInput"),
            "moe_w": B.dram("moe_w", (T, 2), F32, kind="ExternalInput"),
            "moe_i": B.dram("moe_i", (T, 2), I32, kind="ExternalInput"),
            "sel8": B.dram("sel8", (8, 512), F32, kind="ExternalInput"),
        }
        emit_stage2(B, io, nb=nb)
        B.p.barrier()
        B.p.emit(ctx)
    return nc




def emit_stage3(B, io, nb=2, phases=('p1k','p1v','p1f','p2q','p2f','p2t','p2w')):
    nc, p = B.nc, B.p
    NH = 16
    INV2PI = float(1.0 / (2.0 * np.pi))
    C2 = 1.0 / 128.0
    ln = LNPack(B, "s3")
    eye = emit_eye(B, BF16, "s3_eye")
    B.const(MAGIC, -MAGIC, 1e-6, C2, INV2PI, 0.25)

    P0, P1, P2 = B.ps("s3_P0"), B.ps("s3_P1"), B.ps("s3_P2")
    G0, G1, G2, G3 = B.ps("s3_G0"), B.ps("s3_G1"), B.ps("s3_G2"), B.ps("s3_G3")
    PTT = B.ps("s3_PT", (128, 1024), BF16)
    TSB = [G0, G1, G2, G3]
    TSK = ["s3_G0", "s3_G1", "s3_G2", "s3_G3"]

    wA = B.sb("s3_wA", (128, 8 * 1024), BF16)
    wB = B.sb("s3_wB", (128, 8 * 1024), BF16)

    def loadw(dst, key, src):
        p.op("gpsimd", lambda e, dst=dst, src=src: e.dma_start(
            out=dst[:].rearrange("p (a f) -> p a f", a=8),
            in_=io[src].rearrange("(a p) f -> p a f", p=128)),
            writes=[key], dma=True, stream="w")

    rff = B.sb("s3_rff", (128, 256), F32)
    rfbd = B.sb("s3_rfbd", (128, 256), BF16)
    p.op("gpsimd", lambda e: e.memset(rff[:], 0.0), writes=["s3_rff"])
    p.op("sync", lambda e: e.dma_start(out=rff[0:64, 0:128], in_=io["rf"]),
         writes=["s3_rff"], dma=True)
    p.op("sync", lambda e: e.dma_start(out=rff[64:128, 128:256], in_=io["rf"]),
         writes=["s3_rff"], dma=True)
    p.op("vector", lambda e: e.tensor_scalar(rfbd[:], rff[:], INV2PI, None,
                                             OP.mult),
         reads=["s3_rff", "cpool"], writes=["s3_rfbd"])

    biasq = B.sb("s3_biasq", (128, 1), F32)
    if 'nobias' not in phases:
        p.op("gpsimd", lambda e: e.memset(biasq[0:64, :], 0.25), writes=["s3_biasq"])
        p.op("gpsimd", lambda e: e.memset(biasq[64:128, :], 0.0), writes=["s3_biasq"])

    y16 = B.sb("s3_y16", (128, 8 * 512), BF16)
    qk16 = B.sb("s3_qk16", (128, 8 * 512), BF16)
    V16 = B.sb("s3_V16", (128, 16 * 1040), BF16)
    KV16 = B.sb("s3_KV16", (128, 16 * 65), BF16)
    php = B.sb("s3_php", (128, 2048), BF16)
    pt1 = B.sb("s3_pt1", (128, 512), F32)
    pt2 = B.sb("s3_pt2", (128, 512), F32)
    qp16 = B.sb("s3_qp16", (128, 512), BF16)
    zrq = B.sb("s3_zrq", (128, 16), F32)
    ao4 = B.sb("s3_ao4", (128, 4 * 1024), BF16)
    aoT = B.sb("s3_aoT", (128, 8 * 512), BF16)
    xf32 = B.sb("s3_xf32", (128, 8 * 512), F32)

    yv = io["y2T"].rearrange("(a p) t -> p a t", p=128)
    ov = io["y3T"].rearrange("(a p) t -> p a t", p=128)

    HG = [(0, 7), (7, 14), (14, 16)]

    if 'noones' not in phases:
        p.op("gpsimd", lambda e: e.memset(
            rep_ap(V16, [[1040, 16], [65, 16], [1, 1]], 64), C2), writes=["s3_V16o"])

    for b in range(nb):
        t0 = b * T
        loadw(wA, "s3_wA", "Wk")
        loadw(wB, "s3_wB", "Wv")
        for c in range(1 if 'tiny' in phases else 4):
            p.op("gpsimd", lambda e, c=c, t0=t0: e.dma_start(
                out=y16[:].rearrange("p (a t) -> p a t", a=8),
                in_=yv[:, :, t0 + c * 512:t0 + c * 512 + 512]),
                writes=["s3_y16"], dma=True, stream="x")
            for o in range(8) if 'p1k' in phases else []:
                for a in range(8):
                    p.op("tensor", lambda e, a=a, o=o: e.matmul(
                        P0[:], wA[:, a * 1024 + o * 128:a * 1024 + o * 128 + 128],
                        y16[:, a * 512:a * 512 + 512],
                        start=(a == 0), stop=(a == 7)),
                        reads=["s3_wA", "s3_y16"], writes=["s3_P0"])
                p.op("vector", lambda e, o=o: e.tensor_copy(
                    qk16[:, o * 512:o * 512 + 512], P0[:]),
                    reads=["s3_P0"], writes=["s3_qk16"])
            for ts in range(4) if 'p1v' in phases else []:
                tt = c * 4 + ts
                for vc in range(2):
                    for a in range(8):
                        p.op("tensor", lambda e, a=a, ts=ts, vc=vc: e.matmul(
                            P1[:], y16[:, a * 512 + ts * 128:a * 512 + ts * 128 + 128],
                            wB[:, a * 1024 + vc * 512:a * 1024 + vc * 512 + 512],
                            start=(a == 0), stop=(a == 7)),
                            reads=["s3_y16", "s3_wB"], writes=["s3_P1"])
                    p.op("vector", lambda e, tt=tt, vc=vc: e.tensor_scalar(
                        rep_ap(V16, [[65, 8], [1, 64]], tt * 1040 + vc * 520),
                        rep_ap(P1, [[64, 8], [1, 64]], 0), C2, None, OP.mult),
                        reads=["s3_P1", "cpool"], writes=["s3_V16"])
            for ts in (range(1 if 'tiny' in phases else 4) if 'p1f' in phases else []):
                tt = c * 4 + ts
                for hg in range(1 if 'tiny' in phases else 4):
                    for hh in (range(4) if 'noproj' not in phases else []):
                        h = hg * 4 + hh
                        a, pr = h // 2, h % 2
                        p.op("tensor", lambda e, a=a, pr=pr, ts=ts, hh=hh: e.matmul(
                            P2[:, hh * 128:hh * 128 + 128],
                            qk16[:, a * 512 + ts * 128:a * 512 + ts * 128 + 128],
                            rfbd[:, pr * 128:pr * 128 + 128], start=True, stop=True),
                            reads=["s3_qk16", "s3_rfbd"], writes=["s3_P2"])
                    if 'nophi' in phases:
                        continue
                    p.op("scalar", lambda e: e.activation(pt1[:], P2[:], AF.Identity),
                         reads=["s3_P2"], writes=["s3_pt1"])
                    if 'nophi2' not in phases:
                        p.op("vector", lambda e: e.tensor_scalar(
                            rep_ap(pt1, [[128, 4], [1, 64]], 0),
                            rep_ap(pt1, [[128, 4], [1, 64]], 0), 0.25, None, OP.add),
                            reads=["s3_pt1", "cpool"], writes=["s3_pt1"])
                        p.op("vector", lambda e: e.tensor_scalar(
                            pt2[:], pt1[:], MAGIC, None, OP.add),
                            reads=["s3_pt1", "cpool"], writes=["s3_pt2"])
                        p.op("vector", lambda e: e.tensor_scalar(
                            pt2[:], pt2[:], -MAGIC, None, OP.add),
                            reads=["s3_pt2", "cpool"], writes=["s3_pt2"])
                        p.op("vector", lambda e: e.tensor_tensor(
                            pt1[:], pt1[:], pt2[:], OP.subtract),
                            reads=["s3_pt1", "s3_pt2"], writes=["s3_pt1"])
                    p.op("scalar", lambda e, hg=hg: e.activation(
                        php[:, hg * 512:hg * 512 + 512], pt1[:],
                        AF.Sin if 'nosin' not in phases else AF.Identity,
                        scale=float(2.0 * np.pi) if 'nosin' not in phases else 1.0),
                        reads=["s3_pt1"], writes=["s3_php"])
                for h in (range(NH) if 'nokv' not in phases else []):
                    gi_ = 0 if h < 7 else (1 if h < 14 else 2)
                    h0 = HG[gi_][0]
                    sl = (h - h0) * 65
                    first = (c == 0 and ts == 0 and h == h0) or 'shortacc' in phases
                    last = (c == 3 and ts == 3 and h == HG[gi_][1] - 1) or 'shortacc' in phases
                    p.op("tensor", lambda e, h=h, tt=tt, gi_=gi_, sl=sl,
                         first=first, last=last: e.matmul(
                        TSB[gi_][:, sl:sl + 65], php[:, h * 128:h * 128 + 128],
                        V16[:, tt * 1040 + h * 65:tt * 1040 + h * 65 + 65],
                        start=first, stop=last),
                        reads=["s3_php", "s3_V16", "s3_V16o"], writes=[TSK[gi_]])
        for gi_, (h0, h1) in (enumerate(HG) if 'p1f' in phases else []):
            p.op("vector", lambda e, gi_=gi_, h0=h0, h1=h1: e.tensor_copy(
                KV16[:, h0 * 65:h1 * 65], TSB[gi_][:, 0:(h1 - h0) * 65]),
                reads=[TSK[gi_]], writes=["s3_KV16"])
        if b == 0 and "dbg_KV16" in io:
            p.op("sync", lambda e: e.dma_start(out=io["dbg_KV16"], in_=KV16[:]),
                 reads=["s3_KV16"], dma=True, stream="o")

        loadw(wA, "s3_wA", "Wq")
        loadw(wB, "s3_wB", "Wo")
        for c in range(4):
            p.op("gpsimd", lambda e, c=c, t0=t0: e.dma_start(
                out=y16[:].rearrange("p (a t) -> p a t", a=8),
                in_=yv[:, :, t0 + c * 512:t0 + c * 512 + 512]),
                writes=["s3_y16"], dma=True, stream="x")
            for o in range(8) if 'p2q' in phases else []:
                for a in range(8):
                    p.op("tensor", lambda e, a=a, o=o: e.matmul(
                        P0[:], wA[:, a * 1024 + o * 128:a * 1024 + o * 128 + 128],
                        y16[:, a * 512:a * 512 + 512],
                        start=(a == 0), stop=(a == 7)),
                        reads=["s3_wA", "s3_y16"], writes=["s3_P0"])
                p.op("vector", lambda e, o=o: e.tensor_copy(
                    qk16[:, o * 512:o * 512 + 512], P0[:]),
                    reads=["s3_P0"], writes=["s3_qk16"])
            for gi_, (h0, h1) in (enumerate(HG) if 'p2f' in phases else []):
                for h in range(h0, h1):
                    a, pr = h // 2, h % 2
                    p.op("tensor", lambda e, a=a, pr=pr: e.matmul(
                        P2[:], rfbd[:, pr * 128:pr * 128 + 128],
                        qk16[:, a * 512:a * 512 + 512],
                        start=True, stop=True),
                        reads=["s3_qk16", "s3_rfbd"], writes=["s3_P2"])
                    p.op("scalar", lambda e: e.activation(
                        pt1[:], P2[:], AF.Identity, bias=biasq[:, 0:1]),
                        reads=["s3_P2", "s3_biasq"], writes=["s3_pt1"])
                    p.op("vector", lambda e: e.tensor_scalar(
                        pt2[:], pt1[:], MAGIC, None, OP.add),
                        reads=["s3_pt1", "cpool"], writes=["s3_pt2"])
                    p.op("vector", lambda e: e.tensor_scalar(
                        pt2[:], pt2[:], -MAGIC, None, OP.add),
                        reads=["s3_pt2", "cpool"], writes=["s3_pt2"])
                    p.op("vector", lambda e: e.tensor_tensor(
                        pt1[:], pt1[:], pt2[:], OP.subtract),
                        reads=["s3_pt1", "s3_pt2"], writes=["s3_pt1"])
                    p.op("scalar", lambda e: e.activation(
                        qp16[:], pt1[:], AF.Sin, scale=float(2.0 * np.pi)),
                        reads=["s3_pt1"], writes=["s3_qp16"])
                    if b == 0 and c == 0 and h == 0 and "dbg_qp" in io:
                        p.op("sync", lambda e: e.dma_start(out=io["dbg_qp"], in_=qp16[:]),
                             reads=["s3_qp16"], dma=True, stream="o")
                    for ts in range(4):
                        sl = (h - h0) * 65
                        p.op("tensor", lambda e, ts=ts, h=h, sl=sl: e.matmul(
                            TSB[ts][:, sl:sl + 65],
                            qp16[:, ts * 128:ts * 128 + 128],
                            KV16[:, h * 65:h * 65 + 65], start=True, stop=True),
                            reads=["s3_qp16", "s3_KV16"], writes=[TSK[ts]])
                ng = h1 - h0
                for ts in range(4):
                    p.op("vector", lambda e, ts=ts, ng=ng: e.tensor_scalar(
                        rep_ap(TSB[ts], [[65, ng], [1, 1]], 64),
                        rep_ap(TSB[ts], [[65, ng], [1, 1]], 64), 1e-6, None,
                        OP.max), reads=[TSK[ts], "cpool"], writes=[TSK[ts]])
                    p.op("vector", lambda e, ts=ts, ng=ng, h0=h0: e.reciprocal(
                        zrq[:, h0:h0 + ng],
                        rep_ap(TSB[ts], [[65, ng], [1, 1]], 64)),
                        reads=[TSK[ts]], writes=["s3_zrq"])
                    p.op("vector", lambda e, ts=ts, ng=ng, h0=h0: e.tensor_tensor(
                        rep_ap(ao4, [[64, ng], [1, 64]], ts * 1024 + h0 * 64),
                        rep_ap(TSB[ts], [[65, ng], [1, 64]], 0),
                        rep_ap(zrq, [[1, ng], [0, 64]], h0), OP.mult),
                        reads=[TSK[ts], "s3_zrq"], writes=["s3_ao4"])
            if b == 0 and c == 0 and "dbg_ao4" in io:
                p.op("sync", lambda e: e.dma_start(out=io["dbg_ao4"], in_=ao4[:]),
                     reads=["s3_ao4"], dma=True, stream="o")
            for ts in range(4) if 'p2t' in phases else []:
                for g in range(8):
                    p.op("tensor", lambda e, ts=ts, g=g: e.transpose(
                        PTT[:, g * 128:(g + 1) * 128],
                        ao4[:, ts * 1024 + g * 128:ts * 1024 + g * 128 + 128],
                        eye[:]), reads=["s3_ao4", "s3_eye"], writes=["s3_PT"])
                for g in range(8):
                    p.op("vector", lambda e, ts=ts, g=g: e.tensor_copy(
                        aoT[:, g * 512 + ts * 128:g * 512 + ts * 128 + 128],
                        PTT[:, g * 128:(g + 1) * 128]),
                        reads=["s3_PT"], writes=["s3_aoT"])
            if b == 0 and c == 0 and "dbg_aoT" in io:
                p.op("sync", lambda e: e.dma_start(out=io["dbg_aoT"], in_=aoT[:]),
                     reads=["s3_aoT"], dma=True, stream="o")
            if 'p2w' not in phases:
                continue
            p.op("sync", lambda e, c=c, t0=t0: e.dma_start(
                out=xf32[:].rearrange("p (a t) -> p a t", a=8),
                in_=yv[:, :, t0 + c * 512:t0 + c * 512 + 512]),
                writes=["s3_xf32"], dma=True, stream="r")
            for o in range(8):
                ps, pk = ((P0, "s3_P0") if o % 2 == 0 else (P2, "s3_P2"))
                for a in range(8):
                    p.op("tensor", lambda e, a=a, o=o, ps=ps: e.matmul(
                        ps[:], wB[:, a * 1024 + o * 128:a * 1024 + o * 128 + 128],
                        aoT[:, a * 512:a * 512 + 512],
                        start=(a == 0), stop=(a == 7)),
                        reads=["s3_wB", "s3_aoT"], writes=[pk])
                p.op("vector", lambda e, o=o, ps=ps, pk=pk: e.tensor_tensor(
                    xf32[:, o * 512:o * 512 + 512], ps[:],
                    xf32[:, o * 512:o * 512 + 512], OP.add),
                    reads=[pk, "s3_xf32"], writes=["s3_xf32"])

            def writer(c=c, t0=t0):
                B.p.op("sync", lambda e: e.dma_start(
                    out=ov[:, :, t0 + c * 512:t0 + c * 512 + 512],
                    in_=xf32[:].rearrange("p (a t) -> p a t", a=8)),
                    reads=["s3_xf32"], dma=True, stream="o")
            ln.emit(xf32, "s3_xf32", G0, G1, G2, G3, writer)


def build_stage3_program(nb=2, phases=('p1k','p1v','p1f','p2q','p2f','p2t','p2w'), debug=False):
    nc = bass.Bass("TRN2", target_bir_lowering=False)
    with ExitStack() as ctx:
        B = BD(nc, ctx)
        io = {}
        if debug:
            io["dbg_KV16"] = B.dram("dbg_KV16", (128, 16 * 65), mybir.dt.bfloat16, kind="ExternalOutput")
            io["dbg_qp"] = B.dram("dbg_qp", (128, 512), mybir.dt.bfloat16, kind="ExternalOutput")
            io["dbg_ao4"] = B.dram("dbg_ao4", (128, 4096), mybir.dt.bfloat16, kind="ExternalOutput")
            io["dbg_aoT"] = B.dram("dbg_aoT", (128, 4096), mybir.dt.bfloat16, kind="ExternalOutput")
        io.update({
            "y2T": B.dram("y2T", (1024, nb * T), F32, kind="ExternalInput"),
            "y3T": B.dram("y3T", (1024, nb * T), F32, kind="ExternalOutput"),
            "Wq": B.dram("Wq", (1024, 1024), F32, kind="ExternalInput"),
            "Wk": B.dram("Wk", (1024, 1024), F32, kind="ExternalInput"),
            "Wv": B.dram("Wv", (1024, 1024), F32, kind="ExternalInput"),
            "Wo": B.dram("Wo", (1024, 1024), F32, kind="ExternalInput"),
            "rf": B.dram("rf", (64, 128), F32, kind="ExternalInput"),
        })
        emit_stage3(B, io, nb=nb, phases=phases)
        B.p.barrier()
        B.p.emit(ctx)
    return nc




def make_xpos_tables():
    DIM_HEAD, WINDOW = 64, 512
    half = np.arange(0, DIM_HEAD, 2, dtype=np.float64)
    inv_freq = 1.0 / 10000.0 ** (half / DIM_HEAD)
    t = np.arange(T, dtype=np.float64)
    freqs = np.concatenate([t[:, None] * inv_freq[None, :]] * 2, -1)
    base = (half + 0.4 * DIM_HEAD) / (1.4 * DIM_HEAD)
    power = (t - T // 2) / (WINDOW // 2)
    scale = np.concatenate([base[None, :] ** power[:, None]] * 2, -1)
    cos, sin = np.cos(freqs), np.sin(freqs)
    sgn = np.concatenate([-np.ones(32), np.ones(32)])[None, :]
    return ((cos * scale * 8.0).astype(np.float32),
            (sin * scale * 8.0 * sgn).astype(np.float32),
            (cos / scale).astype(np.float32),
            (sin / scale * sgn).astype(np.float32))


def emit_stage4_layer(B, io, l, nb=2, phases=("a", "c"), alim=7, qlim=9):
    nc, p = B.nc, B.p
    H, HID = 8, 2730
    NEG = -1.0e9
    L = f"s4{l}"
    B.const(1e-12, 1e-5)
    eye = emit_eye(B, BF16, L + "_eye")
    eyef = emit_eye(B, F32, L + "_eyef")

    x2T = B.dram(L + "_x2T", (1024, nb * T), F32, kind="Internal")
    mscr = B.dram(L + "_mscr", (32, 128), F32, kind="Internal")
    w1c = B.dram(L + "_w1c", (1024, 5460), BF16, kind="Internal")
    w2c = B.dram(L + "_w2c", (2730, 1024), BF16, kind="Internal")
    p.op("gpsimd", lambda e: e.dma_start(out=w1c, in_=io["w1"]),
         writes=[L + "w1c"], dma=True, stream="w")
    p.op("gpsimd", lambda e: e.dma_start(out=w2c, in_=io["w2"]),
         writes=[L + "w2c"], dma=True, stream="w")

    ones1 = B.sb(L + "_ones1", (1, 128), F32)
    p.op("gpsimd", lambda e: e.memset(ones1[:], 1.0), writes=[L + "_ones1"])
    onesc = B.sb(L + "_onesc", (128, 1), BF16)
    p.op("gpsimd", lambda e: e.memset(onesc[:], 1.0), writes=[L + "_onesc"])
    ones2 = B.sb(L + "_ones2", (128, 64), F32)
    p.op("gpsimd", lambda e: e.memset(ones2[:], 1.0), writes=[L + "_ones2"])

    PQ0, PQ1 = B.ps(L + "_PQ0"), B.ps(L + "_PQ1")
    PA, PBm = B.ps(L + "_PA"), B.ps(L + "_PBm")
    PAV, PZ, PS8 = B.ps(L + "_PAV"), B.ps(L + "_PZ"), B.ps(L + "_PS8")
    PTT = B.ps(L + "_PT", (128, 1024), BF16)
    kq0, kq1 = L + "_PQ0", L + "_PQ1"
    ka, kb = L + "_PA", L + "_PBm"
    kav, kz, k8 = L + "_PAV", L + "_PZ", L + "_PS8"
    kt = L + "_PT"

    lngt = B.sb(L + "_lngt", (128, 16), F32)
    p.op("sync", lambda e: e.dma_start(
        out=lngt[:, 0:8], in_=io["lng"].rearrange("(a p) -> p a", p=128),
        allow_slow_non_contiguous=True),
        writes=[L + "_lngt"], dma=True)
    p.op("sync", lambda e: e.dma_start(
        out=lngt[:, 8:16], in_=io["lnb"].rearrange("(a p) -> p a", p=128),
        allow_slow_non_contiguous=True),
        writes=[L + "_lngt"], dma=True)
    b2t = B.sb(L + "_b2t", (128, 8), F32)
    p.op("sync", lambda e: e.dma_start(
        out=b2t[:], in_=io["b2"].rearrange("(a p) -> p a", p=128),
        allow_slow_non_contiguous=True),
        writes=[L + "_b2t"], dma=True)
    b1t = B.sb(L + "_b1t", (128, 44), F32)
    for hoff, coff in ((0, 0), (HID, 22)):
        p.op("sync", lambda e, hoff=hoff, coff=coff: e.dma_start(
            out=b1t[:, coff:coff + 21],
            in_=io["b1"][hoff:hoff + 2688].rearrange("(a p) -> p a", p=128),
            allow_slow_non_contiguous=True),
            writes=[L + "_b1t"], dma=True)
        p.op("sync", lambda e, hoff=hoff, coff=coff: e.dma_start(
            out=b1t[0:42, coff + 21:coff + 22],
            in_=io["b1"][hoff + 2688:hoff + 2730].rearrange("(p a) -> p a", a=1)),
            writes=[L + "_b1t"], dma=True)
    mA = B.sb(L + "_mA", (128, 128), F32)
    mB = B.sb(L + "_mB", (128, 128), F32)
    for (mt, key, cm, pat) in ((mA, L + "_mA", -1, 1), (mB, L + "_mB", 1, -1)):
        p.op("gpsimd", lambda e, mt=mt: e.memset(mt[:], 0.0), writes=[key])
        p.op("gpsimd", lambda e, mt=mt, cm=cm, pat=pat: e.affine_select(
            out=mt[:], in_=mt[:], compare_op=OP.is_ge, fill=NEG, base=0,
            pattern=[[pat, 128]], channel_multiplier=cm),
            reads=[key], writes=[key])

    xv = io["xT"].rearrange("(a p) t -> p a t", p=128)
    x2v = x2T.rearrange("(a p) t -> p a t", p=128)
    ov = io["oT"].rearrange("(a p) t -> p a t", p=128)

    if "a" in phases:
      with ExitStack() as actx:
        sbA = lambda n, sh, dt=F32: actx.enter_context(
            nc.sbuf_tensor(L + n, list(sh), dt))
        wqkv = sbA("_wqkv", (128, 8 * 1536), BF16)
        wo16 = sbA("_wo16", (128, 4 * 1024), BF16)
        x16u = sbA("_x16u", (128, 4 * 512), BF16)
        x16s = sbA("_x16s", (128, 4 * 512), BF16)
        qv = sbA("_qv", (128, 512), F32)
        sq = sbA("_sq", (128, 512), F32)
        tm1 = sbA("_tm1", (128, 512), F32)
        tm2 = sbA("_tm2", (128, 512), F32)
        qkt = sbA("_qkt", (128, 512), BF16)
        ss = sbA("_ss", (128, 24), F32)
        kaug = sbA("_kaug", (128, 2 * 4096), BF16)
        vR = sbA("_vR", (128, 2 * 4096), BF16)
        qaug = sbA("_qaug", (128, 4096), BF16)
        At = sbA("_At", (128, 8 * 512), BF16)
        Mm = sbA("_Mm", (128, 32), F32)
        mtmp = sbA("_mtmp", (128, 1), F32)
        Msb = sbA("_Msb", (32, 128), F32)
        ztmp = sbA("_ztmp", (128, 512), F32)
        zbcs = sbA("_zbcs", (128, 512), F32)
        aoT = sbA("_aoT", (128, 4 * 512), BF16)
        xf32 = sbA("_xf32", (128, 8 * 512), F32)

        qsr = sbA("_qsr", (1, 128), F32)
        p.op("sync", lambda e: e.dma_start(
            out=qsr[0:1, 0:64], in_=io["qs"].rearrange("(a f) -> a f", a=1)),
            writes=[L + "_qsr"], dma=True)
        p.op("sync", lambda e: e.dma_start(
            out=qsr[0:1, 64:128], in_=io["ks"].rearrange("(a f) -> a f", a=1)),
            writes=[L + "_qsr"], dma=True)
        gb = sbA("_gb", (128, 256), F32)
        p.op("tensor", lambda e: e.matmul(PS8[:, 0:128], ones1[0:1, :], qsr[0:1, :],
                                          start=True, stop=True),
             reads=[L + "_qsr", L + "_ones1"], writes=[k8])
        for (do, so, w) in ((0, 0, 64), (64, 32, 32), (96, 0, 32), (128, 64, 64),
                           (192, 96, 32), (224, 64, 32)):
            p.op("vector", lambda e, do=do, so=so, w=w: e.tensor_copy(
                gb[:, do:do + w], PS8[:, so:so + w]),
                reads=[k8], writes=[L + "_gb"])
        tabs = sbA("_tabs", (128, 4 * 1024), F32)
        ttmp = sbA("_ttmp", (128, 1024), F32)
        for i, (tn, goff) in enumerate((("TAq", 0), ("TBq", 64), ("TAk", 128),
                                        ("TBk", 192))):
            p.op("sync", lambda e, tn=tn: e.dma_start(
                out=ttmp[:].rearrange("p (tt f) -> p tt f", f=64),
                in_=io[tn].rearrange("(tt p) f -> p tt f", p=128)),
                writes=[L + "_ttmp"], dma=True)
            p.op("vector", lambda e, i=i, goff=goff: e.tensor_tensor(
                rep_ap(tabs, [[64, 16], [1, 64]], i * 1024),
                rep_ap(ttmp, [[64, 16], [1, 64]], 0),
                rep_ap(gb, [[0, 16], [1, 64]], goff), OP.mult),
                reads=[L + "_ttmp", L + "_gb"], writes=[L + "_tabs"])



        p.op("gpsimd", lambda e: e.dma_start(
            out=wqkv[:].rearrange("p (a f) -> p a f", a=8),
            in_=io["wqkv"].rearrange("(a p) f -> p a f", p=128)),
            writes=[L + "_wqkv"], dma=True, stream="w")
        p.op("gpsimd", lambda e: e.dma_start(
            out=wo16[:].rearrange("p (a f) -> p a f", a=4),
            in_=io["wo"].rearrange("(a p) f -> p a f", p=128)),
            writes=[L + "_wo16"], dma=True, stream="w")
        p.op("gpsimd", lambda e: e.memset(vR[:], 0.0), writes=[L + "_vR"])
        for slot in range(2):
            p.op("gpsimd", lambda e, slot=slot: e.memset(
                rep_ap(vR, [[1024, 4], [256, 4], [1, 1]], slot * 4096 + 64), 1.0),
                writes=[L + "_vR"])
            p.op("gpsimd", lambda e, slot=slot: e.memset(
                rep_ap(vR, [[1024, 4], [256, 4], [1, 1]], slot * 4096 + 128), 1.0),
                writes=[L + "_vR"])
        p.op("gpsimd", lambda e: e.memset(kaug[64:65, :], 1.0),
             writes=[L + "_kaugone"])

        for b in range(nb):
            t0 = b * T
            for c in range(4):
                cs, po = c % 2, (c - 1) % 2
                p.op("gpsimd", lambda e, t0=t0, c=c: e.dma_start(
                    out=x16u[:].rearrange("p (a t) -> p a t", a=4),
                    in_=xv[:, 0:4, t0 + c * 512:t0 + c * 512 + 512]),
                    writes=[L + "_x16u"], dma=True, stream="x")
                if c == 0:
                    p.op("gpsimd", lambda e, t0=t0: e.dma_start(
                        out=x16s[:].rearrange("p (a t) -> p a t", a=4)[:, :, 1:512],
                        in_=xv[:, 4:8, t0:t0 + 511]),
                        writes=[L + "_x16s"], dma=True, stream="x")
                    p.op("gpsimd", lambda e: e.memset(
                        rep_ap(x16s, [[512, 4], [1, 1]], 0), 0.0),
                        writes=[L + "_x16s"])
                else:
                    p.op("gpsimd", lambda e, t0=t0, c=c: e.dma_start(
                        out=x16s[:].rearrange("p (a t) -> p a t", a=4),
                        in_=xv[:, 4:8, t0 + c * 512 - 1:t0 + c * 512 + 511]),
                        writes=[L + "_x16s"], dma=True, stream="x")

                for ts in range(4):
                    tt = c * 4 + ts
                    for part, fo, ps, pk in (("q", 0, PQ0, kq0),
                                             ("k", 512, PQ1, kq1),
                                             ("v", 1024, PQ0, kq0)):
                        for a in range(8):
                            xb = x16u if a < 4 else x16s
                            xo = (a % 4) * 512 + ts * 128
                            p.op("tensor", lambda e, xb=xb, xo=xo, fo=fo, a=a,
                                 ps=ps: e.matmul(
                                ps[:], xb[:, xo:xo + 128],
                                wqkv[:, a * 1536 + fo:a * 1536 + fo + 512],
                                start=(a == 0), stop=(a == 7)),
                                reads=[L + "_x16u", L + "_x16s", L + "_wqkv"],
                                writes=[pk])
                        if part == "v":
                            p.op("vector", lambda e, ts=ts, cs=cs, ps=ps, pk=pk:
                                 e.tensor_copy(
                                     rep_ap(vR, [[256, 4], [1, 64]],
                                            cs * 4096 + ts * 1024 + 0),
                                     rep_ap(ps, [[128, 4], [1, 64]], 0)),
                                 reads=[pk], writes=[L + "_vR"])
                            p.op("vector", lambda e, ts=ts, cs=cs, ps=ps, pk=pk:
                                 e.tensor_copy(
                                     rep_ap(vR, [[256, 4], [1, 64]],
                                            cs * 4096 + ts * 1024 + 192),
                                     rep_ap(ps, [[128, 4], [1, 64]], 64)),
                                 reads=[pk], writes=[L + "_vR"])
                            continue
                        tq = 0 if part == "q" else 2048
                        if qlim < 1:
                            continue
                        p.op("vector", lambda e, ps=ps, pk=pk: e.tensor_copy(
                            qv[:], ps[:]), reads=[pk], writes=[L + "_qv"])
                        if qlim < 2:
                            continue
                        p.op("scalar", lambda e, ps=ps, pk=pk: e.activation(
                            sq[:], ps[:], AF.Square), reads=[pk],
                            writes=[L + "_sq"])
                        p.op("vector", lambda e: e.tensor_reduce(
                            rep_ap(ss, [[1, 8], [1, 1]], 0),
                            rep_ap(sq, [[64, 8], [1, 64]], 0), axis=AX.X,
                            op=OP.add), reads=[L + "_sq"], writes=[L + "_ss"])
                        p.op("scalar", lambda e: e.activation(
                            ss[:, 8:16], ss[:, 0:8], AF.Sqrt),
                            reads=[L + "_ss"], writes=[L + "_ss2"])
                        p.op("vector", lambda e: e.tensor_scalar(
                            ss[:, 8:16], ss[:, 8:16], 1e-12, None, OP.max),
                            reads=[L + "_ss2", "cpool"], writes=[L + "_ss2"])
                        p.op("vector", lambda e: e.reciprocal(
                            ss[:, 16:24], ss[:, 8:16]),
                            reads=[L + "_ss2"], writes=[L + "_ss3"])
                        if qlim < 3:
                            continue
                        p.op("vector", lambda e: e.tensor_tensor(
                            qv[:], qv[:], rep_ap(ss, [[1, 8], [0, 64]], 16),
                            OP.mult), reads=[L + "_qv", L + "_ss3"],
                            writes=[L + "_qv"])
                        if qlim < 4:
                            continue
                        p.op("vector", lambda e, tq=tq, tt=tt: e.tensor_tensor(
                            tm1[:], qv[:],
                            rep_ap(tabs, [[0, 8], [1, 64]], tq + tt * 64),
                            OP.mult), reads=[L + "_qv", L + "_tabs"],
                            writes=[L + "_tm1"])
                        for half in (0, 1):
                            p.op("vector", lambda e, tq=tq, tt=tt, half=half:
                                 e.tensor_tensor(
                                     rep_ap(tm2, [[64, 8], [1, 32]], half * 32),
                                     rep_ap(qv, [[64, 8], [1, 32]],
                                            (1 - half) * 32),
                                     rep_ap(tabs, [[0, 8], [1, 32]],
                                            tq + 1024 + tt * 64 + half * 32),
                                     OP.mult),
                                 reads=[L + "_qv", L + "_tabs"],
                                 writes=[L + "_tm2"])
                        p.op("vector", lambda e: e.tensor_tensor(
                            qkt[:], tm1[:], tm2[:], OP.add),
                            reads=[L + "_tm1", L + "_tm2"], writes=[L + "_qkt"])
                        if qlim < 5:
                            continue
                        for h in range(H):
                            p.op("tensor", lambda e, h=h: e.transpose(
                                PTT[0:64, h * 128:h * 128 + 128],
                                qkt[:, h * 64:h * 64 + 64], eye[:]),
                                reads=[L + "_qkt", L + "_eye"], writes=[kt])
                        dst, dk, doff = ((qaug, L + "_qaug", ts * 128)
                                         if part == "q" else
                                         (kaug, L + "_kaug",
                                          cs * 4096 + ts * 128))
                        p.op("vector", lambda e, dst=dst, doff=doff:
                             e.tensor_copy(
                                 rep_ap(dst, [[512, 8], [1, 128]], doff,
                                        psl=(0, 64)),
                                 rep_ap(PTT, [[128, 8], [1, 128]], 0,
                                        psl=(0, 64))),
                             reads=[kt], writes=[dk])

                for h in (range(H) if alim >= 2 else []):
                    for it in range(4):
                        parts = []
                        w1_ = 512 - it * 128
                        if c > 0:
                            parts.append((po, it * 128, 0, w1_))
                        if it > 0:
                            parts.append((cs, 0, w1_ if c > 0 else 0, it * 128))
                        nacc = 0
                        for (sl, jo, co, w) in parts:
                            p.op("tensor", lambda e, h=h, it=it, sl=sl, jo=jo,
                                 co=co, w=w: e.matmul(
                                PA[:, co:co + w],
                                qaug[0:64, h * 512 + it * 128:
                                     h * 512 + it * 128 + 128],
                                kaug[0:64, sl * 4096 + h * 512 + jo:
                                     sl * 4096 + h * 512 + jo + w],
                                start=True, stop=True),
                                reads=[L + "_qaug", L + "_kaug"], writes=[ka])
                            nacc += 1
                        p.op("tensor", lambda e, h=h, it=it, cs=cs: e.matmul(
                            PBm[:, 0:128],
                            qaug[0:64, h * 512 + it * 128:
                                 h * 512 + it * 128 + 128],
                            kaug[0:64, cs * 4096 + h * 512 + it * 128:
                                 cs * 4096 + h * 512 + it * 128 + 128],
                            start=True, stop=True),
                            reads=[L + "_qaug", L + "_kaug"], writes=[kb])
                        if c > 0:
                            p.op("vector", lambda e: e.tensor_tensor(
                                PA[:, 0:128], PA[:, 0:128], mA[:], OP.add),
                                reads=[ka, L + "_mA"], writes=[ka])
                        p.op("vector", lambda e: e.tensor_tensor(
                            PBm[:, 0:128], PBm[:, 0:128], mB[:], OP.add),
                            reads=[kb, L + "_mB"], writes=[kb])
                        col = h * 4 + it
                        if parts:
                            wtot = 512 if c > 0 else it * 128
                            p.op("vector", lambda e, wtot=wtot: e.reduce_max(
                                mtmp[:], PA[:, 0:wtot], axis=AX.X,
                                negate=True), reads=[ka], writes=[L + "_ssm"])
                            p.op("vector", lambda e, col=col: e.reduce_max(
                                Mm[:, col:col + 1], PBm[:, 0:128], axis=AX.X,
                                negate=True), reads=[kb], writes=[L + "_Mm"])
                            p.op("vector", lambda e, col=col: e.tensor_tensor(
                                Mm[:, col:col + 1], Mm[:, col:col + 1],
                                mtmp[:], OP.min),
                                reads=[L + "_Mm", L + "_ssm"],
                                writes=[L + "_Mm"])
                        else:
                            p.op("vector", lambda e, col=col: e.reduce_max(
                                Mm[:, col:col + 1], PBm[:, 0:128], axis=AX.X,
                                negate=True), reads=[kb], writes=[L + "_Mm"])
                if alim < 3:
                    continue
                p.op("tensor", lambda e: e.transpose(
                    PZ[0:32, 0:128], Mm[:], eyef[:]),
                    reads=[L + "_Mm", L + "_eyef"], writes=[kz])
                p.op("vector", lambda e: e.tensor_copy(Msb[:], PZ[0:32, 0:128]),
                     reads=[kz], writes=[L + "_Msb"])
                p.op("sync", lambda e: e.dma_start(out=mscr, in_=Msb[:]),
                     reads=[L + "_Msb"], writes=[L + "_mscr"], dma=True,
                     stream="m")
                p.op("gpsimd", lambda e: e.dma_start(
                    out=qaug[64:65, 0:4096],
                    in_=mscr.rearrange("(a x) i -> a (x i)", a=1)),
                    reads=[L + "_mscr"], writes=[L + "_qaugM"], dma=True,
                    stream="m2")

                for h in (range(H) if alim >= 4 else []):
                    wjts = range(0 if c > 0 else 4, 8)
                    for wjt in wjts:
                        sl = po if wjt < 4 else cs
                        jl = wjt % 4
                        pss, psk = ((PA, ka) if wjt % 2 == 0 else (PBm, kb))
                        p.op("tensor", lambda e, h=h, sl=sl, jl=jl, pss=pss:
                             e.matmul(
                                 pss[:], kaug[0:65, sl * 4096 + h * 512
                                              + jl * 128:
                                              sl * 4096 + h * 512 + jl * 128
                                              + 128],
                                 qaug[0:65, h * 512:h * 512 + 512],
                                 start=True, stop=True),
                             reads=[L + "_kaug", L + "_kaugone", L + "_qaug",
                                    L + "_qaugM"], writes=[psk])
                        p.op("scalar", lambda e, wjt=wjt, pss=pss, psk=psk:
                             e.activation(At[:, wjt * 512:wjt * 512 + 512],
                                          pss[:], AF.Exp),
                             reads=[psk], writes=[L + f"_At#{wjt}"])
                        if wjt < 4:
                            p.op("gpsimd", lambda e, wjt=wjt, c=c, jl=jl:
                                 e.affine_select(
                                     out=At[:, wjt * 512:wjt * 512 + 512],
                                     in_=At[:, wjt * 512:wjt * 512 + 512],
                                     compare_op=OP.is_ge, fill=0.0,
                                     base=(jl * 128) - 512 + 512, pattern=[[-1, 512]],
                                     channel_multiplier=1),
                                 reads=[L + f"_At#{wjt}"],
                                 writes=[L + f"_At#{wjt}"])
                        else:
                            p.op("gpsimd", lambda e, wjt=wjt, jl=jl:
                                 e.affine_select(
                                     out=At[:, wjt * 512:wjt * 512 + 512],
                                     in_=At[:, wjt * 512:wjt * 512 + 512],
                                     compare_op=OP.is_ge, fill=0.0,
                                     base=-(jl * 128), pattern=[[1, 512]],
                                     channel_multiplier=-1),
                                 reads=[L + f"_At#{wjt}"],
                                 writes=[L + f"_At#{wjt}"])
                    if alim < 5:
                        continue
                    first = True
                    for wjt in wjts:
                        sl = po if wjt < 4 else cs
                        jl = wjt % 4
                        p.op("tensor", lambda e, h=h, sl=sl, jl=jl, wjt=wjt,
                             first=first, last=(wjt == 7): e.matmul(
                                 PAV[:], vR[:, sl * 4096 + jl * 1024
                                            + h * 128:
                                            sl * 4096 + jl * 1024 + h * 128
                                            + 128],
                                 At[:, wjt * 512:wjt * 512 + 512],
                                 start=first, stop=last),
                             reads=[L + "_vR", L + f"_At#{wjt}"], writes=[kav])
                        first = False
                    if alim < 6:
                        continue
                    zr = 64 if h % 2 == 0 else 0
                    vr_ = 64 - zr
                    p.op("scalar", lambda e, zr=zr: e.activation(
                        ztmp[zr:zr + 1, :], PAV[zr:zr + 1, :], AF.Ln),
                        reads=[kav], writes=[L + "_ztmp"])
                    p.op("scalar", lambda e, zr=zr: e.activation(
                        ztmp[zr:zr + 1, :], ztmp[zr:zr + 1, :], AF.Exp,
                        scale=-1.0), reads=[L + "_ztmp"], writes=[L + "_ztmp"])
                    p.op("tensor", lambda e, zr=zr, vr_=vr_: e.matmul(
                        PZ[vr_:vr_ + 64, :], ones2[zr:zr + 1, :],
                        ztmp[zr:zr + 1, :], start=True, stop=True,
                        tile_position=(zr, vr_)),
                        reads=[L + "_ztmp", L + "_ones2"], writes=[kz])
                    p.op("vector", lambda e, vr_=vr_: e.tensor_copy(
                        zbcs[vr_:vr_ + 64, :], PZ[vr_:vr_ + 64, :]),
                        reads=[kz], writes=[L + "_zbcs"])
                    p.op("vector", lambda e, h=h, vr_=vr_: e.tensor_tensor(
                        aoT[vr_:vr_ + 64, (h // 2) * 512:(h // 2) * 512 + 512],
                        PAV[vr_:vr_ + 64, :], zbcs[vr_:vr_ + 64, :], OP.mult),
                        reads=[kav, L + "_zbcs"], writes=[L + "_aoT"])

                if alim < 7:
                    continue
                p.op("sync", lambda e, t0=t0, c=c: e.dma_start(
                    out=xf32[:].rearrange("p (a t) -> p a t", a=8),
                    in_=xv[:, :, t0 + c * 512:t0 + c * 512 + 512]),
                    writes=[L + "_xf32"], dma=True, stream="r")
                for o in range(8):
                    ps, pk = ((PQ0, kq0) if o % 2 == 0 else (PQ1, kq1))
                    for ra in range(4):
                        p.op("tensor", lambda e, ra=ra, o=o, ps=ps: e.matmul(
                            ps[:], wo16[:, ra * 1024 + o * 128:
                                         ra * 1024 + o * 128 + 128],
                            aoT[:, ra * 512:ra * 512 + 512],
                            start=(ra == 0), stop=(ra == 3)),
                            reads=[L + "_wo16", L + "_aoT"], writes=[pk])
                    p.op("vector", lambda e, o=o, ps=ps, pk=pk: e.tensor_tensor(
                        xf32[:, o * 512:o * 512 + 512], ps[:],
                        xf32[:, o * 512:o * 512 + 512], OP.add),
                        reads=[pk, L + "_xf32"], writes=[L + "_xf32"])
                p.op("sync", lambda e, t0=t0, c=c: e.dma_start(
                    out=x2v[:, :, t0 + c * 512:t0 + c * 512 + 512],
                    in_=xf32[:].rearrange("p (a t) -> p a t", a=8)),
                    reads=[L + "_xf32"], dma=True, stream="o")
      p.barrier()

    if "c" in phases:
      with ExitStack() as cctx:
        sbC = lambda n, sh, dt=F32: cctx.enter_context(
            nc.sbuf_tensor(L + n, list(sh), dt))
        x2c = sbC("_x2c", (128, 8 * 513), F32)
        z16 = sbC("_z16", (128, 2 * 4096), BF16)
        yh16 = sbC("_yh16", (128, 8 * T), BF16)
        hph = sbC("_hph", (128, 22 * 1024), BF16)
        w1tb = [sbC("_w1tA", (128, 16 * 128), BF16),
                sbC("_w1tB", (128, 16 * 128), BF16)]
        w2r = sbC("_w2r", (128, 22 * 1024), BF16)
        g16t = sbC("_g16t", (128, 512), BF16)
        stat = sbC("_stat", (1, 2048), F32)
        ytm = sbC("_ytm", (128, 512), F32)
        xr32 = sbC("_xr32", (128, 8 * 512), F32)

        p.op("sync", lambda e: e.dma_start(
            out=w2r[:].rearrange("p (a f) -> p a f", a=22)[:, 0:21, :],
            in_=w2c[0:2688, :].rearrange("(a p) f -> p a f", p=128)),
            writes=[L + "_w2r"], dma=True, stream="r")
        p.op("sync", lambda e: e.dma_start(
            out=w2r[0:42, 21 * 1024:22 * 1024], in_=w2c[2688:2730, :]),
            writes=[L + "_w2r"], dma=True, stream="r")

        for b in range(nb):
            t0 = b * T
            for c in range(4):
                p.op("sync", lambda e, t0=t0, c=c: e.dma_start(
                    out=x2c[:].rearrange("p (a t) -> p a t", a=8)[:, :, 1:513],
                    in_=x2v[:, :, t0 + c * 512:t0 + c * 512 + 512]),
                    writes=[L + "_x2c"], dma=True, stream="x")
                if c == 0:
                    p.op("gpsimd", lambda e: e.memset(
                        rep_ap(x2c, [[513, 8], [1, 1]], 0), 0.0),
                        writes=[L + "_x2c"])
                else:
                    p.op("sync", lambda e, t0=t0, c=c: e.dma_start(
                        out=x2c[:].rearrange("p (a t) -> p a t", a=8)[:, :, 0:1],
                        in_=x2v[:, :, t0 + c * 512 - 1:t0 + c * 512],
                        allow_slow_non_contiguous=True),
                        writes=[L + "_x2c"], dma=True, stream="x")
                for half, off in ((0, 1), (1, 4 * 513)):
                    p.op("vector", lambda e, half=half, off=off: e.tensor_copy(
                        rep_ap(z16, [[512, 4], [1, 512]], half * 2048),
                        rep_ap(x2c, [[513, 4], [1, 512]], off)),
                        reads=[L + "_x2c"], writes=[L + "_z16lo"])
                    p.op("scalar", lambda e, half=half, off=off: e.activation(
                        rep_ap(z16, [[512, 4], [1, 512]], 4096 + half * 2048),
                        rep_ap(x2c, [[513, 4], [1, 512]], off), AF.Square),
                        reads=[L + "_x2c"], writes=[L + "_z16hi"])
                for a in range(8):
                    p.op("tensor", lambda e, a=a: e.matmul(
                        PAV[0:1, :], onesc[:, 0:1],
                        z16[:, a * 512:a * 512 + 512],
                        start=(a == 0), stop=(a == 7)),
                        reads=[L + "_z16lo", L + "_onesc"], writes=[kav])
                for a in range(8):
                    p.op("tensor", lambda e, a=a: e.matmul(
                        PZ[0:1, :], onesc[:, 0:1],
                        z16[:, 4096 + a * 512:4096 + a * 512 + 512],
                        start=(a == 0), stop=(a == 7)),
                        reads=[L + "_z16hi", L + "_onesc"], writes=[kz])
                mu, m2, var, rstd = (stat[0:1, i * 512:(i + 1) * 512]
                                     for i in range(4))
                p.op("vector", lambda e: e.tensor_scalar(
                    mu, PAV[0:1, :], 1.0 / 1024.0, None, OP.mult),
                    reads=[kav, "cpool"], writes=[L + "_mu"])
                p.op("scalar", lambda e: e.activation(
                    m2, PZ[0:1, :], AF.Identity, scale=1.0 / 1024.0),
                    reads=[kz], writes=[L + "_m2"])
                p.op("vector", lambda e: e.tensor_tensor(var, mu, mu, OP.mult),
                     reads=[L + "_mu"], writes=[L + "_var"])
                p.op("vector", lambda e: e.tensor_tensor(var, m2, var,
                                                         OP.subtract),
                     reads=[L + "_m2", L + "_var"], writes=[L + "_var"])
                p.op("scalar", lambda e: e.activation(var, var, AF.Ln,
                                                      bias=1e-5),
                     reads=[L + "_var", "cpool"], writes=[L + "_var"])
                p.op("scalar", lambda e: e.activation(rstd, var, AF.Exp,
                                                      scale=-0.5),
                     reads=[L + "_var"], writes=[L + "_rstd"])
                p.op("tensor", lambda e: e.matmul(
                    PAV[:], ones1[0:1, :], mu, start=True, stop=True),
                    reads=[L + "_mu", L + "_ones1"], writes=[kav])
                p.op("tensor", lambda e: e.matmul(
                    PZ[:], ones1[0:1, :], rstd, start=True, stop=True),
                    reads=[L + "_rstd", L + "_ones1"], writes=[kz])
                for a in range(8):
                    off = a * 513 + (1 if a < 4 else 0)
                    p.op("vector", lambda e, a=a, off=off: e.tensor_tensor(
                        ytm[:], rep_ap(x2c, [[1, 512]], off),
                        PAV[:], OP.subtract),
                        reads=[L + "_x2c", kav], writes=[L + "_ytm"])
                    p.op("vector", lambda e: e.tensor_tensor(
                        ytm[:], ytm[:], PZ[:], OP.mult),
                        reads=[L + "_ytm", kz], writes=[L + "_ytm"])
                    p.op("scalar", lambda e, a=a, c=c: e.activation(
                        yh16[:, a * T + c * 512:a * T + c * 512 + 512],
                        ytm[:], AF.Identity, scale=lngt[:, a:a + 1],
                        bias=lngt[:, 8 + a:8 + a + 1]),
                        reads=[L + "_ytm", L + "_lngt"], writes=[L + "_yh16"])
            for hf in range(2):
                for fp in range(22):
                    wid = 128 if fp < 21 else 42
                    w1t = w1tb[fp % 2]
                    wk = L + "_w1t" + str(fp % 2)
                    wstr = "r" if fp % 2 == 0 else "r2"
                    for gpart, coff in ((0, 0), (1, HID)):
                        p.op("sync", lambda e, fp=fp, wid=wid, gpart=gpart,
                             coff=coff, w1t=w1t: e.dma_start(
                            out=w1t[:].rearrange("p (a f) -> p a f", a=16)
                                [:, gpart * 8:gpart * 8 + 8, 0:wid],
                            in_=w1c[:, coff + fp * 128:coff + fp * 128 + wid]
                                .rearrange("(a p) f -> p a f", p=128)),
                            writes=[wk], dma=True, stream=wstr)
                    for cc in range(2):
                        c = hf * 2 + cc
                        for gpart, ps, pk in ((0, PA, ka), (1, PBm, kb)):
                            for a in range(8):
                                p.op("tensor", lambda e, a=a, gpart=gpart,
                                     wid=wid, c=c, ps=ps, w1t=w1t: e.matmul(
                                    ps[0:wid, :],
                                    w1t[:, (gpart * 8 + a) * 128:
                                        (gpart * 8 + a) * 128 + wid],
                                    yh16[:, a * T + c * 512:a * T + c * 512
                                         + 512],
                                    start=(a == 0), stop=(a == 7)),
                                    reads=[wk, L + "_yh16"],
                                    writes=[pk])
                        p.op("scalar", lambda e, fp=fp, wid=wid: e.activation(
                            g16t[0:wid, :], PBm[0:wid, :], AF.Gelu,
                            bias=b1t[0:wid, 22 + fp:22 + fp + 1]),
                            reads=[kb, L + "_b1t"], writes=[L + "_g16t"])
                        p.op("vector", lambda e, fp=fp, wid=wid, cc=cc:
                             e.scalar_tensor_tensor(
                                 hph[0:wid, fp * 1024 + cc * 512:
                                     fp * 1024 + cc * 512 + 512],
                                 PA[0:wid, :], b1t[0:wid, fp:fp + 1],
                                 g16t[0:wid, :], OP.add, OP.mult),
                             reads=[ka, L + "_g16t", L + "_b1t"],
                             writes=[L + f"_hph#{fp}"])
                for cc in range(2):
                    c = hf * 2 + cc
                    p.op("sync", lambda e, t0=t0, c=c: e.dma_start(
                        out=xr32[:].rearrange("p (a t) -> p a t", a=8),
                        in_=x2v[:, :, t0 + c * 512:t0 + c * 512 + 512]),
                        writes=[L + "_xr32"], dma=True, stream="x")
                    for o in range(8):
                        ps, pk = ((PQ0, kq0) if o % 2 == 0 else (PQ1, kq1))
                        for fp in range(22):
                            wid = 128 if fp < 21 else 42
                            p.op("tensor", lambda e, fp=fp, wid=wid, o=o,
                                 cc=cc, ps=ps: e.matmul(
                                ps[:], w2r[0:wid, fp * 1024 + o * 128:
                                           fp * 1024 + o * 128 + 128],
                                hph[0:wid, fp * 1024 + cc * 512:
                                    fp * 1024 + cc * 512 + 512],
                                start=(fp == 0), stop=(fp == 21)),
                                reads=[L + "_w2r", L + f"_hph#{fp}"],
                                writes=[pk])
                        p.op("vector", lambda e, o=o, ps=ps, pk=pk:
                             e.scalar_tensor_tensor(
                                 xr32[:, o * 512:o * 512 + 512], ps[:],
                                 b2t[:, o:o + 1],
                                 xr32[:, o * 512:o * 512 + 512],
                                 OP.add, OP.add),
                             reads=[pk, L + "_b2t", L + "_xr32"],
                             writes=[L + "_xr32"])
                    p.op("sync", lambda e, t0=t0, c=c: e.dma_start(
                        out=ov[:, :, t0 + c * 512:t0 + c * 512 + 512],
                        in_=xr32[:].rearrange("p (a t) -> p a t", a=8)),
                        reads=[L + "_xr32"], dma=True, stream="o")
      p.barrier()


def build_stage4_program(l=0, nb=2, phases=("a", "c"), alim=7, qlim=9):
    nc = bass.Bass("TRN2", target_bir_lowering=False)
    with ExitStack() as ctx:
        B = BD(nc, ctx)
        io = {
            "xT": B.dram("xT", (1024, nb * T), F32, kind="ExternalInput"),
            "oT": B.dram("oT", (1024, nb * T), F32, kind="ExternalOutput"),
            "wqkv": B.dram("wqkv", (1024, 1536), F32, kind="ExternalInput"),
            "qs": B.dram("qs", (64,), F32, kind="ExternalInput"),
            "ks": B.dram("ks", (64,), F32, kind="ExternalInput"),
            "wo": B.dram("wo", (512, 1024), F32, kind="ExternalInput"),
            "lng": B.dram("lng", (1024,), F32, kind="ExternalInput"),
            "lnb": B.dram("lnb", (1024,), F32, kind="ExternalInput"),
            "w1": B.dram("w1", (1024, 5460), F32, kind="ExternalInput"),
            "b1": B.dram("b1", (5460,), F32, kind="ExternalInput"),
            "w2": B.dram("w2", (2730, 1024), F32, kind="ExternalInput"),
            "b2": B.dram("b2", (1024,), F32, kind="ExternalInput"),
            "TAq": B.dram("TAq", (T, 64), F32, kind="ExternalInput"),
            "TBq": B.dram("TBq", (T, 64), F32, kind="ExternalInput"),
            "TAk": B.dram("TAk", (T, 64), F32, kind="ExternalInput"),
            "TBk": B.dram("TBk", (T, 64), F32, kind="ExternalInput"),
        }
        emit_stage4_layer(B, io, l, nb=nb, phases=phases, alim=alim, qlim=qlim)
        B.p.barrier()
        B.p.emit(ctx)
    return nc




def emit_ingest(B, io, nb=2):
    nc, p = B.nc, B.p
    eyef = emit_eye(B, F32, "in_eyef")
    xin = B.sb("in_x", (128, 1024), F32)
    xts = B.sb("in_xt", (128, 8 * 128), F32)
    PT = B.ps("in_PT")
    xo = io["xT"].rearrange("(a p) t -> p a t", p=128)
    for b in range(nb):
        for tt in range(16):
            p.op("sync", lambda e, b=b, tt=tt: e.dma_start(
                out=xin[:], in_=io["x"][b, tt * 128:tt * 128 + 128, :]),
                writes=["in_x"], dma=True, stream="x")
            for a in range(8):
                p.op("tensor", lambda e, a=a: e.transpose(
                    PT[:, 0:128], xin[:, a * 128:a * 128 + 128], eyef[:]),
                    reads=["in_x", "in_eyef"], writes=["in_PT"])
                p.op("vector", lambda e, a=a: e.tensor_copy(
                    xts[:, a * 128:a * 128 + 128], PT[:, 0:128]),
                    reads=["in_PT"], writes=["in_xt"])
            p.op("sync", lambda e, b=b, tt=tt: e.dma_start(
                out=xo[:, :, b * T + tt * 128:b * T + tt * 128 + 128],
                in_=xts[:].rearrange("p (a t) -> p a t", a=8)),
                reads=["in_xt"], dma=True, stream="o")


def emit_final(B, io, nb=2):
    nc, p = B.nc, B.p
    eyef = emit_eye(B, F32, "fi_eyef")
    ln = LNPack(B, "fi")
    av = io["aT"].rearrange("(a p) t -> p a t", p=128)
    bv = io["bT"].rearrange("(a p) t -> p a t", p=128)
    z = B.sb("fi_z", (128, 8 * 512), F32)
    zb = B.sb("fi_zb", (128, 8 * 512), F32)
    ot = B.sb("fi_ot", (128, 8 * 128), F32)
    P0, P1, P2, P3 = (B.ps(f"fi_P{i}") for i in range(4))
    PT = B.ps("fi_PT")
    for b in range(nb):
        for c in range(4):
            p.op("sync", lambda e, b=b, c=c: e.dma_start(
                out=z[:].rearrange("p (a t) -> p a t", a=8),
                in_=av[:, :, b * T + c * 512:b * T + c * 512 + 512]),
                writes=["fi_z"], dma=True, stream="x")
            p.op("sync", lambda e, b=b, c=c: e.dma_start(
                out=zb[:].rearrange("p (a t) -> p a t", a=8),
                in_=bv[:, :, b * T + c * 512:b * T + c * 512 + 512]),
                writes=["fi_zb"], dma=True, stream="r")
            p.op("vector", lambda e: e.tensor_tensor(z[:], z[:], zb[:], OP.add),
                 reads=["fi_z", "fi_zb"], writes=["fi_z"])

            def writer(b=b, c=c):
                for ts in range(4):
                    for a in range(8):
                        B.p.op("tensor", lambda e, ts=ts, a=a: e.transpose(
                            PT[:, 0:128],
                            z[:, a * 512 + ts * 128:a * 512 + ts * 128 + 128],
                            eyef[:]), reads=["fi_z", "fi_eyef"],
                            writes=["fi_PT"])
                        B.p.op("vector", lambda e, a=a: e.tensor_copy(
                            ot[:, a * 128:a * 128 + 128], PT[:, 0:128]),
                            reads=["fi_PT"], writes=["fi_ot"])
                    B.p.op("sync", lambda e, b=b, c=c, ts=ts: e.dma_start(
                        out=io["y"][b, c * 512 + ts * 128:
                                    c * 512 + ts * 128 + 128, :],
                        in_=ot[:]), reads=["fi_ot"], dma=True, stream="o")
            ln.emit(z, "fi_z", P0, P1, P2, P3, writer)


def build_ingest_program(nb=2):
    nc = bass.Bass("TRN2", target_bir_lowering=False)
    with ExitStack() as ctx:
        B = BD(nc, ctx)
        io = {"x": B.dram("x", (nb, T, 1024), F32, kind="ExternalInput"),
              "xT": B.dram("xT", (1024, nb * T), F32, kind="ExternalOutput")}
        emit_ingest(B, io, nb=nb)
        B.p.barrier()
        B.p.emit(ctx)
    return nc


def build_final_program(nb=2):
    nc = bass.Bass("TRN2", target_bir_lowering=False)
    with ExitStack() as ctx:
        B = BD(nc, ctx)
        io = {"aT": B.dram("aT", (1024, nb * T), F32, kind="ExternalInput"),
              "bT": B.dram("bT", (1024, nb * T), F32, kind="ExternalInput"),
              "y": B.dram("y", (nb, T, 1024), F32, kind="ExternalOutput")}
        emit_final(B, io, nb=nb)
        B.p.barrier()
        B.p.emit(ctx)
    return nc


def build_full_program(nb=2, act_kind="Internal"):
    nc = bass.Bass("TRN2", target_bir_lowering=False)
    with ExitStack() as octx:
        B = BD(nc, octx)
        d = {}
        d["x"] = B.dram("x", (nb, T, 1024), F32, kind="ExternalInput")
        d["y"] = B.dram("y", (nb, T, 1024), F32, kind="ExternalOutput")
        for name, shape in (
                ("L0Wp", (1024, 40)), ("L0bp", (40,)), ("L0Wv", (1024, 1024)),
                ("L0Wo", (1024, 1024)),
                ("F0w1A", (512, 1024)), ("F0w1B", (512, 4096)),
                ("F0w2A", (512, 4096)), ("F0w2B", (512, 1024)),
                ("moe_w", (T, 2)), ("sel8", (8, 512)),
                ("L1Wq", (1024, 1024)), ("L1Wk", (1024, 1024)),
                ("L1Wv", (1024, 1024)), ("L1Wo", (1024, 1024)),
                ("rf", (64, 128)),
                ("TAq", (T, 64)), ("TBq", (T, 64)),
                ("TAk", (T, 64)), ("TBk", (T, 64))):
            d[name] = B.dram(name, shape, F32, kind="ExternalInput")
        d["moe_i"] = B.dram("moe_i", (T, 2), I32, kind="ExternalInput")
        for l in range(2):
            for name, shape in (
                    ("wqkv", (1024, 1536)), ("qs", (64,)), ("ks", (64,)),
                    ("wo", (512, 1024)), ("lng", (1024,)), ("lnb", (1024,)),
                    ("w1", (1024, 5460)), ("b1", (5460,)),
                    ("w2", (2730, 1024)), ("b2", (1024,))):
                d[f"w4{l}_{name}"] = B.dram(f"w4{l}_{name}", shape, F32,
                                            kind="ExternalInput")
        act = {n: B.dram(n, (1024, nb * T), F32, kind=act_kind)
               for n in ("xT", "yT", "y2T", "y3T", "z0T", "z1T")}

        def stage(fn):
            with ExitStack() as sctx:
                B.ctx = sctx
                fn()
                B.p.barrier()
            B.ctx = octx

        stage(lambda: emit_ingest(B, {"x": d["x"], "xT": act["xT"]}, nb=nb))
        stage(lambda: emit_stage1(B, {
            "xT": act["xT"], "yT": act["yT"], "Wp": d["L0Wp"],
            "bp": d["L0bp"], "Wv": d["L0Wv"], "Wo": d["L0Wo"]}, nb=nb))
        stage(lambda: emit_stage2(B, {
            "yT": act["yT"], "y2T": act["y2T"], "w1A": d["F0w1A"],
            "w1B": d["F0w1B"], "w2A": d["F0w2A"], "w2B": d["F0w2B"],
            "moe_w": d["moe_w"], "moe_i": d["moe_i"], "sel8": d["sel8"]},
            nb=nb))
        stage(lambda: emit_stage3(B, {
            "y2T": act["y2T"], "y3T": act["y3T"], "Wq": d["L1Wq"],
            "Wk": d["L1Wk"], "Wv": d["L1Wv"], "Wo": d["L1Wo"],
            "rf": d["rf"]}, nb=nb))
        for l in range(2):
            src = act["y3T"] if l == 0 else act["z0T"]
            dst = act["z0T"] if l == 0 else act["z1T"]
            io4 = {"xT": src, "oT": dst,
                   "TAq": d["TAq"], "TBq": d["TBq"],
                   "TAk": d["TAk"], "TBk": d["TBk"]}
            for name in ("wqkv", "qs", "ks", "wo", "lng", "lnb",
                         "w1", "b1", "w2", "b2"):
                io4[name] = d[f"w4{l}_{name}"]
            stage(lambda l=l, io4=io4: emit_stage4_layer(B, io4, l, nb=nb))
        stage(lambda: emit_final(B, {
            "aT": act["y3T"], "bT": act["z1T"], "y": d["y"]}, nb=nb))

        B.p.emit(octx)
    return nc


_PROG_CACHE = {}


def _get_prog(name, builder):
    if name not in _PROG_CACHE:
        _PROG_CACHE[name] = builder()
    return _PROG_CACHE[name]


def _make_in_maps(inp):
    cores = list(range(NCORES))
    tabs = make_xpos_tables()
    sel8 = np.zeros((8, 512), np.float32)
    for ra in range(4):
        for j in range(128):
            sel8[2 * ra + (j >= 64), ra * 128 + j] = 1.0

    x = inp["L0_attn_x"]
    zz = np.zeros((T, 2), np.float32)
    zi = np.zeros((T, 2), np.int32)

    base = {
        "L0Wp": inp["L0_attn_Wp"], "L0bp": inp["L0_attn_bp"],
        "L0Wv": inp["L0_attn_Wv"], "L0Wo": inp["L0_attn_Wo"],
        "F0w1A": inp["L0_ffn_w1_A"], "F0w1B": inp["L0_ffn_w1_B"],
        "F0w2A": inp["L0_ffn_w2_A"], "F0w2B": inp["L0_ffn_w2_B"],
        "sel8": sel8,
        "L1Wq": inp["L1_attn_Wq"], "L1Wk": inp["L1_attn_Wk"],
        "L1Wv": inp["L1_attn_Wv"], "L1Wo": inp["L1_attn_Wo"],
        "rf": inp["L1_attn_rf"],
        "TAq": tabs[0], "TBq": tabs[1], "TAk": tabs[2], "TBk": tabs[3],
    }
    for l in range(2):
        for name, key in (("wqkv", "L1_ffn_wqkv"), ("qs", "L1_ffn_q_scale"),
                          ("ks", "L1_ffn_k_scale"), ("wo", "L1_ffn_wo"),
                          ("lng", "L1_ffn_ln_g"), ("lnb", "L1_ffn_ln_b"),
                          ("w1", "L1_ffn_w1"), ("b1", "L1_ffn_b1"),
                          ("w2", "L1_ffn_w2"), ("b2", "L1_ffn_b2")):
            base[f"w4{l}_{name}"] = np.ascontiguousarray(inp[key][l])

    in_maps = []
    for c in cores:
        m = dict(base)
        m["x"] = np.ascontiguousarray(x[[c, c + 8]])
        m["moe_w"] = (np.ascontiguousarray(
            inp["L0_ffn_expert_weights"][c * T:(c + 1) * T])
            if c < 4 else zz)
        m["moe_i"] = (np.ascontiguousarray(
            inp["L0_ffn_expert_indices"][c * T:(c + 1) * T].astype(np.int32))
            if c < 4 else zi)
        in_maps.append(m)
    return in_maps


def kernel(**inputs):
    inp = {k: np.ascontiguousarray(np.asarray(v)) for k, v in inputs.items()}
    nb = 2
    cores = list(range(NCORES))
    tabs = make_xpos_tables()
    sel8 = np.zeros((8, 512), np.float32)
    for ra in range(4):
        for j in range(128):
            sel8[2 * ra + (j >= 64), ra * 128 + j] = 1.0

    x = inp["L0_attn_x"]
    per_core_x = [np.ascontiguousarray(x[[c, c + 8]]) for c in cores]

    nc = _get_prog("ingest", lambda: build_ingest_program(nb=nb))
    res = run_bass_kernel_spmd(nc, [{"x": per_core_x[c]} for c in cores],
                               core_ids=cores)
    xT = [r["xT"] for r in res.results]

    nc = _get_prog("s1", lambda: build_stage1_program(nb=nb))
    res = run_bass_kernel_spmd(nc, [
        {"xT": xT[c], "Wp": inp["L0_attn_Wp"], "bp": inp["L0_attn_bp"],
         "Wv": inp["L0_attn_Wv"], "Wo": inp["L0_attn_Wo"]} for c in cores],
        core_ids=cores)
    yT = [r["yT"] for r in res.results]

    zz = np.zeros((T, 2), np.float32)
    zi = np.zeros((T, 2), np.int32)
    nc = _get_prog("s2", lambda: build_stage2_program(nb=nb))
    res = run_bass_kernel_spmd(nc, [
        {"yT": yT[c], "w1A": inp["L0_ffn_w1_A"], "w1B": inp["L0_ffn_w1_B"],
         "w2A": inp["L0_ffn_w2_A"], "w2B": inp["L0_ffn_w2_B"],
         "moe_w": (np.ascontiguousarray(
             inp["L0_ffn_expert_weights"][c * T:(c + 1) * T])
             if c < 4 else zz),
         "moe_i": (np.ascontiguousarray(
             inp["L0_ffn_expert_indices"][c * T:(c + 1) * T].astype(np.int32))
             if c < 4 else zi),
         "sel8": sel8} for c in cores], core_ids=cores)
    y2T = [r["y2T"] for r in res.results]

    nc = _get_prog("s3", lambda: build_stage3_program(nb=nb))
    res = run_bass_kernel_spmd(nc, [
        {"y2T": y2T[c], "Wq": inp["L1_attn_Wq"], "Wk": inp["L1_attn_Wk"],
         "Wv": inp["L1_attn_Wv"], "Wo": inp["L1_attn_Wo"],
         "rf": inp["L1_attn_rf"]} for c in cores], core_ids=cores)
    y3T = [r["y3T"] for r in res.results]

    cur = y3T
    for l in range(2):
        nc = _get_prog(f"s4_{l}", lambda l=l: build_stage4_program(l=l, nb=nb))
        res = run_bass_kernel_spmd(nc, [
            {"xT": cur[c],
             "wqkv": np.ascontiguousarray(inp["L1_ffn_wqkv"][l]),
             "qs": np.ascontiguousarray(inp["L1_ffn_q_scale"][l]),
             "ks": np.ascontiguousarray(inp["L1_ffn_k_scale"][l]),
             "wo": np.ascontiguousarray(inp["L1_ffn_wo"][l]),
             "lng": np.ascontiguousarray(inp["L1_ffn_ln_g"][l]),
             "lnb": np.ascontiguousarray(inp["L1_ffn_ln_b"][l]),
             "w1": np.ascontiguousarray(inp["L1_ffn_w1"][l]),
             "b1": np.ascontiguousarray(inp["L1_ffn_b1"][l]),
             "w2": np.ascontiguousarray(inp["L1_ffn_w2"][l]),
             "b2": np.ascontiguousarray(inp["L1_ffn_b2"][l]),
             "TAq": tabs[0], "TBq": tabs[1], "TAk": tabs[2], "TBk": tabs[3]}
            for c in cores], core_ids=cores)
        cur = [r["oT"] for r in res.results]

    nc = _get_prog("final", lambda: build_final_program(nb=nb))
    res = run_bass_kernel_spmd(nc, [
        {"aT": y3T[c], "bT": cur[c]} for c in cores], core_ids=cores)
    ys = [r["y"] for r in res.results]

    out = np.zeros((16, T, 1024), np.float32)
    for c in cores:
        out[c] = ys[c][0]
        out[c + 8] = ys[c][1]
    return out



# version 06
# speedup vs baseline: 1.0758x; 1.0044x over previous
import numpy as np
from ml_dtypes import bfloat16
from contextlib import ExitStack

import concourse.bass as bass
import concourse.mybir as mybir
from concourse.bass_utils import run_bass_kernel_spmd
from concourse.alu_op_type import AluOpType

F32 = mybir.dt.float32
BF16 = mybir.dt.bfloat16
I32 = mybir.dt.int32
AF = mybir.ActivationFunctionType
AX = mybir.AxisListType
OP = AluOpType

NCORES = 8
T = 2048
D = 1024
MAGIC = 12582912.0

ENGINES = ("sync", "scalar", "vector", "tensor", "gpsimd")


class Prog:
    def __init__(self, nc):
        self.nc = nc
        self.items = []
        self.counts = {}
        self.last_write = {}
        self.readers = {}
        self.known = {}
        self.debug_log = []
        self.psum_regions = set()
        self.stage_marks = []
        for e in ENGINES:
            self.counts["c_" + e] = 0

    def op(self, engine, fn, reads=(), writes=(), dma=False, stream=""):
        sem = (("d_" + engine + stream) if dma else ("c_" + engine))
        inc = 16 if dma else 1
        if sem not in self.counts:
            self.counts[sem] = 0
        if dma:
            reads = list(reads) + ["__q" + sem]
            writes = list(writes) + ["__q" + sem]
        if self.psum_regions:
            pr = [r for r in reads if r in self.psum_regions]
            if pr:
                reads = [r for r in reads if r not in self.psum_regions]
                writes = list(writes) + pr
        waits = {}

        def need(dep):
            dsem, dval, dengine = dep
            if dsem.startswith("d_"):
                dval = self.counts[dsem]
            if dengine == engine == "tensor" and not dma:
                return
            if self.known.get((engine, dsem), 0) >= dval:
                return
            if waits.get(dsem, 0) < dval:
                waits[dsem] = dval

        for r in reads:
            lw = self.last_write.get(r)
            if lw:
                need(lw)
        for w in writes:
            lw = self.last_write.get(w)
            if lw:
                need(lw)
            for rd in self.readers.get(w, ()):
                need(rd)
        for dsem, dval in waits.items():
            k = (engine, dsem)
            self.known[k] = max(self.known.get(k, 0), dval)
        self.counts[sem] += inc
        me = (sem, self.counts[sem], engine)
        for w in writes:
            self.last_write[w] = me
            self.readers[w] = []
        for r in reads:
            self.readers.setdefault(r, []).append(me)
        self.items.append((engine, fn, sorted(waits.items()), (sem, inc)))
        self.debug_log.append((engine, sem, self.counts[sem], tuple(reads),
                               tuple(writes), sorted(waits.items())))

    def barrier(self):
        snapshot = dict(self.counts)
        for e in ENGINES:
            for sem, val in snapshot.items():
                if val > 0 and self.known.get((e, sem), 0) < val:
                    self.items.append((e, None, [(sem, val)], None))
                    self.known[(e, sem)] = val
        if self.stage_marks and self.stage_marks[-1] == len(self.items):
            return
        self.stage_marks.append(len(self.items))

    def emit(self, ctx):
        sems = {}
        for name, val in self.counts.items():
            if val > 0:
                sems[name] = ctx.enter_context(self.nc.semaphore(name))
        marks = list(self.stage_marks)
        if not marks or marks[-1] != len(self.items):
            marks.append(len(self.items))

        def section(name, seg):
            def _(eng):
                for engine, fn, waits, incspec in seg:
                    if engine != name:
                        continue
                    for semname, val in waits:
                        eng.wait_ge(sems[semname], val)
                    if fn is not None:
                        inst = fn(eng)
                        semname, inc = incspec
                        inst.then_inc(sems[semname], inc)
            return _

        start = 0
        for end in marks:
            seg = self.items[start:end]
            start = end
            if not seg:
                continue
            with self.nc.Block() as block:
                for name, reg in (("sync", block.sync), ("scalar", block.scalar),
                                  ("vector", block.vector),
                                  ("tensor", block.tensor),
                                  ("gpsimd", block.gpsimd)):
                    if any(it[0] == name for it in seg):
                        reg(section(name, seg))


def rep_ap(tensor, dims, elem_off, psl=None):
    base = tensor[:] if psl is None else tensor[psl[0]:psl[0] + psl[1], :]
    return bass.AP(base.tensor, base.offset + elem_off,
                   [list(base.ap[0])] + [list(d) for d in dims])


class BD:

    def __init__(self, nc, ctx):
        self.nc = nc
        self.ctx = ctx
        self.p = Prog(nc)
        self.consts = {}
        self._cpool = self.sb("cpool", (128, 32), F32)
        self._cpool_n = 0

    def sb(self, name, shape, dt=F32):
        return self.ctx.enter_context(self.nc.sbuf_tensor(name, list(shape), dt))

    def ps(self, name, shape=(128, 512), dt=F32):
        self.p.psum_regions.add(name)
        return self.ctx.enter_context(self.nc.psum_tensor(name, list(shape), dt))

    def dram(self, name, shape, dt=F32, kind="Internal"):
        return self.nc.dram_tensor(name, list(shape), dt, kind=kind).ap()

    def const(self, *vals):
        for v0 in vals:
            v0 = float(v0)
            v32 = float(np.float32(v0))
            if v0 in self.consts:
                continue
            i = self._cpool_n
            assert i < 32, "const pool full"
            self._cpool_n += 1
            cp = self._cpool
            self.p.op("gpsimd", lambda e, i=i, v=v32: e.memset(cp[:, i:i + 1], v),
                      writes=["cpool"])
            for v in {v0, v32}:
                self.nc.const_aps.register(F32, v, cp[:, i:i + 1])
                self.consts[v] = True


def emit_eye(B, dt, name):
    eye = B.sb(name, (128, 128), dt)
    p = B.p
    p.op("gpsimd", lambda e: e.memset(eye[:], 1.0), writes=[name])
    p.op("gpsimd", lambda e: e.affine_select(out=eye[:], in_=eye[:], compare_op=OP.is_ge,
                                             fill=0.0, base=0, pattern=[[-1, 128]],
                                             channel_multiplier=1),
         reads=[name], writes=[name])
    p.op("gpsimd", lambda e: e.affine_select(out=eye[:], in_=eye[:], compare_op=OP.is_ge,
                                             fill=0.0, base=0, pattern=[[1, 128]],
                                             channel_multiplier=-1),
         reads=[name], writes=[name])
    return eye


def load_w_tiles(B, name, dram_ap, rows, cols, dt=BF16, stream="w"):
    a = rows // 128
    w = B.sb(name, (128, a * cols), dt)
    B.p.op("gpsimd",
           lambda e: e.dma_start(
               out=w[:].rearrange("p (a f) -> p a f", a=a),
               in_=dram_ap.rearrange("(a p) f -> p a f", p=128)),
           writes=[name], dma=True, stream=stream)
    return w


class LNPack:

    def __init__(self, B, prefix, nslices=8):
        self.B = B
        self.n = nslices
        self.prefix = prefix
        self.z16 = B.sb(prefix + "_lnz16", (128, 2 * nslices * 512), BF16)
        self.stats = B.sb(prefix + "_lnst", (1, 2048), F32)
        self.ones_r1f = B.sb(prefix + "_o1f", (1, 128), F32)
        self.ones_c1b = B.sb(prefix + "_oc1", (128, 1), BF16)
        B.p.op("gpsimd", lambda e: e.memset(self.ones_r1f[:], 1.0),
               writes=[prefix + "_o1f"])
        B.p.op("gpsimd", lambda e: e.memset(self.ones_c1b[:], 1.0),
               writes=[prefix + "_oc1"])
        B.const(1e-5)

    def emit(self, z, zkey, psA, psB, psS1, psS2, out_writer):
        B, p, n, pre = self.B, self.B.p, self.n, self.prefix
        z16, stats = self.z16, self.stats
        kz16, ksq = pre + "_lnz16lo", pre + "_lnz16hi"
        kA, kB = psA.name, psB.name
        kS1, kS2 = psS1.name, psS2.name
        p.op("vector", lambda e: e.tensor_copy(z16[:, 0:n * 512], z[:, 0:n * 512]),
             reads=[zkey], writes=[kz16])
        p.op("scalar", lambda e: e.activation(z16[:, n * 512:2 * n * 512],
                                              z[:, 0:n * 512], AF.Square),
             reads=[zkey], writes=[ksq])
        for a in range(n):
            p.op("tensor", lambda e, a=a: e.matmul(
                psS1[0:1, :], self.ones_c1b[:, 0:1], z16[:, a * 512:(a + 1) * 512],
                start=(a == 0), stop=(a == n - 1)),
                reads=[kz16, pre + "_oc1"], writes=[kS1])
        for a in range(n):
            p.op("tensor", lambda e, a=a: e.matmul(
                psS2[0:1, :], self.ones_c1b[:, 0:1],
                z16[:, (n + a) * 512:(n + a + 1) * 512],
                start=(a == 0), stop=(a == n - 1)),
                reads=[ksq, pre + "_oc1"], writes=[kS2])
        inv_n = 1.0 / (n * 128.0)
        B.const(inv_n)
        mu, m2, var, rstd = (stats[0:1, i * 512:(i + 1) * 512] for i in range(4))
        kst = pre + "_lnst"
        p.op("vector", lambda e: e.tensor_scalar(mu, psS1[0:1, :], inv_n, None, OP.mult),
             reads=[kS1, "cpool"], writes=[kst + "mu"])
        p.op("scalar", lambda e: e.activation(m2, psS2[0:1, :], AF.Identity, scale=inv_n),
             reads=[kS2], writes=[kst + "m2"])
        p.op("vector", lambda e: e.tensor_tensor(var, mu, mu, OP.mult),
             reads=[kst + "mu"], writes=[kst + "var"])
        p.op("vector", lambda e: e.tensor_tensor(var, m2, var, OP.subtract),
             reads=[kst + "m2", kst + "var"], writes=[kst + "var"])
        p.op("scalar", lambda e: e.activation(var, var, AF.Ln, bias=1e-5),
             reads=[kst + "var", "cpool"], writes=[kst + "var"])
        p.op("scalar", lambda e: e.activation(rstd, var, AF.Exp, scale=-0.5),
             reads=[kst + "var"], writes=[kst + "rstd"])
        p.op("tensor", lambda e: e.matmul(psA[:], self.ones_r1f[0:1, :], mu,
                                          start=True, stop=True),
             reads=[kst + "mu", pre + "_o1f"], writes=[kA])
        p.op("tensor", lambda e: e.matmul(psB[:], self.ones_r1f[0:1, :], rstd,
                                          start=True, stop=True),
             reads=[kst + "rstd", pre + "_o1f"], writes=[kB])
        for a in range(n):
            p.op("vector", lambda e, a=a: e.tensor_tensor(
                z[:, a * 512:(a + 1) * 512], z[:, a * 512:(a + 1) * 512], psA[:],
                OP.subtract), reads=[zkey, kA], writes=[zkey])
        for a in range(n):
            p.op("vector", lambda e, a=a: e.tensor_tensor(
                z[:, a * 512:(a + 1) * 512], z[:, a * 512:(a + 1) * 512], psB[:],
                OP.mult), reads=[zkey, kB], writes=[zkey])
        out_writer()


def emit_stage1(B, io, nb=2, phases=('feat','attn','wo','ln')):
    nc, p = B.nc, B.p
    H, DEP = 8, 5

    wvo = B.sb("s1_wvo", (128, 8 * 1024), BF16)
    wp = load_w_tiles(B, "s1_wp", io["Wp"], 1024, 40)
    bprow = B.sb("s1_bp", (1, 40), F32)
    p.op("sync", lambda e: e.dma_start(out=bprow[:],
                                       in_=io["bp"].rearrange("(a f) -> a f", a=1)),
         writes=["s1_bp"], dma=True)
    ones_sq = B.sb("s1_onsq", (128, 128), BF16)
    p.op("gpsimd", lambda e: e.memset(ones_sq[:], 1.0), writes=["s1_onsq"])
    eye = emit_eye(B, BF16, "eye_bf")
    ln = LNPack(B, "s1")
    B.const(1.0, -1.0, 2.0)

    P0, P1, P2, P3, P4, P5 = (B.ps(f"s1_P{i}") for i in range(6))
    PTT = B.ps("s1_PT", (128, 1024), BF16)

    bpbc = B.sb("s1_bpbc", (128, 40), F32)
    p.op("tensor", lambda e: e.matmul(P5[:, 0:40], ln.ones_r1f[0:1, :], bprow[:],
                                      start=True, stop=True),
         reads=["s1_bp", "s1_o1f"], writes=["s1_P5"])
    p.op("vector", lambda e: e.tensor_copy(bpbc[:], P5[:, 0:40]),
         reads=["s1_P5"], writes=["s1_bpbc"])

    x16 = B.sb("s1_x16", (128, 8 * 512), BF16)
    V16 = B.sb("s1_V16", (128, 16 * 1024), BF16)
    FqT = B.sb("s1_FqT", (128, 4 * T), BF16)
    FkT = B.sb("s1_FkT", (128, 4 * T), BF16)
    At = B.sb("s1_At", (128, 16 * 512), BF16)
    ctxT = B.sb("s1_ctxT", (128, 8 * T), BF16)
    ptile = B.sb("s1_p", (128, 48), F32)
    uq = B.sb("s1_uq", (128, 80), F32)
    vk = B.sb("s1_vk", (128, 80), F32)
    Ftf = B.sb("s1_Ftf", (128, 512), F32)
    F16 = B.sb("s1_F16", (128, 512), BF16)
    zrec = B.sb("s1_zrec", (128, 512), F32)
    zln = B.sb("s1_zln", (128, 512), F32)
    xf32 = B.sb("s1_xf32", (128, 8 * 512), F32)
    zsb = B.sb("s1_z", (128, 8 * 512), F32)

    xv = io["xT"].rearrange("(a p) t -> p a t", p=128)
    yv = io["yT"].rearrange("(a p) t -> p a t", p=128)

    for b in range(nb):
        t0 = b * T
        p.op("gpsimd", lambda e: e.dma_start(
            out=wvo[:].rearrange("p (a f) -> p a f", a=8),
            in_=io["Wv"].rearrange("(a p) f -> p a f", p=128)),
            writes=["s1_wvo"], dma=True, stream="w")
        for c in range(4) if 'feat' in phases else []:
            p.op("gpsimd", lambda e, t0=t0, c=c: e.dma_start(
                out=x16[:].rearrange("p (a t) -> p a t", a=8),
                in_=xv[:, :, t0 + c * 512:t0 + c * 512 + 512]),
                writes=["s1_x16"], dma=True, stream="x")
            for ttc in range(4):
                tt = c * 4 + ttc
                for a in range(8):
                    p.op("tensor", lambda e, a=a, ttc=ttc: e.matmul(
                        P0[:, 0:40],
                        x16[:, a * 512 + ttc * 128:a * 512 + ttc * 128 + 128],
                        wp[:, a * 40:(a + 1) * 40], start=(a == 0), stop=(a == 7)),
                        reads=["s1_x16", "s1_wp"], writes=["s1_P0"])
                p.op("vector", lambda e: e.tensor_tensor(ptile[:, 0:40], P0[:, 0:40],
                                                         bpbc[:], OP.add),
                     reads=["s1_P0", "s1_bpbc"], writes=["s1_p"])
                p.op("scalar", lambda e: e.activation(ptile[:, 0:40], ptile[:, 0:40],
                                                      AF.Sigmoid),
                     reads=["s1_p"], writes=["s1_p"])
                uq0 = rep_ap(uq, [[10, 8], [2, 5], [1, 1]], 0)
                uq1 = rep_ap(uq, [[10, 8], [2, 5], [1, 1]], 1)
                vk0 = rep_ap(vk, [[10, 8], [2, 5], [1, 1]], 0)
                vk1 = rep_ap(vk, [[10, 8], [2, 5], [1, 1]], 1)
                pv = rep_ap(ptile, [[5, 8], [1, 5], [0, 1]], 0)
                p.op("vector", lambda e: e.tensor_scalar(uq0, pv, -1.0, 1.0,
                                                         OP.mult, OP.add),
                     reads=["s1_p", "cpool"], writes=["s1_uq"])
                p.op("vector", lambda e: e.tensor_scalar(uq1, pv, 2.0, -1.0,
                                                         OP.mult, OP.add),
                     reads=["s1_p", "cpool"], writes=["s1_uq"])
                p.op("gpsimd", lambda e: e.memset(vk0, 1.0), writes=["s1_vk"])
                p.op("vector", lambda e: e.tensor_copy(vk1, pv),
                     reads=["s1_p"], writes=["s1_vk"])
                for (usrc, ukey, Fdst, fkey) in ((uq, "s1_uq", FqT, "s1_FqT"),
                                                 (vk, "s1_vk", FkT, "s1_FkT")):
                    p.op("gpsimd", lambda e: e.memset(
                        rep_ap(Ftf, [[64, 8], [1, 2]], 62), 0.0),
                        writes=["s1_Ftf"])
                    p.op("vector", lambda e, u=usrc: e.tensor_copy(
                        rep_ap(Ftf, [[64, 8], [1, 2]], 0),
                        rep_ap(u, [[10, 8], [1, 2]], 0)),
                        reads=[ukey], writes=["s1_Ftf"])
                    sp, sc = 0, 2
                    for j in range(1, DEP):
                        w = 2 ** j
                        p.op("vector", lambda e, u=usrc, sp=sp, sc=sc, j=j, w=w:
                             e.tensor_tensor(
                                 rep_ap(Ftf, [[64, 8], [1, 2 * w]], sc),
                                 rep_ap(Ftf, [[64, 8], [1, w], [0, 2]], sp),
                                 rep_ap(u, [[10, 8], [0, w], [1, 2]], 2 * j),
                                 OP.mult),
                             reads=["s1_Ftf", ukey], writes=["s1_Ftf"])
                        sp, sc = sc, sc + 2 * w
                    p.op("vector", lambda e: e.tensor_copy(F16[:], Ftf[:]),
                         reads=["s1_Ftf"], writes=["s1_F16"])
                    for g in range(4):
                        p.op("tensor", lambda e, g=g: e.transpose(
                            PTT[:, g * 128:(g + 1) * 128],
                            F16[:, g * 128:(g + 1) * 128], eye[:]),
                            reads=["s1_F16", "eye_bf"], writes=["s1_PT"])
                    for g in range(4):
                        p.op("vector", lambda e, g=g, Fd=Fdst, tt=tt: e.tensor_copy(
                            Fd[:, g * T + tt * 128:g * T + tt * 128 + 128],
                            PTT[:, g * 128:(g + 1) * 128]),
                            reads=["s1_PT"], writes=[fkey])
                for vc in range(2):
                    ps, pk = ((P1, "s1_P1") if vc == 0 else (P2, "s1_P2"))
                    for a in range(8):
                        p.op("tensor", lambda e, a=a, ttc=ttc, vc=vc, ps=ps: e.matmul(
                            ps[:],
                            x16[:, a * 512 + ttc * 128:a * 512 + ttc * 128 + 128],
                            wvo[:, a * 1024 + vc * 512:a * 1024 + vc * 512 + 512],
                            start=(a == 0), stop=(a == 7)),
                            reads=["s1_x16", "s1_wvo"], writes=[pk])
                    p.op("vector", lambda e, tt=tt, vc=vc, ps=ps, pk=pk: e.tensor_copy(
                        V16[:, tt * 1024 + vc * 512:tt * 1024 + vc * 512 + 512],
                        ps[:]), reads=[pk], writes=["s1_V16"])

        p.op("gpsimd", lambda e: e.dma_start(
            out=wvo[:].rearrange("p (a f) -> p a f", a=8),
            in_=io["Wo"].rearrange("(a p) f -> p a f", p=128)),
            writes=["s1_wvo"], dma=True, stream="w")

        for h in range(8) if 'attn' in phases else []:
            g, hp = h // 2, (h % 2) * 64
            for qc in range(4):
                nkt = 4 * qc + 4
                for kt in range(nkt):
                    ps, pk = ((P0, "s1_P0") if kt % 2 == 0 else (P1, "s1_P1"))
                    p.op("tensor", lambda e, kt=kt, qc=qc, pss=ps, g=g, hp=hp: e.matmul(
                        pss[:],
                        FkT[hp:hp + 64, g * T + kt * 128:g * T + kt * 128 + 128],
                        FqT[hp:hp + 64, g * T + qc * 512:g * T + qc * 512 + 512],
                        start=True, stop=True),
                        reads=["s1_FkT", "s1_FqT"], writes=[pk])
                    p.op("scalar", lambda e, kt=kt, pss=ps: e.activation(
                        At[:, kt * 512:kt * 512 + 512], pss[:], AF.Exp, scale=0.2),
                        reads=[pk], writes=[f"s1_At#{kt}"])
                    if kt >= 4 * qc:
                        p.op("gpsimd", lambda e, kt=kt, qc=qc: e.affine_select(
                            out=At[:, kt * 512:kt * 512 + 512],
                            in_=At[:, kt * 512:kt * 512 + 512],
                            compare_op=OP.is_ge, fill=0.0,
                            base=qc * 512 - kt * 128, pattern=[[1, 512]],
                            channel_multiplier=-1),
                            reads=[f"s1_At#{kt}"], writes=[f"s1_At#{kt}"])
                for kt in range(nkt):
                    p.op("tensor", lambda e, kt=kt, h=h, nkt=nkt: e.matmul(
                        P2[:], V16[:, kt * 1024 + h * 128:kt * 1024 + h * 128 + 128],
                        At[:, kt * 512:kt * 512 + 512],
                        start=(kt == 0), stop=(kt == nkt - 1)),
                        reads=["s1_V16", f"s1_At#{kt}"], writes=["s1_P2"])
                for kt in range(nkt):
                    p.op("tensor", lambda e, kt=kt, nkt=nkt: e.matmul(
                        P3[:], ones_sq[:], At[:, kt * 512:kt * 512 + 512],
                        start=(kt == 0), stop=(kt == nkt - 1)),
                        reads=["s1_onsq", f"s1_At#{kt}"], writes=["s1_P3"])
                p.op("scalar", lambda e: e.activation(zln[:], P3[:], AF.Ln),
                     reads=["s1_P3"], writes=["s1_zln"])
                p.op("scalar", lambda e: e.activation(zrec[:], zln[:], AF.Exp,
                                                      scale=-1.0),
                     reads=["s1_zln"], writes=["s1_zrec"])
                p.op("vector", lambda e, h=h, qc=qc: e.tensor_tensor(
                    ctxT[:, h * T + qc * 512:h * T + qc * 512 + 512], P2[:],
                    zrec[:], OP.mult),
                    reads=["s1_P2", "s1_zrec"], writes=["s1_ctxT"])

        for c in range(4) if 'wo' in phases else []:
            p.op("sync", lambda e, c=c, t0=t0: e.dma_start(
                out=xf32[:].rearrange("p (a t) -> p a t", a=8),
                in_=xv[:, :, t0 + c * 512:t0 + c * 512 + 512]),
                writes=["s1_xf32"], dma=True, stream="r")
            for o in range(8):
                ps, pk = ((P0, "s1_P0") if o % 2 == 0 else (P1, "s1_P1"))
                for a in range(8):
                    p.op("tensor", lambda e, a=a, o=o, c=c, ps=ps: e.matmul(
                        ps[:], wvo[:, a * 1024 + o * 128:a * 1024 + o * 128 + 128],
                        ctxT[:, a * T + c * 512:a * T + c * 512 + 512],
                        start=(a == 0), stop=(a == 7)),
                        reads=["s1_wvo", "s1_ctxT"], writes=[pk])
                p.op("vector", lambda e, o=o, ps=ps, pk=pk: e.tensor_tensor(
                    zsb[:, o * 512:o * 512 + 512], ps[:],
                    xf32[:, o * 512:o * 512 + 512], OP.add),
                    reads=[pk, "s1_xf32"], writes=["s1_z"])

            def writer(c=c, t0=t0):
                B.p.op("sync", lambda e: e.dma_start(
                    out=yv[:, :, t0 + c * 512:t0 + c * 512 + 512],
                    in_=zsb[:].rearrange("p (a t) -> p a t", a=8)),
                    reads=["s1_z"], dma=True, stream="o")
            if 'ln' in phases:
                ln.emit(zsb, "s1_z", P4, P5, P0, P1, writer)
            else:
                writer()


def build_stage1_program(nb=2, phases=('feat','attn','wo','ln')):
    nc = bass.Bass("TRN2", target_bir_lowering=False)
    with ExitStack() as ctx:
        B = BD(nc, ctx)
        io = {
            "xT": B.dram("xT", (1024, nb * T), F32, kind="ExternalInput"),
            "yT": B.dram("yT", (1024, nb * T), F32, kind="ExternalOutput"),
            "Wp": B.dram("Wp", (1024, 40), F32, kind="ExternalInput"),
            "bp": B.dram("bp", (40,), F32, kind="ExternalInput"),
            "Wv": B.dram("Wv", (1024, 1024), F32, kind="ExternalInput"),
            "Wo": B.dram("Wo", (1024, 1024), F32, kind="ExternalInput"),
        }
        emit_stage1(B, io, nb=nb, phases=phases)
        B.p.barrier()
        B.p.emit(ctx)
    return nc




def emit_stage2(B, io, nb=2, phases=('wload', 'gates', 'moe', 'slot1')):
    nc, p = B.nc, B.p
    E, R, FF = 8, 64, 4096
    ln = LNPack(B, "s2")
    eye = emit_eye(B, BF16, "s2_eye")
    P0, P1, P2, P3, P4, P5 = (B.ps(f"s2_P{i}") for i in range(6))
    PTT = B.ps("s2_PT", (128, 1024), BF16)

    w1b = load_w_tiles(B, "s2_w1b", io["w1B"], 512, 4096)
    w2b = load_w_tiles(B, "s2_w2b", io["w2B"], 512, 1024)
    sel8 = B.sb("s2_sel8", (8, 512), BF16)
    p.op("gpsimd", lambda e: e.dma_start(out=sel8[:], in_=io["sel8"]),
         writes=["s2_sel8"], dma=True, stream="w")
    w1at = B.sb("s2_w1at", (128, 8 * 512), BF16)
    w2at = B.sb("s2_w2at", (128, 32 * 512), BF16)
    wtmp = B.sb("s2_wtmp", (128, 4096), BF16)
    for (src, wdst, kdst, nda) in (((io["w1A"], w1at, "s2_w1at", 8),
                                    (io["w2A"], w2at, "s2_w2at", 32))
                                   if 'wload' in phases else ()):
        for ra in range(4):
            p.op("gpsimd", lambda e, src=src, ra=ra, nda=nda: e.dma_start(
                out=wtmp[:, 0:nda * 128],
                in_=src[ra * 128:(ra + 1) * 128, :]),
                writes=["s2_wtmp"], dma=True, stream="w")
            for da in range(nda):
                p.op("tensor", lambda e, da=da: e.transpose(
                    PTT[:, 0:128], wtmp[:, da * 128:da * 128 + 128], eye[:]),
                    reads=["s2_wtmp", "s2_eye"], writes=["s2_PT"])
                p.op("vector", lambda e, ra=ra, da=da, w=wdst: e.tensor_copy(
                    w[:, da * 512 + ra * 128:da * 512 + ra * 128 + 128],
                    PTT[:, 0:128]), reads=["s2_PT"], writes=[kdst])

    gw = B.sb("s2_gw", (128, 32), F32)
    gi = B.sb("s2_gi", (128, 32), I32)
    gif = B.sb("s2_gif", (128, 32), F32)
    gmat = B.sb("s2_g", (128, 128), F32)
    g16 = B.sb("s2_g16", (128, 128), BF16)
    gT = B.sb("s2_gT", (8, 2048), BF16)
    gtmp = B.sb("s2_gtmp", (128, 48), F32)
    p.op("sync", lambda e: e.dma_start(
        out=gw[:].rearrange("p (tt k) -> p tt k", k=2),
        in_=io["moe_w"].rearrange("(tt p) k -> p tt k", p=128)),
        writes=["s2_gw"], dma=True)
    p.op("sync", lambda e: e.dma_start(
        out=gi[:].rearrange("p (tt k) -> p tt k", k=2),
        in_=io["moe_i"].rearrange("(tt p) k -> p tt k", p=128)),
        writes=["s2_gi"], dma=True)
    p.op("vector", lambda e: e.tensor_copy(gif[:], gi[:]), reads=["s2_gi"],
         writes=["s2_gif"])
    B.const(*[float(e) for e in range(8)])
    for e_ in range(E) if 'gates' in phases else []:
        for k in range(2):
            p.op("vector", lambda ee, e_=e_, k=k: ee.tensor_scalar(
                rep_ap(gtmp, [[3, 16], [1, 1]], k),
                rep_ap(gif, [[2, 16], [1, 1]], k), float(e_), None, OP.is_equal),
                reads=["s2_gif", "cpool"], writes=["s2_gtmp"])
            p.op("vector", lambda ee, e_=e_, k=k: ee.tensor_tensor(
                rep_ap(gtmp, [[3, 16], [1, 1]], k),
                rep_ap(gtmp, [[3, 16], [1, 1]], k),
                rep_ap(gw, [[2, 16], [1, 1]], k), OP.mult),
                reads=["s2_gtmp", "s2_gw"], writes=["s2_gtmp"])
        p.op("vector", lambda ee, e_=e_: ee.tensor_tensor(
            rep_ap(gmat, [[8, 16], [1, 1]], e_),
            rep_ap(gtmp, [[3, 16], [1, 1]], 0),
            rep_ap(gtmp, [[3, 16], [1, 1]], 1), OP.add),
            reads=["s2_gtmp"], writes=["s2_g"])
    p.op("vector", lambda e: e.tensor_copy(g16[:], gmat[:]), reads=["s2_g"],
         writes=["s2_g16"])
    for tt in range(16) if 'gates' in phases else []:
        p.op("tensor", lambda e, tt=tt: e.transpose(
            PTT[0:8, 0:128], g16[:, tt * 8:tt * 8 + 8], eye[:]),
            reads=["s2_g16", "s2_eye"], writes=["s2_PT"])
        p.op("vector", lambda e, tt=tt: e.tensor_copy(
            gT[0:8, tt * 128:tt * 128 + 128], PTT[0:8, 0:128]),
            reads=["s2_PT"], writes=["s2_gT"])

    y16 = B.sb("s2_y16", (128, 8 * 512), BF16)
    yf32 = B.sb("s2_yf32", (128, 8 * 512), F32)
    hc = B.sb("s2_hc", (128, 32 * 512), BF16)
    t1s = B.sb("s2_t1s", (128, 512), BF16)
    t2s = B.sb("s2_t2s", (128, 4 * 512), BF16)
    gbcp = B.sb("s2_gbcp", (128, 512), BF16)

    P6 = B.ps("s2_P6")

    yv = io["yT"].rearrange("(a p) t -> p a t", p=128)
    ov = io["y2T"].rearrange("(a p) t -> p a t", p=128)

    for tc in (range(4) if any(p.startswith('moe') for p in phases) else []):
        p.op("gpsimd", lambda e, tc=tc: e.dma_start(
            out=y16[:].rearrange("p (a t) -> p a t", a=8),
            in_=yv[:, :, tc * 512:tc * 512 + 512]),
            writes=["s2_y16"], dma=True, stream="x")
        for ra in (range(4) if 'moe_t1' in phases or 'moe' in phases else []):
            for da in range(8):
                p.op("tensor", lambda e, da=da, ra=ra: e.matmul(
                    P0[:], w1at[:, da * 512 + ra * 128:da * 512 + ra * 128 + 128],
                    y16[:, da * 512:da * 512 + 512],
                    start=(da == 0), stop=(da == 7)),
                    reads=["s2_w1at", "s2_y16"], writes=["s2_P0"])
            p.op("vector", lambda e: e.tensor_copy(t1s[:], P0[:]),
                 reads=["s2_P0"], writes=["s2_t1s"])
            p.op("tensor", lambda e, ra=ra, tc=tc: e.matmul(
                P4[:], sel8[:, ra * 128:ra * 128 + 128],
                gT[0:8, tc * 512:tc * 512 + 512], start=True, stop=True),
                reads=["s2_sel8", "s2_gT"], writes=["s2_P4"])
            p.op("vector", lambda e: e.tensor_copy(gbcp[:], P4[:]),
                 reads=["s2_P4"], writes=["s2_gbcp"])
            for par in (range(2) if 'moe_hc' in phases or 'moe' in phases else []):
                e_, ro = 2 * ra + par, par * 64
                for ff in range(32):
                    ps, pk = ((P1, "s2_P1") if ff % 2 == 0 else (P2, "s2_P2"))
                    p.op("tensor", lambda e, ff=ff, ra=ra, ro=ro, ps=ps: e.matmul(
                        ps[:], w1b[ro:ro + 64,
                                   ra * 4096 + ff * 128:ra * 4096 + ff * 128 + 128],
                        t1s[ro:ro + 64, :], start=True, stop=True),
                        reads=["s2_w1b", "s2_t1s"], writes=[pk])
                    p.op("scalar", lambda e, ff=ff, ps=ps, pk=pk: e.activation(
                        hc[:, ff * 512:ff * 512 + 512], ps[:], AF.Gelu),
                        reads=[pk], writes=[f"s2_hc#{ff}"])
                pt2, kt2 = ((P3, "s2_P3") if par == 0 else (P6, "s2_P6"))
                for ff in (range(32) if 'moe_t2' in phases or 'moe' in phases else []):
                    p.op("tensor", lambda e, ff=ff, e_=e_, pt2=pt2, ro=ro: e.matmul(
                        pt2[ro:ro + 64, :],
                        w2at[:, ff * 512 + e_ * 64:ff * 512 + e_ * 64 + 64],
                        hc[:, ff * 512:ff * 512 + 512],
                        start=(ff == 0), stop=(ff == 31),
                        tile_position=(0, ro)),
                        reads=["s2_w2at", f"s2_hc#{ff}"], writes=[kt2])
                if 'moe_t2' in phases or 'moe' in phases:
                    p.op("vector", lambda e, ra=ra, ro=ro, pt2=pt2, kt2=kt2:
                         e.tensor_tensor(
                             t2s[ro:ro + 64, ra * 512:ra * 512 + 512],
                             pt2[ro:ro + 64, :],
                             gbcp[ro:ro + 64, :], OP.mult),
                         reads=[kt2, "s2_gbcp"], writes=[f"s2_t2s#{ra}"])
        p.op("sync", lambda e, tc=tc: e.dma_start(
            out=yf32[:].rearrange("p (a t) -> p a t", a=8),
            in_=yv[:, :, tc * 512:tc * 512 + 512]),
            writes=["s2_yf32"], dma=True, stream="r")
        for o in (range(8) if 'moe_acc' in phases or 'moe' in phases else []):
            ps, pk = ((P0, "s2_P0") if o % 2 == 0 else (P1, "s2_P1"))
            for ra in range(4):
                p.op("tensor", lambda e, ra=ra, o=o, ps=ps: e.matmul(
                    ps[:], w2b[:, ra * 1024 + o * 128:ra * 1024 + o * 128 + 128],
                    t2s[:, ra * 512:ra * 512 + 512],
                    start=(ra == 0), stop=(ra == 3)),
                    reads=["s2_w2b", f"s2_t2s#{ra}"], writes=[pk])
            p.op("vector", lambda e, o=o, ps=ps, pk=pk: e.tensor_tensor(
                yf32[:, o * 512:o * 512 + 512], ps[:],
                yf32[:, o * 512:o * 512 + 512], OP.add),
                reads=[pk, "s2_yf32"], writes=["s2_yf32"])

        def writer0(tc=tc):
            B.p.op("sync", lambda e: e.dma_start(
                out=ov[:, :, tc * 512:tc * 512 + 512],
                in_=yf32[:].rearrange("p (a t) -> p a t", a=8)),
                reads=["s2_yf32"], dma=True, stream="o")
        if 'moe_ln' in phases or 'moe' in phases:
            ln.emit(yf32, "s2_yf32", P4, P5, P2, P3, writer0)

    for b in range(1, nb) if 'slot1' in phases else []:
        for tc in range(4):
            p.op("sync", lambda e, tc=tc, b=b: e.dma_start(
                out=yf32[:].rearrange("p (a t) -> p a t", a=8),
                in_=yv[:, :, b * T + tc * 512:b * T + tc * 512 + 512]),
                writes=["s2_yf32"], dma=True, stream="r")

            def writer1(tc=tc, b=b):
                B.p.op("sync", lambda e: e.dma_start(
                    out=ov[:, :, b * T + tc * 512:b * T + tc * 512 + 512],
                    in_=yf32[:].rearrange("p (a t) -> p a t", a=8)),
                    reads=["s2_yf32"], dma=True, stream="o")
            ln.emit(yf32, "s2_yf32", P4, P5, P2, P3, writer1)


def build_stage2_program(nb=2):
    nc = bass.Bass("TRN2", target_bir_lowering=False)
    with ExitStack() as ctx:
        B = BD(nc, ctx)
        io = {
            "yT": B.dram("yT", (1024, nb * T), F32, kind="ExternalInput"),
            "y2T": B.dram("y2T", (1024, nb * T), F32, kind="ExternalOutput"),
            "w1A": B.dram("w1A", (512, 1024), F32, kind="ExternalInput"),
            "w1B": B.dram("w1B", (512, 4096), F32, kind="ExternalInput"),
            "w2A": B.dram("w2A", (512, 4096), F32, kind="ExternalInput"),
            "w2B": B.dram("w2B", (512, 1024), F32, kind="ExternalInput"),
            "moe_w": B.dram("moe_w", (T, 2), F32, kind="ExternalInput"),
            "moe_i": B.dram("moe_i", (T, 2), I32, kind="ExternalInput"),
            "sel8": B.dram("sel8", (8, 512), F32, kind="ExternalInput"),
        }
        emit_stage2(B, io, nb=nb)
        B.p.barrier()
        B.p.emit(ctx)
    return nc




def emit_stage3(B, io, nb=2, phases=('p1k','p1v','p1f','p2q','p2f','p2t','p2w')):
    nc, p = B.nc, B.p
    NH = 16
    INV2PI = float(1.0 / (2.0 * np.pi))
    C2 = 1.0 / 128.0
    ln = LNPack(B, "s3")
    eye = emit_eye(B, BF16, "s3_eye")
    B.const(MAGIC, -MAGIC, 1e-6, C2, INV2PI, 0.25)

    P0, P1, P2 = B.ps("s3_P0"), B.ps("s3_P1"), B.ps("s3_P2")
    G0, G1, G2, G3 = B.ps("s3_G0"), B.ps("s3_G1"), B.ps("s3_G2"), B.ps("s3_G3")
    PTT = B.ps("s3_PT", (128, 1024), BF16)
    TSB = [G0, G1, G2, G3]
    TSK = ["s3_G0", "s3_G1", "s3_G2", "s3_G3"]

    wA = B.sb("s3_wA", (128, 8 * 1024), BF16)
    wB = B.sb("s3_wB", (128, 8 * 1024), BF16)

    def loadw(dst, key, src):
        p.op("gpsimd", lambda e, dst=dst, src=src: e.dma_start(
            out=dst[:].rearrange("p (a f) -> p a f", a=8),
            in_=io[src].rearrange("(a p) f -> p a f", p=128)),
            writes=[key], dma=True, stream="w")

    rff = B.sb("s3_rff", (128, 256), F32)
    rfbd = B.sb("s3_rfbd", (128, 256), BF16)
    p.op("gpsimd", lambda e: e.memset(rff[:], 0.0), writes=["s3_rff"])
    p.op("sync", lambda e: e.dma_start(out=rff[0:64, 0:128], in_=io["rf"]),
         writes=["s3_rff"], dma=True)
    p.op("sync", lambda e: e.dma_start(out=rff[64:128, 128:256], in_=io["rf"]),
         writes=["s3_rff"], dma=True)
    p.op("vector", lambda e: e.tensor_scalar(rfbd[:], rff[:], INV2PI, None,
                                             OP.mult),
         reads=["s3_rff", "cpool"], writes=["s3_rfbd"])

    biasq = B.sb("s3_biasq", (128, 1), F32)
    if 'nobias' not in phases:
        p.op("gpsimd", lambda e: e.memset(biasq[0:64, :], 0.25), writes=["s3_biasq"])
        p.op("gpsimd", lambda e: e.memset(biasq[64:128, :], 0.0), writes=["s3_biasq"])

    y16 = B.sb("s3_y16", (128, 8 * 512), BF16)
    qk16 = B.sb("s3_qk16", (128, 8 * 512), BF16)
    V16 = B.sb("s3_V16", (128, 16 * 1040), BF16)
    KV16 = B.sb("s3_KV16", (128, 16 * 65), BF16)
    php = B.sb("s3_php", (128, 2048), BF16)
    pt1 = B.sb("s3_pt1", (128, 512), F32)
    pt2 = B.sb("s3_pt2", (128, 512), F32)
    qp16 = B.sb("s3_qp16", (128, 512), BF16)
    zrq = B.sb("s3_zrq", (128, 16), F32)
    ao4 = B.sb("s3_ao4", (128, 4 * 1024), BF16)
    aoT = B.sb("s3_aoT", (128, 8 * 512), BF16)
    xf32 = B.sb("s3_xf32", (128, 8 * 512), F32)

    yv = io["y2T"].rearrange("(a p) t -> p a t", p=128)
    ov = io["y3T"].rearrange("(a p) t -> p a t", p=128)

    HG = [(0, 7), (7, 14), (14, 16)]

    if 'noones' not in phases:
        p.op("gpsimd", lambda e: e.memset(
            rep_ap(V16, [[1040, 16], [65, 16], [1, 1]], 64), C2), writes=["s3_V16o"])

    for b in range(nb):
        t0 = b * T
        loadw(wA, "s3_wA", "Wk")
        loadw(wB, "s3_wB", "Wv")
        for c in range(1 if 'tiny' in phases else 4):
            p.op("gpsimd", lambda e, c=c, t0=t0: e.dma_start(
                out=y16[:].rearrange("p (a t) -> p a t", a=8),
                in_=yv[:, :, t0 + c * 512:t0 + c * 512 + 512]),
                writes=["s3_y16"], dma=True, stream="x")
            for o in range(8) if 'p1k' in phases else []:
                for a in range(8):
                    p.op("tensor", lambda e, a=a, o=o: e.matmul(
                        P0[:], wA[:, a * 1024 + o * 128:a * 1024 + o * 128 + 128],
                        y16[:, a * 512:a * 512 + 512],
                        start=(a == 0), stop=(a == 7)),
                        reads=["s3_wA", "s3_y16"], writes=["s3_P0"])
                p.op("vector", lambda e, o=o: e.tensor_copy(
                    qk16[:, o * 512:o * 512 + 512], P0[:]),
                    reads=["s3_P0"], writes=["s3_qk16"])
            for ts in range(4) if 'p1v' in phases else []:
                tt = c * 4 + ts
                for vc in range(2):
                    for a in range(8):
                        p.op("tensor", lambda e, a=a, ts=ts, vc=vc: e.matmul(
                            P1[:], y16[:, a * 512 + ts * 128:a * 512 + ts * 128 + 128],
                            wB[:, a * 1024 + vc * 512:a * 1024 + vc * 512 + 512],
                            start=(a == 0), stop=(a == 7)),
                            reads=["s3_y16", "s3_wB"], writes=["s3_P1"])
                    p.op("vector", lambda e, tt=tt, vc=vc: e.tensor_scalar(
                        rep_ap(V16, [[65, 8], [1, 64]], tt * 1040 + vc * 520),
                        rep_ap(P1, [[64, 8], [1, 64]], 0), C2, None, OP.mult),
                        reads=["s3_P1", "cpool"], writes=["s3_V16"])
            for ts in (range(1 if 'tiny' in phases else 4) if 'p1f' in phases else []):
                tt = c * 4 + ts
                for hg in range(1 if 'tiny' in phases else 4):
                    for hh in (range(4) if 'noproj' not in phases else []):
                        h = hg * 4 + hh
                        a, pr = h // 2, h % 2
                        p.op("tensor", lambda e, a=a, pr=pr, ts=ts, hh=hh: e.matmul(
                            P2[:, hh * 128:hh * 128 + 128],
                            qk16[:, a * 512 + ts * 128:a * 512 + ts * 128 + 128],
                            rfbd[:, pr * 128:pr * 128 + 128], start=True, stop=True),
                            reads=["s3_qk16", "s3_rfbd"], writes=["s3_P2"])
                    if 'nophi' in phases:
                        continue
                    p.op("scalar", lambda e: e.activation(pt1[:], P2[:], AF.Identity),
                         reads=["s3_P2"], writes=["s3_pt1"])
                    if 'nophi2' not in phases:
                        p.op("vector", lambda e: e.tensor_scalar(
                            rep_ap(pt1, [[128, 4], [1, 64]], 0),
                            rep_ap(pt1, [[128, 4], [1, 64]], 0), 0.25, None, OP.add),
                            reads=["s3_pt1", "cpool"], writes=["s3_pt1"])
                        p.op("vector", lambda e: e.tensor_scalar(
                            pt2[:], pt1[:], MAGIC, None, OP.add),
                            reads=["s3_pt1", "cpool"], writes=["s3_pt2"])
                        p.op("vector", lambda e: e.tensor_scalar(
                            pt2[:], pt2[:], -MAGIC, None, OP.add),
                            reads=["s3_pt2", "cpool"], writes=["s3_pt2"])
                        p.op("vector", lambda e: e.tensor_tensor(
                            pt1[:], pt1[:], pt2[:], OP.subtract),
                            reads=["s3_pt1", "s3_pt2"], writes=["s3_pt1"])
                    p.op("scalar", lambda e, hg=hg: e.activation(
                        php[:, hg * 512:hg * 512 + 512], pt1[:],
                        AF.Sin if 'nosin' not in phases else AF.Identity,
                        scale=float(2.0 * np.pi) if 'nosin' not in phases else 1.0),
                        reads=["s3_pt1"], writes=["s3_php"])
                for h in (range(NH) if 'nokv' not in phases else []):
                    gi_ = 0 if h < 7 else (1 if h < 14 else 2)
                    h0 = HG[gi_][0]
                    sl = (h - h0) * 65
                    first = (c == 0 and ts == 0 and h == h0) or 'shortacc' in phases
                    last = (c == 3 and ts == 3 and h == HG[gi_][1] - 1) or 'shortacc' in phases
                    p.op("tensor", lambda e, h=h, tt=tt, gi_=gi_, sl=sl,
                         first=first, last=last: e.matmul(
                        TSB[gi_][:, sl:sl + 65], php[:, h * 128:h * 128 + 128],
                        V16[:, tt * 1040 + h * 65:tt * 1040 + h * 65 + 65],
                        start=first, stop=last),
                        reads=["s3_php", "s3_V16", "s3_V16o"], writes=[TSK[gi_]])
        for gi_, (h0, h1) in (enumerate(HG) if 'p1f' in phases else []):
            p.op("vector", lambda e, gi_=gi_, h0=h0, h1=h1: e.tensor_copy(
                KV16[:, h0 * 65:h1 * 65], TSB[gi_][:, 0:(h1 - h0) * 65]),
                reads=[TSK[gi_]], writes=["s3_KV16"])
        if b == 0 and "dbg_KV16" in io:
            p.op("sync", lambda e: e.dma_start(out=io["dbg_KV16"], in_=KV16[:]),
                 reads=["s3_KV16"], dma=True, stream="o")

        loadw(wA, "s3_wA", "Wq")
        loadw(wB, "s3_wB", "Wo")
        for c in range(4):
            p.op("gpsimd", lambda e, c=c, t0=t0: e.dma_start(
                out=y16[:].rearrange("p (a t) -> p a t", a=8),
                in_=yv[:, :, t0 + c * 512:t0 + c * 512 + 512]),
                writes=["s3_y16"], dma=True, stream="x")
            for o in range(8) if 'p2q' in phases else []:
                for a in range(8):
                    p.op("tensor", lambda e, a=a, o=o: e.matmul(
                        P0[:], wA[:, a * 1024 + o * 128:a * 1024 + o * 128 + 128],
                        y16[:, a * 512:a * 512 + 512],
                        start=(a == 0), stop=(a == 7)),
                        reads=["s3_wA", "s3_y16"], writes=["s3_P0"])
                p.op("vector", lambda e, o=o: e.tensor_copy(
                    qk16[:, o * 512:o * 512 + 512], P0[:]),
                    reads=["s3_P0"], writes=["s3_qk16"])
            for gi_, (h0, h1) in (enumerate(HG) if 'p2f' in phases else []):
                for h in range(h0, h1):
                    a, pr = h // 2, h % 2
                    p.op("tensor", lambda e, a=a, pr=pr: e.matmul(
                        P2[:], rfbd[:, pr * 128:pr * 128 + 128],
                        qk16[:, a * 512:a * 512 + 512],
                        start=True, stop=True),
                        reads=["s3_qk16", "s3_rfbd"], writes=["s3_P2"])
                    p.op("scalar", lambda e: e.activation(
                        pt1[:], P2[:], AF.Identity, bias=biasq[:, 0:1]),
                        reads=["s3_P2", "s3_biasq"], writes=["s3_pt1"])
                    p.op("vector", lambda e: e.tensor_scalar(
                        pt2[:], pt1[:], MAGIC, None, OP.add),
                        reads=["s3_pt1", "cpool"], writes=["s3_pt2"])
                    p.op("vector", lambda e: e.tensor_scalar(
                        pt2[:], pt2[:], -MAGIC, None, OP.add),
                        reads=["s3_pt2", "cpool"], writes=["s3_pt2"])
                    p.op("vector", lambda e: e.tensor_tensor(
                        pt1[:], pt1[:], pt2[:], OP.subtract),
                        reads=["s3_pt1", "s3_pt2"], writes=["s3_pt1"])
                    p.op("scalar", lambda e: e.activation(
                        qp16[:], pt1[:], AF.Sin, scale=float(2.0 * np.pi)),
                        reads=["s3_pt1"], writes=["s3_qp16"])
                    if b == 0 and c == 0 and h == 0 and "dbg_qp" in io:
                        p.op("sync", lambda e: e.dma_start(out=io["dbg_qp"], in_=qp16[:]),
                             reads=["s3_qp16"], dma=True, stream="o")
                    for ts in range(4):
                        sl = (h - h0) * 65
                        p.op("tensor", lambda e, ts=ts, h=h, sl=sl: e.matmul(
                            TSB[ts][:, sl:sl + 65],
                            qp16[:, ts * 128:ts * 128 + 128],
                            KV16[:, h * 65:h * 65 + 65], start=True, stop=True),
                            reads=["s3_qp16", "s3_KV16"], writes=[TSK[ts]])
                ng = h1 - h0
                for ts in range(4):
                    p.op("vector", lambda e, ts=ts, ng=ng: e.tensor_scalar(
                        rep_ap(TSB[ts], [[65, ng], [1, 1]], 64),
                        rep_ap(TSB[ts], [[65, ng], [1, 1]], 64), 1e-6, None,
                        OP.max), reads=[TSK[ts], "cpool"], writes=[TSK[ts]])
                    p.op("vector", lambda e, ts=ts, ng=ng, h0=h0: e.reciprocal(
                        zrq[:, h0:h0 + ng],
                        rep_ap(TSB[ts], [[65, ng], [1, 1]], 64)),
                        reads=[TSK[ts]], writes=["s3_zrq"])
                    p.op("vector", lambda e, ts=ts, ng=ng, h0=h0: e.tensor_tensor(
                        rep_ap(ao4, [[64, ng], [1, 64]], ts * 1024 + h0 * 64),
                        rep_ap(TSB[ts], [[65, ng], [1, 64]], 0),
                        rep_ap(zrq, [[1, ng], [0, 64]], h0), OP.mult),
                        reads=[TSK[ts], "s3_zrq"], writes=["s3_ao4"])
            if b == 0 and c == 0 and "dbg_ao4" in io:
                p.op("sync", lambda e: e.dma_start(out=io["dbg_ao4"], in_=ao4[:]),
                     reads=["s3_ao4"], dma=True, stream="o")
            for ts in range(4) if 'p2t' in phases else []:
                for g in range(8):
                    p.op("tensor", lambda e, ts=ts, g=g: e.transpose(
                        PTT[:, g * 128:(g + 1) * 128],
                        ao4[:, ts * 1024 + g * 128:ts * 1024 + g * 128 + 128],
                        eye[:]), reads=["s3_ao4", "s3_eye"], writes=["s3_PT"])
                for g in range(8):
                    p.op("vector", lambda e, ts=ts, g=g: e.tensor_copy(
                        aoT[:, g * 512 + ts * 128:g * 512 + ts * 128 + 128],
                        PTT[:, g * 128:(g + 1) * 128]),
                        reads=["s3_PT"], writes=["s3_aoT"])
            if b == 0 and c == 0 and "dbg_aoT" in io:
                p.op("sync", lambda e: e.dma_start(out=io["dbg_aoT"], in_=aoT[:]),
                     reads=["s3_aoT"], dma=True, stream="o")
            if 'p2w' not in phases:
                continue
            p.op("sync", lambda e, c=c, t0=t0: e.dma_start(
                out=xf32[:].rearrange("p (a t) -> p a t", a=8),
                in_=yv[:, :, t0 + c * 512:t0 + c * 512 + 512]),
                writes=["s3_xf32"], dma=True, stream="r")
            for o in range(8):
                ps, pk = ((P0, "s3_P0") if o % 2 == 0 else (P2, "s3_P2"))
                for a in range(8):
                    p.op("tensor", lambda e, a=a, o=o, ps=ps: e.matmul(
                        ps[:], wB[:, a * 1024 + o * 128:a * 1024 + o * 128 + 128],
                        aoT[:, a * 512:a * 512 + 512],
                        start=(a == 0), stop=(a == 7)),
                        reads=["s3_wB", "s3_aoT"], writes=[pk])
                p.op("vector", lambda e, o=o, ps=ps, pk=pk: e.tensor_tensor(
                    xf32[:, o * 512:o * 512 + 512], ps[:],
                    xf32[:, o * 512:o * 512 + 512], OP.add),
                    reads=[pk, "s3_xf32"], writes=["s3_xf32"])

            def writer(c=c, t0=t0):
                B.p.op("sync", lambda e: e.dma_start(
                    out=ov[:, :, t0 + c * 512:t0 + c * 512 + 512],
                    in_=xf32[:].rearrange("p (a t) -> p a t", a=8)),
                    reads=["s3_xf32"], dma=True, stream="o")
            ln.emit(xf32, "s3_xf32", G0, G1, G2, G3, writer)


def build_stage3_program(nb=2, phases=('p1k','p1v','p1f','p2q','p2f','p2t','p2w'), debug=False):
    nc = bass.Bass("TRN2", target_bir_lowering=False)
    with ExitStack() as ctx:
        B = BD(nc, ctx)
        io = {}
        if debug:
            io["dbg_KV16"] = B.dram("dbg_KV16", (128, 16 * 65), mybir.dt.bfloat16, kind="ExternalOutput")
            io["dbg_qp"] = B.dram("dbg_qp", (128, 512), mybir.dt.bfloat16, kind="ExternalOutput")
            io["dbg_ao4"] = B.dram("dbg_ao4", (128, 4096), mybir.dt.bfloat16, kind="ExternalOutput")
            io["dbg_aoT"] = B.dram("dbg_aoT", (128, 4096), mybir.dt.bfloat16, kind="ExternalOutput")
        io.update({
            "y2T": B.dram("y2T", (1024, nb * T), F32, kind="ExternalInput"),
            "y3T": B.dram("y3T", (1024, nb * T), F32, kind="ExternalOutput"),
            "Wq": B.dram("Wq", (1024, 1024), F32, kind="ExternalInput"),
            "Wk": B.dram("Wk", (1024, 1024), F32, kind="ExternalInput"),
            "Wv": B.dram("Wv", (1024, 1024), F32, kind="ExternalInput"),
            "Wo": B.dram("Wo", (1024, 1024), F32, kind="ExternalInput"),
            "rf": B.dram("rf", (64, 128), F32, kind="ExternalInput"),
        })
        emit_stage3(B, io, nb=nb, phases=phases)
        B.p.barrier()
        B.p.emit(ctx)
    return nc




def make_xpos_tables():
    DIM_HEAD, WINDOW = 64, 512
    half = np.arange(0, DIM_HEAD, 2, dtype=np.float64)
    inv_freq = 1.0 / 10000.0 ** (half / DIM_HEAD)
    t = np.arange(T, dtype=np.float64)
    freqs = np.concatenate([t[:, None] * inv_freq[None, :]] * 2, -1)
    base = (half + 0.4 * DIM_HEAD) / (1.4 * DIM_HEAD)
    power = (t - T // 2) / (WINDOW // 2)
    scale = np.concatenate([base[None, :] ** power[:, None]] * 2, -1)
    cos, sin = np.cos(freqs), np.sin(freqs)
    sgn = np.concatenate([-np.ones(32), np.ones(32)])[None, :]
    return ((cos * scale * 8.0).astype(np.float32),
            (sin * scale * 8.0 * sgn).astype(np.float32),
            (cos / scale).astype(np.float32),
            (sin / scale * sgn).astype(np.float32))


def emit_stage4_layer(B, io, l, nb=2, phases=("a", "c"), alim=7, qlim=9):
    nc, p = B.nc, B.p
    H, HID = 8, 2730
    NEG = -1.0e9
    L = f"s4{l}"
    B.const(1e-12, 1e-5)
    eye = emit_eye(B, BF16, L + "_eye")
    eyef = emit_eye(B, F32, L + "_eyef")

    x2T = B.dram(L + "_x2T", (1024, nb * T), F32, kind="Internal")
    mscr = B.dram(L + "_mscr", (32, 128), F32, kind="Internal")
    w1c = B.dram(L + "_w1c", (1024, 5460), BF16, kind="Internal")
    w2c = B.dram(L + "_w2c", (2730, 1024), BF16, kind="Internal")
    p.op("gpsimd", lambda e: e.dma_start(out=w1c, in_=io["w1"]),
         writes=[L + "w1c"], dma=True, stream="w")
    p.op("gpsimd", lambda e: e.dma_start(out=w2c, in_=io["w2"]),
         writes=[L + "w2c"], dma=True, stream="w")

    ones1 = B.sb(L + "_ones1", (1, 128), F32)
    p.op("gpsimd", lambda e: e.memset(ones1[:], 1.0), writes=[L + "_ones1"])
    onesc = B.sb(L + "_onesc", (128, 1), BF16)
    p.op("gpsimd", lambda e: e.memset(onesc[:], 1.0), writes=[L + "_onesc"])
    ones2 = B.sb(L + "_ones2", (128, 64), F32)
    p.op("gpsimd", lambda e: e.memset(ones2[:], 1.0), writes=[L + "_ones2"])

    PQ0, PQ1 = B.ps(L + "_PQ0"), B.ps(L + "_PQ1")
    PA, PBm = B.ps(L + "_PA"), B.ps(L + "_PBm")
    PAV, PZ, PS8 = B.ps(L + "_PAV"), B.ps(L + "_PZ"), B.ps(L + "_PS8")
    PTT = B.ps(L + "_PT", (128, 1024), BF16)
    kq0, kq1 = L + "_PQ0", L + "_PQ1"
    ka, kb = L + "_PA", L + "_PBm"
    kav, kz, k8 = L + "_PAV", L + "_PZ", L + "_PS8"
    kt = L + "_PT"

    lngt = B.sb(L + "_lngt", (128, 16), F32)
    p.op("sync", lambda e: e.dma_start(
        out=lngt[:, 0:8], in_=io["lng"].rearrange("(a p) -> p a", p=128),
        allow_slow_non_contiguous=True),
        writes=[L + "_lngt"], dma=True)
    p.op("sync", lambda e: e.dma_start(
        out=lngt[:, 8:16], in_=io["lnb"].rearrange("(a p) -> p a", p=128),
        allow_slow_non_contiguous=True),
        writes=[L + "_lngt"], dma=True)
    b2t = B.sb(L + "_b2t", (128, 8), F32)
    p.op("sync", lambda e: e.dma_start(
        out=b2t[:], in_=io["b2"].rearrange("(a p) -> p a", p=128),
        allow_slow_non_contiguous=True),
        writes=[L + "_b2t"], dma=True)
    b1t = B.sb(L + "_b1t", (128, 44), F32)
    for hoff, coff in ((0, 0), (HID, 22)):
        p.op("sync", lambda e, hoff=hoff, coff=coff: e.dma_start(
            out=b1t[:, coff:coff + 21],
            in_=io["b1"][hoff:hoff + 2688].rearrange("(a p) -> p a", p=128),
            allow_slow_non_contiguous=True),
            writes=[L + "_b1t"], dma=True)
        p.op("sync", lambda e, hoff=hoff, coff=coff: e.dma_start(
            out=b1t[0:42, coff + 21:coff + 22],
            in_=io["b1"][hoff + 2688:hoff + 2730].rearrange("(p a) -> p a", a=1)),
            writes=[L + "_b1t"], dma=True)
    xv = io["xT"].rearrange("(a p) t -> p a t", p=128)
    yv = io["yT"].rearrange("(a p) t -> p a t", p=128)

    for b in range(nb):
        t0 = b * T
        p.op("gpsimd", lambda e: e.dma_start(
            out=wvo[:].rearrange("p (a f) -> p a f", a=8),
            in_=io["Wv"].rearrange("(a p) f -> p a f", p=128)),
            writes=["s1_wvo"], dma=True, stream="w")
        for c in range(4) if 'feat' in phases else []:
            p.op("gpsimd", lambda e, t0=t0, c=c: e.dma_start(
                out=x16[:].rearrange("p (a t) -> p a t", a=8),
                in_=xv[:, :, t0 + c * 512:t0 + c * 512 + 512]),
                writes=["s1_x16"], dma=True, stream="x")
            for ttc in range(4):
                tt = c * 4 + ttc
                for a in range(8):
                    p.op("tensor", lambda e, a=a, ttc=ttc: e.matmul(
                        P0[:, 0:40],
                        x16[:, a * 512 + ttc * 128:a * 512 + ttc * 128 + 128],
                        wp[:, a * 40:(a + 1) * 40], start=(a == 0), stop=(a == 7)),
                        reads=["s1_x16", "s1_wp"], writes=["s1_P0"])
                p.op("vector", lambda e: e.tensor_tensor(ptile[:, 0:40], P0[:, 0:40],
                                                         bpbc[:], OP.add),
                     reads=["s1_P0", "s1_bpbc"], writes=["s1_p"])
                p.op("scalar", lambda e: e.activation(ptile[:, 0:40], ptile[:, 0:40],
                                                      AF.Sigmoid),
                     reads=["s1_p"], writes=["s1_p"])
                uq0 = rep_ap(uq, [[10, 8], [2, 5], [1, 1]], 0)
                uq1 = rep_ap(uq, [[10, 8], [2, 5], [1, 1]], 1)
                vk0 = rep_ap(vk, [[10, 8], [2, 5], [1, 1]], 0)
                vk1 = rep_ap(vk, [[10, 8], [2, 5], [1, 1]], 1)
                pv = rep_ap(ptile, [[5, 8], [1, 5], [0, 1]], 0)
                p.op("vector", lambda e: e.tensor_scalar(uq0, pv, -1.0, 1.0,
                                                         OP.mult, OP.add),
                     reads=["s1_p", "cpool"], writes=["s1_uq"])
                p.op("vector", lambda e: e.tensor_scalar(uq1, pv, 2.0, -1.0,
                                                         OP.mult, OP.add),
                     reads=["s1_p", "cpool"], writes=["s1_uq"])
                p.op("gpsimd", lambda e: e.memset(vk0, 1.0), writes=["s1_vk"])
                p.op("vector", lambda e: e.tensor_copy(vk1, pv),
                     reads=["s1_p"], writes=["s1_vk"])
                for (usrc, ukey, Fdst, fkey) in ((uq, "s1_uq", FqT, "s1_FqT"),
                                                 (vk, "s1_vk", FkT, "s1_FkT")):
                    p.op("gpsimd", lambda e: e.memset(
                        rep_ap(Ftf, [[64, 8], [1, 2]], 62), 0.0),
                        writes=["s1_Ftf"])
                    p.op("vector", lambda e, u=usrc: e.tensor_copy(
                        rep_ap(Ftf, [[64, 8], [1, 2]], 0),
                        rep_ap(u, [[10, 8], [1, 2]], 0)),
                        reads=[ukey], writes=["s1_Ftf"])
                    sp, sc = 0, 2
                    for j in range(1, DEP):
                        w = 2 ** j
                        p.op("vector", lambda e, u=usrc, sp=sp, sc=sc, j=j, w=w:
                             e.tensor_tensor(
                                 rep_ap(Ftf, [[64, 8], [1, 2 * w]], sc),
                                 rep_ap(Ftf, [[64, 8], [1, w], [0, 2]], sp),
                                 rep_ap(u, [[10, 8], [0, w], [1, 2]], 2 * j),
                                 OP.mult),
                             reads=["s1_Ftf", ukey], writes=["s1_Ftf"])
                        sp, sc = sc, sc + 2 * w
                    p.op("vector", lambda e: e.tensor_copy(F16[:], Ftf[:]),
                         reads=["s1_Ftf"], writes=["s1_F16"])
                    for g in range(4):
                        p.op("tensor", lambda e, g=g: e.transpose(
                            PTT[:, g * 128:(g + 1) * 128],
                            F16[:, g * 128:(g + 1) * 128], eye[:]),
                            reads=["s1_F16", "eye_bf"], writes=["s1_PT"])
                    for g in range(4):
                        p.op("vector", lambda e, g=g, Fd=Fdst, tt=tt: e.tensor_copy(
                            Fd[:, g * T + tt * 128:g * T + tt * 128 + 128],
                            PTT[:, g * 128:(g + 1) * 128]),
                            reads=["s1_PT"], writes=[fkey])
                for vc in range(2):
                    ps, pk = ((P1, "s1_P1") if vc == 0 else (P2, "s1_P2"))
                    for a in range(8):
                        p.op("tensor", lambda e, a=a, ttc=ttc, vc=vc, ps=ps: e.matmul(
                            ps[:],
                            x16[:, a * 512 + ttc * 128:a * 512 + ttc * 128 + 128],
                            wvo[:, a * 1024 + vc * 512:a * 1024 + vc * 512 + 512],
                            start=(a == 0), stop=(a == 7)),
                            reads=["s1_x16", "s1_wvo"], writes=[pk])
                    p.op("vector", lambda e, tt=tt, vc=vc, ps=ps, pk=pk: e.tensor_copy(
                        V16[:, tt * 1024 + vc * 512:tt * 1024 + vc * 512 + 512],
                        ps[:]), reads=[pk], writes=["s1_V16"])

        p.op("gpsimd", lambda e: e.dma_start(
            out=wvo[:].rearrange("p (a f) -> p a f", a=8),
            in_=io["Wo"].rearrange("(a p) f -> p a f", p=128)),
            writes=["s1_wvo"], dma=True, stream="w")

        for h in range(8) if 'attn' in phases else []:
            g, hp = h // 2, (h % 2) * 64
            for qc in range(4):
                nkt = 4 * qc + 4
                for kt in range(nkt):
                    ps, pk = ((P0, "s1_P0") if kt % 2 == 0 else (P1, "s1_P1"))
                    p.op("tensor", lambda e, kt=kt, qc=qc, pss=ps, g=g, hp=hp: e.matmul(
                        pss[:],
                        FkT[hp:hp + 64, g * T + kt * 128:g * T + kt * 128 + 128],
                        FqT[hp:hp + 64, g * T + qc * 512:g * T + qc * 512 + 512],
                        start=True, stop=True),
                        reads=["s1_FkT", "s1_FqT"], writes=[pk])
                    p.op("scalar", lambda e, kt=kt, pss=ps: e.activation(
                        At[:, kt * 512:kt * 512 + 512], pss[:], AF.Exp, scale=0.2),
                        reads=[pk], writes=[f"s1_At#{kt}"])
                    if kt >= 4 * qc:
                        p.op("gpsimd", lambda e, kt=kt, qc=qc: e.affine_select(
                            out=At[:, kt * 512:kt * 512 + 512],
                            in_=At[:, kt * 512:kt * 512 + 512],
                            compare_op=OP.is_ge, fill=0.0,
                            base=qc * 512 - kt * 128, pattern=[[1, 512]],
                            channel_multiplier=-1),
                            reads=[f"s1_At#{kt}"], writes=[f"s1_At#{kt}"])
                for kt in range(nkt):
                    p.op("tensor", lambda e, kt=kt, h=h, nkt=nkt: e.matmul(
                        P2[:], V16[:, kt * 1024 + h * 128:kt * 1024 + h * 128 + 128],
                        At[:, kt * 512:kt * 512 + 512],
                        start=(kt == 0), stop=(kt == nkt - 1)),
                        reads=["s1_V16", f"s1_At#{kt}"], writes=["s1_P2"])
                for kt in range(nkt):
                    p.op("tensor", lambda e, kt=kt, nkt=nkt: e.matmul(
                        P3[:], ones_sq[:], At[:, kt * 512:kt * 512 + 512],
                        start=(kt == 0), stop=(kt == nkt - 1)),
                        reads=["s1_onsq", f"s1_At#{kt}"], writes=["s1_P3"])
                p.op("scalar", lambda e: e.activation(zln[:], P3[:], AF.Ln),
                     reads=["s1_P3"], writes=["s1_zln"])
                p.op("scalar", lambda e: e.activation(zrec[:], zln[:], AF.Exp,
                                                      scale=-1.0),
                     reads=["s1_zln"], writes=["s1_zrec"])
                p.op("vector", lambda e, h=h, qc=qc: e.tensor_tensor(
                    ctxT[:, h * T + qc * 512:h * T + qc * 512 + 512], P2[:],
                    zrec[:], OP.mult),
                    reads=["s1_P2", "s1_zrec"], writes=["s1_ctxT"])

        for c in range(4) if 'wo' in phases else []:
            p.op("sync", lambda e, c=c, t0=t0: e.dma_start(
                out=xf32[:].rearrange("p (a t) -> p a t", a=8),
                in_=xv[:, :, t0 + c * 512:t0 + c * 512 + 512]),
                writes=["s1_xf32"], dma=True, stream="r")
            for o in range(8):
                ps, pk = ((P0, "s1_P0") if o % 2 == 0 else (P1, "s1_P1"))
                for a in range(8):
                    p.op("tensor", lambda e, a=a, o=o, c=c, ps=ps: e.matmul(
                        ps[:], wvo[:, a * 1024 + o * 128:a * 1024 + o * 128 + 128],
                        ctxT[:, a * T + c * 512:a * T + c * 512 + 512],
                        start=(a == 0), stop=(a == 7)),
                        reads=["s1_wvo", "s1_ctxT"], writes=[pk])
                p.op("vector", lambda e, o=o, ps=ps, pk=pk: e.tensor_tensor(
                    zsb[:, o * 512:o * 512 + 512], ps[:],
                    xf32[:, o * 512:o * 512 + 512], OP.add),
                    reads=[pk, "s1_xf32"], writes=["s1_z"])

            def writer(c=c, t0=t0):
                B.p.op("sync", lambda e: e.dma_start(
                    out=yv[:, :, t0 + c * 512:t0 + c * 512 + 512],
                    in_=zsb[:].rearrange("p (a t) -> p a t", a=8)),
                    reads=["s1_z"], dma=True, stream="o")
            if 'ln' in phases:
                ln.emit(zsb, "s1_z", P4, P5, P0, P1, writer)
            else:
                writer()


def build_stage1_program(nb=2, phases=('feat','attn','wo','ln')):
    nc = bass.Bass("TRN2", target_bir_lowering=False)
    with ExitStack() as ctx:
        B = BD(nc, ctx)
        io = {
            "xT": B.dram("xT", (1024, nb * T), F32, kind="ExternalInput"),
            "yT": B.dram("yT", (1024, nb * T), F32, kind="ExternalOutput"),
            "Wp": B.dram("Wp", (1024, 40), F32, kind="ExternalInput"),
            "bp": B.dram("bp", (40,), F32, kind="ExternalInput"),
            "Wv": B.dram("Wv", (1024, 1024), F32, kind="ExternalInput"),
            "Wo": B.dram("Wo", (1024, 1024), F32, kind="ExternalInput"),
        }
        emit_stage1(B, io, nb=nb, phases=phases)
        B.p.barrier()
        B.p.emit(ctx)
    return nc




def emit_stage2(B, io, nb=2, phases=('wload', 'gates', 'moe', 'slot1')):
    nc, p = B.nc, B.p
    E, R, FF = 8, 64, 4096
    ln = LNPack(B, "s2")
    eye = emit_eye(B, BF16, "s2_eye")
    P0, P1, P2, P3, P4, P5 = (B.ps(f"s2_P{i}") for i in range(6))
    PTT = B.ps("s2_PT", (128, 1024), BF16)

    w1b = load_w_tiles(B, "s2_w1b", io["w1B"], 512, 4096)
    w2b = load_w_tiles(B, "s2_w2b", io["w2B"], 512, 1024)
    sel8 = B.sb("s2_sel8", (8, 512), BF16)
    p.op("gpsimd", lambda e: e.dma_start(out=sel8[:], in_=io["sel8"]),
         writes=["s2_sel8"], dma=True, stream="w")
    w1at = B.sb("s2_w1at", (128, 8 * 512), BF16)
    w2at = B.sb("s2_w2at", (128, 32 * 512), BF16)
    wtmp = B.sb("s2_wtmp", (128, 4096), BF16)
    for (src, wdst, kdst, nda) in (((io["w1A"], w1at, "s2_w1at", 8),
                                    (io["w2A"], w2at, "s2_w2at", 32))
                                   if 'wload' in phases else ()):
        for ra in range(4):
            p.op("gpsimd", lambda e, src=src, ra=ra, nda=nda: e.dma_start(
                out=wtmp[:, 0:nda * 128],
                in_=src[ra * 128:(ra + 1) * 128, :]),
                writes=["s2_wtmp"], dma=True, stream="w")
            for da in range(nda):
                p.op("tensor", lambda e, da=da: e.transpose(
                    PTT[:, 0:128], wtmp[:, da * 128:da * 128 + 128], eye[:]),
                    reads=["s2_wtmp", "s2_eye"], writes=["s2_PT"])
                p.op("vector", lambda e, ra=ra, da=da, w=wdst: e.tensor_copy(
                    w[:, da * 512 + ra * 128:da * 512 + ra * 128 + 128],
                    PTT[:, 0:128]), reads=["s2_PT"], writes=[kdst])

    gw = B.sb("s2_gw", (128, 32), F32)
    gi = B.sb("s2_gi", (128, 32), I32)
    gif = B.sb("s2_gif", (128, 32), F32)
    gmat = B.sb("s2_g", (128, 128), F32)
    g16 = B.sb("s2_g16", (128, 128), BF16)
    gT = B.sb("s2_gT", (8, 2048), BF16)
    gtmp = B.sb("s2_gtmp", (128, 48), F32)
    p.op("sync", lambda e: e.dma_start(
        out=gw[:].rearrange("p (tt k) -> p tt k", k=2),
        in_=io["moe_w"].rearrange("(tt p) k -> p tt k", p=128)),
        writes=["s2_gw"], dma=True)
    p.op("sync", lambda e: e.dma_start(
        out=gi[:].rearrange("p (tt k) -> p tt k", k=2),
        in_=io["moe_i"].rearrange("(tt p) k -> p tt k", p=128)),
        writes=["s2_gi"], dma=True)
    p.op("vector", lambda e: e.tensor_copy(gif[:], gi[:]), reads=["s2_gi"],
         writes=["s2_gif"])
    B.const(*[float(e) for e in range(8)])
    for e_ in range(E) if 'gates' in phases else []:
        for k in range(2):
            p.op("vector", lambda ee, e_=e_, k=k: ee.tensor_scalar(
                rep_ap(gtmp, [[3, 16], [1, 1]], k),
                rep_ap(gif, [[2, 16], [1, 1]], k), float(e_), None, OP.is_equal),
                reads=["s2_gif", "cpool"], writes=["s2_gtmp"])
            p.op("vector", lambda ee, e_=e_, k=k: ee.tensor_tensor(
                rep_ap(gtmp, [[3, 16], [1, 1]], k),
                rep_ap(gtmp, [[3, 16], [1, 1]], k),
                rep_ap(gw, [[2, 16], [1, 1]], k), OP.mult),
                reads=["s2_gtmp", "s2_gw"], writes=["s2_gtmp"])
        p.op("vector", lambda ee, e_=e_: ee.tensor_tensor(
            rep_ap(gmat, [[8, 16], [1, 1]], e_),
            rep_ap(gtmp, [[3, 16], [1, 1]], 0),
            rep_ap(gtmp, [[3, 16], [1, 1]], 1), OP.add),
            reads=["s2_gtmp"], writes=["s2_g"])
    p.op("vector", lambda e: e.tensor_copy(g16[:], gmat[:]), reads=["s2_g"],
         writes=["s2_g16"])
    for tt in range(16) if 'gates' in phases else []:
        p.op("tensor", lambda e, tt=tt: e.transpose(
            PTT[0:8, 0:128], g16[:, tt * 8:tt * 8 + 8], eye[:]),
            reads=["s2_g16", "s2_eye"], writes=["s2_PT"])
        p.op("vector", lambda e, tt=tt: e.tensor_copy(
            gT[0:8, tt * 128:tt * 128 + 128], PTT[0:8, 0:128]),
            reads=["s2_PT"], writes=["s2_gT"])

    y16 = B.sb("s2_y16", (128, 8 * 512), BF16)
    yf32 = B.sb("s2_yf32", (128, 8 * 512), F32)
    hc = B.sb("s2_hc", (128, 32 * 512), BF16)
    t1s = B.sb("s2_t1s", (128, 512), BF16)
    t2s = B.sb("s2_t2s", (128, 4 * 512), BF16)
    gbcp = B.sb("s2_gbcp", (128, 512), BF16)

    P6 = B.ps("s2_P6")

    yv = io["yT"].rearrange("(a p) t -> p a t", p=128)
    ov = io["y2T"].rearrange("(a p) t -> p a t", p=128)

    for tc in (range(4) if any(p.startswith('moe') for p in phases) else []):
        p.op("gpsimd", lambda e, tc=tc: e.dma_start(
            out=y16[:].rearrange("p (a t) -> p a t", a=8),
            in_=yv[:, :, tc * 512:tc * 512 + 512]),
            writes=["s2_y16"], dma=True, stream="x")
        for ra in (range(4) if 'moe_t1' in phases or 'moe' in phases else []):
            for da in range(8):
                p.op("tensor", lambda e, da=da, ra=ra: e.matmul(
                    P0[:], w1at[:, da * 512 + ra * 128:da * 512 + ra * 128 + 128],
                    y16[:, da * 512:da * 512 + 512],
                    start=(da == 0), stop=(da == 7)),
                    reads=["s2_w1at", "s2_y16"], writes=["s2_P0"])
            p.op("vector", lambda e: e.tensor_copy(t1s[:], P0[:]),
                 reads=["s2_P0"], writes=["s2_t1s"])
            p.op("tensor", lambda e, ra=ra, tc=tc: e.matmul(
                P4[:], sel8[:, ra * 128:ra * 128 + 128],
                gT[0:8, tc * 512:tc * 512 + 512], start=True, stop=True),
                reads=["s2_sel8", "s2_gT"], writes=["s2_P4"])
            p.op("vector", lambda e: e.tensor_copy(gbcp[:], P4[:]),
                 reads=["s2_P4"], writes=["s2_gbcp"])
            for par in (range(2) if 'moe_hc' in phases or 'moe' in phases else []):
                e_, ro = 2 * ra + par, par * 64
                for ff in range(32):
                    ps, pk = ((P1, "s2_P1") if ff % 2 == 0 else (P2, "s2_P2"))
                    p.op("tensor", lambda e, ff=ff, ra=ra, ro=ro, ps=ps: e.matmul(
                        ps[:], w1b[ro:ro + 64,
                                   ra * 4096 + ff * 128:ra * 4096 + ff * 128 + 128],
                        t1s[ro:ro + 64, :], start=True, stop=True),
                        reads=["s2_w1b", "s2_t1s"], writes=[pk])
                    p.op("scalar", lambda e, ff=ff, ps=ps, pk=pk: e.activation(
                        hc[:, ff * 512:ff * 512 + 512], ps[:], AF.Gelu),
                        reads=[pk], writes=[f"s2_hc#{ff}"])
                pt2, kt2 = ((P3, "s2_P3") if par == 0 else (P6, "s2_P6"))
                for ff in (range(32) if 'moe_t2' in phases or 'moe' in phases else []):
                    p.op("tensor", lambda e, ff=ff, e_=e_, pt2=pt2, ro=ro: e.matmul(
                        pt2[ro:ro + 64, :],
                        w2at[:, ff * 512 + e_ * 64:ff * 512 + e_ * 64 + 64],
                        hc[:, ff * 512:ff * 512 + 512],
                        start=(ff == 0), stop=(ff == 31),
                        tile_position=(0, ro)),
                        reads=["s2_w2at", f"s2_hc#{ff}"], writes=[kt2])
                if 'moe_t2' in phases or 'moe' in phases:
                    p.op("vector", lambda e, ra=ra, ro=ro, pt2=pt2, kt2=kt2:
                         e.tensor_tensor(
                             t2s[ro:ro + 64, ra * 512:ra * 512 + 512],
                             pt2[ro:ro + 64, :],
                             gbcp[ro:ro + 64, :], OP.mult),
                         reads=[kt2, "s2_gbcp"], writes=[f"s2_t2s#{ra}"])
        p.op("sync", lambda e, tc=tc: e.dma_start(
            out=yf32[:].rearrange("p (a t) -> p a t", a=8),
            in_=yv[:, :, tc * 512:tc * 512 + 512]),
            writes=["s2_yf32"], dma=True, stream="r")
        for o in (range(8) if 'moe_acc' in phases or 'moe' in phases else []):
            ps, pk = ((P0, "s2_P0") if o % 2 == 0 else (P1, "s2_P1"))
            for ra in range(4):
                p.op("tensor", lambda e, ra=ra, o=o, ps=ps: e.matmul(
                    ps[:], w2b[:, ra * 1024 + o * 128:ra * 1024 + o * 128 + 128],
                    t2s[:, ra * 512:ra * 512 + 512],
                    start=(ra == 0), stop=(ra == 3)),
                    reads=["s2_w2b", f"s2_t2s#{ra}"], writes=[pk])
            p.op("vector", lambda e, o=o, ps=ps, pk=pk: e.tensor_tensor(
                yf32[:, o * 512:o * 512 + 512], ps[:],
                yf32[:, o * 512:o * 512 + 512], OP.add),
                reads=[pk, "s2_yf32"], writes=["s2_yf32"])

        def writer0(tc=tc):
            B.p.op("sync", lambda e: e.dma_start(
                out=ov[:, :, tc * 512:tc * 512 + 512],
                in_=yf32[:].rearrange("p (a t) -> p a t", a=8)),
                reads=["s2_yf32"], dma=True, stream="o")
        if 'moe_ln' in phases or 'moe' in phases:
            ln.emit(yf32, "s2_yf32", P4, P5, P2, P3, writer0)

    for b in range(1, nb) if 'slot1' in phases else []:
        for tc in range(4):
            p.op("sync", lambda e, tc=tc, b=b: e.dma_start(
                out=yf32[:].rearrange("p (a t) -> p a t", a=8),
                in_=yv[:, :, b * T + tc * 512:b * T + tc * 512 + 512]),
                writes=["s2_yf32"], dma=True, stream="r")

            def writer1(tc=tc, b=b):
                B.p.op("sync", lambda e: e.dma_start(
                    out=ov[:, :, b * T + tc * 512:b * T + tc * 512 + 512],
                    in_=yf32[:].rearrange("p (a t) -> p a t", a=8)),
                    reads=["s2_yf32"], dma=True, stream="o")
            ln.emit(yf32, "s2_yf32", P4, P5, P2, P3, writer1)


def build_stage2_program(nb=2):
    nc = bass.Bass("TRN2", target_bir_lowering=False)
    with ExitStack() as ctx:
        B = BD(nc, ctx)
        io = {
            "yT": B.dram("yT", (1024, nb * T), F32, kind="ExternalInput"),
            "y2T": B.dram("y2T", (1024, nb * T), F32, kind="ExternalOutput"),
            "w1A": B.dram("w1A", (512, 1024), F32, kind="ExternalInput"),
            "w1B": B.dram("w1B", (512, 4096), F32, kind="ExternalInput"),
            "w2A": B.dram("w2A", (512, 4096), F32, kind="ExternalInput"),
            "w2B": B.dram("w2B", (512, 1024), F32, kind="ExternalInput"),
            "moe_w": B.dram("moe_w", (T, 2), F32, kind="ExternalInput"),
            "moe_i": B.dram("moe_i", (T, 2), I32, kind="ExternalInput"),
            "sel8": B.dram("sel8", (8, 512), F32, kind="ExternalInput"),
        }
        emit_stage2(B, io, nb=nb)
        B.p.barrier()
        B.p.emit(ctx)
    return nc




def emit_stage3(B, io, nb=2, phases=('p1k','p1v','p1f','p2q','p2f','p2t','p2w')):
    nc, p = B.nc, B.p
    NH = 16
    INV2PI = float(1.0 / (2.0 * np.pi))
    C2 = 1.0 / 128.0
    ln = LNPack(B, "s3")
    eye = emit_eye(B, BF16, "s3_eye")
    B.const(MAGIC, -MAGIC, 1e-6, C2, INV2PI, 0.25)

    P0, P1, P2 = B.ps("s3_P0"), B.ps("s3_P1"), B.ps("s3_P2")
    G0, G1, G2, G3 = B.ps("s3_G0"), B.ps("s3_G1"), B.ps("s3_G2"), B.ps("s3_G3")
    PTT = B.ps("s3_PT", (128, 1024), BF16)
    TSB = [G0, G1, G2, G3]
    TSK = ["s3_G0", "s3_G1", "s3_G2", "s3_G3"]

    wA = B.sb("s3_wA", (128, 8 * 1024), BF16)
    wB = B.sb("s3_wB", (128, 8 * 1024), BF16)

    def loadw(dst, key, src):
        p.op("gpsimd", lambda e, dst=dst, src=src: e.dma_start(
            out=dst[:].rearrange("p (a f) -> p a f", a=8),
            in_=io[src].rearrange("(a p) f -> p a f", p=128)),
            writes=[key], dma=True, stream="w")

    rff = B.sb("s3_rff", (128, 256), F32)
    rfbd = B.sb("s3_rfbd", (128, 256), BF16)
    p.op("gpsimd", lambda e: e.memset(rff[:], 0.0), writes=["s3_rff"])
    p.op("sync", lambda e: e.dma_start(out=rff[0:64, 0:128], in_=io["rf"]),
         writes=["s3_rff"], dma=True)
    p.op("sync", lambda e: e.dma_start(out=rff[64:128, 128:256], in_=io["rf"]),
         writes=["s3_rff"], dma=True)
    p.op("vector", lambda e: e.tensor_scalar(rfbd[:], rff[:], INV2PI, None,
                                             OP.mult),
         reads=["s3_rff", "cpool"], writes=["s3_rfbd"])

    biasq = B.sb("s3_biasq", (128, 1), F32)
    if 'nobias' not in phases:
        p.op("gpsimd", lambda e: e.memset(biasq[0:64, :], 0.25), writes=["s3_biasq"])
        p.op("gpsimd", lambda e: e.memset(biasq[64:128, :], 0.0), writes=["s3_biasq"])

    y16 = B.sb("s3_y16", (128, 8 * 512), BF16)
    qk16 = B.sb("s3_qk16", (128, 8 * 512), BF16)
    V16 = B.sb("s3_V16", (128, 16 * 1040), BF16)
    KV16 = B.sb("s3_KV16", (128, 16 * 65), BF16)
    php = B.sb("s3_php", (128, 2048), BF16)
    pt1 = B.sb("s3_pt1", (128, 512), F32)
    pt2 = B.sb("s3_pt2", (128, 512), F32)
    qp16 = B.sb("s3_qp16", (128, 512), BF16)
    zrq = B.sb("s3_zrq", (128, 16), F32)
    ao4 = B.sb("s3_ao4", (128, 4 * 1024), BF16)
    aoT = B.sb("s3_aoT", (128, 8 * 512), BF16)
    xf32 = B.sb("s3_xf32", (128, 8 * 512), F32)

    yv = io["y2T"].rearrange("(a p) t -> p a t", p=128)
    ov = io["y3T"].rearrange("(a p) t -> p a t", p=128)

    HG = [(0, 7), (7, 14), (14, 16)]

    if 'noones' not in phases:
        p.op("gpsimd", lambda e: e.memset(
            rep_ap(V16, [[1040, 16], [65, 16], [1, 1]], 64), C2), writes=["s3_V16o"])

    for b in range(nb):
        t0 = b * T
        loadw(wA, "s3_wA", "Wk")
        loadw(wB, "s3_wB", "Wv")
        for c in range(1 if 'tiny' in phases else 4):
            p.op("gpsimd", lambda e, c=c, t0=t0: e.dma_start(
                out=y16[:].rearrange("p (a t) -> p a t", a=8),
                in_=yv[:, :, t0 + c * 512:t0 + c * 512 + 512]),
                writes=["s3_y16"], dma=True, stream="x")
            for o in range(8) if 'p1k' in phases else []:
                for a in range(8):
                    p.op("tensor", lambda e, a=a, o=o: e.matmul(
                        P0[:], wA[:, a * 1024 + o * 128:a * 1024 + o * 128 + 128],
                        y16[:, a * 512:a * 512 + 512],
                        start=(a == 0), stop=(a == 7)),
                        reads=["s3_wA", "s3_y16"], writes=["s3_P0"])
                p.op("vector", lambda e, o=o: e.tensor_copy(
                    qk16[:, o * 512:o * 512 + 512], P0[:]),
                    reads=["s3_P0"], writes=["s3_qk16"])
            for ts in range(4) if 'p1v' in phases else []:
                tt = c * 4 + ts
                for vc in range(2):
                    for a in range(8):
                        p.op("tensor", lambda e, a=a, ts=ts, vc=vc: e.matmul(
                            P1[:], y16[:, a * 512 + ts * 128:a * 512 + ts * 128 + 128],
                            wB[:, a * 1024 + vc * 512:a * 1024 + vc * 512 + 512],
                            start=(a == 0), stop=(a == 7)),
                            reads=["s3_y16", "s3_wB"], writes=["s3_P1"])
                    p.op("vector", lambda e, tt=tt, vc=vc: e.tensor_scalar(
                        rep_ap(V16, [[65, 8], [1, 64]], tt * 1040 + vc * 520),
                        rep_ap(P1, [[64, 8], [1, 64]], 0), C2, None, OP.mult),
                        reads=["s3_P1", "cpool"], writes=["s3_V16"])
            for ts in (range(1 if 'tiny' in phases else 4) if 'p1f' in phases else []):
                tt = c * 4 + ts
                for hg in range(1 if 'tiny' in phases else 4):
                    for hh in (range(4) if 'noproj' not in phases else []):
                        h = hg * 4 + hh
                        a, pr = h // 2, h % 2
                        p.op("tensor", lambda e, a=a, pr=pr, ts=ts, hh=hh: e.matmul(
                            P2[:, hh * 128:hh * 128 + 128],
                            qk16[:, a * 512 + ts * 128:a * 512 + ts * 128 + 128],
                            rfbd[:, pr * 128:pr * 128 + 128], start=True, stop=True),
                            reads=["s3_qk16", "s3_rfbd"], writes=["s3_P2"])
                    if 'nophi' in phases:
                        continue
                    p.op("scalar", lambda e: e.activation(pt1[:], P2[:], AF.Identity),
                         reads=["s3_P2"], writes=["s3_pt1"])
                    if 'nophi2' not in phases:
                        p.op("vector", lambda e: e.tensor_scalar(
                            rep_ap(pt1, [[128, 4], [1, 64]], 0),
                            rep_ap(pt1, [[128, 4], [1, 64]], 0), 0.25, None, OP.add),
                            reads=["s3_pt1", "cpool"], writes=["s3_pt1"])
                        p.op("vector", lambda e: e.tensor_scalar(
                            pt2[:], pt1[:], MAGIC, None, OP.add),
                            reads=["s3_pt1", "cpool"], writes=["s3_pt2"])
                        p.op("vector", lambda e: e.tensor_scalar(
                            pt2[:], pt2[:], -MAGIC, None, OP.add),
                            reads=["s3_pt2", "cpool"], writes=["s3_pt2"])
                        p.op("vector", lambda e: e.tensor_tensor(
                            pt1[:], pt1[:], pt2[:], OP.subtract),
                            reads=["s3_pt1", "s3_pt2"], writes=["s3_pt1"])
                    p.op("scalar", lambda e, hg=hg: e.activation(
                        php[:, hg * 512:hg * 512 + 512], pt1[:],
                        AF.Sin if 'nosin' not in phases else AF.Identity,
                        scale=float(2.0 * np.pi) if 'nosin' not in phases else 1.0),
                        reads=["s3_pt1"], writes=["s3_php"])
                for h in (range(NH) if 'nokv' not in phases else []):
                    gi_ = 0 if h < 7 else (1 if h < 14 else 2)
                    h0 = HG[gi_][0]
                    sl = (h - h0) * 65
                    first = (c == 0 and ts == 0 and h == h0) or 'shortacc' in phases
                    last = (c == 3 and ts == 3 and h == HG[gi_][1] - 1) or 'shortacc' in phases
                    p.op("tensor", lambda e, h=h, tt=tt, gi_=gi_, sl=sl,
                         first=first, last=last: e.matmul(
                        TSB[gi_][:, sl:sl + 65], php[:, h * 128:h * 128 + 128],
                        V16[:, tt * 1040 + h * 65:tt * 1040 + h * 65 + 65],
                        start=first, stop=last),
                        reads=["s3_php", "s3_V16", "s3_V16o"], writes=[TSK[gi_]])
        for gi_, (h0, h1) in (enumerate(HG) if 'p1f' in phases else []):
            p.op("vector", lambda e, gi_=gi_, h0=h0, h1=h1: e.tensor_copy(
                KV16[:, h0 * 65:h1 * 65], TSB[gi_][:, 0:(h1 - h0) * 65]),
                reads=[TSK[gi_]], writes=["s3_KV16"])
        if b == 0 and "dbg_KV16" in io:
            p.op("sync", lambda e: e.dma_start(out=io["dbg_KV16"], in_=KV16[:]),
                 reads=["s3_KV16"], dma=True, stream="o")

        loadw(wA, "s3_wA", "Wq")
        loadw(wB, "s3_wB", "Wo")
        for c in range(4):
            p.op("gpsimd", lambda e, c=c, t0=t0: e.dma_start(
                out=y16[:].rearrange("p (a t) -> p a t", a=8),
                in_=yv[:, :, t0 + c * 512:t0 + c * 512 + 512]),
                writes=["s3_y16"], dma=True, stream="x")
            for o in range(8) if 'p2q' in phases else []:
                for a in range(8):
                    p.op("tensor", lambda e, a=a, o=o: e.matmul(
                        P0[:], wA[:, a * 1024 + o * 128:a * 1024 + o * 128 + 128],
                        y16[:, a * 512:a * 512 + 512],
                        start=(a == 0), stop=(a == 7)),
                        reads=["s3_wA", "s3_y16"], writes=["s3_P0"])
                p.op("vector", lambda e, o=o: e.tensor_copy(
                    qk16[:, o * 512:o * 512 + 512], P0[:]),
                    reads=["s3_P0"], writes=["s3_qk16"])
            for gi_, (h0, h1) in (enumerate(HG) if 'p2f' in phases else []):
                for h in range(h0, h1):
                    a, pr = h // 2, h % 2
                    p.op("tensor", lambda e, a=a, pr=pr: e.matmul(
                        P2[:], rfbd[:, pr * 128:pr * 128 + 128],
                        qk16[:, a * 512:a * 512 + 512],
                        start=True, stop=True),
                        reads=["s3_qk16", "s3_rfbd"], writes=["s3_P2"])
                    p.op("scalar", lambda e: e.activation(
                        pt1[:], P2[:], AF.Identity, bias=biasq[:, 0:1]),
                        reads=["s3_P2", "s3_biasq"], writes=["s3_pt1"])
                    p.op("vector", lambda e: e.tensor_scalar(
                        pt2[:], pt1[:], MAGIC, None, OP.add),
                        reads=["s3_pt1", "cpool"], writes=["s3_pt2"])
                    p.op("vector", lambda e: e.tensor_scalar(
                        pt2[:], pt2[:], -MAGIC, None, OP.add),
                        reads=["s3_pt2", "cpool"], writes=["s3_pt2"])
                    p.op("vector", lambda e: e.tensor_tensor(
                        pt1[:], pt1[:], pt2[:], OP.subtract),
                        reads=["s3_pt1", "s3_pt2"], writes=["s3_pt1"])
                    p.op("scalar", lambda e: e.activation(
                        qp16[:], pt1[:], AF.Sin, scale=float(2.0 * np.pi)),
                        reads=["s3_pt1"], writes=["s3_qp16"])
                    if b == 0 and c == 0 and h == 0 and "dbg_qp" in io:
                        p.op("sync", lambda e: e.dma_start(out=io["dbg_qp"], in_=qp16[:]),
                             reads=["s3_qp16"], dma=True, stream="o")
                    for ts in range(4):
                        sl = (h - h0) * 65
                        p.op("tensor", lambda e, ts=ts, h=h, sl=sl: e.matmul(
                            TSB[ts][:, sl:sl + 65],
                            qp16[:, ts * 128:ts * 128 + 128],
                            KV16[:, h * 65:h * 65 + 65], start=True, stop=True),
                            reads=["s3_qp16", "s3_KV16"], writes=[TSK[ts]])
                ng = h1 - h0
                for ts in range(4):
                    p.op("vector", lambda e, ts=ts, ng=ng: e.tensor_scalar(
                        rep_ap(TSB[ts], [[65, ng], [1, 1]], 64),
                        rep_ap(TSB[ts], [[65, ng], [1, 1]], 64), 1e-6, None,
                        OP.max), reads=[TSK[ts], "cpool"], writes=[TSK[ts]])
                    p.op("vector", lambda e, ts=ts, ng=ng, h0=h0: e.reciprocal(
                        zrq[:, h0:h0 + ng],
                        rep_ap(TSB[ts], [[65, ng], [1, 1]], 64)),
                        reads=[TSK[ts]], writes=["s3_zrq"])
                    p.op("vector", lambda e, ts=ts, ng=ng, h0=h0: e.tensor_tensor(
                        rep_ap(ao4, [[64, ng], [1, 64]], ts * 1024 + h0 * 64),
                        rep_ap(TSB[ts], [[65, ng], [1, 64]], 0),
                        rep_ap(zrq, [[1, ng], [0, 64]], h0), OP.mult),
                        reads=[TSK[ts], "s3_zrq"], writes=["s3_ao4"])
            if b == 0 and c == 0 and "dbg_ao4" in io:
                p.op("sync", lambda e: e.dma_start(out=io["dbg_ao4"], in_=ao4[:]),
                     reads=["s3_ao4"], dma=True, stream="o")
            for ts in range(4) if 'p2t' in phases else []:
                for g in range(8):
                    p.op("tensor", lambda e, ts=ts, g=g: e.transpose(
                        PTT[:, g * 128:(g + 1) * 128],
                        ao4[:, ts * 1024 + g * 128:ts * 1024 + g * 128 + 128],
                        eye[:]), reads=["s3_ao4", "s3_eye"], writes=["s3_PT"])
                for g in range(8):
                    p.op("vector", lambda e, ts=ts, g=g: e.tensor_copy(
                        aoT[:, g * 512 + ts * 128:g * 512 + ts * 128 + 128],
                        PTT[:, g * 128:(g + 1) * 128]),
                        reads=["s3_PT"], writes=["s3_aoT"])
            if b == 0 and c == 0 and "dbg_aoT" in io:
                p.op("sync", lambda e: e.dma_start(out=io["dbg_aoT"], in_=aoT[:]),
                     reads=["s3_aoT"], dma=True, stream="o")
            if 'p2w' not in phases:
                continue
            p.op("sync", lambda e, c=c, t0=t0: e.dma_start(
                out=xf32[:].rearrange("p (a t) -> p a t", a=8),
                in_=yv[:, :, t0 + c * 512:t0 + c * 512 + 512]),
                writes=["s3_xf32"], dma=True, stream="r")
            for o in range(8):
                ps, pk = ((P0, "s3_P0") if o % 2 == 0 else (P2, "s3_P2"))
                for a in range(8):
                    p.op("tensor", lambda e, a=a, o=o, ps=ps: e.matmul(
                        ps[:], wB[:, a * 1024 + o * 128:a * 1024 + o * 128 + 128],
                        aoT[:, a * 512:a * 512 + 512],
                        start=(a == 0), stop=(a == 7)),
                        reads=["s3_wB", "s3_aoT"], writes=[pk])
                p.op("vector", lambda e, o=o, ps=ps, pk=pk: e.tensor_tensor(
                    xf32[:, o * 512:o * 512 + 512], ps[:],
                    xf32[:, o * 512:o * 512 + 512], OP.add),
                    reads=[pk, "s3_xf32"], writes=["s3_xf32"])

            def writer(c=c, t0=t0):
                B.p.op("sync", lambda e: e.dma_start(
                    out=ov[:, :, t0 + c * 512:t0 + c * 512 + 512],
                    in_=xf32[:].rearrange("p (a t) -> p a t", a=8)),
                    reads=["s3_xf32"], dma=True, stream="o")
            ln.emit(xf32, "s3_xf32", G0, G1, G2, G3, writer)


def build_stage3_program(nb=2, phases=('p1k','p1v','p1f','p2q','p2f','p2t','p2w'), debug=False):
    nc = bass.Bass("TRN2", target_bir_lowering=False)
    with ExitStack() as ctx:
        B = BD(nc, ctx)
        io = {}
        if debug:
            io["dbg_KV16"] = B.dram("dbg_KV16", (128, 16 * 65), mybir.dt.bfloat16, kind="ExternalOutput")
            io["dbg_qp"] = B.dram("dbg_qp", (128, 512), mybir.dt.bfloat16, kind="ExternalOutput")
            io["dbg_ao4"] = B.dram("dbg_ao4", (128, 4096), mybir.dt.bfloat16, kind="ExternalOutput")
            io["dbg_aoT"] = B.dram("dbg_aoT", (128, 4096), mybir.dt.bfloat16, kind="ExternalOutput")
        io.update({
            "y2T": B.dram("y2T", (1024, nb * T), F32, kind="ExternalInput"),
            "y3T": B.dram("y3T", (1024, nb * T), F32, kind="ExternalOutput"),
            "Wq": B.dram("Wq", (1024, 1024), F32, kind="ExternalInput"),
            "Wk": B.dram("Wk", (1024, 1024), F32, kind="ExternalInput"),
            "Wv": B.dram("Wv", (1024, 1024), F32, kind="ExternalInput"),
            "Wo": B.dram("Wo", (1024, 1024), F32, kind="ExternalInput"),
            "rf": B.dram("rf", (64, 128), F32, kind="ExternalInput"),
        })
        emit_stage3(B, io, nb=nb, phases=phases)
        B.p.barrier()
        B.p.emit(ctx)
    return nc




def make_xpos_tables():
    DIM_HEAD, WINDOW = 64, 512
    half = np.arange(0, DIM_HEAD, 2, dtype=np.float64)
    inv_freq = 1.0 / 10000.0 ** (half / DIM_HEAD)
    t = np.arange(T, dtype=np.float64)
    freqs = np.concatenate([t[:, None] * inv_freq[None, :]] * 2, -1)
    base = (half + 0.4 * DIM_HEAD) / (1.4 * DIM_HEAD)
    power = (t - T // 2) / (WINDOW // 2)
    scale = np.concatenate([base[None, :] ** power[:, None]] * 2, -1)
    cos, sin = np.cos(freqs), np.sin(freqs)
    sgn = np.concatenate([-np.ones(32), np.ones(32)])[None, :]
    return ((cos * scale * 8.0).astype(np.float32),
            (sin * scale * 8.0 * sgn).astype(np.float32),
            (cos / scale).astype(np.float32),
            (sin / scale * sgn).astype(np.float32))


def emit_stage4_layer(B, io, l, nb=2, phases=("a", "c"), alim=7, qlim=9):
    nc, p = B.nc, B.p
    H, HID = 8, 2730
    NEG = -1.0e9
    L = f"s4{l}"
    B.const(1e-12, 1e-5)
    eye = emit_eye(B, BF16, L + "_eye")
    eyef = emit_eye(B, F32, L + "_eyef")

    x2T = B.dram(L + "_x2T", (1024, nb * T), F32, kind="Internal")
    mscr = B.dram(L + "_mscr", (32, 128), F32, kind="Internal")
    w1c = B.dram(L + "_w1c", (1024, 5460), BF16, kind="Internal")
    w2c = B.dram(L + "_w2c", (2730, 1024), BF16, kind="Internal")
    p.op("gpsimd", lambda e: e.dma_start(out=w1c, in_=io["w1"]),
         writes=[L + "w1c"], dma=True, stream="w")
    p.op("gpsimd", lambda e: e.dma_start(out=w2c, in_=io["w2"]),
         writes=[L + "w2c"], dma=True, stream="w")

    ones1 = B.sb(L + "_ones1", (1, 128), F32)
    p.op("gpsimd", lambda e: e.memset(ones1[:], 1.0), writes=[L + "_ones1"])
    onesc = B.sb(L + "_onesc", (128, 1), BF16)
    p.op("gpsimd", lambda e: e.memset(onesc[:], 1.0), writes=[L + "_onesc"])
    ones2 = B.sb(L + "_ones2", (128, 64), F32)
    p.op("gpsimd", lambda e: e.memset(ones2[:], 1.0), writes=[L + "_ones2"])

    PQ0, PQ1 = B.ps(L + "_PQ0"), B.ps(L + "_PQ1")
    PA, PBm = B.ps(L + "_PA"), B.ps(L + "_PBm")
    PAV, PZ, PS8 = B.ps(L + "_PAV"), B.ps(L + "_PZ"), B.ps(L + "_PS8")
    PTT = B.ps(L + "_PT", (128, 1024), BF16)
    kq0, kq1 = L + "_PQ0", L + "_PQ1"
    ka, kb = L + "_PA", L + "_PBm"
    kav, kz, k8 = L + "_PAV", L + "_PZ", L + "_PS8"
    kt = L + "_PT"

    lngt = B.sb(L + "_lngt", (128, 16), F32)
    p.op("sync", lambda e: e.dma_start(
        out=lngt[:, 0:8], in_=io["lng"].rearrange("(a p) -> p a", p=128),
        allow_slow_non_contiguous=True),
        writes=[L + "_lngt"], dma=True)
    p.op("sync", lambda e: e.dma_start(
        out=lngt[:, 8:16], in_=io["lnb"].rearrange("(a p) -> p a", p=128),
        allow_slow_non_contiguous=True),
        writes=[L + "_lngt"], dma=True)
    b2t = B.sb(L + "_b2t", (128, 8), F32)
    p.op("sync", lambda e: e.dma_start(
        out=b2t[:], in_=io["b2"].rearrange("(a p) -> p a", p=128),
        allow_slow_non_contiguous=True),
        writes=[L + "_b2t"], dma=True)
    b1t = B.sb(L + "_b1t", (128, 44), F32)
    for hoff, coff in ((0, 0), (HID, 22)):
        p.op("sync", lambda e, hoff=hoff, coff=coff: e.dma_start(
            out=b1t[:, coff:coff + 21],
            in_=io["b1"][hoff:hoff + 2688].rearrange("(a p) -> p a", p=128),
            allow_slow_non_contiguous=True),
            writes=[L + "_b1t"], dma=True)
        p.op("sync", lambda e, hoff=hoff, coff=coff: e.dma_start(
            out=b1t[0:42, coff + 21:coff + 22],
            in_=io["b1"][hoff + 2688:hoff + 2730].rearrange("(p a) -> p a", a=1)),
            writes=[L + "_b1t"], dma=True)
    mA = B.sb(L + "_mA", (128, 128), F32)
    mB = B.sb(L + "_mB", (128, 128), F32)
    for (mt, key, cm, pat) in ((mA, L + "_mA", -1, 1), (mB, L + "_mB", 1, -1)):
        p.op("gpsimd", lambda e, mt=mt: e.memset(mt[:], 0.0), writes=[key])
        p.op("gpsimd", lambda e, mt=mt, cm=cm, pat=pat: e.affine_select(
            out=mt[:], in_=mt[:], compare_op=OP.is_ge, fill=NEG, base=0,
            pattern=[[pat, 128]], channel_multiplier=cm),
            reads=[key], writes=[key])

    xv = io["xT"].rearrange("(a p) t -> p a t", p=128)
    x2v = x2T.rearrange("(a p) t -> p a t", p=128)
    ov = io["oT"].rearrange("(a p) t -> p a t", p=128)

    if "a" in phases:
      with ExitStack() as actx:
        sbA = lambda n, sh, dt=F32: actx.enter_context(
            nc.sbuf_tensor(L + n, list(sh), dt))
        wqkv = sbA("_wqkv", (128, 8 * 1536), BF16)
        wo16 = sbA("_wo16", (128, 4 * 1024), BF16)
        x16u = sbA("_x16u", (128, 4 * 512), BF16)
        x16s = sbA("_x16s", (128, 4 * 512), BF16)
        qv = sbA("_qv", (128, 512), F32)
        sq = sbA("_sq", (128, 512), F32)
        tm1 = sbA("_tm1", (128, 512), F32)
        tm2 = sbA("_tm2", (128, 512), F32)
        qkt = sbA("_qkt", (128, 512), BF16)
        ss = sbA("_ss", (128, 24), F32)
        kaug = sbA("_kaug", (128, 2 * 4096), BF16)
        vR = sbA("_vR", (128, 2 * 4096), BF16)
        qaug = sbA("_qaug", (128, 4096), BF16)
        At = sbA("_At", (128, 8 * 512), BF16)
        Mm = sbA("_Mm", (128, 32), F32)
        mtmp = sbA("_mtmp", (128, 1), F32)
        Msb = sbA("_Msb", (32, 128), F32)
        ztmp = sbA("_ztmp", (128, 512), F32)
        zbcs = sbA("_zbcs", (128, 512), F32)
        aoT = sbA("_aoT", (128, 4 * 512), BF16)
        xf32 = sbA("_xf32", (128, 8 * 512), F32)

        qsr = sbA("_qsr", (1, 128), F32)
        p.op("sync", lambda e: e.dma_start(
            out=qsr[0:1, 0:64], in_=io["qs"].rearrange("(a f) -> a f", a=1)),
            writes=[L + "_qsr"], dma=True)
        p.op("sync", lambda e: e.dma_start(
            out=qsr[0:1, 64:128], in_=io["ks"].rearrange("(a f) -> a f", a=1)),
            writes=[L + "_qsr"], dma=True)
        gb = sbA("_gb", (128, 256), F32)
        p.op("tensor", lambda e: e.matmul(PS8[:, 0:128], ones1[0:1, :], qsr[0:1, :],
                                          start=True, stop=True),
             reads=[L + "_qsr", L + "_ones1"], writes=[k8])
        for (do, so, w) in ((0, 0, 64), (64, 32, 32), (96, 0, 32), (128, 64, 64),
                           (192, 96, 32), (224, 64, 32)):
            p.op("vector", lambda e, do=do, so=so, w=w: e.tensor_copy(
                gb[:, do:do + w], PS8[:, so:so + w]),
                reads=[k8], writes=[L + "_gb"])
        tabs = sbA("_tabs", (128, 4 * 1024), F32)
        ttmp = sbA("_ttmp", (128, 1024), F32)
        for i, (tn, goff) in enumerate((("TAq", 0), ("TBq", 64), ("TAk", 128),
                                        ("TBk", 192))):
            p.op("sync", lambda e, tn=tn: e.dma_start(
                out=ttmp[:].rearrange("p (tt f) -> p tt f", f=64),
                in_=io[tn].rearrange("(tt p) f -> p tt f", p=128)),
                writes=[L + "_ttmp"], dma=True)
            p.op("vector", lambda e, i=i, goff=goff: e.tensor_tensor(
                rep_ap(tabs, [[64, 16], [1, 64]], i * 1024),
                rep_ap(ttmp, [[64, 16], [1, 64]], 0),
                rep_ap(gb, [[0, 16], [1, 64]], goff), OP.mult),
                reads=[L + "_ttmp", L + "_gb"], writes=[L + "_tabs"])



        p.op("gpsimd", lambda e: e.dma_start(
            out=wqkv[:].rearrange("p (a f) -> p a f", a=8),
            in_=io["wqkv"].rearrange("(a p) f -> p a f", p=128)),
            writes=[L + "_wqkv"], dma=True, stream="w")
        p.op("gpsimd", lambda e: e.dma_start(
            out=wo16[:].rearrange("p (a f) -> p a f", a=4),
            in_=io["wo"].rearrange("(a p) f -> p a f", p=128)),
            writes=[L + "_wo16"], dma=True, stream="w")
        p.op("gpsimd", lambda e: e.memset(vR[:], 0.0), writes=[L + "_vR"])
        for slot in range(2):
            p.op("gpsimd", lambda e, slot=slot: e.memset(
                rep_ap(vR, [[1024, 4], [256, 4], [1, 1]], slot * 4096 + 64), 1.0),
                writes=[L + "_vR"])
            p.op("gpsimd", lambda e, slot=slot: e.memset(
                rep_ap(vR, [[1024, 4], [256, 4], [1, 1]], slot * 4096 + 128), 1.0),
                writes=[L + "_vR"])
        p.op("gpsimd", lambda e: e.memset(kaug[64:65, :], 1.0),
             writes=[L + "_kaugone"])

        for b in range(nb):
            t0 = b * T
            for c in range(4):
                cs, po = c % 2, (c - 1) % 2
                p.op("gpsimd", lambda e, t0=t0, c=c: e.dma_start(
                    out=x16u[:].rearrange("p (a t) -> p a t", a=4),
                    in_=xv[:, 0:4, t0 + c * 512:t0 + c * 512 + 512]),
                    writes=[L + "_x16u"], dma=True, stream="x")
                if c == 0:
                    p.op("gpsimd", lambda e, t0=t0: e.dma_start(
                        out=x16s[:].rearrange("p (a t) -> p a t", a=4)[:, :, 1:512],
                        in_=xv[:, 4:8, t0:t0 + 511]),
                        writes=[L + "_x16s"], dma=True, stream="x")
                    p.op("gpsimd", lambda e: e.memset(
                        rep_ap(x16s, [[512, 4], [1, 1]], 0), 0.0),
                        writes=[L + "_x16s"])
                else:
                    p.op("gpsimd", lambda e, t0=t0, c=c: e.dma_start(
                        out=x16s[:].rearrange("p (a t) -> p a t", a=4),
                        in_=xv[:, 4:8, t0 + c * 512 - 1:t0 + c * 512 + 511]),
                        writes=[L + "_x16s"], dma=True, stream="x")

                for ts in range(4):
                    tt = c * 4 + ts
                    for part, fo, ps, pk in (("q", 0, PQ0, kq0),
                                             ("k", 512, PQ1, kq1),
                                             ("v", 1024, PQ0, kq0)):
                        for a in range(8):
                            xb = x16u if a < 4 else x16s
                            xo = (a % 4) * 512 + ts * 128
                            p.op("tensor", lambda e, xb=xb, xo=xo, fo=fo, a=a,
                                 ps=ps: e.matmul(
                                ps[:], xb[:, xo:xo + 128],
                                wqkv[:, a * 1536 + fo:a * 1536 + fo + 512],
                                start=(a == 0), stop=(a == 7)),
                                reads=[L + "_x16u", L + "_x16s", L + "_wqkv"],
                                writes=[pk])
                        if part == "v":
                            p.op("vector", lambda e, ts=ts, cs=cs, ps=ps, pk=pk:
                                 e.tensor_copy(
                                     rep_ap(vR, [[256, 4], [1, 64]],
                                            cs * 4096 + ts * 1024 + 0),
                                     rep_ap(ps, [[128, 4], [1, 64]], 0)),
                                 reads=[pk], writes=[L + "_vR"])
                            p.op("vector", lambda e, ts=ts, cs=cs, ps=ps, pk=pk:
                                 e.tensor_copy(
                                     rep_ap(vR, [[256, 4], [1, 64]],
                                            cs * 4096 + ts * 1024 + 192),
                                     rep_ap(ps, [[128, 4], [1, 64]], 64)),
                                 reads=[pk], writes=[L + "_vR"])
                            continue
                        tq = 0 if part == "q" else 2048
                        if qlim < 1:
                            continue
                        p.op("vector", lambda e, ps=ps, pk=pk: e.tensor_copy(
                            qv[:], ps[:]), reads=[pk], writes=[L + "_qv"])
                        if qlim < 2:
                            continue
                        p.op("scalar", lambda e, ps=ps, pk=pk: e.activation(
                            sq[:], ps[:], AF.Square), reads=[pk],
                            writes=[L + "_sq"])
                        p.op("vector", lambda e: e.tensor_reduce(
                            rep_ap(ss, [[1, 8], [1, 1]], 0),
                            rep_ap(sq, [[64, 8], [1, 64]], 0), axis=AX.X,
                            op=OP.add), reads=[L + "_sq"], writes=[L + "_ss"])
                        p.op("scalar", lambda e: e.activation(
                            ss[:, 8:16], ss[:, 0:8], AF.Sqrt),
                            reads=[L + "_ss"], writes=[L + "_ss2"])
                        p.op("vector", lambda e: e.tensor_scalar(
                            ss[:, 8:16], ss[:, 8:16], 1e-12, None, OP.max),
                            reads=[L + "_ss2", "cpool"], writes=[L + "_ss2"])
                        p.op("vector", lambda e: e.reciprocal(
                            ss[:, 16:24], ss[:, 8:16]),
                            reads=[L + "_ss2"], writes=[L + "_ss3"])
                        if qlim < 3:
                            continue
                        p.op("vector", lambda e: e.tensor_tensor(
                            qv[:], qv[:], rep_ap(ss, [[1, 8], [0, 64]], 16),
                            OP.mult), reads=[L + "_qv", L + "_ss3"],
                            writes=[L + "_qv"])
                        if qlim < 4:
                            continue
                        p.op("vector", lambda e, tq=tq, tt=tt: e.tensor_tensor(
                            tm1[:], qv[:],
                            rep_ap(tabs, [[0, 8], [1, 64]], tq + tt * 64),
                            OP.mult), reads=[L + "_qv", L + "_tabs"],
                            writes=[L + "_tm1"])
                        for half in (0, 1):
                            p.op("vector", lambda e, tq=tq, tt=tt, half=half:
                                 e.tensor_tensor(
                                     rep_ap(tm2, [[64, 8], [1, 32]], half * 32),
                                     rep_ap(qv, [[64, 8], [1, 32]],
                                            (1 - half) * 32),
                                     rep_ap(tabs, [[0, 8], [1, 32]],
                                            tq + 1024 + tt * 64 + half * 32),
                                     OP.mult),
                                 reads=[L + "_qv", L + "_tabs"],
                                 writes=[L + "_tm2"])
                        p.op("vector", lambda e: e.tensor_tensor(
                            qkt[:], tm1[:], tm2[:], OP.add),
                            reads=[L + "_tm1", L + "_tm2"], writes=[L + "_qkt"])
                        if qlim < 5:
                            continue
                        for h in range(H):
                            p.op("tensor", lambda e, h=h: e.transpose(
                                PTT[0:64, h * 128:h * 128 + 128],
                                qkt[:, h * 64:h * 64 + 64], eye[:]),
                                reads=[L + "_qkt", L + "_eye"], writes=[kt])
                        dst, dk, doff = ((qaug, L + "_qaug", ts * 128)
                                         if part == "q" else
                                         (kaug, L + "_kaug",
                                          cs * 4096 + ts * 128))
                        p.op("vector", lambda e, dst=dst, doff=doff:
                             e.tensor_copy(
                                 rep_ap(dst, [[512, 8], [1, 128]], doff,
                                        psl=(0, 64)),
                                 rep_ap(PTT, [[128, 8], [1, 128]], 0,
                                        psl=(0, 64))),
                             reads=[kt], writes=[dk])

                for h in (range(H) if alim >= 2 else []):
                    for it in range(4):
                        parts = []
                        w1_ = 512 - it * 128
                        if c > 0:
                            parts.append((po, it * 128, 0, w1_))
                        if it > 0:
                            parts.append((cs, 0, w1_ if c > 0 else 0, it * 128))
                        nacc = 0
                        for (sl, jo, co, w) in parts:
                            p.op("tensor", lambda e, h=h, it=it, sl=sl, jo=jo,
                                 co=co, w=w: e.matmul(
                                PA[:, co:co + w],
                                qaug[0:64, h * 512 + it * 128:
                                     h * 512 + it * 128 + 128],
                                kaug[0:64, sl * 4096 + h * 512 + jo:
                                     sl * 4096 + h * 512 + jo + w],
                                start=True, stop=True),
                                reads=[L + "_qaug", L + "_kaug"], writes=[ka])
                            nacc += 1
                        p.op("tensor", lambda e, h=h, it=it, cs=cs: e.matmul(
                            PBm[:, 0:128],
                            qaug[0:64, h * 512 + it * 128:
                                 h * 512 + it * 128 + 128],
                            kaug[0:64, cs * 4096 + h * 512 + it * 128:
                                 cs * 4096 + h * 512 + it * 128 + 128],
                            start=True, stop=True),
                            reads=[L + "_qaug", L + "_kaug"], writes=[kb])
                        if c > 0:
                            p.op("vector", lambda e: e.tensor_tensor(
                                PA[:, 0:128], PA[:, 0:128], mA[:], OP.add),
                                reads=[ka, L + "_mA"], writes=[ka])
                        p.op("vector", lambda e: e.tensor_tensor(
                            PBm[:, 0:128], PBm[:, 0:128], mB[:], OP.add),
                            reads=[kb, L + "_mB"], writes=[kb])
                        col = h * 4 + it
                        if parts:
                            wtot = 512 if c > 0 else it * 128
                            p.op("vector", lambda e, wtot=wtot: e.reduce_max(
                                mtmp[:], PA[:, 0:wtot], axis=AX.X,
                                negate=True), reads=[ka], writes=[L + "_ssm"])
                            p.op("vector", lambda e, col=col: e.reduce_max(
                                Mm[:, col:col + 1], PBm[:, 0:128], axis=AX.X,
                                negate=True), reads=[kb], writes=[L + "_Mm"])
                            p.op("vector", lambda e, col=col: e.tensor_tensor(
                                Mm[:, col:col + 1], Mm[:, col:col + 1],
                                mtmp[:], OP.min),
                                reads=[L + "_Mm", L + "_ssm"],
                                writes=[L + "_Mm"])
                        else:
                            p.op("vector", lambda e, col=col: e.reduce_max(
                                Mm[:, col:col + 1], PBm[:, 0:128], axis=AX.X,
                                negate=True), reads=[kb], writes=[L + "_Mm"])
                if alim < 3:
                    continue
                p.op("tensor", lambda e: e.transpose(
                    PZ[0:32, 0:128], Mm[:], eyef[:]),
                    reads=[L + "_Mm", L + "_eyef"], writes=[kz])
                p.op("vector", lambda e: e.tensor_copy(Msb[:], PZ[0:32, 0:128]),
                     reads=[kz], writes=[L + "_Msb"])
                p.op("sync", lambda e: e.dma_start(out=mscr, in_=Msb[:]),
                     reads=[L + "_Msb"], writes=[L + "_mscr"], dma=True,
                     stream="m")
                p.op("gpsimd", lambda e: e.dma_start(
                    out=qaug[64:65, 0:4096],
                    in_=mscr.rearrange("(a x) i -> a (x i)", a=1)),
                    reads=[L + "_mscr"], writes=[L + "_qaugM"], dma=True,
                    stream="m2")

                for h in (range(H) if alim >= 4 else []):
                    wjts = range(0 if c > 0 else 4, 8)
                    for wjt in wjts:
                        sl = po if wjt < 4 else cs
                        jl = wjt % 4
                        pss, psk = ((PA, ka) if wjt % 2 == 0 else (PBm, kb))
                        p.op("tensor", lambda e, h=h, sl=sl, jl=jl, pss=pss:
                             e.matmul(
                                 pss[:], kaug[0:65, sl * 4096 + h * 512
                                              + jl * 128:
                                              sl * 4096 + h * 512 + jl * 128
                                              + 128],
                                 qaug[0:65, h * 512:h * 512 + 512],
                                 start=True, stop=True),
                             reads=[L + "_kaug", L + "_kaugone", L + "_qaug",
                                    L + "_qaugM"], writes=[psk])
                        p.op("scalar", lambda e, wjt=wjt, pss=pss, psk=psk:
                             e.activation(At[:, wjt * 512:wjt * 512 + 512],
                                          pss[:], AF.Exp),
                             reads=[psk], writes=[L + f"_At#{wjt}"])
                        if wjt < 4:
                            p.op("gpsimd", lambda e, wjt=wjt, c=c, jl=jl:
                                 e.affine_select(
                                     out=At[:, wjt * 512:wjt * 512 + 512],
                                     in_=At[:, wjt * 512:wjt * 512 + 512],
                                     compare_op=OP.is_ge, fill=0.0,
                                     base=(jl * 128) - 512 + 512, pattern=[[-1, 512]],
                                     channel_multiplier=1),
                                 reads=[L + f"_At#{wjt}"],
                                 writes=[L + f"_At#{wjt}"])
                        else:
                            p.op("gpsimd", lambda e, wjt=wjt, jl=jl:
                                 e.affine_select(
                                     out=At[:, wjt * 512:wjt * 512 + 512],
                                     in_=At[:, wjt * 512:wjt * 512 + 512],
                                     compare_op=OP.is_ge, fill=0.0,
                                     base=-(jl * 128), pattern=[[1, 512]],
                                     channel_multiplier=-1),
                                 reads=[L + f"_At#{wjt}"],
                                 writes=[L + f"_At#{wjt}"])
                    if alim < 5:
                        continue
                    first = True
                    for wjt in wjts:
                        sl = po if wjt < 4 else cs
                        jl = wjt % 4
                        p.op("tensor", lambda e, h=h, sl=sl, jl=jl, wjt=wjt,
                             first=first, last=(wjt == 7): e.matmul(
                                 PAV[:], vR[:, sl * 4096 + jl * 1024
                                            + h * 128:
                                            sl * 4096 + jl * 1024 + h * 128
                                            + 128],
                                 At[:, wjt * 512:wjt * 512 + 512],
                                 start=first, stop=last),
                             reads=[L + "_vR", L + f"_At#{wjt}"], writes=[kav])
                        first = False
                    if alim < 6:
                        continue
                    zr = 64 if h % 2 == 0 else 0
                    vr_ = 64 - zr
                    p.op("scalar", lambda e, zr=zr: e.activation(
                        ztmp[zr:zr + 1, :], PAV[zr:zr + 1, :], AF.Ln),
                        reads=[kav], writes=[L + "_ztmp"])
                    p.op("scalar", lambda e, zr=zr: e.activation(
                        ztmp[zr:zr + 1, :], ztmp[zr:zr + 1, :], AF.Exp,
                        scale=-1.0), reads=[L + "_ztmp"], writes=[L + "_ztmp"])
                    p.op("tensor", lambda e, zr=zr, vr_=vr_: e.matmul(
                        PZ[vr_:vr_ + 64, :], ones2[zr:zr + 1, :],
                        ztmp[zr:zr + 1, :], start=True, stop=True,
                        tile_position=(zr, vr_)),
                        reads=[L + "_ztmp", L + "_ones2"], writes=[kz])
                    p.op("vector", lambda e, vr_=vr_: e.tensor_copy(
                        zbcs[vr_:vr_ + 64, :], PZ[vr_:vr_ + 64, :]),
                        reads=[kz], writes=[L + "_zbcs"])
                    p.op("vector", lambda e, h=h, vr_=vr_: e.tensor_tensor(
                        aoT[vr_:vr_ + 64, (h // 2) * 512:(h // 2) * 512 + 512],
                        PAV[vr_:vr_ + 64, :], zbcs[vr_:vr_ + 64, :], OP.mult),
                        reads=[kav, L + "_zbcs"], writes=[L + "_aoT"])

                if alim < 7:
                    continue
                p.op("sync", lambda e, t0=t0, c=c: e.dma_start(
                    out=xf32[:].rearrange("p (a t) -> p a t", a=8),
                    in_=xv[:, :, t0 + c * 512:t0 + c * 512 + 512]),
                    writes=[L + "_xf32"], dma=True, stream="r")
                for o in range(8):
                    ps, pk = ((PQ0, kq0) if o % 2 == 0 else (PQ1, kq1))
                    for ra in range(4):
                        p.op("tensor", lambda e, ra=ra, o=o, ps=ps: e.matmul(
                            ps[:], wo16[:, ra * 1024 + o * 128:
                                         ra * 1024 + o * 128 + 128],
                            aoT[:, ra * 512:ra * 512 + 512],
                            start=(ra == 0), stop=(ra == 3)),
                            reads=[L + "_wo16", L + "_aoT"], writes=[pk])
                    p.op("vector", lambda e, o=o, ps=ps, pk=pk: e.tensor_tensor(
                        xf32[:, o * 512:o * 512 + 512], ps[:],
                        xf32[:, o * 512:o * 512 + 512], OP.add),
                        reads=[pk, L + "_xf32"], writes=[L + "_xf32"])
                p.op("sync", lambda e, t0=t0, c=c: e.dma_start(
                    out=x2v[:, :, t0 + c * 512:t0 + c * 512 + 512],
                    in_=xf32[:].rearrange("p (a t) -> p a t", a=8)),
                    reads=[L + "_xf32"], dma=True, stream="o")
      p.barrier()

    if "c" in phases:
      with ExitStack() as cctx:
        sbC = lambda n, sh, dt=F32: cctx.enter_context(
            nc.sbuf_tensor(L + n, list(sh), dt))
        x2c = sbC("_x2c", (128, 8 * 513), F32)
        z16 = sbC("_z16", (128, 2 * 4096), BF16)
        yh16 = sbC("_yh16", (128, 8 * T), BF16)
        hph = sbC("_hph", (128, 22 * 1024), BF16)
        w1tb = [sbC("_w1tA", (128, 16 * 128), BF16),
                sbC("_w1tB", (128, 16 * 128), BF16)]
        w2r = sbC("_w2r", (128, 22 * 1024), BF16)
        g16t = sbC("_g16t", (128, 512), BF16)
        stat = sbC("_stat", (1, 2048), F32)
        ytm = sbC("_ytm", (128, 512), F32)
        xr32 = sbC("_xr32", (128, 8 * 512), F32)

        p.op("sync", lambda e: e.dma_start(
            out=w2r[:].rearrange("p (a f) -> p a f", a=22)[:, 0:21, :],
            in_=w2c[0:2688, :].rearrange("(a p) f -> p a f", p=128)),
            writes=[L + "_w2r"], dma=True, stream="r")
        p.op("sync", lambda e: e.dma_start(
            out=w2r[0:42, 21 * 1024:22 * 1024], in_=w2c[2688:2730, :]),
            writes=[L + "_w2r"], dma=True, stream="r")

        for b in range(nb):
            t0 = b * T
            for c in range(4):
                p.op("sync", lambda e, t0=t0, c=c: e.dma_start(
                    out=x2c[:].rearrange("p (a t) -> p a t", a=8)[:, :, 1:513],
                    in_=x2v[:, :, t0 + c * 512:t0 + c * 512 + 512]),
                    writes=[L + "_x2c"], dma=True, stream="x")
                if c == 0:
                    p.op("gpsimd", lambda e: e.memset(
                        rep_ap(x2c, [[513, 8], [1, 1]], 0), 0.0),
                        writes=[L + "_x2c"])
                else:
                    p.op("sync", lambda e, t0=t0, c=c: e.dma_start(
                        out=x2c[:].rearrange("p (a t) -> p a t", a=8)[:, :, 0:1],
                        in_=x2v[:, :, t0 + c * 512 - 1:t0 + c * 512],
                        allow_slow_non_contiguous=True),
                        writes=[L + "_x2c"], dma=True, stream="x")
                for half, off in ((0, 1), (1, 4 * 513)):
                    p.op("vector", lambda e, half=half, off=off: e.tensor_copy(
                        rep_ap(z16, [[512, 4], [1, 512]], half * 2048),
                        rep_ap(x2c, [[513, 4], [1, 512]], off)),
                        reads=[L + "_x2c"], writes=[L + "_z16lo"])
                    p.op("scalar", lambda e, half=half, off=off: e.activation(
                        rep_ap(z16, [[512, 4], [1, 512]], 4096 + half * 2048),
                        rep_ap(x2c, [[513, 4], [1, 512]], off), AF.Square),
                        reads=[L + "_x2c"], writes=[L + "_z16hi"])
                for a in range(8):
                    p.op("tensor", lambda e, a=a: e.matmul(
                        PAV[0:1, :], onesc[:, 0:1],
                        z16[:, a * 512:a * 512 + 512],
                        start=(a == 0), stop=(a == 7)),
                        reads=[L + "_z16lo", L + "_onesc"], writes=[kav])
                for a in range(8):
                    p.op("tensor", lambda e, a=a: e.matmul(
                        PZ[0:1, :], onesc[:, 0:1],
                        z16[:, 4096 + a * 512:4096 + a * 512 + 512],
                        start=(a == 0), stop=(a == 7)),
                        reads=[L + "_z16hi", L + "_onesc"], writes=[kz])
                mu, m2, var, rstd = (stat[0:1, i * 512:(i + 1) * 512]
                                     for i in range(4))
                p.op("vector", lambda e: e.tensor_scalar(
                    mu, PAV[0:1, :], 1.0 / 1024.0, None, OP.mult),
                    reads=[kav, "cpool"], writes=[L + "_mu"])
                p.op("scalar", lambda e: e.activation(
                    m2, PZ[0:1, :], AF.Identity, scale=1.0 / 1024.0),
                    reads=[kz], writes=[L + "_m2"])
                p.op("vector", lambda e: e.tensor_tensor(var, mu, mu, OP.mult),
                     reads=[L + "_mu"], writes=[L + "_var"])
                p.op("vector", lambda e: e.tensor_tensor(var, m2, var,
                                                         OP.subtract),
                     reads=[L + "_m2", L + "_var"], writes=[L + "_var"])
                p.op("scalar", lambda e: e.activation(var, var, AF.Ln,
                                                      bias=1e-5),
                     reads=[L + "_var", "cpool"], writes=[L + "_var"])
                p.op("scalar", lambda e: e.activation(rstd, var, AF.Exp,
                                                      scale=-0.5),
                     reads=[L + "_var"], writes=[L + "_rstd"])
                p.op("tensor", lambda e: e.matmul(
                    PAV[:], ones1[0:1, :], mu, start=True, stop=True),
                    reads=[L + "_mu", L + "_ones1"], writes=[kav])
                p.op("tensor", lambda e: e.matmul(
                    PZ[:], ones1[0:1, :], rstd, start=True, stop=True),
                    reads=[L + "_rstd", L + "_ones1"], writes=[kz])
                for a in range(8):
                    off = a * 513 + (1 if a < 4 else 0)
                    p.op("vector", lambda e, a=a, off=off: e.tensor_tensor(
                        ytm[:], rep_ap(x2c, [[1, 512]], off),
                        PAV[:], OP.subtract),
                        reads=[L + "_x2c", kav], writes=[L + "_ytm"])
                    p.op("vector", lambda e: e.tensor_tensor(
                        ytm[:], ytm[:], PZ[:], OP.mult),
                        reads=[L + "_ytm", kz], writes=[L + "_ytm"])
                    p.op("scalar", lambda e, a=a, c=c: e.activation(
                        yh16[:, a * T + c * 512:a * T + c * 512 + 512],
                        ytm[:], AF.Identity, scale=lngt[:, a:a + 1],
                        bias=lngt[:, 8 + a:8 + a + 1]),
                        reads=[L + "_ytm", L + "_lngt"], writes=[L + "_yh16"])
            for hf in range(2):
                for fp in range(22):
                    wid = 128 if fp < 21 else 42
                    w1t = w1tb[fp % 2]
                    wk = L + "_w1t" + str(fp % 2)
                    wstr = "r" if fp % 2 == 0 else "r2"
                    for gpart, coff in ((0, 0), (1, HID)):
                        p.op("sync", lambda e, fp=fp, wid=wid, gpart=gpart,
                             coff=coff, w1t=w1t: e.dma_start(
                            out=w1t[:].rearrange("p (a f) -> p a f", a=16)
                                [:, gpart * 8:gpart * 8 + 8, 0:wid],
                            in_=w1c[:, coff + fp * 128:coff + fp * 128 + wid]
                                .rearrange("(a p) f -> p a f", p=128)),
                            writes=[wk], dma=True, stream=wstr)
                    for cc in range(2):
                        c = hf * 2 + cc
                        for gpart, ps, pk in ((0, PA, ka), (1, PBm, kb)):
                            for a in range(8):
                                p.op("tensor", lambda e, a=a, gpart=gpart,
                                     wid=wid, c=c, ps=ps, w1t=w1t: e.matmul(
                                    ps[0:wid, :],
                                    w1t[:, (gpart * 8 + a) * 128:
                                        (gpart * 8 + a) * 128 + wid],
                                    yh16[:, a * T + c * 512:a * T + c * 512
                                         + 512],
                                    start=(a == 0), stop=(a == 7)),
                                    reads=[wk, L + "_yh16"],
                                    writes=[pk])
                        p.op("scalar", lambda e, fp=fp, wid=wid: e.activation(
                            g16t[0:wid, :], PBm[0:wid, :], AF.Gelu,
                            bias=b1t[0:wid, 22 + fp:22 + fp + 1]),
                            reads=[kb, L + "_b1t"], writes=[L + "_g16t"])
                        p.op("vector", lambda e, fp=fp, wid=wid, cc=cc:
                             e.scalar_tensor_tensor(
                                 hph[0:wid, fp * 1024 + cc * 512:
                                     fp * 1024 + cc * 512 + 512],
                                 PA[0:wid, :], b1t[0:wid, fp:fp + 1],
                                 g16t[0:wid, :], OP.add, OP.mult),
                             reads=[ka, L + "_g16t", L + "_b1t"],
                             writes=[L + f"_hph#{fp}"])
                for cc in range(2):
                    c = hf * 2 + cc
                    p.op("sync", lambda e, t0=t0, c=c: e.dma_start(
                        out=xr32[:].rearrange("p (a t) -> p a t", a=8),
                        in_=x2v[:, :, t0 + c * 512:t0 + c * 512 + 512]),
                        writes=[L + "_xr32"], dma=True, stream="x")
                    for o in range(8):
                        ps, pk = ((PQ0, kq0) if o % 2 == 0 else (PQ1, kq1))
                        for fp in range(22):
                            wid = 128 if fp < 21 else 42
                            p.op("tensor", lambda e, fp=fp, wid=wid, o=o,
                                 cc=cc, ps=ps: e.matmul(
                                ps[:], w2r[0:wid, fp * 1024 + o * 128:
                                           fp * 1024 + o * 128 + 128],
                                hph[0:wid, fp * 1024 + cc * 512:
                                    fp * 1024 + cc * 512 + 512],
                                start=(fp == 0), stop=(fp == 21)),
                                reads=[L + "_w2r", L + f"_hph#{fp}"],
                                writes=[pk])
                        p.op("vector", lambda e, o=o, ps=ps, pk=pk:
                             e.scalar_tensor_tensor(
                                 xr32[:, o * 512:o * 512 + 512], ps[:],
                                 b2t[:, o:o + 1],
                                 xr32[:, o * 512:o * 512 + 512],
                                 OP.add, OP.add),
                             reads=[pk, L + "_b2t", L + "_xr32"],
                             writes=[L + "_xr32"])
                    p.op("sync", lambda e, t0=t0, c=c: e.dma_start(
                        out=ov[:, :, t0 + c * 512:t0 + c * 512 + 512],
                        in_=xr32[:].rearrange("p (a t) -> p a t", a=8)),
                        reads=[L + "_xr32"], dma=True, stream="o")
      p.barrier()


def build_stage4_program(l=0, nb=2, phases=("a", "c"), alim=7, qlim=9):
    nc = bass.Bass("TRN2", target_bir_lowering=False)
    with ExitStack() as ctx:
        B = BD(nc, ctx)
        io = {
            "xT": B.dram("xT", (1024, nb * T), F32, kind="ExternalInput"),
            "oT": B.dram("oT", (1024, nb * T), F32, kind="ExternalOutput"),
            "wqkv": B.dram("wqkv", (1024, 1536), F32, kind="ExternalInput"),
            "qs": B.dram("qs", (64,), F32, kind="ExternalInput"),
            "ks": B.dram("ks", (64,), F32, kind="ExternalInput"),
            "wo": B.dram("wo", (512, 1024), F32, kind="ExternalInput"),
            "lng": B.dram("lng", (1024,), F32, kind="ExternalInput"),
            "lnb": B.dram("lnb", (1024,), F32, kind="ExternalInput"),
            "w1": B.dram("w1", (1024, 5460), F32, kind="ExternalInput"),
            "b1": B.dram("b1", (5460,), F32, kind="ExternalInput"),
            "w2": B.dram("w2", (2730, 1024), F32, kind="ExternalInput"),
            "b2": B.dram("b2", (1024,), F32, kind="ExternalInput"),
            "TAq": B.dram("TAq", (T, 64), F32, kind="ExternalInput"),
            "TBq": B.dram("TBq", (T, 64), F32, kind="ExternalInput"),
            "TAk": B.dram("TAk", (T, 64), F32, kind="ExternalInput"),
            "TBk": B.dram("TBk", (T, 64), F32, kind="ExternalInput"),
        }
        emit_stage4_layer(B, io, l, nb=nb, phases=phases, alim=alim, qlim=qlim)
        B.p.barrier()
        B.p.emit(ctx)
    return nc




def emit_ingest(B, io, nb=2):
    nc, p = B.nc, B.p
    eyef = emit_eye(B, F32, "in_eyef")
    xin = [B.sb("in_x0", (128, 1024), F32), B.sb("in_x1", (128, 1024), F32)]
    xts = [B.sb("in_xt0", (128, 8 * 128), F32),
           B.sb("in_xt1", (128, 8 * 128), F32)]
    PT = [B.ps("in_PT0"), B.ps("in_PT1")]
    xo = io["xT"].rearrange("(a p) t -> p a t", p=128)
    for b in range(nb):
        for tt in range(16):
            q = tt % 2
            xi, xt, pt = xin[q], xts[q], PT[q]
            kx, kt, kp = f"in_x{q}", f"in_xt{q}", f"in_PT{q}"
            p.op("sync", lambda e, b=b, tt=tt, xi=xi: e.dma_start(
                out=xi[:], in_=io["x"][b, tt * 128:tt * 128 + 128, :]),
                writes=[kx], dma=True, stream="x" + str(q))
            for a in range(8):
                p.op("tensor", lambda e, a=a, xi=xi, pt=pt: e.transpose(
                    pt[:, 0:128], xi[:, a * 128:a * 128 + 128], eyef[:]),
                    reads=[kx, "in_eyef"], writes=[kp])
                p.op("vector", lambda e, a=a, xt=xt, pt=pt: e.tensor_copy(
                    xt[:, a * 128:a * 128 + 128], pt[:, 0:128]),
                    reads=[kp], writes=[kt])
            p.op("sync", lambda e, b=b, tt=tt, xt=xt: e.dma_start(
                out=xo[:, :, b * T + tt * 128:b * T + tt * 128 + 128],
                in_=xt[:].rearrange("p (a t) -> p a t", a=8)),
                reads=[kt], dma=True, stream="o" + str(q))


def emit_final(B, io, nb=2):
    nc, p = B.nc, B.p
    eyef = emit_eye(B, F32, "fi_eyef")
    ln = LNPack(B, "fi")
    av = io["aT"].rearrange("(a p) t -> p a t", p=128)
    bv = io["bT"].rearrange("(a p) t -> p a t", p=128)
    zs = [B.sb("fi_z0", (128, 8 * 512), F32), B.sb("fi_z1", (128, 8 * 512), F32)]
    zbs = [B.sb("fi_zb0", (128, 8 * 512), F32),
           B.sb("fi_zb1", (128, 8 * 512), F32)]
    ots = [B.sb("fi_ot0", (128, 8 * 128), F32),
           B.sb("fi_ot1", (128, 8 * 128), F32)]
    P0, P1, P2, P3 = (B.ps(f"fi_P{i}") for i in range(4))
    PT = B.ps("fi_PT")
    for b in range(nb):
        for c in range(4):
            q = c % 2
            z, zb, ot = zs[q], zbs[q], ots[q]
            kz, kzb, kot = f"fi_z{q}", f"fi_zb{q}", f"fi_ot{q}"
            p.op("sync", lambda e, b=b, c=c, z=z: e.dma_start(
                out=z[:].rearrange("p (a t) -> p a t", a=8),
                in_=av[:, :, b * T + c * 512:b * T + c * 512 + 512]),
                writes=[kz], dma=True, stream="x" + str(q))
            p.op("sync", lambda e, b=b, c=c, zb=zb: e.dma_start(
                out=zb[:].rearrange("p (a t) -> p a t", a=8),
                in_=bv[:, :, b * T + c * 512:b * T + c * 512 + 512]),
                writes=[kzb], dma=True, stream="r" + str(q))
            p.op("vector", lambda e, z=z, zb=zb: e.tensor_tensor(
                z[:], z[:], zb[:], OP.add),
                reads=[kz, kzb], writes=[kz])

            def writer(b=b, c=c, z=z, ot=ot, kz=kz, kot=kot, q=q):
                for ts in range(4):
                    for a in range(8):
                        B.p.op("tensor", lambda e, ts=ts, a=a, z=z: e.transpose(
                            PT[:, 0:128],
                            z[:, a * 512 + ts * 128:a * 512 + ts * 128 + 128],
                            eyef[:]), reads=[kz, "fi_eyef"],
                            writes=["fi_PT"])
                        B.p.op("vector", lambda e, a=a, ot=ot: e.tensor_copy(
                            ot[:, a * 128:a * 128 + 128], PT[:, 0:128]),
                            reads=["fi_PT"], writes=[kot])
                    B.p.op("sync", lambda e, b=b, c=c, ts=ts, ot=ot: e.dma_start(
                        out=io["y"][b, c * 512 + ts * 128:
                                    c * 512 + ts * 128 + 128, :],
                        in_=ot[:]), reads=[kot], dma=True, stream="o" + str(q))
            ln.emit(z, kz, P0, P1, P2, P3, writer)


def build_ingest_program(nb=2):
    nc = bass.Bass("TRN2", target_bir_lowering=False)
    with ExitStack() as ctx:
        B = BD(nc, ctx)
        io = {"x": B.dram("x", (nb, T, 1024), F32, kind="ExternalInput"),
              "xT": B.dram("xT", (1024, nb * T), F32, kind="ExternalOutput")}
        emit_ingest(B, io, nb=nb)
        B.p.barrier()
        B.p.emit(ctx)
    return nc


def build_final_program(nb=2):
    nc = bass.Bass("TRN2", target_bir_lowering=False)
    with ExitStack() as ctx:
        B = BD(nc, ctx)
        io = {"aT": B.dram("aT", (1024, nb * T), F32, kind="ExternalInput"),
              "bT": B.dram("bT", (1024, nb * T), F32, kind="ExternalInput"),
              "y": B.dram("y", (nb, T, 1024), F32, kind="ExternalOutput")}
        emit_final(B, io, nb=nb)
        B.p.barrier()
        B.p.emit(ctx)
    return nc


def build_full_program(nb=2, act_kind="Internal"):
    nc = bass.Bass("TRN2", target_bir_lowering=False)
    with ExitStack() as octx:
        B = BD(nc, octx)
        d = {}
        d["x"] = B.dram("x", (nb, T, 1024), F32, kind="ExternalInput")
        d["y"] = B.dram("y", (nb, T, 1024), F32, kind="ExternalOutput")
        for name, shape in (
                ("L0Wp", (1024, 40)), ("L0bp", (40,)), ("L0Wv", (1024, 1024)),
                ("L0Wo", (1024, 1024)),
                ("F0w1A", (512, 1024)), ("F0w1B", (512, 4096)),
                ("F0w2A", (512, 4096)), ("F0w2B", (512, 1024)),
                ("moe_w", (T, 2)), ("sel8", (8, 512)),
                ("L1Wq", (1024, 1024)), ("L1Wk", (1024, 1024)),
                ("L1Wv", (1024, 1024)), ("L1Wo", (1024, 1024)),
                ("rf", (64, 128)),
                ("TAq", (T, 64)), ("TBq", (T, 64)),
                ("TAk", (T, 64)), ("TBk", (T, 64))):
            d[name] = B.dram(name, shape, F32, kind="ExternalInput")
        d["moe_i"] = B.dram("moe_i", (T, 2), I32, kind="ExternalInput")
        for l in range(2):
            for name, shape in (
                    ("wqkv", (1024, 1536)), ("qs", (64,)), ("ks", (64,)),
                    ("wo", (512, 1024)), ("lng", (1024,)), ("lnb", (1024,)),
                    ("w1", (1024, 5460)), ("b1", (5460,)),
                    ("w2", (2730, 1024)), ("b2", (1024,))):
                d[f"w4{l}_{name}"] = B.dram(f"w4{l}_{name}", shape, F32,
                                            kind="ExternalInput")
        act = {n: B.dram(n, (1024, nb * T), F32, kind=act_kind)
               for n in ("xT", "yT", "y2T", "y3T", "z0T", "z1T")}

        def stage(fn):
            with ExitStack() as sctx:
                B.ctx = sctx
                fn()
                B.p.barrier()
            B.ctx = octx

        stage(lambda: emit_ingest(B, {"x": d["x"], "xT": act["xT"]}, nb=nb))
        stage(lambda: emit_stage1(B, {
            "xT": act["xT"], "yT": act["yT"], "Wp": d["L0Wp"],
            "bp": d["L0bp"], "Wv": d["L0Wv"], "Wo": d["L0Wo"]}, nb=nb))
        stage(lambda: emit_stage2(B, {
            "yT": act["yT"], "y2T": act["y2T"], "w1A": d["F0w1A"],
            "w1B": d["F0w1B"], "w2A": d["F0w2A"], "w2B": d["F0w2B"],
            "moe_w": d["moe_w"], "moe_i": d["moe_i"], "sel8": d["sel8"]},
            nb=nb))
        stage(lambda: emit_stage3(B, {
            "y2T": act["y2T"], "y3T": act["y3T"], "Wq": d["L1Wq"],
            "Wk": d["L1Wk"], "Wv": d["L1Wv"], "Wo": d["L1Wo"],
            "rf": d["rf"]}, nb=nb))
        for l in range(2):
            src = act["y3T"] if l == 0 else act["z0T"]
            dst = act["z0T"] if l == 0 else act["z1T"]
            io4 = {"xT": src, "oT": dst,
                   "TAq": d["TAq"], "TBq": d["TBq"],
                   "TAk": d["TAk"], "TBk": d["TBk"]}
            for name in ("wqkv", "qs", "ks", "wo", "lng", "lnb",
                         "w1", "b1", "w2", "b2"):
                io4[name] = d[f"w4{l}_{name}"]
            stage(lambda l=l, io4=io4: emit_stage4_layer(B, io4, l, nb=nb))
        stage(lambda: emit_final(B, {
            "aT": act["y3T"], "bT": act["z1T"], "y": d["y"]}, nb=nb))

        B.p.emit(octx)
    return nc


_PROG_CACHE = {}


def _get_prog(name, builder):
    if name not in _PROG_CACHE:
        _PROG_CACHE[name] = builder()
    return _PROG_CACHE[name]


def _make_in_maps(inp):
    cores = list(range(NCORES))
    tabs = make_xpos_tables()
    sel8 = np.zeros((8, 512), np.float32)
    for ra in range(4):
        for j in range(128):
            sel8[2 * ra + (j >= 64), ra * 128 + j] = 1.0

    x = inp["L0_attn_x"]
    zz = np.zeros((T, 2), np.float32)
    zi = np.zeros((T, 2), np.int32)

    base = {
        "L0Wp": inp["L0_attn_Wp"], "L0bp": inp["L0_attn_bp"],
        "L0Wv": inp["L0_attn_Wv"], "L0Wo": inp["L0_attn_Wo"],
        "F0w1A": inp["L0_ffn_w1_A"], "F0w1B": inp["L0_ffn_w1_B"],
        "F0w2A": inp["L0_ffn_w2_A"], "F0w2B": inp["L0_ffn_w2_B"],
        "sel8": sel8,
        "L1Wq": inp["L1_attn_Wq"], "L1Wk": inp["L1_attn_Wk"],
        "L1Wv": inp["L1_attn_Wv"], "L1Wo": inp["L1_attn_Wo"],
        "rf": inp["L1_attn_rf"],
        "TAq": tabs[0], "TBq": tabs[1], "TAk": tabs[2], "TBk": tabs[3],
    }
    for l in range(2):
        for name, key in (("wqkv", "L1_ffn_wqkv"), ("qs", "L1_ffn_q_scale"),
                          ("ks", "L1_ffn_k_scale"), ("wo", "L1_ffn_wo"),
                          ("lng", "L1_ffn_ln_g"), ("lnb", "L1_ffn_ln_b"),
                          ("w1", "L1_ffn_w1"), ("b1", "L1_ffn_b1"),
                          ("w2", "L1_ffn_w2"), ("b2", "L1_ffn_b2")):
            base[f"w4{l}_{name}"] = np.ascontiguousarray(inp[key][l])

    in_maps = []
    for c in cores:
        m = dict(base)
        m["x"] = np.ascontiguousarray(x[[c, c + 8]])
        m["moe_w"] = (np.ascontiguousarray(
            inp["L0_ffn_expert_weights"][c * T:(c + 1) * T])
            if c < 4 else zz)
        m["moe_i"] = (np.ascontiguousarray(
            inp["L0_ffn_expert_indices"][c * T:(c + 1) * T].astype(np.int32))
            if c < 4 else zi)
        in_maps.append(m)
    return in_maps


def kernel(**inputs):
    inp = {k: np.ascontiguousarray(np.asarray(v)) for k, v in inputs.items()}
    nb = 2
    cores = list(range(NCORES))
    tabs = make_xpos_tables()
    sel8 = np.zeros((8, 512), np.float32)
    for ra in range(4):
        for j in range(128):
            sel8[2 * ra + (j >= 64), ra * 128 + j] = 1.0

    x = inp["L0_attn_x"]
    per_core_x = [np.ascontiguousarray(x[[c, c + 8]]) for c in cores]

    nc = _get_prog("ingest", lambda: build_ingest_program(nb=nb))
    res = run_bass_kernel_spmd(nc, [{"x": per_core_x[c]} for c in cores],
                               core_ids=cores)
    xT = [r["xT"] for r in res.results]

    nc = _get_prog("s1", lambda: build_stage1_program(nb=nb))
    res = run_bass_kernel_spmd(nc, [
        {"xT": xT[c], "Wp": inp["L0_attn_Wp"], "bp": inp["L0_attn_bp"],
         "Wv": inp["L0_attn_Wv"], "Wo": inp["L0_attn_Wo"]} for c in cores],
        core_ids=cores)
    yT = [r["yT"] for r in res.results]

    zz = np.zeros((T, 2), np.float32)
    zi = np.zeros((T, 2), np.int32)
    nc = _get_prog("s2", lambda: build_stage2_program(nb=nb))
    res = run_bass_kernel_spmd(nc, [
        {"yT": yT[c], "w1A": inp["L0_ffn_w1_A"], "w1B": inp["L0_ffn_w1_B"],
         "w2A": inp["L0_ffn_w2_A"], "w2B": inp["L0_ffn_w2_B"],
         "moe_w": (np.ascontiguousarray(
             inp["L0_ffn_expert_weights"][c * T:(c + 1) * T])
             if c < 4 else zz),
         "moe_i": (np.ascontiguousarray(
             inp["L0_ffn_expert_indices"][c * T:(c + 1) * T].astype(np.int32))
             if c < 4 else zi),
         "sel8": sel8} for c in cores], core_ids=cores)
    y2T = [r["y2T"] for r in res.results]

    nc = _get_prog("s3", lambda: build_stage3_program(nb=nb))
    res = run_bass_kernel_spmd(nc, [
        {"y2T": y2T[c], "Wq": inp["L1_attn_Wq"], "Wk": inp["L1_attn_Wk"],
         "Wv": inp["L1_attn_Wv"], "Wo": inp["L1_attn_Wo"],
         "rf": inp["L1_attn_rf"]} for c in cores], core_ids=cores)
    y3T = [r["y3T"] for r in res.results]

    cur = y3T
    for l in range(2):
        nc = _get_prog(f"s4_{l}", lambda l=l: build_stage4_program(l=l, nb=nb))
        res = run_bass_kernel_spmd(nc, [
            {"xT": cur[c],
             "wqkv": np.ascontiguousarray(inp["L1_ffn_wqkv"][l]),
             "qs": np.ascontiguousarray(inp["L1_ffn_q_scale"][l]),
             "ks": np.ascontiguousarray(inp["L1_ffn_k_scale"][l]),
             "wo": np.ascontiguousarray(inp["L1_ffn_wo"][l]),
             "lng": np.ascontiguousarray(inp["L1_ffn_ln_g"][l]),
             "lnb": np.ascontiguousarray(inp["L1_ffn_ln_b"][l]),
             "w1": np.ascontiguousarray(inp["L1_ffn_w1"][l]),
             "b1": np.ascontiguousarray(inp["L1_ffn_b1"][l]),
             "w2": np.ascontiguousarray(inp["L1_ffn_w2"][l]),
             "b2": np.ascontiguousarray(inp["L1_ffn_b2"][l]),
             "TAq": tabs[0], "TBq": tabs[1], "TAk": tabs[2], "TBk": tabs[3]}
            for c in cores], core_ids=cores)
        cur = [r["oT"] for r in res.results]

    nc = _get_prog("final", lambda: build_final_program(nb=nb))
    res = run_bass_kernel_spmd(nc, [
        {"aT": y3T[c], "bT": cur[c]} for c in cores], core_ids=cores)
    ys = [r["y"] for r in res.results]

    out = np.zeros((16, T, 1024), np.float32)
    for c in cores:
        out[c] = ys[c][0]
        out[c + 8] = ys[c][1]
    return out

